# version 07
# baseline (speedup 1.0000x reference)
import os
import numpy as np
import ml_dtypes
import concourse.bass as bass
import concourse.mybir as mybir
from concourse.bass_utils import run_bass_kernel_spmd

F32 = mybir.dt.float32
BF16 = mybir.dt.bfloat16
I32 = mybir.dt.int32
AF = mybir.ActivationFunctionType
ALU = mybir.AluOpType
BF = ml_dtypes.bfloat16
AXX = mybir.AxisListType.X

B, S, E = 4, 4096, 2048
NH, NKV, HD = 8, 4, 256
QSCALE = 256 ** (-0.5)
EPS_QK = 1e-6
LN_EPS = 1e-5
EPS_NORM = 1e-6
EPS_SCALE = 1e-5
NLOC = 3072
KB = 1024

DEBUG = int(os.environ.get("MK_DEBUG", "0"))


class TView:

    def __init__(self, t, off, width):
        self.t = t
        self.off = off
        self.width = width

    def __getitem__(self, key):
        if isinstance(key, tuple):
            r, c = key
            start = (c.start or 0) + self.off
            stop = (c.stop if c.stop is not None else self.width) + self.off
            return self.t[r, start:stop]
        return self.t[key, self.off:self.off + self.width]


class Sched:
    ENGS = ("sync", "scalar", "vector", "tensor", "gpsimd")

    def __init__(self):
        self.prog = {e: [] for e in self.ENGS}
        self.C = {}
        self.hw = {e: {} for e in self.ENGS}
        self.pend = {}

    def count(self, sem):
        return self.C.get(sem, 0)

    DEFSEM = {"vector": "s_v", "scalar": "s_s"}

    def op(self, eng, fn, wait=(), inc=None, inc_by=1):
        if inc is None and eng in self.DEFSEM:
            inc = self.DEFSEM[eng]
        waits = list(wait)
        if self.pend.pop(eng, None):
            dsem = self.DEFSEM[eng]
            waits.append((dsem, self.count(dsem)))
        wl = []
        for sem, val in waits:
            if val > self.hw[eng].get(sem, 0):
                wl.append((sem, val))
                self.hw[eng][sem] = val
        if inc is not None:
            self.C[inc] = self.count(inc) + inc_by

        def emit(e, sems, waits=wl, fn=fn, inc=inc, inc_by=inc_by):
            for sem, val in waits:
                e.wait_ge(sems[sem], val)
            instr = fn(e)
            if inc is not None:
                instr.then_inc(sems[inc], inc_by)

        self.prog[eng].append(emit)
        return self.count(inc) if inc is not None else None

    def drain(self, eng):
        self.pend[eng] = True


def build_nc():
    nc = bass.Bass("TRN2", target_bir_lowering=False)

    xT_d = nc.dram_tensor("xT", [E, NLOC], F32, kind="ExternalInput").ap()
    wq_d = nc.dram_tensor("wq", [E, 2048], F32, kind="ExternalInput").ap()
    wk_d = nc.dram_tensor("wk", [E, 1024], F32, kind="ExternalInput").ap()
    wv_d = nc.dram_tensor("wv", [E, 1024], F32, kind="ExternalInput").ap()
    wo_d = nc.dram_tensor("wo", [E, 2048], F32, kind="ExternalInput").ap()
    wt_d = nc.dram_tensor("wt", [E, 2048], F32, kind="ExternalInput").ap()
    qs2_d = nc.dram_tensor("qs2", [128, 2], F32, kind="ExternalInput").ap()
    ks2_d = nc.dram_tensor("ks2", [128, 2], F32, kind="ExternalInput").ap()
    gam_d = nc.dram_tensor("gam", [128, 16], F32, kind="ExternalInput").ap()
    cos_d = nc.dram_tensor("cosT", [128, NLOC], F32, kind="ExternalInput").ap()
    sin_d = nc.dram_tensor("sinT", [128, NLOC], F32, kind="ExternalInput").ap()
    masks_d = nc.dram_tensor("masks", [48, 128, 512], BF16, kind="ExternalInput").ap()
    ones_d = nc.dram_tensor("ones128", [128, 128], BF16, kind="ExternalInput").ap()
    id_d = nc.dram_tensor("ident128", [128, 128], F32, kind="ExternalInput").ap()
    out_d = nc.dram_tensor("out", [E, 2048], F32, kind="ExternalOutput").ap()
    x1q0_d = nc.dram_tensor("x1q0", [E, 1024], F32, kind="Internal").ap()
    ccin_d = nc.dram_tensor("ccin", [1, 16], F32, kind="Internal").ap()
    bc_d = nc.dram_tensor("bc_d", [1, 16], F32, kind="Internal").ap()
    ccout_d = nc.dram_tensor("ccout", [1, 16], F32, kind="Internal").ap()

    dbg = {}
    if DEBUG:
        def dbgt(name, shape, dt=F32):
            dbg[name] = nc.dram_tensor("dbg_" + name, shape, dt,
                                       kind="ExternalOutput").ap()
        dbgt("r1", [128, NLOC]); dbgt("k00", [128, 2048], BF16)
        dbgt("v8", [128, 1024], BF16)
        dbgt("q0", [128, 1024], BF16); dbgt("q1", [128, 1024], BF16)
        dbgt("pt", [128, 12 * 512], BF16)
        dbgt("attn0", [128, 1024], BF16); dbgt("attn1", [128, 1024], BF16)
        dbgt("x1c0", [128, 1024]); dbgt("m2", [128, 2048]); dbgt("rr", [128, 2048])
        dbgt("sS", [1, 16]); dbgt("xq0", [128, 512], BF16); dbgt("wq0", [128, 2048], BF16)
        dbgt("ss_k", [128, 512]); dbgt("F_k", [128, 512]); dbgt("r1sq_k", [128, 512])
        dbgt("maskb", [128, 12 * 512], BF16); dbgt("pta", [128, 512], BF16)
        dbgt("smalls", [128, 16]); dbgt("wabs", [128, 16])
        dbgt("wcol", [128, 1]); dbgt("wrowT", [1, 128])
        dbgt("sums", [128, 512]); dbgt("ttmp", [128, 512])
        dbgt("kstar", [128, 512])

    SBUF_BASE = 16512

    def at(name, shape, dt, off):
        return nc.alloc_sbuf_tensor_at(name, shape, dt, offset=SBUF_BASE + off)

    o = [0]

    def nxt(sz):
        off = (o[0] + 31) // 32 * 32
        o[0] = off + sz
        return off

    ones = at("ones", [128, 128], BF16, nxt(256))
    onesf = at("onesf", [128, 128], F32, nxt(512))
    ident = at("ident", [128, 128], F32, nxt(512))
    qs16 = at("qs16", [128, 2], F32, nxt(32))
    ks16 = at("ks16", [128, 2], F32, nxt(32))
    gam16 = at("gam16", [128, 16], F32, nxt(64))
    smalls = at("smalls", [128, 16], F32, nxt(64))
    SM = dict(runmax=0, lmax=1, Scol=2, rS127=3, swcol=4, halfsw=5,
              neghalfsw=6, fscale=7, swrow=8, eps5=10, eps6=11, rmx=12)
    ccs = at("ccs", [1, 16], F32, nxt(64))
    _mo_off = (o[0] + 31) // 32 * 32
    mrow_own = at("mrow_own", [1, 1024], BF16, nxt(2048))
    dbgw_s = at("dbgw_s", [1, 128], F32, _mo_off)
    r1col = at("r1col", [128, 16], F32, nxt(64))
    wabs = at("wabs", [128, 16], F32, nxt(64))
    wcol = at("wcol", [128, 1], F32, nxt(32))
    wrow = at("wrow", [1, 2048], BF16, nxt(4096))
    mrow_kv = at("mrow_kv", [1, 512], BF16, nxt(1024))
    r1_rep = at("r1_rep", [128, NLOC], F32, nxt(NLOC * 4))
    rr_rep = at("rr_rep", [128, 2048], F32, nxt(8192))
    m2_rep = at("m2_rep", [128, 2048], F32, nxt(8192))
    PERM_END = (o[0] + 31) // 32 * 32
    R_K = PERM_END
    R_V = R_K + 32 * KB
    R_Q = R_V + 32 * KB
    R_AT = R_Q + 32 * KB
    R_FX = R_AT + 32 * KB
    FX_END = R_FX + 43 * KB
    assert FX_END + 16512 <= 229376, (PERM_END, FX_END)

    kT = [[at(f"kT{j}_{d}", [128, 2048], BF16, R_K + (j * 2 + d) * 4 * KB)
           for d in range(2)] for j in range(NKV)]
    vt = [at(f"v{t}", [128, 1024], BF16, R_V + t * 2 * KB) for t in range(16)]
    qT = [at(f"qT{c}", [128, 1024], BF16, R_Q + c * 2 * KB) for c in range(16)]
    att = [at(f"att{c}", [128, 1024], BF16, R_AT + c * 2 * KB) for c in range(16)]

    WkbA = at("WkbA", [128, 16 * 1024], BF16, R_Q)
    Wkb = [TView(WkbA, c * 1024, 1024) for c in range(16)]
    WvbA = at("WvbA", [128, 16 * 1024], BF16, R_AT)
    Wvb = [TView(WvbA, c * 1024, 1024) for c in range(16)]
    xbA = [at(f"xbA{g}", [128, 16 * 512], BF16, R_FX + g * 16 * KB)
           for g in range(2)]
    xb = [[TView(xbA[g], c * 512, 512) for c in range(16)] for g in range(2)]
    xstg = [at(f"xstg{i}", [128, 1024], F32, R_FX + 16 * KB + i * 4 * KB)
            for i in range(2)]
    kvs = R_FX + 24 * KB
    kf = [at(f"kf{i}", [128, 512], F32, kvs + i * 2 * KB) for i in range(2)]
    krt = at("krt", [128, 512], F32, kvs + 4 * KB)
    krs = at("krs", [128, 512], F32, kvs + 6 * KB)
    Fsc = at("Fsc", [128, 512], F32, kvs + 8 * KB)
    r1sq = at("r1sq", [128, 512], F32, kvs + 10 * KB)
    cosk = at("cosk", [128, 512], F32, kvs + 12 * KB)
    sink = at("sink", [128, 512], F32, kvs + 14 * KB)
    ksqb = at("ksqb", [128, 512], BF16, kvs + 16 * KB)
    mscr = at("mscr", [128, 512], F32, kvs + 4 * KB)
    vscr = at("vscr", [128, 512], F32, kvs + 6 * KB)

    wqstg = at("wqstg8", [128, 8 * 256], F32, R_FX + 32 * KB)
    WqbA = [at(f"WqbA{i}", [128, 16 * 256], BF16, R_AT + i * 8 * KB)
            for i in range(2)]
    qf = [at(f"qf{i}", [128, 512], F32, R_AT + 16 * KB + i * 2 * KB)
          for i in range(2)]
    qFsc = at("qFsc", [128, 512], F32, R_AT + 20 * KB)
    qr1sq = at("qr1sq", [128, 512], F32, R_AT + 22 * KB)
    qsqb = at("qsqb", [128, 512], BF16, R_AT + 24 * KB)
    cosq = at("cosq", [128, 512], F32, R_AT + 25 * KB)
    sinq = at("sinq", [128, 512], F32, R_AT + 27 * KB)
    qrt = at("qrt", [128, 512], F32, R_AT + 29 * KB)

    maskb = at("maskb", [128, 12 * 512], BF16, R_FX)
    pT = [at(f"pT{i}", [128, 12 * 512], BF16, R_FX + 12 * KB + i * 12 * KB)
          for i in range(2)]
    pTa = [at(f"pTa{i}", [128, 1024], BF16, R_FX + 36 * KB + i * 2 * KB)
           for i in range(2)]
    rec = at("rec", [128, 512], F32, R_FX + 40 * KB)

    wostg = at("wostg", [128, 16 * 256], F32, R_FX)
    Wob = [at(f"Wob{i}", [128, 16 * 256], BF16, R_FX + 16 * KB + i * 8 * KB)
           for i in range(2)]
    xck = [at(f"xck{i}", [128, 512], F32, R_FX + 32 * KB + i * 2 * KB)
           for i in range(4)]
    accA = [at(f"accA{n}", [128, 512], F32, R_Q + n * 2 * KB) for n in range(2)]
    accB = [at(f"accB{n}", [128, 512], F32, R_Q + 4 * KB + n * 2 * KB)
            for n in range(2)]
    x1b_s = at("x1b_s", [128, 512], BF16, R_Q + 8 * KB)
    x1sq_s = at("x1sq_s", [128, 512], BF16, R_Q + 9 * KB)
    m2scr = at("m2scr", [128, 512], F32, R_Q + 10 * KB)
    r2p = at("r2p", [128, 512], F32, R_Q + 12 * KB)
    r3s = at("r3s", [128, 512], F32, R_Q + 14 * KB)
    xnt = at("xnt", [128, 512], F32, R_Q + 16 * KB)
    xnu = at("xnu", [128, 512], F32, R_Q + 18 * KB)
    x1stg = [at(f"x1stg{i}", [128, 512], F32, R_Q + 20 * KB + i * 2 * KB)
             for i in range(2)]
    mxstg = [at(f"mxstg{i}", [128, 1024], F32, R_Q + 24 * KB + i * 4 * KB)
             for i in range(2)]
    rmx = None
    x1rA = at("x1rA", [128, 16 * 1024], F32, R_K)
    x1r = [TView(x1rA, c * 1024, 1024) for c in range(16)]

    wqTt = [at(f"wqT{c}", [128, 2048], BF16, R_Q + c * 4 * KB) for c in range(16)]
    wstg = [at(f"wstg{i}", [128, 2048], F32, R_FX + i * 8 * KB) for i in range(2)]
    xqA_t = at("xqA", [128, 16 * 512], BF16, R_FX + 16 * KB)
    xqB_t = at("xqB", [128, 16 * 512], BF16, R_FX)
    xq = [TView(xqA_t, c * 512, 512) for c in range(16)]
    xq2 = [[TView(xqA_t, c * 512, 512) for c in range(16)],
           [TView(xqB_t, c * 512, 512) for c in range(16)]]
    ostg = [at(f"ostg{i}", [128, 512], F32, R_FX + 32 * KB + i * 2 * KB)
            for i in range(2)]
    qnt = [at(f"qnt{i}", [128, 512], F32, R_FX + 36 * KB + i * 2 * KB)
           for i in range(3)]
    qni = at("qni", [128, 512], I32, R_FX + 40 * KB)

    ps_guards = []
    g01 = nc.psum_tensor("ps01", [128, 1024], F32)
    P01t = g01.__enter__()
    ps_guards.append(g01)
    pbt = [TView(P01t, 0, 512), TView(P01t, 512, 512)]
    for i in range(2, 8):
        g = nc.psum_tensor(f"ps{i}", [128, 512], F32)
        t = g.__enter__()
        ps_guards.append(g)
        pbt.append(t)

    def pb(i):
        return pbt[i]

    sems = {}
    sem_guards = {}
    for n in (["s_t", "s_s", "s_v", "s_g", "s_g2"]
              + [f"sx{i}" for i in range(8)] + [f"sy{i}" for i in range(8)]
              + [f"sg{i}" for i in range(8)]):
        g = nc.semaphore(n)
        sems[n] = g.__enter__()
        sem_guards[n] = g

    sch = Sched()
    st = {}
    bankf = {i: ("s_s", 0) for i in range(8)}

    qidx = {"sync": 0, "scalar": 0, "gpsimd": 0}

    def dma(queue, out, in_, wait=()):
        pref = "sx" if queue == "sync" else "sy"
        i = qidx[queue]
        qidx[queue] = i + 1
        lane = f"{pref}{i % 8}"
        prev = sch.count(lane)
        wl = list(wait) + ([(lane, prev)] if prev else [])
        cnt = sch.op(queue, lambda e, out=out, in_=in_:
                     e.dma_start(out=out, in_=in_), wait=wl, inc=lane, inc_by=16)
        return (lane, cnt)

    def dma_barrier(queue):
        pref = {"sync": "sx", "scalar": "sy", "gpsimd": "sg"}[queue]
        return [(f"{pref}{i}", sch.count(f"{pref}{i}"))
                for i in range(8) if sch.count(f"{pref}{i}") > 0]

    def gdma(out, in_, wait=()):
        i = qidx["gpsimd"]
        qidx["gpsimd"] = i + 1
        lane = f"sg{i % 8}"
        prev = sch.count(lane)
        wl = list(wait) + ([(lane, prev)] if prev else [])
        cnt = sch.op("gpsimd", lambda e, out=out, in_=in_:
                     e.dma_start(out=out, in_=in_), wait=wl, inc=lane, inc_by=16)
        return (lane, cnt)

    dma("sync", ones[:], ones_d)
    dma("sync", ident[:], id_d)
    dma("sync", qs16[:], qs2_d)
    dma("sync", ks16[:], ks2_d)
    w0 = dma("sync", gam16[:], gam_d)
    CONSTW = dma_barrier("sync")
    sch.op("vector", lambda e: e.tensor_scalar_mul(qs16[:], qs16[:], QSCALE),
           wait=CONSTW)
    sch.op("vector", lambda e: e.memset(onesf[:], 1.0))
    sch.op("vector", lambda e: e.memset(smalls[:, 0:1], 0.0))
    sch.op("vector", lambda e: e.memset(smalls[:, SM["eps5"]:SM["eps5"] + 1], LN_EPS))
    sch.op("vector", lambda e: e.memset(smalls[:, SM["eps6"]:SM["eps6"] + 1], EPS_QK))
    sch.op("vector", lambda e: e.memset(ccs[:], 0.0), inc="s_v")
    CONST_V = sch.count("s_v")

    for p in range(16):
        wdone = st.get("wmean_used", 0)
        w = dma("scalar", wstg[p % 2][:], wt_d[p * 128:(p + 1) * 128, :],
                wait=([("s_v", wdone)] if p >= 2 else ()))
        st["wmean_used"] = sch.op(
            "vector", lambda e, p=p: e.tensor_reduce(
                wabs[:, p:p + 1], wstg[p % 2][:], AXX, ALU.add,
                apply_absolute_value=True),
            wait=[w], inc="s_v")
    sch.drain("vector")
    sch.op("vector", lambda e: e.tensor_reduce(wcol[:], wabs[:], AXX, ALU.add),
           inc="s_v")
    t1 = sch.op("tensor", lambda e: e.transpose(pb(7)[0:1, 0:128], wcol[:],
                                                ident[:]),
                wait=[("s_v", sch.count("s_v"))] + CONSTW, inc="s_t")
    if DEBUG:
        sdw = sch.op("scalar", lambda e: e.copy(wabs[0:1, 0:16] if False else dbgw_s[:], pb(7)[0:1, 0:128]),
                     wait=[("s_t", t1)], inc="s_s")
        dma("scalar", dbg["wrowT"], dbgw_s[:], wait=[("s_s", sdw)])
        dma("scalar", dbg["wcol"], wcol[:], wait=[("s_v", sch.count("s_v"))])
    sch.op("vector", lambda e: e.tensor_reduce(
        smalls[0:1, SM["swrow"]:SM["swrow"] + 1], pb(7)[0:1, 0:128], AXX,
        ALU.add), wait=[("s_t", t1)])
    sch.drain("vector")
    vv = sch.op("vector", lambda e: e.tensor_scalar(
        smalls[0:1, SM["swrow"]:SM["swrow"] + 1],
        smalls[0:1, SM["swrow"]:SM["swrow"] + 1],
        1.0 / (2048.0 * 2048.0), EPS_SCALE, ALU.mult, ALU.max), inc="s_v")
    wv_ = dma("sync", bc_d[0:1, 0:1],
              smalls[0:1, SM["swrow"]:SM["swrow"] + 1], wait=[("s_v", vv)])
    wv_ = dma("sync", smalls[:, SM["swcol"]:SM["swcol"] + 1],
              bc_d[0:1, 0:1].partition_broadcast(128), wait=[wv_])
    sch.op("vector", lambda e: e.tensor_scalar_mul(
        smalls[:, SM["halfsw"]:SM["halfsw"] + 1],
        smalls[:, SM["swcol"]:SM["swcol"] + 1], 0.5), wait=[wv_])
    sch.drain("vector")
    sch.op("vector", lambda e: e.tensor_scalar_mul(
        smalls[:, SM["neghalfsw"]:SM["neghalfsw"] + 1],
        smalls[:, SM["halfsw"]:SM["halfsw"] + 1], -1.0), inc="s_v")
    bankf[7] = ("s_v", sch.count("s_v"))
    st0_wmean_v = sch.count("s_v")

    def load_panels_cast(w_dram, destA, ncols, extra_wait=(), per_g=None):
        ws = []
        for g in range(4):
            wt = list(extra_wait) if g == 0 else []
            if per_g:
                wt += per_g[g]
            ws.append(gdma(
                destA[:, g * 4 * ncols:(g + 1) * 4 * ncols]
                .rearrange("p (a t) -> p a t", a=4),
                w_dram[g * 512:(g + 1) * 512, 0:ncols]
                .rearrange("(a p) t -> p a t", p=128),
                wait=wt))
        return ws

    def colsum_row(chunks, ncols, row_off, wwait):
        for ng in range(0, ncols, 512):
            nn = min(512, ncols - ng)
            for c in range(16):
                sch.op("tensor", lambda e, c=c, ng=ng, nn=nn, chunks=chunks:
                       e.matmul(pb(4)[0:1, 0:nn], ones[:, 0:1],
                                chunks[c][:, ng:ng + nn],
                                start=(c == 0), stop=(c == 15)),
                       wait=(list(wwait) + [bankf[4]] if c == 0 else ()),
                       inc=("s_t" if c == 15 else None))
            tcur = sch.count("s_t")
            scur = sch.op("scalar", lambda e, ng=ng, nn=nn, row_off=row_off:
                          e.copy(wrow[0:1, row_off + ng:row_off + ng + nn],
                                 pb(4)[0:1, 0:nn]),
                          wait=[("s_t", tcur)], inc="s_s")
            bankf[4] = ("s_s", scur)

    def rsqrt_act(dst, src, bias, wait=(), pre_scale=1.0):
        dap = dst[:] if hasattr(dst, "ap") else dst
        sap = src[:] if hasattr(src, "ap") else src
        bcol = SM["eps5"] if bias == LN_EPS else SM["eps6"]
        nparts = dap.shape[0]
        bap = smalls[0:nparts, bcol:bcol + 1]
        sch.op("scalar", lambda e: e.activation(dap, sap, AF.Ln,
                                                bias=bap, scale=pre_scale),
               wait=wait)
        sch.drain("scalar")
        return sch.op("scalar", lambda e: e.activation(dap, dap, AF.Exp,
                                                       scale=-0.5), inc="s_s")

    def kv_phase(qq):
        mxg = [("s_v", st.get("mx0_v", 0))]
        first = 0 if qq == 0 else 16
        ngrp = 4 if qq == 0 else 2

        def load_xgroup(gi):
            t0 = first + gi * 4
            c0 = t0 * 128
            xgate = (st.get("gate", []) if qq == 0
                     else [("s_t", st.get("rq_free", 0))])
            ws = []
            for g4 in range(4):
                ws.append(gdma(
                    xbA[0][:, g4 * 4 * 512:(g4 + 1) * 4 * 512]
                    .rearrange("p (a t) -> p a t", a=4),
                    xT_d[g4 * 512:(g4 + 1) * 512, c0:c0 + 512]
                    .rearrange("(a p) t -> p a t", p=128),
                    wait=(([("s_t", st.get("xb0_fr", 0)),
                            ("s_v", st0_wmean_v)] + xgate)
                          if g4 == 0 else ())))
            return ws

        xws0 = load_xgroup(0)
        wWk = load_panels_cast(wk_d, WkbA, 1024,
                               extra_wait=st.get("gate", []),
                               per_g=[[], [], mxg, mxg])
        wWv = load_panels_cast(wv_d, WvbA, 1024,
                               extra_wait=st.get("gate", []))

        for gi in range(ngrp):
            t0 = first + gi * 4
            c0 = t0 * 128
            xws = xws0 if gi == 0 else load_xgroup(gi)
            for c in range(16):
                scur = sch.op("scalar", lambda e, c=c:
                              e.activation(ksqb[:], xb[0][c][:], AF.Square),
                              wait=[xws[c // 4],
                                    ("s_t", st.get("ksqb_fr", 0))], inc="s_s")
                sch.op("tensor", lambda e, c=c: e.matmul(
                    pb(5)[:, 0:512], ones[:], xb[0][c][:],
                    start=(c == 0), stop=(c == 15)),
                    wait=([bankf[5]] if c == 0 else []) + [xws[c // 4]],
                    inc=("s_t" if c == 15 else None))
                st["ksqb_fr"] = sch.op("tensor", lambda e, c=c: e.matmul(
                    pb(6)[:, 0:512], ones[:], ksqb[:],
                    start=(c == 0), stop=(c == 15)),
                    wait=([bankf[6], ("s_s", scur)] if c == 0
                          else [("s_s", scur)]),
                    inc="s_t")
            tstats = sch.count("s_t")
            st["ksq_mm"] = tstats
            if gi == 0:
                colsum_row(Wkb, 1024, 0, wWk)
            sch.op("vector", lambda e: e.tensor_scalar_mul(
                mscr[:], pb(5)[:, 0:512], 1.0 / 2048.0), wait=[("s_t", tstats)])
            sch.drain("vector")
            sch.op("vector", lambda e: e.tensor_mul(vscr[:], mscr[:], mscr[:]))
            sch.drain("vector")
            v1 = sch.op("vector", lambda e: e.scalar_tensor_tensor(
                vscr[:], pb(6)[:, 0:512], 1.0 / 2048.0, vscr[:],
                ALU.mult, ALU.subtract), inc="s_v")
            bankf[5] = ("s_v", v1)
            bankf[6] = ("s_v", v1)
            s1 = rsqrt_act(r1_rep[:, c0:c0 + 512], vscr, LN_EPS,
                           wait=[("s_v", v1)])
            v2 = sch.op("vector", lambda e: e.tensor_scalar_mul(
                mrow_kv[0:1, 0:512], mscr[0:1, 0:512], -1.0), inc="s_v")
            if qq == 1 or gi >= 2:
                og = gi if qq == 1 else gi - 2
                sch.op("vector", lambda e, og=og: e.tensor_scalar_mul(
                    mrow_own[0:1, og * 512:(og + 1) * 512],
                    mscr[0:1, 0:512], -1.0), inc="s_v")
            for tt_ in range(4):
                slot = (t0 + tt_) % 16
                tcur = sch.op("tensor", lambda e, tt_=tt_, c0=c0:
                              e.transpose(pb(7)[0:128, 0:1],
                                          r1_rep[0:1, c0 + tt_ * 128:c0 + (tt_ + 1) * 128],
                                          ident[0:1, 0:1]),
                              wait=[("s_s", s1), bankf[7]], inc="s_t")
                scur = sch.op("scalar", lambda e, slot=slot:
                              e.copy(r1col[:, slot:slot + 1], pb(7)[0:128, 0:1]),
                              wait=[("s_t", tcur)], inc="s_s")
                bankf[7] = ("s_s", scur)

            cos_w1 = dma("sync", cosk[:], cos_d[:, c0:c0 + 512],
                         wait=[("s_v", st.get("cosk_fr", 0))] + st.get("gate", []))
            cos_w2 = dma("sync", sink[:], sin_d[:, c0:c0 + 512],
                         wait=st.get("gate", []))
            r1sq_v = sch.op("vector", lambda e, c0=c0: e.tensor_mul(
                r1sq[:], r1_rep[:, c0:c0 + 512], r1_rep[:, c0:c0 + 512]),
                wait=[("s_s", s1), ("s_v", st.get("r1sq_fr", 0))], inc="s_v")
            for j in range(NKV):
                for par in range(2):
                    kc = 2 * j + par
                    bk = kc % 2
                    for ic in range(16):
                        sch.op("tensor", lambda e, ic=ic, kc=kc, bk=bk:
                               e.matmul(pb(bk)[:, 0:512],
                                        Wkb[ic][:, kc * 128:(kc + 1) * 128],
                                        xb[0][ic][:], start=(ic == 0), stop=False),
                               wait=([bankf[bk], ("s_v", v2)] if ic == 0 else ()))
                    tcur = sch.op("tensor", lambda e, kc=kc, bk=bk:
                                  e.matmul(pb(bk)[:, 0:512],
                                           wrow[0:1, kc * 128:(kc + 1) * 128],
                                           mrow_kv[0:1, 0:512],
                                           start=False, stop=True),
                                  inc="s_t")
                    scur = sch.op("scalar", lambda e, par=par, bk=bk:
                                  e.copy(kf[par][:], pb(bk)[:, 0:512]),
                                  wait=[("s_t", tcur),
                                        ("s_v", st.get("kf_fr", 0))], inc="s_s")
                    sch.drain("scalar")
                    scur = sch.op("scalar", lambda e, bk=bk:
                                  e.activation(ksqb[:], pb(bk)[:, 0:512],
                                               AF.Square),
                                  wait=[("s_t", st.get("ksq_mm", 0))], inc="s_s")
                    bankf[bk] = ("s_s", scur)
                    st["ksq_mm"] = sch.op("tensor", lambda e, par=par:
                                          e.matmul(pb(4)[:, 0:512], ones[:], ksqb[:],
                                                   start=(par == 0), stop=(par == 1)),
                                          wait=[("s_s", scur)] + ([bankf[4]] if par == 0 else []),
                                          inc="s_t")
                tsum = sch.count("s_t")
                dbg_this = DEBUG and qq == 0 and gi == 2 and j == 0
                if dbg_this:
                    dma("scalar", dbg["kstar"], kf[0][:], wait=[("s_s", sch.count("s_s"))])
                vF = sch.op("vector", lambda e: e.tensor_mul(
                    Fsc[:], pb(4)[:, 0:512], r1sq[:]),
                    wait=[("s_t", tsum), ("s_v", r1sq_v)], inc="s_v")
                if dbg_this:
                    sdbg = sch.op("scalar", lambda e: e.copy(vscr[:], pb(4)[:, 0:512]),
                                  wait=[("s_t", tsum), ("s_v", vF)], inc="s_s")
                    bankf[4] = ("s_s", sdbg)
                    dma("scalar", dbg["ss_k"], vscr[:], wait=[("s_s", sdbg)])
                bankf[4] = ("s_v", vF)
                sF = rsqrt_act(Fsc, Fsc, EPS_QK, wait=[("s_v", vF)],
                               pre_scale=1.0 / HD)
                vF = sch.op("vector", lambda e, c0=c0: e.tensor_mul(
                    Fsc[:], Fsc[:], r1_rep[:, c0:c0 + 512]),
                    wait=[("s_s", sF)], inc="s_v")
                sch.drain("vector")
                if dbg_this:
                    dma("scalar", dbg["F_k"], Fsc[:], wait=[("s_v", vF)])
                    dma("scalar", dbg["r1sq_k"], r1sq[:], wait=[("s_v", vF)])
                slotc = (t0 % 16) * 128
                vpl = sch.op("vector", lambda e, j=j, slotc=slotc:
                             e.scalar_tensor_tensor(
                                 kT[j][1][:, slotc:slotc + 512], kf[1][:],
                                 ks16[:, 1:2], Fsc[:], ALU.mult, ALU.mult),
                             wait=[("s_v", vF)], inc="s_v")
                sch.op("vector", lambda e: e.scalar_tensor_tensor(
                    krt[:], kf[0][:], ks16[:, 0:1], Fsc[:], ALU.mult, ALU.mult))
                sch.drain("vector")
                shmask = []
                for i2 in range(16):
                    shmask += [2 * i2 + 1, 2 * i2]
                sch.op("vector", lambda e: e.stream_shuffle(krs[:], krt[:], shmask))
                sch.drain("vector")
                sch.op("vector", lambda e: e.tensor_mul(krt[:], krt[:], cosk[:]),
                       wait=[cos_w1, cos_w2])
                sch.drain("vector")
                sch.op("vector", lambda e: e.tensor_mul(krs[:], krs[:], sink[:]))
                sch.drain("vector")
                vrot = sch.op("vector", lambda e, j=j, slotc=slotc:
                              e.tensor_add(kT[j][0][:, slotc:slotc + 512],
                                           krt[:], krs[:]), inc="s_v")
                st["kf_fr"] = vrot
            st["cosk_fr"] = sch.count("s_v")
            st["r1sq_fr"] = sch.count("s_v")
            if gi == 0:
                colsum_row(Wvb, 1024, 1024, wWv)
                st["wsumv"] = bankf[4]

            for tt_ in range(4):
                slot = (t0 + tt_) % 16
                for ng in range(2):
                    bk = ng % 2
                    for ic in range(16):
                        sch.op("tensor", lambda e, ic=ic, tt_=tt_, ng=ng, bk=bk:
                               e.matmul(pb(bk)[:, 0:512],
                                        xb[0][ic][:, tt_ * 128:(tt_ + 1) * 128],
                                        Wvb[ic][:, ng * 512:(ng + 1) * 512],
                                        start=(ic == 0), stop=False),
                               wait=([bankf[bk]] if ic == 0 else ()))
                    tcur = sch.op("tensor", lambda e, tt_=tt_, ng=ng, bk=bk:
                                  e.matmul(pb(bk)[:, 0:512],
                                           mrow_kv[0:1, tt_ * 128:(tt_ + 1) * 128],
                                           wrow[0:1, 1024 + ng * 512:1024 + (ng + 1) * 512],
                                           start=False, stop=True),
                                  wait=[st.get("wsumv", ("s_s", 0))], inc="s_t")
                    vcur = sch.op("vector", lambda e, slot=slot, ng=ng, bk=bk:
                                  e.tensor_scalar_mul(
                                      vt[slot][:, ng * 512:(ng + 1) * 512],
                                      pb(bk)[:, 0:512], r1col[:, slot:slot + 1]),
                                  wait=[("s_t", tcur)], inc="s_v")
                    bankf[bk] = ("s_v", vcur)
            st["xb0_fr"] = sch.count("s_t")
        st["kv_done_t"] = sch.count("s_t")
        st["kv_done_v"] = sch.count("s_v")

    def q_phase(qq):
        xbws = []
        for og in range(2):
            c0 = 1024 + qq * 1024 + og * 512
            for g4 in range(4):
                xbws.append(gdma(
                    xbA[og][:, g4 * 4 * 512:(g4 + 1) * 4 * 512]
                    .rearrange("p (a t) -> p a t", a=4),
                    xT_d[g4 * 512:(g4 + 1) * 512, c0:c0 + 512]
                    .rearrange("(a p) t -> p a t", p=128),
                    wait=([("s_t", st.get("xb_own_fr", 0)),
                           ("s_t", st.get("kv_done_t", 0)),
                           ("s_v", st.get("kv_done_v", 0))] if g4 == 0 and og == 0
                          else ())))
        vxb_w = xbws

        shmask = []
        for i2 in range(16):
            shmask += [2 * i2 + 1, 2 * i2]
        for blk in range(8):
            h = blk
            wqw = gdma(WqbA[blk % 2][:].rearrange("p (a t) -> p a t", a=16),
                       wq_d[:, h * 256:(h + 1) * 256]
                       .rearrange("(a p) t -> p a t", p=128),
                       wait=[("s_t", st.get(f"wqb_fr{blk % 2}", 0))]
                       + (vxb_w if blk == 0 else []))
            vwq = wqw
            for c in range(16):
                sch.op("tensor", lambda e, c=c, blk=blk:
                       e.matmul(pb(4)[0:1, 0:256], ones[:, 0:1],
                                WqbA[blk % 2][:, c * 256:(c + 1) * 256],
                                start=(c == 0), stop=(c == 15)),
                       wait=([vwq, bankf[4]] if c == 0 else ()),
                       inc=("s_t" if c == 15 else None))
            tcur = sch.count("s_t")
            swsum = sch.op("scalar", lambda e: e.copy(wrow[0:1, 0:256],
                                                      pb(4)[0:1, 0:256]),
                           wait=[("s_t", tcur)], inc="s_s")
            bankf[4] = ("s_s", swsum)

            for og in range(2):
                c0_loc = 1024 + qq * 1024 + og * 512
                cw1 = dma("sync", cosq[:], cos_d[:, c0_loc:c0_loc + 512],
                          wait=[("s_v", st.get("cosq_fr", 0)),
                                ("s_t", st.get("kv_done_t", 0))])
                cw2 = dma("sync", sinq[:], sin_d[:, c0_loc:c0_loc + 512],
                          wait=[("s_t", st.get("kv_done_t", 0))])
                vr1 = sch.op("vector", lambda e, c0_loc=c0_loc: e.tensor_mul(
                    qr1sq[:], r1_rep[:, c0_loc:c0_loc + 512],
                    r1_rep[:, c0_loc:c0_loc + 512]),
                    wait=[("s_v", st.get("qr1sq_fr", 0))], inc="s_v")
                for par in range(2):
                    bk = par
                    for ic in range(16):
                        sch.op("tensor", lambda e, ic=ic, par=par, og=og, bk=bk, blk=blk:
                               e.matmul(pb(bk)[:, 0:512],
                                        WqbA[blk % 2][:, ic * 256 + par * 128:
                                                      ic * 256 + (par + 1) * 128],
                                        xb[og][ic][:], start=(ic == 0), stop=False),
                               wait=(([bankf[bk], ("s_s", swsum)] + vxb_w)
                                     if ic == 0 else ()))
                    tcur = sch.op("tensor", lambda e, par=par, og=og, bk=bk:
                                  e.matmul(pb(bk)[:, 0:512],
                                           wrow[0:1, par * 128:(par + 1) * 128],
                                           mrow_own[0:1, og * 512:(og + 1) * 512],
                                           start=False, stop=True), inc="s_t")
                    st[f"wqb_fr{blk % 2}"] = tcur
                    scur = sch.op("scalar", lambda e, par=par, bk=bk:
                                  e.copy(qf[par][:], pb(bk)[:, 0:512]),
                                  wait=[("s_t", tcur),
                                        ("s_v", st.get("qf_fr", 0))], inc="s_s")
                    sch.drain("scalar")
                    scur = sch.op("scalar", lambda e, bk=bk:
                                  e.activation(qsqb[:], pb(bk)[:, 0:512], AF.Square),
                                  wait=[("s_t", st.get("qsq_mm", 0))], inc="s_s")
                    bankf[bk] = ("s_s", scur)
                    st["qsq_mm"] = sch.op(
                        "tensor", lambda e, par=par:
                        e.matmul(pb(4)[:, 0:512], ones[:], qsqb[:],
                                 start=(par == 0), stop=(par == 1)),
                        wait=[("s_s", scur)] + ([bankf[4]] if par == 0 else []),
                        inc="s_t")
                tsum = sch.count("s_t")
                vF = sch.op("vector", lambda e: e.tensor_mul(
                    qFsc[:], pb(4)[:, 0:512], qr1sq[:]),
                    wait=[("s_t", tsum), ("s_v", vr1)], inc="s_v")
                bankf[4] = ("s_v", vF)
                sF = rsqrt_act(qFsc, qFsc, EPS_QK, wait=[("s_v", vF)],
                               pre_scale=1.0 / HD)
                vF = sch.op("vector", lambda e, c0_loc=c0_loc: e.tensor_mul(
                    qFsc[:], qFsc[:], r1_rep[:, c0_loc:c0_loc + 512]),
                    wait=[("s_s", sF)], inc="s_v")
                sch.drain("vector")
                sch.op("vector", lambda e, h=h, og=og: e.scalar_tensor_tensor(
                    qT[2 * h + 1][:, og * 512:(og + 1) * 512], qf[1][:],
                    qs16[:, 1:2], qFsc[:], ALU.mult, ALU.mult),
                    wait=[("s_v", vF)], inc="s_v")
                sch.op("vector", lambda e: e.scalar_tensor_tensor(
                    qrt[:], qf[0][:], qs16[:, 0:1], qFsc[:], ALU.mult, ALU.mult))
                sch.drain("vector")
                sch.op("vector", lambda e: e.stream_shuffle(krs[:] if False else qf[0][:], qrt[:], shmask))
                sch.drain("vector")
                sch.op("vector", lambda e: e.tensor_mul(qrt[:], qrt[:], cosq[:]),
                       wait=[cw1, cw2])
                sch.drain("vector")
                sch.op("vector", lambda e: e.tensor_mul(qf[0][:], qf[0][:], sinq[:]))
                sch.drain("vector")
                st["cosq_fr"] = sch.op(
                    "vector", lambda e, h=h, og=og:
                    e.tensor_add(qT[2 * h][:, og * 512:(og + 1) * 512],
                                 qrt[:], qf[0][:]), inc="s_v")
                st["qr1sq_fr"] = st["cosq_fr"]
                st["qf_fr"] = st["cosq_fr"]
        st["xb_own_fr"] = sch.count("s_t")
        st["q_done_v"] = sch.count("s_v")

    def attn_phase(qq):
        qv = st["q_done_v"]
        sch.op("vector", lambda e: e.memset(pTa[0][:], 0.0),
               wait=[("s_v", st.get("kv_done_v", 0)), ("s_v", qv)])
        sch.op("vector", lambda e: e.memset(pTa[1][:], 0.0), inc="s_v")
        st["pTa_fr0"] = sch.count("s_v")
        st["pTa_fr1"] = sch.count("s_v")
        for og in range(2):
            mset = (qq * 2 + og) * 12
            mw = dma("sync", maskb[:].rearrange("p (k f) -> p k f", k=12),
                     masks_d[mset:mset + 12].rearrange("k p f -> p k f"),
                     wait=[("s_v", st.get("maskb_fr", 0)),
                           ("s_t", st.get("xb_own_fr", 0))])
            A_loc = 8 + qq * 8 + og * 4
            for h in range(NH):
                j = h // 2
                buf = h % 2
                pready = {}
                for kp in range(6):
                    k0, k1 = 2 * kp, 2 * kp + 1
                    for kji in (k0, k1):
                        kjt = A_loc - 8 + kji
                        slot = (kjt % 16) * 128
                        a0 = max(0, (kji - 8) * 128)
                        a1 = min(512, (kji - 8) * 128 + 1152)
                        bk = kji % 2
                        for dc in range(2):
                            sch.op("tensor",
                                   lambda e, j=j, dc=dc, slot=slot, h=h, og=og,
                                   a0=a0, a1=a1, bk=bk:
                                   e.matmul(pb(bk)[:, a0:a1],
                                            kT[j][dc][:, slot:slot + 128],
                                            qT[2 * h + dc][:, og * 512 + a0:og * 512 + a1],
                                            start=(dc == 0), stop=(dc == 1)),
                                   wait=([bankf[bk], ("s_v", qv)] if dc == 0 else ()),
                                   inc=("s_t" if dc == 1 else None))
                    tsc = sch.count("s_t")
                    lo = max(0, (k0 - 8) * 128)
                    hi = 512 + min(512, (k1 - 8) * 128 + 1152)
                    skip_mask = (k0 == 4 or k0 == 6) and not (qq == 0 and og == 0)
                    sch.op("scalar", lambda e, lo=lo, hi=hi:
                           e.activation(P01t[:, lo:hi], P01t[:, lo:hi],
                                        AF.Tanh, scale=0.02),
                           wait=[("s_t", tsc)])
                    sch.drain("scalar")
                    if skip_mask:
                        scur = sch.op("scalar", lambda e, buf=buf, k0=k0, lo=lo, hi=hi:
                                      e.activation(pT[buf][:, k0 * 512 + lo:k0 * 512 + hi],
                                                   P01t[:, lo:hi], AF.Exp, scale=50.0),
                                      wait=[("s_t", st.get(f"pv_done{buf}", 0))],
                                      inc="s_s")
                        sch.drain("scalar")
                        bankf[0] = bankf[1] = ("s_s", scur)
                        pready[k0] = pready[k1] = ("s_s", scur)
                    else:
                        scur = sch.op("scalar", lambda e, kp=kp, lo=lo, hi=hi:
                                      e.activation(pTa[kp % 2][:, lo:hi],
                                                   P01t[:, lo:hi], AF.Exp, scale=50.0),
                                      wait=[("s_v", st.get(f"pTa_fr{kp % 2}", 0))],
                                      inc="s_s")
                        sch.drain("scalar")
                        bankf[0] = bankf[1] = ("s_s", scur)
                        st[f"pTa_fr{kp % 2}"] = sch.op(
                            "vector", lambda e, buf=buf, kp=kp, k0=k0:
                            e.tensor_mul(pT[buf][:, k0 * 512:(k0 + 2) * 512],
                                         pTa[kp % 2][:],
                                         maskb[:, k0 * 512:(k0 + 2) * 512]),
                            wait=[("s_s", scur), mw,
                                  ("s_t", st.get(f"pv_done{buf}", 0))],
                            inc="s_v")
                        pready[k0] = pready[k1] = ("s_v", st[f"pTa_fr{kp % 2}"])
                for kji in range(12):
                    kjt = A_loc - 8 + kji
                    slot = (kjt % 16)
                    for dc in range(2):
                        bo = 2 + dc
                        sch.op("tensor",
                               lambda e, slot=slot, j=j, dc=dc, buf=buf, kji=kji, bo=bo:
                               e.matmul(pb(bo)[:, 0:512],
                                        vt[slot][:, j * 256 + dc * 128:j * 256 + (dc + 1) * 128],
                                        pT[buf][:, kji * 512:(kji + 1) * 512],
                                        start=(kji == 0), stop=(kji == 11)),
                               wait=([pready[kji]] +
                                     ([bankf[2], bankf[3]] if kji == 0 and dc == 0 else [])),
                               inc=("s_t" if (kji == 11 and dc == 1) else None))
                st[f"pv_done{buf}"] = sch.count("s_t")
                tpv = sch.count("s_t")
                sch.op("vector", lambda e, buf=buf: e.tensor_add(
                    rec[:], pT[buf][:, 0:512], pT[buf][:, 512:1024]),
                    wait=[pready[0], pready[1]])
                for kji in range(2, 12):
                    sch.drain("vector")
                    sch.op("vector", lambda e, buf=buf, kji=kji:
                           e.tensor_add(rec[:], rec[:],
                                        pT[buf][:, kji * 512:(kji + 1) * 512]),
                           wait=[pready[kji]])
                chv = sch.count("s_v")
                tsum = sch.op("tensor", lambda e: e.matmul(
                    pb(4)[0:128, 0:512], onesf[:], rec[:],
                    start=True, stop=True),
                    wait=[("s_v", chv), bankf[4]], inc="s_t")
                vrec = sch.op("vector", lambda e: e.reciprocal(
                    rec[:], pb(4)[:, 0:512]), wait=[("s_t", tsum)], inc="s_v")
                bankf[4] = ("s_v", vrec)
                sch.drain("vector")
                v1 = sch.op("vector", lambda e, h=h, og=og:
                            e.tensor_tensor(att[2 * h][:, og * 512:(og + 1) * 512],
                                            pb(2)[:, 0:512], rec[:], ALU.mult),
                            wait=[("s_t", tpv)], inc="s_v")
                v2 = sch.op("vector", lambda e, h=h, og=og:
                            e.tensor_tensor(att[2 * h + 1][:, og * 512:(og + 1) * 512],
                                            pb(3)[:, 0:512], rec[:], ALU.mult),
                            inc="s_v")
                bankf[2] = ("s_v", v1)
                bankf[3] = ("s_v", v2)
            st["maskb_fr"] = sch.count("s_v")
        st["attn_done_v"] = sch.count("s_v")
        st["rq_free"] = sch.count("s_t")
        st["kv_ring_free"] = sch.count("s_t")

    def wo_phase(qq):
        av = st["attn_done_v"]
        z = None
        for n in range(2):
            sch.op("vector", lambda e, n=n: e.memset(accA[n][:], 0.0),
                   wait=[("s_v", st.get("accs_fr", 0))])
            z = sch.op("vector", lambda e, n=n: e.memset(accB[n][:], 0.0),
                       inc="s_v")
        for blk in range(8):
            wws = [gdma(Wob[blk % 2][:].rearrange("p (a t) -> p a t", a=16),
                        wo_d[:, blk * 256:(blk + 1) * 256]
                        .rearrange("(a p) t -> p a t", p=128),
                        wait=[("s_t", st.get(f"wob_fr{blk % 2}", 0))]
                        + ([("s_v", av)] if blk < 2 else []))]
            vcast = None
            for po in range(2):
                oc = blk * 2 + po
                for ng in range(2):
                    bk = po * 2 + ng
                    xw = dma("sync", xck[bk][:],
                             xT_d[oc * 128:(oc + 1) * 128,
                                  1024 + qq * 1024 + ng * 512:
                                  1024 + qq * 1024 + (ng + 1) * 512],
                             wait=[("s_v", st.get(f"xck_fr{bk}", 0)),
                                   ("s_v", av)])
                    for ic in range(16):
                        sch.op("tensor",
                               lambda e, blk=blk, ic=ic, po=po, ng=ng, bk=bk:
                               e.matmul(pb(bk)[:, 0:512],
                                        Wob[blk % 2][:, ic * 256 + po * 128:ic * 256 + (po + 1) * 128],
                                        att[ic][:, ng * 512:(ng + 1) * 512],
                                        start=(ic == 0), stop=(ic == 15)),
                               wait=(wws + [bankf[bk], ("s_v", av)]
                                     if ic == 0 else ()),
                               inc=("s_t" if ic == 15 else None))
                    st[f"wob_fr{blk % 2}"] = sch.count("s_t")
                    tcur = sch.count("s_t")
                    if qq == 1:
                        vx1 = sch.op("vector",
                                     lambda e, oc=oc, ng=ng, bk=bk:
                                     e.tensor_add(x1r[oc][:, ng * 512:(ng + 1) * 512],
                                                  pb(bk)[:, 0:512], xck[bk][:]),
                                     wait=[("s_t", tcur), xw], inc="s_v")
                        x1ap = lambda oc=oc, ng=ng: x1r[oc][:, ng * 512:(ng + 1) * 512]
                    else:
                        vx1 = sch.op("vector",
                                     lambda e, ng=ng, bk=bk:
                                     e.tensor_add(x1stg[bk % 2][:],
                                                  pb(bk)[:, 0:512], xck[bk][:]),
                                     wait=[("s_t", tcur), xw,
                                           st.get(f"x1stg_fr{bk % 2}", ("s_v", 0))],
                                     inc="s_v")
                        st[f"x1stg_fr{bk % 2}"] = dma(
                            "scalar", x1q0_d[oc * 128:(oc + 1) * 128,
                                             ng * 512:(ng + 1) * 512],
                            x1stg[bk % 2][:], wait=[("s_v", vx1)])
                        x1ap = lambda bk=bk: x1stg[bk % 2][:]
                    st[f"xck_fr{bk}"] = vx1
                    bankf[bk] = ("s_v", vx1)
                    vb = sch.op("vector", lambda e, x1ap=x1ap: e.tensor_copy(
                        x1b_s[:], x1ap()),
                        wait=[("s_v", vx1), ("s_t", st.get("x1bs_fr", 0))],
                        inc="s_v")
                    sq = sch.op("scalar", lambda e: e.activation(
                        x1sq_s[:], x1b_s[:], AF.Square), wait=[("s_v", vb)],
                        inc="s_s")
                    t5 = sch.op("tensor", lambda e: e.matmul(
                        pb(5)[:, 0:512], ones[:], x1b_s[:], start=True, stop=True),
                        wait=[bankf[5], ("s_v", vb)], inc="s_t")
                    t6 = sch.op("tensor", lambda e: e.matmul(
                        pb(6)[:, 0:512], ones[:], x1sq_s[:], start=True, stop=True),
                        wait=[bankf[6], ("s_s", sq)], inc="s_t")
                    st["x1bs_fr"] = t6
                    va = sch.op("vector", lambda e, ng=ng: e.tensor_add(
                        accA[ng][:], accA[ng][:], pb(5)[:, 0:512]),
                        wait=[("s_t", t5)], inc="s_v")
                    bankf[5] = ("s_v", va)
                    sch.drain("vector")
                    vb2 = sch.op("vector", lambda e, ng=ng: e.tensor_add(
                        accB[ng][:], accB[ng][:], pb(6)[:, 0:512]),
                        wait=[("s_t", t6)], inc="s_v")
                    bankf[6] = ("s_v", vb2)
                    sch.drain("vector")
        for ng in range(2):
            coff = qq * 1024 + ng * 512
            sch.op("vector", lambda e, ng=ng, coff=coff: e.tensor_scalar_mul(
                m2_rep[:, coff:coff + 512], accA[ng][:], 1.0 / 2048.0))
            sch.drain("vector")
            sch.op("vector", lambda e, coff=coff: e.tensor_mul(
                m2scr[:], m2_rep[:, coff:coff + 512], m2_rep[:, coff:coff + 512]))
            sch.drain("vector")
            vv2 = sch.op("vector", lambda e, ng=ng: e.scalar_tensor_tensor(
                m2scr[:], accB[ng][:], 1.0 / 2048.0, m2scr[:],
                ALU.mult, ALU.subtract), inc="s_v")
            s2 = rsqrt_act(r2p, m2scr, LN_EPS, wait=[("s_v", vv2)])
            sch.op("vector", lambda e: e.tensor_mul(r3s[:], r2p[:], r2p[:]),
                   wait=[("s_s", s2)])
            sch.drain("vector")
            vv3 = sch.op("vector", lambda e: e.tensor_mul(
                r3s[:], r3s[:], m2scr[:]), inc="s_v")
            s3 = rsqrt_act(r3s, r3s, EPS_NORM, wait=[("s_v", vv3)])
            vv4 = sch.op("vector", lambda e, coff=coff: e.tensor_mul(
                rr_rep[:, coff:coff + 512], r2p[:], r3s[:]),
                wait=[("s_s", s3)], inc="s_v")
        st["accs_fr"] = sch.count("s_v")

    def max_pass(qq):
        if qq == 1:
            for oc in range(16):
                for ng in range(2):
                    coff = qq * 1024 + ng * 512
                    sch.op("vector", lambda e, oc=oc, ng=ng, coff=coff: e.tensor_sub(
                        xnt[:], x1r[oc][:, ng * 512:(ng + 1) * 512],
                        m2_rep[:, coff:coff + 512]))
                    sch.drain("vector")
                    sch.op("vector", lambda e, oc=oc, coff=coff: e.scalar_tensor_tensor(
                        xnu[:], xnt[:], gam16[:, oc:oc + 1],
                        rr_rep[:, coff:coff + 512], ALU.mult, ALU.mult))
                    sch.drain("vector")
                    sch.op("vector", lambda e: e.tensor_reduce(
                        smalls[:, SM["rmx"]:SM["rmx"] + 1], xnu[:], AXX, ALU.max, apply_absolute_value=True))
                    sch.drain("vector")
                    sch.op("vector", lambda e: e.tensor_max(
                        smalls[:, SM["runmax"]:SM["runmax"] + 1],
                        smalls[:, SM["runmax"]:SM["runmax"] + 1], smalls[:, SM["rmx"]:SM["rmx"] + 1]), inc="s_v")
                    sch.drain("vector")
            return
        for sl in range(16):
            oc0 = (sl // 2) * 2
            ng = sl % 2
            coff = ng * 512
            w = dma("scalar", mxstg[sl % 2][:].rearrange("p (a t) -> p a t", a=2),
                    x1q0_d[oc0 * 128:(oc0 + 2) * 128, ng * 512:(ng + 1) * 512]
                    .rearrange("(a p) t -> p a t", p=128),
                    wait=[("s_v", st.get(f"mx_fr{sl % 2}", 0))])
            for half in range(2):
                oc = oc0 + half
                sch.op("vector", lambda e, half=half, coff=coff, sl=sl: e.tensor_sub(
                    xnt[:], mxstg[sl % 2][:, half * 512:(half + 1) * 512],
                    m2_rep[:, coff:coff + 512]), wait=[w])
                sch.drain("vector")
                sch.op("vector", lambda e, oc=oc, coff=coff: e.scalar_tensor_tensor(
                    xnu[:], xnt[:], gam16[:, oc:oc + 1], rr_rep[:, coff:coff + 512],
                    ALU.mult, ALU.mult))
                sch.drain("vector")
                vm = sch.op("vector", lambda e: e.tensor_reduce(
                    smalls[:, SM["rmx"]:SM["rmx"] + 1], xnu[:], AXX, ALU.max, apply_absolute_value=True))
                st[f"mx_fr{sl % 2}"] = vm
                sch.drain("vector")
                sch.op("vector", lambda e: e.tensor_max(
                    smalls[:, SM["runmax"]:SM["runmax"] + 1],
                    smalls[:, SM["runmax"]:SM["runmax"] + 1], smalls[:, SM["rmx"]:SM["rmx"] + 1]), inc="s_v")
                sch.drain("vector")

    def collective_trigger():
        tcur = sch.op("tensor", lambda e: e.transpose(
            pb(7)[0:1, 0:128], smalls[:, SM["runmax"]:SM["runmax"] + 1], ident[:]),
            wait=[("s_v", sch.count("s_v")), bankf[7]], inc="s_t")
        sch.op("vector", lambda e: e.tensor_reduce(
            smalls[0:1, SM["lmax"]:SM["lmax"] + 1], pb(7)[0:1, 0:128], AXX,
            ALU.max), wait=[("s_t", tcur)])
        sch.drain("vector")
        sch.op("vector", lambda e: e.tensor_scalar_max(
            smalls[0:1, SM["lmax"]:SM["lmax"] + 1],
            smalls[0:1, SM["lmax"]:SM["lmax"] + 1], EPS_SCALE))
        sch.drain("vector")
        vv1 = sch.op("vector", lambda e: e.tensor_copy(
            ccs[0:1, 0:1], smalls[0:1, SM["lmax"]:SM["lmax"] + 1]), inc="s_v")
        bankf[7] = ("s_v", vv1)
        w = dma("sync", ccin_d, ccs[:], wait=[("s_v", vv1)])
        g1 = sch.op("gpsimd", lambda e: e.collective_compute(
            "AllReduce", ALU.max, replica_groups=[[0, 1, 2, 3, 4, 5, 6, 7]],
            ins=[ccin_d], outs=[ccout_d]), wait=[w], inc="s_g")
        g2 = sch.op("gpsimd", lambda e: e.dma_start(out=ccs[:], in_=ccout_d),
                    wait=[("s_g", g1)], inc="s_g2", inc_by=16)
        st["g2"] = g2

    def collective_scales():
        g2 = st["g2"]
        w2 = dma("sync", smalls[:, SM["Scol"]:SM["Scol"] + 1],
                 ccout_d[0:1, 0:1].partition_broadcast(128), wait=[("s_g2", g2)])
        sch.op("vector", lambda e: e.reciprocal(
            smalls[:, SM["rS127"]:SM["rS127"] + 1],
            smalls[:, SM["Scol"]:SM["Scol"] + 1]), wait=[w2])
        sch.drain("vector")
        sch.op("vector", lambda e: e.tensor_scalar_mul(
            smalls[:, SM["rS127"]:SM["rS127"] + 1],
            smalls[:, SM["rS127"]:SM["rS127"] + 1], 127.0))
        sch.drain("vector")
        sch.op("vector", lambda e: e.tensor_mul(
            smalls[:, SM["fscale"]:SM["fscale"] + 1],
            smalls[:, SM["Scol"]:SM["Scol"] + 1],
            smalls[:, SM["swcol"]:SM["swcol"] + 1]))
        sch.drain("vector")
        sch.op("vector", lambda e: e.tensor_scalar_mul(
            smalls[:, SM["fscale"]:SM["fscale"] + 1],
            smalls[:, SM["fscale"]:SM["fscale"] + 1], -1.0 / 127.0))
        sch.drain("vector")
        vv2 = sch.op("vector", lambda e: e.tensor_scalar_mul(
            rr_rep[:], rr_rep[:], smalls[:, SM["rS127"]:SM["rS127"] + 1]),
            inc="s_v")
        sch.drain("vector")
        if DEBUG:
            dma("scalar", dbg["sS"], ccs[:], wait=[("s_g2", g2)])
        st["scales_v"] = vv2

    def wtern():
        for p in range(16):
            w = dma("scalar", wstg[p % 2][:], wt_d[p * 128:(p + 1) * 128, :],
                    wait=[("s_v", st.get(f"wstg_fr{p % 2}", 0)),
                          ("s_v", st["pre_coll_v"])])
            va = sch.op("vector", lambda e, p=p: e.tensor_single_scalar(
                wqTt[p][:], wstg[p % 2][:],
                smalls[:, SM["halfsw"]:SM["halfsw"] + 1], ALU.is_gt),
                wait=[w, ("s_v", st["pre_coll_v"]),
                      ("s_t", st.get("rq_free2", 0))], inc="s_v")
            sch.drain("vector")
            for ng in range(4):
                vb = sch.op("vector", lambda e, p=p, ng=ng: e.tensor_single_scalar(
                    xq[ng][:], wstg[p % 2][:, ng * 512:(ng + 1) * 512],
                    smalls[:, SM["neghalfsw"]:SM["neghalfsw"] + 1], ALU.is_lt),
                    wait=[("s_v", va)], inc="s_v")
                sch.drain("vector")
                st[f"wstg_fr{p % 2}"] = sch.op(
                    "vector", lambda e, p=p, ng=ng: e.tensor_sub(
                        wqTt[p][:, ng * 512:(ng + 1) * 512],
                        wqTt[p][:, ng * 512:(ng + 1) * 512], xq[ng][:]),
                    inc="s_v")
                sch.drain("vector")
        st["wq_ready"] = sch.count("s_v")
        if DEBUG:
            dma("scalar", dbg["wq0"], wqTt[0][:], wait=[("s_v", st["wq_ready"])])

    def ffn():
        groups = [(1, 0), (1, 1), (0, 0), (0, 1)]

        def prefetch(gi):
            qqf, ng = groups[gi]
            if qqf != 0:
                return []
            rj = x1rA[:].rearrange("p (a t) -> p a t", a=16)
            w = dma("scalar",
                    rj[:, :, ng * 512:(ng + 1) * 512],
                    x1q0_d[:, ng * 512:(ng + 1) * 512]
                    .rearrange("(a p) t -> p a t", p=128),
                    wait=[("s_v", st.get(f"x1half_fr{ng}", 0))])
            return [w]

        def quant_chunk(gi, ic, x1w):
            qqf, ng = groups[gi]
            par = gi % 2
            xqg = xq2[par]
            coff = qqf * 1024 + ng * 512
            xsl = lambda oc, ng=ng: x1r[oc][:, ng * 512:(ng + 1) * 512]
            sch.op("vector", lambda e, ic=ic, coff=coff, xsl=xsl: e.tensor_sub(
                qnt[0][:], xsl(ic), m2_rep[:, coff:coff + 512]),
                wait=list(x1w) + [("s_t", st.get(f"xq_fr{par}", 0))])
            sch.drain("vector")
            sch.op("vector", lambda e, ic=ic, coff=coff: e.scalar_tensor_tensor(
                qnt[1][:], qnt[0][:], gam16[:, ic:ic + 1],
                rr_rep[:, coff:coff + 512], ALU.mult, ALU.mult))
            sch.drain("vector")
            vcl = sch.op("vector", lambda e: e.tensor_scalar(
                qnt[0][:], qnt[1][:], 127.0, -128.0, ALU.min, ALU.max),
                inc="s_v")
            sch.op("scalar", lambda e: e.copy(qni[:], qnt[0][:]),
                   wait=[("s_v", vcl)], inc="s_s")
            sch.drain("scalar")
            scp2 = sch.op("scalar", lambda e: e.copy(qnt[1][:], qni[:]),
                          inc="s_s")
            sch.op("vector", lambda e: e.tensor_sub(qnt[2][:], qnt[0][:],
                                                    qnt[1][:]),
                   wait=[("s_s", scp2)])
            sch.drain("vector")
            sch.op("vector", lambda e: e.scalar_tensor_tensor(
                qnt[0][:], qnt[2][:], 0.5, qnt[1][:], ALU.is_gt, ALU.add))
            sch.drain("vector")
            v = sch.op("vector", lambda e, ic=ic, xqg=xqg: e.scalar_tensor_tensor(
                xqg[ic][:], qnt[2][:], -0.5, qnt[0][:], ALU.is_lt,
                ALU.subtract), inc="s_v")
            sch.drain("vector")
            return v

        x1w = prefetch(0)
        for ic in range(16):
            quant_chunk(0, ic, x1w)
        vq = {0: sch.count("s_v")}
        x1w_next = prefetch(1)
        for gi, (qqf, ng) in enumerate(groups):
            par = gi % 2
            xqg = xq2[par]
            coff = qqf * 1024 + ng * 512
            xsl = lambda oc, ng=ng: x1r[oc][:, ng * 512:(ng + 1) * 512]
            for oc in range(16):
                bk = oc % 2
                for ic in range(16):
                    sch.op("tensor", lambda e, oc=oc, ic=ic, bk=bk, xqg=xqg:
                           e.matmul(pb(bk)[:, 0:512],
                                    wqTt[ic][:, oc * 128:(oc + 1) * 128],
                                    xqg[ic][:], start=(ic == 0), stop=(ic == 15)),
                           wait=([bankf[bk], ("s_v", vq[gi]),
                                  ("s_v", st.get("wq_ready", 0))]
                                 if ic == 0 else ()),
                           inc=("s_t" if ic == 15 else None))
                tcur = sch.count("s_t")
                vo = sch.op("vector", lambda e, oc=oc, bk=bk, xsl=xsl:
                            e.scalar_tensor_tensor(
                                ostg[bk][:], pb(bk)[:, 0:512],
                                smalls[:, SM["fscale"]:SM["fscale"] + 1],
                                xsl(oc), ALU.mult, ALU.add),
                            wait=[("s_t", tcur),
                                  st.get(f"ostg_fr{bk}", ("s_v", 0))], inc="s_v")
                bankf[bk] = ("s_v", vo)
                st[f"ostg_fr{bk}"] = dma(
                    "scalar", out_d[oc * 128:(oc + 1) * 128, coff:coff + 512],
                    ostg[bk][:], wait=[("s_v", vo)])
                if gi + 1 < 4:
                    quant_chunk(gi + 1, oc, x1w_next)
            st[f"xq_fr{par}"] = sch.count("s_t")
            st[f"x1half_fr{ng}"] = sch.count("s_v")
            if gi + 1 < 4:
                vq[gi + 1] = sch.count("s_v")
            if gi + 2 < 4:
                x1w_next = prefetch(gi + 2)

    def dbg_dump(qq):
        if not DEBUG or qq != 0:
            return
        v = sch.count("s_v")
        s = sch.count("s_s")
        dma("scalar", dbg["r1"], r1_rep[:], wait=[("s_s", s), ("s_v", v)])
        dma("scalar", dbg["k00"], kT[0][0][:], wait=[("s_v", v)])
        dma("scalar", dbg["v8"], vt[8][:], wait=[("s_v", v)])

    def dbg_dump_q(qq):
        if not DEBUG or qq != 0:
            return
        v = sch.count("s_v")
        dma("scalar", dbg["q0"], qT[0][:], wait=[("s_v", v)])
        dma("scalar", dbg["q1"], qT[1][:], wait=[("s_v", v)])

    def dbg_dump_attn(qq):
        if not DEBUG or qq != 0:
            return
        v = sch.count("s_v")
        dma("scalar", dbg["attn0"], att[0][:], wait=[("s_v", v)])
        dma("scalar", dbg["attn1"], att[1][:], wait=[("s_v", v)])

    def dbg_dump_wo(qq):
        if not DEBUG:
            return
        v = sch.count("s_v")
        if qq == 1:
            dma("scalar", dbg["x1c0"], x1r[0][:], wait=[("s_v", v)])
            dma("scalar", dbg["m2"], m2_rep[:], wait=[("s_v", v)])
            dma("scalar", dbg["rr"], rr_rep[:], wait=[("s_v", v)])

    st["gate"] = [("s_v", CONST_V)]
    for qq in (0, 1):
        kv_phase(qq)
        dbg_dump(qq)
        q_phase(qq)
        dbg_dump_q(qq)
        attn_phase(qq)
        dbg_dump_attn(qq)
        st["accs_fr"] = st.get("accs_fr", 0)
        wo_phase(qq)
        if qq == 0:
            st["gate"] = [("s_t", sch.count("s_t")),
                          ("s_v", sch.count("s_v")),
                          ("s_s", sch.count("s_s"))]
        max_pass(qq)
        dbg_dump_wo(qq)
        if qq == 0:
            st["mx0_v"] = sch.count("s_v")
    st["pre_coll_v"] = sch.count("s_v")
    st["rq_free2"] = sch.count("s_t")
    collective_trigger()
    wtern()
    collective_scales()
    ffn()

    if DEBUG:
        dma("scalar", dbg["smalls"], smalls[:], wait=[("s_v", sch.count("s_v"))])
        dma("scalar", dbg["wabs"], wabs[:], wait=[("s_v", sch.count("s_v"))])
    for sem_n, v in dma_barrier("scalar") + dma_barrier("sync"):
        sch.prog["gpsimd"].append(
            lambda e, sems, sem_n=sem_n, v=v: e.wait_ge(sems[sem_n], v))

    with nc.Block() as block:
        @block.sync
        def _(eng):
            for f in sch.prog["sync"]:
                f(eng, sems)

        @block.scalar
        def _(eng):
            for f in sch.prog["scalar"]:
                f(eng, sems)

        @block.vector
        def _(eng):
            for f in sch.prog["vector"]:
                f(eng, sems)

        @block.tensor
        def _(eng):
            for f in sch.prog["tensor"]:
                f(eng, sems)

        @block.gpsimd
        def _(eng):
            for f in sch.prog["gpsimd"]:
                f(eng, sems)

    return nc, dbg


def _host_prep(inputs):
    x = np.asarray(inputs["L0_attn_x"], np.float32)
    wq = np.ascontiguousarray(np.asarray(inputs["L0_attn_Wq"], np.float32))
    wk = np.ascontiguousarray(np.asarray(inputs["L0_attn_Wk"], np.float32))
    wv = np.ascontiguousarray(np.asarray(inputs["L0_attn_Wv"], np.float32))
    wo = np.ascontiguousarray(np.asarray(inputs["L0_attn_Wo"], np.float32))
    wt = np.ascontiguousarray(np.asarray(inputs["L0_ffn_weight"], np.float32).T)
    qsc = np.asarray(inputs["L0_attn_q_scale"], np.float32)
    ksc = np.asarray(inputs["L0_attn_k_scale"], np.float32)
    gam = np.asarray(inputs["L0_ffn_norm_weight"], np.float32)

    qs2 = np.ascontiguousarray(qsc.reshape(2, 128).T)
    ks2 = np.ascontiguousarray(ksc.reshape(2, 128).T)
    gam16 = np.ascontiguousarray(gam.reshape(16, 128).T)
    ones128 = np.ones((128, 128), dtype=BF)
    ident = np.eye(128, dtype=np.float32)

    half = 64
    theta = 1.0 / (10000.0 ** (np.arange(0, ROT_, 2, dtype=np.float32) / ROT_))
    in_maps = []
    for b in range(B):
        xb_T = np.ascontiguousarray(x[b].T)
        for hh in range(2):
            lo = hh * 2048 - 1024
            xT = np.zeros((E, NLOC), np.float32)
            s0 = max(0, lo)
            xT[:, s0 - lo:3072] = xb_T[:, s0:lo + 3072]
            pos = np.arange(lo, lo + NLOC, dtype=np.float32)
            ang = pos[None, :] * theta[:, None]
            cosT = np.zeros((128, NLOC), np.float32)
            sinT = np.zeros((128, NLOC), np.float32)
            cosT[0::2] = np.cos(ang)
            cosT[1::2] = np.cos(ang)
            sinT[0::2] = -np.sin(ang)
            sinT[1::2] = np.sin(ang)
            masks = np.zeros((48, 128, 512), dtype=BF)
            for qq in range(2):
                for og in range(2):
                    A_loc = 8 + qq * 8 + og * 4
                    for kji in range(12):
                        kjt = A_loc - 8 + kji
                        K = lo + kjt * 128 + np.arange(128)[:, None]
                        Q = lo + A_loc * 128 + np.arange(512)[None, :]
                        ok = (K >= 0) & (K <= Q) & (Q - K < WIN_)
                        masks[(qq * 2 + og) * 12 + kji] = ok.astype(BF)
            in_maps.append({
                "xT": xT, "wq": wq, "wk": wk, "wv": wv, "wo": wo, "wt": wt,
                "qs2": qs2, "ks2": ks2, "gam": gam16,
                "cosT": cosT, "sinT": sinT, "masks": masks,
                "ones128": ones128, "ident128": ident,
            })
    return in_maps


ROT_ = 128
WIN_ = 1024

_NC_CACHE = {}


def kernel(**inputs):
    key = "nc"
    if key not in _NC_CACHE:
        _NC_CACHE[key] = build_nc()
    nc, dbg = _NC_CACHE[key]
    in_maps = _host_prep(inputs)
    res = run_bass_kernel_spmd(nc, in_maps, core_ids=list(range(8)))
    out = np.zeros((B, S, E), np.float32)
    for b in range(B):
        for hh in range(2):
            core = b * 2 + hh
            out[b, hh * 2048:(hh + 1) * 2048, :] = res.results[core]["out"].T
    if DEBUG:
        kernel._last = res
    return out


# version 08
# speedup vs baseline: 1.0460x; 1.0460x over previous
import os
import numpy as np
import ml_dtypes
import concourse.bass as bass
import concourse.mybir as mybir
from concourse.bass_utils import run_bass_kernel_spmd

F32 = mybir.dt.float32
BF16 = mybir.dt.bfloat16
I32 = mybir.dt.int32
AF = mybir.ActivationFunctionType
ALU = mybir.AluOpType
BF = ml_dtypes.bfloat16
AXX = mybir.AxisListType.X

B, S, E = 4, 4096, 2048
NH, NKV, HD = 8, 4, 256
QSCALE = 256 ** (-0.5)
EPS_QK = 1e-6
LN_EPS = 1e-5
EPS_NORM = 1e-6
EPS_SCALE = 1e-5
NLOC = 3072
KB = 1024

DEBUG = int(os.environ.get("MK_DEBUG", "0"))


class TView:

    def __init__(self, t, off, width):
        self.t = t
        self.off = off
        self.width = width

    def __getitem__(self, key):
        if isinstance(key, tuple):
            r, c = key
            start = (c.start or 0) + self.off
            stop = (c.stop if c.stop is not None else self.width) + self.off
            return self.t[r, start:stop]
        return self.t[key, self.off:self.off + self.width]


class Sched:
    ENGS = ("sync", "scalar", "vector", "tensor", "gpsimd")

    def __init__(self):
        self.prog = {e: [] for e in self.ENGS}
        self.C = {}
        self.hw = {e: {} for e in self.ENGS}
        self.pend = {}

    def count(self, sem):
        return self.C.get(sem, 0)

    DEFSEM = {"vector": "s_v", "scalar": "s_s"}

    def op(self, eng, fn, wait=(), inc=None, inc_by=1):
        if inc is None and eng in self.DEFSEM:
            inc = self.DEFSEM[eng]
        waits = list(wait)
        if self.pend.pop(eng, None):
            dsem = self.DEFSEM[eng]
            waits.append((dsem, self.count(dsem)))
        wl = []
        for sem, val in waits:
            if val > self.hw[eng].get(sem, 0):
                wl.append((sem, val))
                self.hw[eng][sem] = val
        if inc is not None:
            self.C[inc] = self.count(inc) + inc_by

        def emit(e, sems, waits=wl, fn=fn, inc=inc, inc_by=inc_by):
            for sem, val in waits:
                e.wait_ge(sems[sem], val)
            instr = fn(e)
            if inc is not None:
                instr.then_inc(sems[inc], inc_by)

        self.prog[eng].append(emit)
        return self.count(inc) if inc is not None else None

    def drain(self, eng):
        self.pend[eng] = True


def build_nc():
    nc = bass.Bass("TRN2", target_bir_lowering=False)

    xT_d = nc.dram_tensor("xT", [E, NLOC], F32, kind="ExternalInput").ap()
    wq_d = nc.dram_tensor("wq", [E, 2048], F32, kind="ExternalInput").ap()
    wk_d = nc.dram_tensor("wk", [E, 1024], F32, kind="ExternalInput").ap()
    wv_d = nc.dram_tensor("wv", [E, 1024], F32, kind="ExternalInput").ap()
    wo_d = nc.dram_tensor("wo", [E, 2048], F32, kind="ExternalInput").ap()
    wt_d = nc.dram_tensor("wt", [E, 2048], F32, kind="ExternalInput").ap()
    qs2_d = nc.dram_tensor("qs2", [128, 2], F32, kind="ExternalInput").ap()
    ks2_d = nc.dram_tensor("ks2", [128, 2], F32, kind="ExternalInput").ap()
    gam_d = nc.dram_tensor("gam", [128, 16], F32, kind="ExternalInput").ap()
    cos_d = nc.dram_tensor("cosT", [128, NLOC], F32, kind="ExternalInput").ap()
    sin_d = nc.dram_tensor("sinT", [128, NLOC], F32, kind="ExternalInput").ap()
    masks_d = nc.dram_tensor("masks", [48, 128, 512], BF16, kind="ExternalInput").ap()
    ones_d = nc.dram_tensor("ones128", [128, 128], BF16, kind="ExternalInput").ap()
    id_d = nc.dram_tensor("ident128", [128, 128], F32, kind="ExternalInput").ap()
    out_d = nc.dram_tensor("out", [E, 2048], F32, kind="ExternalOutput").ap()
    x1q0_d = nc.dram_tensor("x1q0", [E, 1024], F32, kind="Internal").ap()
    ccin_d = nc.dram_tensor("ccin", [1, 16], F32, kind="Internal").ap()
    bc_d = nc.dram_tensor("bc_d", [1, 16], F32, kind="Internal").ap()
    ccout_d = nc.dram_tensor("ccout", [1, 16], F32, kind="Internal").ap()

    dbg = {}
    if DEBUG:
        def dbgt(name, shape, dt=F32):
            dbg[name] = nc.dram_tensor("dbg_" + name, shape, dt,
                                       kind="ExternalOutput").ap()
        dbgt("r1", [128, NLOC]); dbgt("k00", [128, 2048], BF16)
        dbgt("v8", [128, 1024], BF16)
        dbgt("q0", [128, 1024], BF16); dbgt("q1", [128, 1024], BF16)
        dbgt("pt", [128, 12 * 512], BF16)
        dbgt("attn0", [128, 1024], BF16); dbgt("attn1", [128, 1024], BF16)
        dbgt("x1c0", [128, 1024]); dbgt("m2", [128, 2048]); dbgt("rr", [128, 2048])
        dbgt("sS", [1, 16]); dbgt("xq0", [128, 512], BF16); dbgt("wq0", [128, 2048], BF16)
        dbgt("ss_k", [128, 512]); dbgt("F_k", [128, 512]); dbgt("r1sq_k", [128, 512])
        dbgt("maskb", [128, 12 * 512], BF16); dbgt("pta", [128, 512], BF16)
        dbgt("smalls", [128, 16]); dbgt("wabs", [128, 16])
        dbgt("wcol", [128, 1]); dbgt("wrowT", [1, 128])
        dbgt("sums", [128, 512]); dbgt("ttmp", [128, 512])
        dbgt("kstar", [128, 512])

    SBUF_BASE = 16512

    def at(name, shape, dt, off):
        return nc.alloc_sbuf_tensor_at(name, shape, dt, offset=SBUF_BASE + off)

    o = [0]

    def nxt(sz):
        off = (o[0] + 31) // 32 * 32
        o[0] = off + sz
        return off

    ones = at("ones", [128, 128], BF16, nxt(256))
    onesf = at("onesf", [128, 128], F32, nxt(512))
    ident = at("ident", [128, 128], F32, nxt(512))
    qs16 = at("qs16", [128, 2], F32, nxt(32))
    ks16 = at("ks16", [128, 2], F32, nxt(32))
    gam16 = at("gam16", [128, 16], F32, nxt(64))
    smalls = at("smalls", [128, 16], F32, nxt(64))
    SM = dict(runmax=0, lmax=1, Scol=2, rS127=3, swcol=4, halfsw=5,
              neghalfsw=6, fscale=7, swrow=8, eps5=10, eps6=11, rmx=12)
    ccs = at("ccs", [1, 16], F32, nxt(64))
    _mo_off = (o[0] + 31) // 32 * 32
    mrow_own = at("mrow_own", [1, 1024], BF16, nxt(2048))
    dbgw_s = at("dbgw_s", [1, 128], F32, _mo_off)
    r1col = at("r1col", [128, 16], F32, nxt(64))
    wabs = at("wabs", [128, 16], F32, nxt(64))
    wcol = at("wcol", [128, 1], F32, nxt(32))
    wrow = at("wrow", [1, 2048], BF16, nxt(4096))
    mrow_kv = at("mrow_kv", [1, 512], BF16, nxt(1024))
    r1_rep = at("r1_rep", [128, NLOC], F32, nxt(NLOC * 4))
    rr_rep = at("rr_rep", [128, 2048], F32, nxt(8192))
    m2_rep = at("m2_rep", [128, 2048], F32, nxt(8192))
    PERM_END = (o[0] + 31) // 32 * 32
    R_K = PERM_END
    R_V = R_K + 32 * KB
    R_Q = R_V + 32 * KB
    R_AT = R_Q + 32 * KB
    R_FX = R_AT + 32 * KB
    FX_END = R_FX + 43 * KB
    assert FX_END + 16512 <= 229376, (PERM_END, FX_END)

    kT = [[at(f"kT{j}_{d}", [128, 2048], BF16, R_K + (j * 2 + d) * 4 * KB)
           for d in range(2)] for j in range(NKV)]
    vt = [at(f"v{t}", [128, 1024], BF16, R_V + t * 2 * KB) for t in range(16)]
    qT = [at(f"qT{c}", [128, 1024], BF16, R_Q + c * 2 * KB) for c in range(16)]
    att = [at(f"att{c}", [128, 1024], BF16, R_AT + c * 2 * KB) for c in range(16)]

    WkbA = at("WkbA", [128, 16 * 1024], BF16, R_Q)
    Wkb = [TView(WkbA, c * 1024, 1024) for c in range(16)]
    WvbA = at("WvbA", [128, 16 * 1024], BF16, R_AT)
    Wvb = [TView(WvbA, c * 1024, 1024) for c in range(16)]
    xbA = [at(f"xbA{g}", [128, 16 * 512], BF16, R_FX + g * 16 * KB)
           for g in range(2)]
    xb = [[TView(xbA[g], c * 512, 512) for c in range(16)] for g in range(2)]
    xstg = [at(f"xstg{i}", [128, 1024], F32, R_FX + 16 * KB + i * 4 * KB)
            for i in range(2)]
    kvs = R_FX + 24 * KB
    kf = [at(f"kf{i}", [128, 512], F32, kvs + i * 2 * KB) for i in range(2)]
    krt = at("krt", [128, 512], F32, kvs + 4 * KB)
    krs = at("krs", [128, 512], F32, kvs + 6 * KB)
    Fsc = at("Fsc", [128, 512], F32, kvs + 8 * KB)
    r1sq = at("r1sq", [128, 512], F32, kvs + 10 * KB)
    cosk = at("cosk", [128, 512], F32, kvs + 12 * KB)
    sink = at("sink", [128, 512], F32, kvs + 14 * KB)
    ksqb = at("ksqb", [128, 512], BF16, kvs + 16 * KB)
    mscr = at("mscr", [128, 512], F32, kvs + 4 * KB)
    vscr = at("vscr", [128, 512], F32, kvs + 6 * KB)

    wqstg = at("wqstg8", [128, 8 * 256], F32, R_FX + 32 * KB)
    WqbA = [at(f"WqbA{i}", [128, 16 * 256], BF16, R_AT + i * 8 * KB)
            for i in range(2)]
    qf = [at(f"qf{i}", [128, 512], F32, R_AT + 16 * KB + i * 2 * KB)
          for i in range(2)]
    qFsc = at("qFsc", [128, 512], F32, R_AT + 20 * KB)
    qr1sq = at("qr1sq", [128, 512], F32, R_AT + 22 * KB)
    qsqb = at("qsqb", [128, 512], BF16, R_AT + 24 * KB)
    cosq = at("cosq", [128, 512], F32, R_AT + 25 * KB)
    sinq = at("sinq", [128, 512], F32, R_AT + 27 * KB)
    qrt = at("qrt", [128, 512], F32, R_AT + 29 * KB)

    maskb = at("maskb", [128, 12 * 512], BF16, R_FX)
    pT = [at(f"pT{i}", [128, 12 * 512], BF16, R_FX + 12 * KB + i * 12 * KB)
          for i in range(2)]
    pTa = [at(f"pTa{i}", [128, 1024], BF16, R_FX + 36 * KB + i * 2 * KB)
           for i in range(2)]
    rec = at("rec", [128, 512], F32, R_FX + 40 * KB)

    wostg = at("wostg", [128, 16 * 256], F32, R_FX)
    Wob = [at(f"Wob{i}", [128, 16 * 256], BF16, R_FX + 16 * KB + i * 8 * KB)
           for i in range(2)]
    xck = [at(f"xck{i}", [128, 512], F32, R_FX + 32 * KB + i * 2 * KB)
           for i in range(4)]
    accA = [at(f"accA{n}", [128, 512], F32, R_Q + n * 2 * KB) for n in range(2)]
    accB = [at(f"accB{n}", [128, 512], F32, R_Q + 4 * KB + n * 2 * KB)
            for n in range(2)]
    x1b_s = at("x1b_s", [128, 512], BF16, R_Q + 8 * KB)
    x1sq_s = at("x1sq_s", [128, 512], BF16, R_Q + 9 * KB)
    m2scr = at("m2scr", [128, 512], F32, R_Q + 10 * KB)
    r2p = at("r2p", [128, 512], F32, R_Q + 12 * KB)
    r3s = at("r3s", [128, 512], F32, R_Q + 14 * KB)
    xnt = at("xnt", [128, 512], F32, R_Q + 16 * KB)
    xnu = at("xnu", [128, 512], F32, R_Q + 18 * KB)
    x1stg = [at(f"x1stg{i}", [128, 512], F32, R_Q + 20 * KB + i * 2 * KB)
             for i in range(2)]
    mxstg = [at(f"mxstg{i}", [128, 1024], F32, R_Q + 24 * KB + i * 4 * KB)
             for i in range(2)]
    rmx = None
    x1rA = at("x1rA", [128, 16 * 1024], F32, R_K)
    x1r = [TView(x1rA, c * 1024, 1024) for c in range(16)]

    wqTt = [at(f"wqT{c}", [128, 2048], BF16, R_Q + c * 4 * KB) for c in range(16)]
    wstg = [at(f"wstg{i}", [128, 2048], F32, R_FX + i * 8 * KB) for i in range(2)]
    xqA_t = at("xqA", [128, 16 * 512], BF16, R_FX + 16 * KB)
    xqB_t = at("xqB", [128, 16 * 512], BF16, R_FX)
    xq = [TView(xqA_t, c * 512, 512) for c in range(16)]
    xq2 = [[TView(xqA_t, c * 512, 512) for c in range(16)],
           [TView(xqB_t, c * 512, 512) for c in range(16)]]
    ostg = [at(f"ostg{i}", [128, 512], F32, R_FX + 32 * KB + i * 2 * KB)
            for i in range(2)]
    qnt = [at(f"qnt{i}", [128, 512], F32, R_FX + 36 * KB + i * 2 * KB)
           for i in range(3)]
    qni = at("qni", [128, 512], I32, R_FX + 40 * KB)

    ps_guards = []
    g01 = nc.psum_tensor("ps01", [128, 1024], F32)
    P01t = g01.__enter__()
    ps_guards.append(g01)
    pbt = [TView(P01t, 0, 512), TView(P01t, 512, 512)]
    for i in range(2, 8):
        g = nc.psum_tensor(f"ps{i}", [128, 512], F32)
        t = g.__enter__()
        ps_guards.append(g)
        pbt.append(t)

    def pb(i):
        return pbt[i]

    sems = {}
    sem_guards = {}
    for n in (["s_t", "s_s", "s_v", "s_g", "s_g2"]
              + [f"sx{i}" for i in range(8)] + [f"sy{i}" for i in range(8)]
              + [f"sg{i}" for i in range(8)]):
        g = nc.semaphore(n)
        sems[n] = g.__enter__()
        sem_guards[n] = g

    sch = Sched()
    st = {}
    bankf = {i: ("s_s", 0) for i in range(8)}

    qidx = {"sync": 0, "scalar": 0, "gpsimd": 0}

    def dma(queue, out, in_, wait=()):
        pref = "sx" if queue == "sync" else "sy"
        i = qidx[queue]
        qidx[queue] = i + 1
        lane = f"{pref}{i % 8}"
        prev = sch.count(lane)
        wl = list(wait) + ([(lane, prev)] if prev else [])
        cnt = sch.op(queue, lambda e, out=out, in_=in_:
                     e.dma_start(out=out, in_=in_), wait=wl, inc=lane, inc_by=16)
        return (lane, cnt)

    def dma_barrier(queue):
        pref = {"sync": "sx", "scalar": "sy", "gpsimd": "sg"}[queue]
        return [(f"{pref}{i}", sch.count(f"{pref}{i}"))
                for i in range(8) if sch.count(f"{pref}{i}") > 0]

    def gdma(out, in_, wait=()):
        i = qidx["gpsimd"]
        qidx["gpsimd"] = i + 1
        lane = f"sg{i % 8}"
        prev = sch.count(lane)
        wl = list(wait) + ([(lane, prev)] if prev else [])
        cnt = sch.op("gpsimd", lambda e, out=out, in_=in_:
                     e.dma_start(out=out, in_=in_), wait=wl, inc=lane, inc_by=16)
        return (lane, cnt)

    dma("sync", ones[:], ones_d)
    dma("sync", ident[:], id_d)
    dma("sync", qs16[:], qs2_d)
    dma("sync", ks16[:], ks2_d)
    w0 = dma("sync", gam16[:], gam_d)
    CONSTW = dma_barrier("sync")
    sch.op("vector", lambda e: e.tensor_scalar_mul(qs16[:], qs16[:], QSCALE),
           wait=CONSTW)
    sch.op("vector", lambda e: e.memset(onesf[:], 1.0))
    sch.op("vector", lambda e: e.memset(smalls[:, 0:1], 0.0))
    sch.op("vector", lambda e: e.memset(smalls[:, SM["eps5"]:SM["eps5"] + 1], LN_EPS))
    sch.op("vector", lambda e: e.memset(smalls[:, SM["eps6"]:SM["eps6"] + 1], EPS_QK))
    sch.op("vector", lambda e: e.memset(ccs[:], 0.0), inc="s_v")
    CONST_V = sch.count("s_v")

    for p in range(16):
        wdone = st.get("wmean_used", 0)
        w = dma("scalar", wstg[p % 2][:], wt_d[p * 128:(p + 1) * 128, :],
                wait=([("s_v", wdone)] if p >= 2 else ()))
        st["wmean_used"] = sch.op(
            "vector", lambda e, p=p: e.tensor_reduce(
                wabs[:, p:p + 1], wstg[p % 2][:], AXX, ALU.add,
                apply_absolute_value=True),
            wait=[w], inc="s_v")
    sch.drain("vector")
    sch.op("vector", lambda e: e.tensor_reduce(wcol[:], wabs[:], AXX, ALU.add),
           inc="s_v")
    t1 = sch.op("tensor", lambda e: e.transpose(pb(7)[0:1, 0:128], wcol[:],
                                                ident[:]),
                wait=[("s_v", sch.count("s_v"))] + CONSTW, inc="s_t")
    if DEBUG:
        sdw = sch.op("scalar", lambda e: e.copy(wabs[0:1, 0:16] if False else dbgw_s[:], pb(7)[0:1, 0:128]),
                     wait=[("s_t", t1)], inc="s_s")
        dma("scalar", dbg["wrowT"], dbgw_s[:], wait=[("s_s", sdw)])
        dma("scalar", dbg["wcol"], wcol[:], wait=[("s_v", sch.count("s_v"))])
    sch.op("vector", lambda e: e.tensor_reduce(
        smalls[0:1, SM["swrow"]:SM["swrow"] + 1], pb(7)[0:1, 0:128], AXX,
        ALU.add), wait=[("s_t", t1)])
    sch.drain("vector")
    vv = sch.op("vector", lambda e: e.tensor_scalar(
        smalls[0:1, SM["swrow"]:SM["swrow"] + 1],
        smalls[0:1, SM["swrow"]:SM["swrow"] + 1],
        1.0 / (2048.0 * 2048.0), EPS_SCALE, ALU.mult, ALU.max), inc="s_v")
    wv_ = dma("sync", bc_d[0:1, 0:1],
              smalls[0:1, SM["swrow"]:SM["swrow"] + 1], wait=[("s_v", vv)])
    wv_ = dma("sync", smalls[:, SM["swcol"]:SM["swcol"] + 1],
              bc_d[0:1, 0:1].partition_broadcast(128), wait=[wv_])
    sch.op("vector", lambda e: e.tensor_scalar_mul(
        smalls[:, SM["halfsw"]:SM["halfsw"] + 1],
        smalls[:, SM["swcol"]:SM["swcol"] + 1], 0.5), wait=[wv_])
    sch.drain("vector")
    sch.op("vector", lambda e: e.tensor_scalar_mul(
        smalls[:, SM["neghalfsw"]:SM["neghalfsw"] + 1],
        smalls[:, SM["halfsw"]:SM["halfsw"] + 1], -1.0), inc="s_v")
    bankf[7] = ("s_v", sch.count("s_v"))
    st0_wmean_v = sch.count("s_v")

    def load_panels_cast(w_dram, destA, ncols, extra_wait=(), per_g=None):
        ws = []
        for g in range(4):
            wt = list(extra_wait) if g == 0 else []
            if per_g:
                wt += per_g[g]
            ws.append(gdma(
                destA[:, g * 4 * ncols:(g + 1) * 4 * ncols]
                .rearrange("p (a t) -> p a t", a=4),
                w_dram[g * 512:(g + 1) * 512, 0:ncols]
                .rearrange("(a p) t -> p a t", p=128),
                wait=wt))
        return ws

    def colsum_row(chunks, ncols, row_off, wwait):
        for ng in range(0, ncols, 512):
            nn = min(512, ncols - ng)
            for c in range(16):
                sch.op("tensor", lambda e, c=c, ng=ng, nn=nn, chunks=chunks:
                       e.matmul(pb(4)[0:1, 0:nn], ones[:, 0:1],
                                chunks[c][:, ng:ng + nn],
                                start=(c == 0), stop=(c == 15)),
                       wait=(list(wwait) + [bankf[4]] if c == 0 else ()),
                       inc=("s_t" if c == 15 else None))
            tcur = sch.count("s_t")
            scur = sch.op("scalar", lambda e, ng=ng, nn=nn, row_off=row_off:
                          e.copy(wrow[0:1, row_off + ng:row_off + ng + nn],
                                 pb(4)[0:1, 0:nn]),
                          wait=[("s_t", tcur)], inc="s_s")
            bankf[4] = ("s_s", scur)

    def rsqrt_act(dst, src, bias, wait=(), pre_scale=1.0):
        dap = dst[:] if hasattr(dst, "ap") else dst
        sap = src[:] if hasattr(src, "ap") else src
        bcol = SM["eps5"] if bias == LN_EPS else SM["eps6"]
        nparts = dap.shape[0]
        bap = smalls[0:nparts, bcol:bcol + 1]
        sch.op("scalar", lambda e: e.activation(dap, sap, AF.Ln,
                                                bias=bap, scale=pre_scale),
               wait=wait)
        sch.drain("scalar")
        return sch.op("scalar", lambda e: e.activation(dap, dap, AF.Exp,
                                                       scale=-0.5), inc="s_s")

    def kv_phase(qq):
        mxg = [("s_v", st.get("mx0_v", 0))]
        first = 0 if qq == 0 else 16
        ngrp = 4 if qq == 0 else 2

        def load_xgroup(gi):
            t0 = first + gi * 4
            c0 = t0 * 128
            xgate = (st.get("gate", []) if qq == 0
                     else [("s_t", st.get("rq_free", 0))])
            ws = []
            for g4 in range(4):
                ws.append(gdma(
                    xbA[0][:, g4 * 4 * 512:(g4 + 1) * 4 * 512]
                    .rearrange("p (a t) -> p a t", a=4),
                    xT_d[g4 * 512:(g4 + 1) * 512, c0:c0 + 512]
                    .rearrange("(a p) t -> p a t", p=128),
                    wait=(([("s_t", st.get("xb0_fr", 0)),
                            ("s_v", st0_wmean_v)] + xgate)
                          if g4 == 0 else ())))
            return ws

        xws0 = load_xgroup(0)
        wWk = load_panels_cast(wk_d, WkbA, 1024,
                               extra_wait=st.get("gate", []),
                               per_g=[[], [], mxg, mxg])
        wWv = load_panels_cast(wv_d, WvbA, 1024,
                               extra_wait=st.get("gate", []))

        for gi in range(ngrp):
            t0 = first + gi * 4
            c0 = t0 * 128
            xws = xws0 if gi == 0 else load_xgroup(gi)
            for c in range(16):
                scur = sch.op("scalar", lambda e, c=c:
                              e.activation(ksqb[:], xb[0][c][:], AF.Square),
                              wait=[xws[c // 4],
                                    ("s_t", st.get("ksqb_fr", 0))], inc="s_s")
                sch.op("tensor", lambda e, c=c: e.matmul(
                    pb(5)[:, 0:512], ones[:], xb[0][c][:],
                    start=(c == 0), stop=(c == 15)),
                    wait=([bankf[5]] if c == 0 else []) + [xws[c // 4]],
                    inc=("s_t" if c == 15 else None))
                st["ksqb_fr"] = sch.op("tensor", lambda e, c=c: e.matmul(
                    pb(6)[:, 0:512], ones[:], ksqb[:],
                    start=(c == 0), stop=(c == 15)),
                    wait=([bankf[6], ("s_s", scur)] if c == 0
                          else [("s_s", scur)]),
                    inc="s_t")
            tstats = sch.count("s_t")
            st["ksq_mm"] = tstats
            if gi == 0:
                colsum_row(Wkb, 1024, 0, wWk)
            sch.op("vector", lambda e: e.tensor_scalar_mul(
                mscr[:], pb(5)[:, 0:512], 1.0 / 2048.0), wait=[("s_t", tstats)])
            sch.drain("vector")
            sch.op("vector", lambda e: e.tensor_mul(vscr[:], mscr[:], mscr[:]))
            sch.drain("vector")
            v1 = sch.op("vector", lambda e: e.scalar_tensor_tensor(
                vscr[:], pb(6)[:, 0:512], 1.0 / 2048.0, vscr[:],
                ALU.mult, ALU.subtract), inc="s_v")
            bankf[5] = ("s_v", v1)
            bankf[6] = ("s_v", v1)
            s1 = rsqrt_act(r1_rep[:, c0:c0 + 512], vscr, LN_EPS,
                           wait=[("s_v", v1)])
            v2 = sch.op("vector", lambda e: e.tensor_scalar_mul(
                mrow_kv[0:1, 0:512], mscr[0:1, 0:512], -1.0), inc="s_v")
            if qq == 1 or gi >= 2:
                og = gi if qq == 1 else gi - 2
                sch.op("vector", lambda e, og=og: e.tensor_scalar_mul(
                    mrow_own[0:1, og * 512:(og + 1) * 512],
                    mscr[0:1, 0:512], -1.0), inc="s_v")
            for tt_ in range(4):
                slot = (t0 + tt_) % 16
                tcur = sch.op("tensor", lambda e, tt_=tt_, c0=c0:
                              e.transpose(pb(7)[0:128, 0:1],
                                          r1_rep[0:1, c0 + tt_ * 128:c0 + (tt_ + 1) * 128],
                                          ident[0:1, 0:1]),
                              wait=[("s_s", s1), bankf[7]], inc="s_t")
                scur = sch.op("scalar", lambda e, slot=slot:
                              e.copy(r1col[:, slot:slot + 1], pb(7)[0:128, 0:1]),
                              wait=[("s_t", tcur)], inc="s_s")
                bankf[7] = ("s_s", scur)

            cos_w1 = dma("sync", cosk[:], cos_d[:, c0:c0 + 512],
                         wait=[("s_v", st.get("cosk_fr", 0))] + st.get("gate", []))
            cos_w2 = dma("sync", sink[:], sin_d[:, c0:c0 + 512],
                         wait=st.get("gate", []))
            r1sq_v = sch.op("vector", lambda e, c0=c0: e.tensor_mul(
                r1sq[:], r1_rep[:, c0:c0 + 512], r1_rep[:, c0:c0 + 512]),
                wait=[("s_s", s1), ("s_v", st.get("r1sq_fr", 0))], inc="s_v")
            for j in range(NKV):
                for par in range(2):
                    kc = 2 * j + par
                    bk = kc % 2
                    for ic in range(16):
                        sch.op("tensor", lambda e, ic=ic, kc=kc, bk=bk:
                               e.matmul(pb(bk)[:, 0:512],
                                        Wkb[ic][:, kc * 128:(kc + 1) * 128],
                                        xb[0][ic][:], start=(ic == 0), stop=False),
                               wait=([bankf[bk], ("s_v", v2)] if ic == 0 else ()))
                    tcur = sch.op("tensor", lambda e, kc=kc, bk=bk:
                                  e.matmul(pb(bk)[:, 0:512],
                                           wrow[0:1, kc * 128:(kc + 1) * 128],
                                           mrow_kv[0:1, 0:512],
                                           start=False, stop=True),
                                  inc="s_t")
                    scur = sch.op("scalar", lambda e, par=par, bk=bk:
                                  e.copy(kf[par][:], pb(bk)[:, 0:512]),
                                  wait=[("s_t", tcur),
                                        ("s_v", st.get("kf_fr", 0))], inc="s_s")
                    sch.drain("scalar")
                    scur = sch.op("scalar", lambda e, bk=bk:
                                  e.activation(ksqb[:], pb(bk)[:, 0:512],
                                               AF.Square),
                                  wait=[("s_t", st.get("ksq_mm", 0))], inc="s_s")
                    bankf[bk] = ("s_s", scur)
                    st["ksq_mm"] = sch.op("tensor", lambda e, par=par:
                                          e.matmul(pb(4)[:, 0:512], ones[:], ksqb[:],
                                                   start=(par == 0), stop=(par == 1)),
                                          wait=[("s_s", scur)] + ([bankf[4]] if par == 0 else []),
                                          inc="s_t")
                tsum = sch.count("s_t")
                dbg_this = DEBUG and qq == 0 and gi == 2 and j == 0
                if dbg_this:
                    dma("scalar", dbg["kstar"], kf[0][:], wait=[("s_s", sch.count("s_s"))])
                vF = sch.op("vector", lambda e: e.tensor_mul(
                    Fsc[:], pb(4)[:, 0:512], r1sq[:]),
                    wait=[("s_t", tsum), ("s_v", r1sq_v)], inc="s_v")
                if dbg_this:
                    sdbg = sch.op("scalar", lambda e: e.copy(vscr[:], pb(4)[:, 0:512]),
                                  wait=[("s_t", tsum), ("s_v", vF)], inc="s_s")
                    bankf[4] = ("s_s", sdbg)
                    dma("scalar", dbg["ss_k"], vscr[:], wait=[("s_s", sdbg)])
                bankf[4] = ("s_v", vF)
                sF = rsqrt_act(Fsc, Fsc, EPS_QK, wait=[("s_v", vF)],
                               pre_scale=1.0 / HD)
                vF = sch.op("vector", lambda e, c0=c0: e.tensor_mul(
                    Fsc[:], Fsc[:], r1_rep[:, c0:c0 + 512]),
                    wait=[("s_s", sF)], inc="s_v")
                sch.drain("vector")
                if dbg_this:
                    dma("scalar", dbg["F_k"], Fsc[:], wait=[("s_v", vF)])
                    dma("scalar", dbg["r1sq_k"], r1sq[:], wait=[("s_v", vF)])
                slotc = (t0 % 16) * 128
                vpl = sch.op("vector", lambda e, j=j, slotc=slotc:
                             e.scalar_tensor_tensor(
                                 kT[j][1][:, slotc:slotc + 512], kf[1][:],
                                 ks16[:, 1:2], Fsc[:], ALU.mult, ALU.mult),
                             wait=[("s_v", vF)], inc="s_v")
                sch.op("vector", lambda e: e.scalar_tensor_tensor(
                    krt[:], kf[0][:], ks16[:, 0:1], Fsc[:], ALU.mult, ALU.mult))
                sch.drain("vector")
                shmask = []
                for i2 in range(16):
                    shmask += [2 * i2 + 1, 2 * i2]
                sch.op("vector", lambda e: e.stream_shuffle(krs[:], krt[:], shmask))
                sch.drain("vector")
                sch.op("vector", lambda e: e.tensor_mul(krt[:], krt[:], cosk[:]),
                       wait=[cos_w1, cos_w2])
                sch.drain("vector")
                sch.op("vector", lambda e: e.tensor_mul(krs[:], krs[:], sink[:]))
                sch.drain("vector")
                vrot = sch.op("vector", lambda e, j=j, slotc=slotc:
                              e.tensor_add(kT[j][0][:, slotc:slotc + 512],
                                           krt[:], krs[:]), inc="s_v")
                st["kf_fr"] = vrot
            st["cosk_fr"] = sch.count("s_v")
            st["r1sq_fr"] = sch.count("s_v")
            if gi == 0:
                colsum_row(Wvb, 1024, 1024, wWv)
                st["wsumv"] = bankf[4]

            for tt_ in range(4):
                slot = (t0 + tt_) % 16
                for ng in range(2):
                    bk = ng % 2
                    for ic in range(16):
                        sch.op("tensor", lambda e, ic=ic, tt_=tt_, ng=ng, bk=bk:
                               e.matmul(pb(bk)[:, 0:512],
                                        xb[0][ic][:, tt_ * 128:(tt_ + 1) * 128],
                                        Wvb[ic][:, ng * 512:(ng + 1) * 512],
                                        start=(ic == 0), stop=False),
                               wait=([bankf[bk]] if ic == 0 else ()))
                    tcur = sch.op("tensor", lambda e, tt_=tt_, ng=ng, bk=bk:
                                  e.matmul(pb(bk)[:, 0:512],
                                           mrow_kv[0:1, tt_ * 128:(tt_ + 1) * 128],
                                           wrow[0:1, 1024 + ng * 512:1024 + (ng + 1) * 512],
                                           start=False, stop=True),
                                  wait=[st.get("wsumv", ("s_s", 0))], inc="s_t")
                    vcur = sch.op("vector", lambda e, slot=slot, ng=ng, bk=bk:
                                  e.tensor_scalar_mul(
                                      vt[slot][:, ng * 512:(ng + 1) * 512],
                                      pb(bk)[:, 0:512], r1col[:, slot:slot + 1]),
                                  wait=[("s_t", tcur)], inc="s_v")
                    bankf[bk] = ("s_v", vcur)
            st["xb0_fr"] = sch.count("s_t")
        st["kv_done_t"] = sch.count("s_t")
        st["kv_done_v"] = sch.count("s_v")

    def q_phase(qq):
        xbws = []
        for og in range(2):
            c0 = 1024 + qq * 1024 + og * 512
            for g4 in range(4):
                xbws.append(gdma(
                    xbA[og][:, g4 * 4 * 512:(g4 + 1) * 4 * 512]
                    .rearrange("p (a t) -> p a t", a=4),
                    xT_d[g4 * 512:(g4 + 1) * 512, c0:c0 + 512]
                    .rearrange("(a p) t -> p a t", p=128),
                    wait=([("s_t", st.get("xb_own_fr", 0)),
                           ("s_t", st.get("kv_done_t", 0)),
                           ("s_v", st.get("kv_done_v", 0))] if g4 == 0 and og == 0
                          else ())))
        vxb_w = xbws

        shmask = []
        for i2 in range(16):
            shmask += [2 * i2 + 1, 2 * i2]
        for blk in range(8):
            h = blk
            wqw = gdma(WqbA[blk % 2][:].rearrange("p (a t) -> p a t", a=16),
                       wq_d[:, h * 256:(h + 1) * 256]
                       .rearrange("(a p) t -> p a t", p=128),
                       wait=[("s_t", st.get(f"wqb_fr{blk % 2}", 0))]
                       + (vxb_w if blk == 0 else []))
            vwq = wqw
            for c in range(16):
                sch.op("tensor", lambda e, c=c, blk=blk:
                       e.matmul(pb(4)[0:1, 0:256], ones[:, 0:1],
                                WqbA[blk % 2][:, c * 256:(c + 1) * 256],
                                start=(c == 0), stop=(c == 15)),
                       wait=([vwq, bankf[4]] if c == 0 else ()),
                       inc=("s_t" if c == 15 else None))
            tcur = sch.count("s_t")
            swsum = sch.op("scalar", lambda e: e.copy(wrow[0:1, 0:256],
                                                      pb(4)[0:1, 0:256]),
                           wait=[("s_t", tcur)], inc="s_s")
            bankf[4] = ("s_s", swsum)

            for og in range(2):
                c0_loc = 1024 + qq * 1024 + og * 512
                cw1 = dma("sync", cosq[:], cos_d[:, c0_loc:c0_loc + 512],
                          wait=[("s_v", st.get("cosq_fr", 0)),
                                ("s_t", st.get("kv_done_t", 0))])
                cw2 = dma("sync", sinq[:], sin_d[:, c0_loc:c0_loc + 512],
                          wait=[("s_t", st.get("kv_done_t", 0))])
                vr1 = sch.op("vector", lambda e, c0_loc=c0_loc: e.tensor_mul(
                    qr1sq[:], r1_rep[:, c0_loc:c0_loc + 512],
                    r1_rep[:, c0_loc:c0_loc + 512]),
                    wait=[("s_v", st.get("qr1sq_fr", 0))], inc="s_v")
                for par in range(2):
                    bk = par
                    for ic in range(16):
                        sch.op("tensor", lambda e, ic=ic, par=par, og=og, bk=bk, blk=blk:
                               e.matmul(pb(bk)[:, 0:512],
                                        WqbA[blk % 2][:, ic * 256 + par * 128:
                                                      ic * 256 + (par + 1) * 128],
                                        xb[og][ic][:], start=(ic == 0), stop=False),
                               wait=(([bankf[bk], ("s_s", swsum)] + vxb_w)
                                     if ic == 0 else ()))
                    tcur = sch.op("tensor", lambda e, par=par, og=og, bk=bk:
                                  e.matmul(pb(bk)[:, 0:512],
                                           wrow[0:1, par * 128:(par + 1) * 128],
                                           mrow_own[0:1, og * 512:(og + 1) * 512],
                                           start=False, stop=True), inc="s_t")
                    st[f"wqb_fr{blk % 2}"] = tcur
                    scur = sch.op("scalar", lambda e, par=par, bk=bk:
                                  e.copy(qf[par][:], pb(bk)[:, 0:512]),
                                  wait=[("s_t", tcur),
                                        ("s_v", st.get("qf_fr", 0))], inc="s_s")
                    sch.drain("scalar")
                    scur = sch.op("scalar", lambda e, bk=bk:
                                  e.activation(qsqb[:], pb(bk)[:, 0:512], AF.Square),
                                  wait=[("s_t", st.get("qsq_mm", 0))], inc="s_s")
                    bankf[bk] = ("s_s", scur)
                    st["qsq_mm"] = sch.op(
                        "tensor", lambda e, par=par:
                        e.matmul(pb(4)[:, 0:512], ones[:], qsqb[:],
                                 start=(par == 0), stop=(par == 1)),
                        wait=[("s_s", scur)] + ([bankf[4]] if par == 0 else []),
                        inc="s_t")
                tsum = sch.count("s_t")
                vF = sch.op("vector", lambda e: e.tensor_mul(
                    qFsc[:], pb(4)[:, 0:512], qr1sq[:]),
                    wait=[("s_t", tsum), ("s_v", vr1)], inc="s_v")
                bankf[4] = ("s_v", vF)
                sF = rsqrt_act(qFsc, qFsc, EPS_QK, wait=[("s_v", vF)],
                               pre_scale=1.0 / HD)
                vF = sch.op("vector", lambda e, c0_loc=c0_loc: e.tensor_mul(
                    qFsc[:], qFsc[:], r1_rep[:, c0_loc:c0_loc + 512]),
                    wait=[("s_s", sF)], inc="s_v")
                sch.drain("vector")
                sch.op("vector", lambda e, h=h, og=og: e.scalar_tensor_tensor(
                    qT[2 * h + 1][:, og * 512:(og + 1) * 512], qf[1][:],
                    qs16[:, 1:2], qFsc[:], ALU.mult, ALU.mult),
                    wait=[("s_v", vF)], inc="s_v")
                sch.op("vector", lambda e: e.scalar_tensor_tensor(
                    qrt[:], qf[0][:], qs16[:, 0:1], qFsc[:], ALU.mult, ALU.mult))
                sch.drain("vector")
                sch.op("vector", lambda e: e.stream_shuffle(krs[:] if False else qf[0][:], qrt[:], shmask))
                sch.drain("vector")
                sch.op("vector", lambda e: e.tensor_mul(qrt[:], qrt[:], cosq[:]),
                       wait=[cw1, cw2])
                sch.drain("vector")
                sch.op("vector", lambda e: e.tensor_mul(qf[0][:], qf[0][:], sinq[:]))
                sch.drain("vector")
                st["cosq_fr"] = sch.op(
                    "vector", lambda e, h=h, og=og:
                    e.tensor_add(qT[2 * h][:, og * 512:(og + 1) * 512],
                                 qrt[:], qf[0][:]), inc="s_v")
                st["qr1sq_fr"] = st["cosq_fr"]
                st["qf_fr"] = st["cosq_fr"]
        st["xb_own_fr"] = sch.count("s_t")
        st["q_done_v"] = sch.count("s_v")

    def attn_phase(qq):
        qv = st["q_done_v"]
        sch.op("vector", lambda e: e.memset(pTa[0][:], 0.0),
               wait=[("s_v", st.get("kv_done_v", 0)), ("s_v", qv)])
        sch.op("vector", lambda e: e.memset(pTa[1][:], 0.0), inc="s_v")
        st["pTa_fr0"] = sch.count("s_v")
        st["pTa_fr1"] = sch.count("s_v")
        for og in range(2):
            mset = (qq * 2 + og) * 12
            mw = dma("sync", maskb[:].rearrange("p (k f) -> p k f", k=12),
                     masks_d[mset:mset + 12].rearrange("k p f -> p k f"),
                     wait=[("s_v", st.get("maskb_fr", 0)),
                           ("s_t", st.get("xb_own_fr", 0))])
            A_loc = 8 + qq * 8 + og * 4
            for h in range(NH):
                j = h // 2
                buf = h % 2
                pready = {}
                for kp in range(6):
                    k0, k1 = 2 * kp, 2 * kp + 1
                    for kji in (k0, k1):
                        kjt = A_loc - 8 + kji
                        slot = (kjt % 16) * 128
                        a0 = max(0, (kji - 8) * 128)
                        a1 = min(512, (kji - 8) * 128 + 1152)
                        bk = kji % 2
                        for dc in range(2):
                            sch.op("tensor",
                                   lambda e, j=j, dc=dc, slot=slot, h=h, og=og,
                                   a0=a0, a1=a1, bk=bk:
                                   e.matmul(pb(bk)[:, a0:a1],
                                            kT[j][dc][:, slot:slot + 128],
                                            qT[2 * h + dc][:, og * 512 + a0:og * 512 + a1],
                                            start=(dc == 0), stop=(dc == 1)),
                                   wait=([bankf[bk], ("s_v", qv)] if dc == 0 else ()),
                                   inc=("s_t" if dc == 1 else None))
                    tsc = sch.count("s_t")
                    lo = max(0, (k0 - 8) * 128)
                    hi = 512 + min(512, (k1 - 8) * 128 + 1152)
                    skip_mask = (k0 == 4 or k0 == 6) and not (qq == 0 and og == 0)
                    sch.op("scalar", lambda e, lo=lo, hi=hi:
                           e.activation(P01t[:, lo:hi], P01t[:, lo:hi],
                                        AF.Tanh, scale=0.02),
                           wait=[("s_t", tsc)])
                    sch.drain("scalar")
                    if skip_mask:
                        scur = sch.op("scalar", lambda e, buf=buf, k0=k0, lo=lo, hi=hi:
                                      e.activation(pT[buf][:, k0 * 512 + lo:k0 * 512 + hi],
                                                   P01t[:, lo:hi], AF.Exp, scale=50.0),
                                      wait=[("s_t", st.get(f"pv_done{buf}", 0))],
                                      inc="s_s")
                        sch.drain("scalar")
                        bankf[0] = bankf[1] = ("s_s", scur)
                        pready[k0] = pready[k1] = ("s_s", scur)
                    else:
                        scur = sch.op("scalar", lambda e, kp=kp, lo=lo, hi=hi:
                                      e.activation(pTa[kp % 2][:, lo:hi],
                                                   P01t[:, lo:hi], AF.Exp, scale=50.0),
                                      wait=[("s_v", st.get(f"pTa_fr{kp % 2}", 0))],
                                      inc="s_s")
                        sch.drain("scalar")
                        bankf[0] = bankf[1] = ("s_s", scur)
                        st[f"pTa_fr{kp % 2}"] = sch.op(
                            "vector", lambda e, buf=buf, kp=kp, k0=k0:
                            e.tensor_mul(pT[buf][:, k0 * 512:(k0 + 2) * 512],
                                         pTa[kp % 2][:],
                                         maskb[:, k0 * 512:(k0 + 2) * 512]),
                            wait=[("s_s", scur), mw,
                                  ("s_t", st.get(f"pv_done{buf}", 0))],
                            inc="s_v")
                        pready[k0] = pready[k1] = ("s_v", st[f"pTa_fr{kp % 2}"])
                for kji in range(12):
                    kjt = A_loc - 8 + kji
                    slot = (kjt % 16)
                    for dc in range(2):
                        bo = 2 + dc
                        sch.op("tensor",
                               lambda e, slot=slot, j=j, dc=dc, buf=buf, kji=kji, bo=bo:
                               e.matmul(pb(bo)[:, 0:512],
                                        vt[slot][:, j * 256 + dc * 128:j * 256 + (dc + 1) * 128],
                                        pT[buf][:, kji * 512:(kji + 1) * 512],
                                        start=(kji == 0), stop=(kji == 11)),
                               wait=([pready[kji]] +
                                     ([bankf[2], bankf[3]] if kji == 0 and dc == 0 else [])),
                               inc=("s_t" if (kji == 11 and dc == 1) else None))
                for kji in range(12):
                    sch.op("tensor", lambda e, buf=buf, kji=kji:
                           e.matmul(pb(4)[0:128, 0:512], ones[:],
                                    pT[buf][:, kji * 512:(kji + 1) * 512],
                                    start=(kji == 0), stop=(kji == 11)),
                           wait=([pready[kji]] + ([bankf[4]] if kji == 0 else [])),
                           inc=("s_t" if kji == 11 else None))
                st[f"pv_done{buf}"] = sch.count("s_t")
                tpv = sch.count("s_t")
                vrec = sch.op("vector", lambda e: e.reciprocal(
                    rec[:], pb(4)[:, 0:512]), wait=[("s_t", tpv)], inc="s_v")
                bankf[4] = ("s_v", vrec)
                sch.drain("vector")
                v1 = sch.op("vector", lambda e, h=h, og=og:
                            e.tensor_tensor(att[2 * h][:, og * 512:(og + 1) * 512],
                                            pb(2)[:, 0:512], rec[:], ALU.mult),
                            wait=[("s_t", tpv)], inc="s_v")
                v2 = sch.op("vector", lambda e, h=h, og=og:
                            e.tensor_tensor(att[2 * h + 1][:, og * 512:(og + 1) * 512],
                                            pb(3)[:, 0:512], rec[:], ALU.mult),
                            inc="s_v")
                bankf[2] = ("s_v", v1)
                bankf[3] = ("s_v", v2)
            st["maskb_fr"] = sch.count("s_v")
        st["attn_done_v"] = sch.count("s_v")
        st["rq_free"] = sch.count("s_t")
        st["kv_ring_free"] = sch.count("s_t")

    def wo_phase(qq):
        av = st["attn_done_v"]
        z = None
        for n in range(2):
            sch.op("vector", lambda e, n=n: e.memset(accA[n][:], 0.0),
                   wait=[("s_v", st.get("accs_fr", 0))])
            z = sch.op("vector", lambda e, n=n: e.memset(accB[n][:], 0.0),
                       inc="s_v")
        for blk in range(8):
            wws = [gdma(Wob[blk % 2][:].rearrange("p (a t) -> p a t", a=16),
                        wo_d[:, blk * 256:(blk + 1) * 256]
                        .rearrange("(a p) t -> p a t", p=128),
                        wait=[("s_t", st.get(f"wob_fr{blk % 2}", 0))]
                        + ([("s_v", av)] if blk < 2 else []))]
            vcast = None
            for po in range(2):
                oc = blk * 2 + po
                for ng in range(2):
                    bk = po * 2 + ng
                    xw = dma("sync", xck[bk][:],
                             xT_d[oc * 128:(oc + 1) * 128,
                                  1024 + qq * 1024 + ng * 512:
                                  1024 + qq * 1024 + (ng + 1) * 512],
                             wait=[("s_v", st.get(f"xck_fr{bk}", 0)),
                                   ("s_v", av)])
                    for ic in range(16):
                        sch.op("tensor",
                               lambda e, blk=blk, ic=ic, po=po, ng=ng, bk=bk:
                               e.matmul(pb(bk)[:, 0:512],
                                        Wob[blk % 2][:, ic * 256 + po * 128:ic * 256 + (po + 1) * 128],
                                        att[ic][:, ng * 512:(ng + 1) * 512],
                                        start=(ic == 0), stop=(ic == 15)),
                               wait=(wws + [bankf[bk], ("s_v", av)]
                                     if ic == 0 else ()),
                               inc=("s_t" if ic == 15 else None))
                    st[f"wob_fr{blk % 2}"] = sch.count("s_t")
                    tcur = sch.count("s_t")
                    if qq == 1:
                        vx1 = sch.op("vector",
                                     lambda e, oc=oc, ng=ng, bk=bk:
                                     e.tensor_add(x1r[oc][:, ng * 512:(ng + 1) * 512],
                                                  pb(bk)[:, 0:512], xck[bk][:]),
                                     wait=[("s_t", tcur), xw], inc="s_v")
                        x1ap = lambda oc=oc, ng=ng: x1r[oc][:, ng * 512:(ng + 1) * 512]
                    else:
                        vx1 = sch.op("vector",
                                     lambda e, ng=ng, bk=bk:
                                     e.tensor_add(x1stg[bk % 2][:],
                                                  pb(bk)[:, 0:512], xck[bk][:]),
                                     wait=[("s_t", tcur), xw,
                                           st.get(f"x1stg_fr{bk % 2}", ("s_v", 0))],
                                     inc="s_v")
                        st[f"x1stg_fr{bk % 2}"] = dma(
                            "scalar", x1q0_d[oc * 128:(oc + 1) * 128,
                                             ng * 512:(ng + 1) * 512],
                            x1stg[bk % 2][:], wait=[("s_v", vx1)])
                        x1ap = lambda bk=bk: x1stg[bk % 2][:]
                    st[f"xck_fr{bk}"] = vx1
                    bankf[bk] = ("s_v", vx1)
                    vb = sch.op("vector", lambda e, x1ap=x1ap: e.tensor_copy(
                        x1b_s[:], x1ap()),
                        wait=[("s_v", vx1), ("s_t", st.get("x1bs_fr", 0))],
                        inc="s_v")
                    sq = sch.op("scalar", lambda e: e.activation(
                        x1sq_s[:], x1b_s[:], AF.Square), wait=[("s_v", vb)],
                        inc="s_s")
                    t5 = sch.op("tensor", lambda e: e.matmul(
                        pb(5)[:, 0:512], ones[:], x1b_s[:], start=True, stop=True),
                        wait=[bankf[5], ("s_v", vb)], inc="s_t")
                    t6 = sch.op("tensor", lambda e: e.matmul(
                        pb(6)[:, 0:512], ones[:], x1sq_s[:], start=True, stop=True),
                        wait=[bankf[6], ("s_s", sq)], inc="s_t")
                    st["x1bs_fr"] = t6
                    va = sch.op("vector", lambda e, ng=ng: e.tensor_add(
                        accA[ng][:], accA[ng][:], pb(5)[:, 0:512]),
                        wait=[("s_t", t5)], inc="s_v")
                    bankf[5] = ("s_v", va)
                    sch.drain("vector")
                    vb2 = sch.op("vector", lambda e, ng=ng: e.tensor_add(
                        accB[ng][:], accB[ng][:], pb(6)[:, 0:512]),
                        wait=[("s_t", t6)], inc="s_v")
                    bankf[6] = ("s_v", vb2)
                    sch.drain("vector")
        for ng in range(2):
            coff = qq * 1024 + ng * 512
            sch.op("vector", lambda e, ng=ng, coff=coff: e.tensor_scalar_mul(
                m2_rep[:, coff:coff + 512], accA[ng][:], 1.0 / 2048.0))
            sch.drain("vector")
            sch.op("vector", lambda e, coff=coff: e.tensor_mul(
                m2scr[:], m2_rep[:, coff:coff + 512], m2_rep[:, coff:coff + 512]))
            sch.drain("vector")
            vv2 = sch.op("vector", lambda e, ng=ng: e.scalar_tensor_tensor(
                m2scr[:], accB[ng][:], 1.0 / 2048.0, m2scr[:],
                ALU.mult, ALU.subtract), inc="s_v")
            s2 = rsqrt_act(r2p, m2scr, LN_EPS, wait=[("s_v", vv2)])
            sch.op("vector", lambda e: e.tensor_mul(r3s[:], r2p[:], r2p[:]),
                   wait=[("s_s", s2)])
            sch.drain("vector")
            vv3 = sch.op("vector", lambda e: e.tensor_mul(
                r3s[:], r3s[:], m2scr[:]), inc="s_v")
            s3 = rsqrt_act(r3s, r3s, EPS_NORM, wait=[("s_v", vv3)])
            vv4 = sch.op("vector", lambda e, coff=coff: e.tensor_mul(
                rr_rep[:, coff:coff + 512], r2p[:], r3s[:]),
                wait=[("s_s", s3)], inc="s_v")
        st["accs_fr"] = sch.count("s_v")

    def max_pass(qq):
        if qq == 1:
            for oc in range(16):
                for ng in range(2):
                    coff = qq * 1024 + ng * 512
                    sch.op("vector", lambda e, oc=oc, ng=ng, coff=coff: e.tensor_sub(
                        xnt[:], x1r[oc][:, ng * 512:(ng + 1) * 512],
                        m2_rep[:, coff:coff + 512]))
                    sch.drain("vector")
                    sch.op("vector", lambda e, oc=oc, coff=coff: e.scalar_tensor_tensor(
                        xnu[:], xnt[:], gam16[:, oc:oc + 1],
                        rr_rep[:, coff:coff + 512], ALU.mult, ALU.mult))
                    sch.drain("vector")
                    sch.op("vector", lambda e: e.tensor_reduce(
                        smalls[:, SM["rmx"]:SM["rmx"] + 1], xnu[:], AXX, ALU.max, apply_absolute_value=True))
                    sch.drain("vector")
                    sch.op("vector", lambda e: e.tensor_max(
                        smalls[:, SM["runmax"]:SM["runmax"] + 1],
                        smalls[:, SM["runmax"]:SM["runmax"] + 1], smalls[:, SM["rmx"]:SM["rmx"] + 1]), inc="s_v")
                    sch.drain("vector")
            return
        for sl in range(16):
            oc0 = (sl // 2) * 2
            ng = sl % 2
            coff = ng * 512
            w = dma("scalar", mxstg[sl % 2][:].rearrange("p (a t) -> p a t", a=2),
                    x1q0_d[oc0 * 128:(oc0 + 2) * 128, ng * 512:(ng + 1) * 512]
                    .rearrange("(a p) t -> p a t", p=128),
                    wait=[("s_v", st.get(f"mx_fr{sl % 2}", 0))])
            for half in range(2):
                oc = oc0 + half
                sch.op("vector", lambda e, half=half, coff=coff, sl=sl: e.tensor_sub(
                    xnt[:], mxstg[sl % 2][:, half * 512:(half + 1) * 512],
                    m2_rep[:, coff:coff + 512]), wait=[w])
                sch.drain("vector")
                sch.op("vector", lambda e, oc=oc, coff=coff: e.scalar_tensor_tensor(
                    xnu[:], xnt[:], gam16[:, oc:oc + 1], rr_rep[:, coff:coff + 512],
                    ALU.mult, ALU.mult))
                sch.drain("vector")
                vm = sch.op("vector", lambda e: e.tensor_reduce(
                    smalls[:, SM["rmx"]:SM["rmx"] + 1], xnu[:], AXX, ALU.max, apply_absolute_value=True))
                st[f"mx_fr{sl % 2}"] = vm
                sch.drain("vector")
                sch.op("vector", lambda e: e.tensor_max(
                    smalls[:, SM["runmax"]:SM["runmax"] + 1],
                    smalls[:, SM["runmax"]:SM["runmax"] + 1], smalls[:, SM["rmx"]:SM["rmx"] + 1]), inc="s_v")
                sch.drain("vector")

    def collective_trigger():
        tcur = sch.op("tensor", lambda e: e.transpose(
            pb(7)[0:1, 0:128], smalls[:, SM["runmax"]:SM["runmax"] + 1], ident[:]),
            wait=[("s_v", sch.count("s_v")), bankf[7]], inc="s_t")
        sch.op("vector", lambda e: e.tensor_reduce(
            smalls[0:1, SM["lmax"]:SM["lmax"] + 1], pb(7)[0:1, 0:128], AXX,
            ALU.max), wait=[("s_t", tcur)])
        sch.drain("vector")
        sch.op("vector", lambda e: e.tensor_scalar_max(
            smalls[0:1, SM["lmax"]:SM["lmax"] + 1],
            smalls[0:1, SM["lmax"]:SM["lmax"] + 1], EPS_SCALE))
        sch.drain("vector")
        vv1 = sch.op("vector", lambda e: e.tensor_copy(
            ccs[0:1, 0:1], smalls[0:1, SM["lmax"]:SM["lmax"] + 1]), inc="s_v")
        bankf[7] = ("s_v", vv1)
        w = dma("sync", ccin_d, ccs[:], wait=[("s_v", vv1)])
        g1 = sch.op("gpsimd", lambda e: e.collective_compute(
            "AllReduce", ALU.max, replica_groups=[[0, 1, 2, 3, 4, 5, 6, 7]],
            ins=[ccin_d], outs=[ccout_d]), wait=[w], inc="s_g")
        g2 = sch.op("gpsimd", lambda e: e.dma_start(out=ccs[:], in_=ccout_d),
                    wait=[("s_g", g1)], inc="s_g2", inc_by=16)
        st["g2"] = g2

    def collective_scales():
        g2 = st["g2"]
        w2 = dma("sync", smalls[:, SM["Scol"]:SM["Scol"] + 1],
                 ccout_d[0:1, 0:1].partition_broadcast(128), wait=[("s_g2", g2)])
        sch.op("vector", lambda e: e.reciprocal(
            smalls[:, SM["rS127"]:SM["rS127"] + 1],
            smalls[:, SM["Scol"]:SM["Scol"] + 1]), wait=[w2])
        sch.drain("vector")
        sch.op("vector", lambda e: e.tensor_scalar_mul(
            smalls[:, SM["rS127"]:SM["rS127"] + 1],
            smalls[:, SM["rS127"]:SM["rS127"] + 1], 127.0))
        sch.drain("vector")
        sch.op("vector", lambda e: e.tensor_mul(
            smalls[:, SM["fscale"]:SM["fscale"] + 1],
            smalls[:, SM["Scol"]:SM["Scol"] + 1],
            smalls[:, SM["swcol"]:SM["swcol"] + 1]))
        sch.drain("vector")
        sch.op("vector", lambda e: e.tensor_scalar_mul(
            smalls[:, SM["fscale"]:SM["fscale"] + 1],
            smalls[:, SM["fscale"]:SM["fscale"] + 1], -1.0 / 127.0))
        sch.drain("vector")
        vv2 = sch.op("vector", lambda e: e.tensor_scalar_mul(
            rr_rep[:], rr_rep[:], smalls[:, SM["rS127"]:SM["rS127"] + 1]),
            inc="s_v")
        sch.drain("vector")
        if DEBUG:
            dma("scalar", dbg["sS"], ccs[:], wait=[("s_g2", g2)])
        st["scales_v"] = vv2

    def wtern():
        for p in range(16):
            w = dma("scalar", wstg[p % 2][:], wt_d[p * 128:(p + 1) * 128, :],
                    wait=[("s_v", st.get(f"wstg_fr{p % 2}", 0)),
                          ("s_v", st["pre_coll_v"])])
            va = sch.op("vector", lambda e, p=p: e.tensor_single_scalar(
                wqTt[p][:], wstg[p % 2][:],
                smalls[:, SM["halfsw"]:SM["halfsw"] + 1], ALU.is_gt),
                wait=[w, ("s_v", st["pre_coll_v"]),
                      ("s_t", st.get("rq_free2", 0))], inc="s_v")
            sch.drain("vector")
            for ng in range(4):
                vb = sch.op("vector", lambda e, p=p, ng=ng: e.tensor_single_scalar(
                    xq[ng][:], wstg[p % 2][:, ng * 512:(ng + 1) * 512],
                    smalls[:, SM["neghalfsw"]:SM["neghalfsw"] + 1], ALU.is_lt),
                    wait=[("s_v", va)], inc="s_v")
                sch.drain("vector")
                st[f"wstg_fr{p % 2}"] = sch.op(
                    "vector", lambda e, p=p, ng=ng: e.tensor_sub(
                        wqTt[p][:, ng * 512:(ng + 1) * 512],
                        wqTt[p][:, ng * 512:(ng + 1) * 512], xq[ng][:]),
                    inc="s_v")
                sch.drain("vector")
        st["wq_ready"] = sch.count("s_v")
        if DEBUG:
            dma("scalar", dbg["wq0"], wqTt[0][:], wait=[("s_v", st["wq_ready"])])

    def ffn():
        groups = [(1, 0), (1, 1), (0, 0), (0, 1)]

        def prefetch(gi):
            qqf, ng = groups[gi]
            if qqf != 0:
                return []
            rj = x1rA[:].rearrange("p (a t) -> p a t", a=16)
            w = dma("scalar",
                    rj[:, :, ng * 512:(ng + 1) * 512],
                    x1q0_d[:, ng * 512:(ng + 1) * 512]
                    .rearrange("(a p) t -> p a t", p=128),
                    wait=[("s_v", st.get(f"x1half_fr{ng}", 0))])
            return [w]

        def quant_chunk(gi, ic, x1w):
            qqf, ng = groups[gi]
            par = gi % 2
            xqg = xq2[par]
            coff = qqf * 1024 + ng * 512
            xsl = lambda oc, ng=ng: x1r[oc][:, ng * 512:(ng + 1) * 512]
            sch.op("vector", lambda e, ic=ic, coff=coff, xsl=xsl: e.tensor_sub(
                qnt[0][:], xsl(ic), m2_rep[:, coff:coff + 512]),
                wait=list(x1w) + [("s_t", st.get(f"xq_fr{par}", 0))])
            sch.drain("vector")
            sch.op("vector", lambda e, ic=ic, coff=coff: e.scalar_tensor_tensor(
                qnt[1][:], qnt[0][:], gam16[:, ic:ic + 1],
                rr_rep[:, coff:coff + 512], ALU.mult, ALU.mult))
            sch.drain("vector")
            vcl = sch.op("vector", lambda e: e.tensor_scalar(
                qnt[0][:], qnt[1][:], 127.0, -128.0, ALU.min, ALU.max),
                inc="s_v")
            sch.op("scalar", lambda e: e.copy(qni[:], qnt[0][:]),
                   wait=[("s_v", vcl)], inc="s_s")
            sch.drain("scalar")
            scp2 = sch.op("scalar", lambda e: e.copy(qnt[1][:], qni[:]),
                          inc="s_s")
            sch.op("vector", lambda e: e.tensor_sub(qnt[2][:], qnt[0][:],
                                                    qnt[1][:]),
                   wait=[("s_s", scp2)])
            sch.drain("vector")
            sch.op("vector", lambda e: e.scalar_tensor_tensor(
                qnt[0][:], qnt[2][:], 0.5, qnt[1][:], ALU.is_gt, ALU.add))
            sch.drain("vector")
            v = sch.op("vector", lambda e, ic=ic, xqg=xqg: e.scalar_tensor_tensor(
                xqg[ic][:], qnt[2][:], -0.5, qnt[0][:], ALU.is_lt,
                ALU.subtract), inc="s_v")
            sch.drain("vector")
            return v

        x1w = prefetch(0)
        for ic in range(16):
            quant_chunk(0, ic, x1w)
        vq = {0: sch.count("s_v")}
        x1w_next = prefetch(1)
        for gi, (qqf, ng) in enumerate(groups):
            par = gi % 2
            xqg = xq2[par]
            coff = qqf * 1024 + ng * 512
            xsl = lambda oc, ng=ng: x1r[oc][:, ng * 512:(ng + 1) * 512]
            for oc in range(16):
                bk = oc % 2
                for ic in range(16):
                    sch.op("tensor", lambda e, oc=oc, ic=ic, bk=bk, xqg=xqg:
                           e.matmul(pb(bk)[:, 0:512],
                                    wqTt[ic][:, oc * 128:(oc + 1) * 128],
                                    xqg[ic][:], start=(ic == 0), stop=(ic == 15)),
                           wait=([bankf[bk], ("s_v", vq[gi]),
                                  ("s_v", st.get("wq_ready", 0))]
                                 if ic == 0 else ()),
                           inc=("s_t" if ic == 15 else None))
                tcur = sch.count("s_t")
                vo = sch.op("vector", lambda e, oc=oc, bk=bk, xsl=xsl:
                            e.scalar_tensor_tensor(
                                ostg[bk][:], pb(bk)[:, 0:512],
                                smalls[:, SM["fscale"]:SM["fscale"] + 1],
                                xsl(oc), ALU.mult, ALU.add),
                            wait=[("s_t", tcur),
                                  st.get(f"ostg_fr{bk}", ("s_v", 0))], inc="s_v")
                bankf[bk] = ("s_v", vo)
                st[f"ostg_fr{bk}"] = dma(
                    "scalar", out_d[oc * 128:(oc + 1) * 128, coff:coff + 512],
                    ostg[bk][:], wait=[("s_v", vo)])
                if gi + 1 < 4:
                    quant_chunk(gi + 1, oc, x1w_next)
            st[f"xq_fr{par}"] = sch.count("s_t")
            st[f"x1half_fr{ng}"] = sch.count("s_v")
            if gi + 1 < 4:
                vq[gi + 1] = sch.count("s_v")
            if gi + 2 < 4:
                x1w_next = prefetch(gi + 2)

    def dbg_dump(qq):
        if not DEBUG or qq != 0:
            return
        v = sch.count("s_v")
        s = sch.count("s_s")
        dma("scalar", dbg["r1"], r1_rep[:], wait=[("s_s", s), ("s_v", v)])
        dma("scalar", dbg["k00"], kT[0][0][:], wait=[("s_v", v)])
        dma("scalar", dbg["v8"], vt[8][:], wait=[("s_v", v)])

    def dbg_dump_q(qq):
        if not DEBUG or qq != 0:
            return
        v = sch.count("s_v")
        dma("scalar", dbg["q0"], qT[0][:], wait=[("s_v", v)])
        dma("scalar", dbg["q1"], qT[1][:], wait=[("s_v", v)])

    def dbg_dump_attn(qq):
        if not DEBUG or qq != 0:
            return
        v = sch.count("s_v")
        dma("scalar", dbg["attn0"], att[0][:], wait=[("s_v", v)])
        dma("scalar", dbg["attn1"], att[1][:], wait=[("s_v", v)])

    def dbg_dump_wo(qq):
        if not DEBUG:
            return
        v = sch.count("s_v")
        if qq == 1:
            dma("scalar", dbg["x1c0"], x1r[0][:], wait=[("s_v", v)])
            dma("scalar", dbg["m2"], m2_rep[:], wait=[("s_v", v)])
            dma("scalar", dbg["rr"], rr_rep[:], wait=[("s_v", v)])

    st["gate"] = [("s_v", CONST_V)]
    for qq in (0, 1):
        kv_phase(qq)
        dbg_dump(qq)
        q_phase(qq)
        dbg_dump_q(qq)
        attn_phase(qq)
        dbg_dump_attn(qq)
        st["accs_fr"] = st.get("accs_fr", 0)
        wo_phase(qq)
        if qq == 0:
            st["gate"] = [("s_t", sch.count("s_t")),
                          ("s_v", sch.count("s_v")),
                          ("s_s", sch.count("s_s"))]
        max_pass(qq)
        dbg_dump_wo(qq)
        if qq == 0:
            st["mx0_v"] = sch.count("s_v")
    st["pre_coll_v"] = sch.count("s_v")
    st["rq_free2"] = sch.count("s_t")
    collective_trigger()
    wtern()
    collective_scales()
    ffn()

    if DEBUG:
        dma("scalar", dbg["smalls"], smalls[:], wait=[("s_v", sch.count("s_v"))])
        dma("scalar", dbg["wabs"], wabs[:], wait=[("s_v", sch.count("s_v"))])
    for sem_n, v in dma_barrier("scalar") + dma_barrier("sync"):
        sch.prog["gpsimd"].append(
            lambda e, sems, sem_n=sem_n, v=v: e.wait_ge(sems[sem_n], v))

    with nc.Block() as block:
        @block.sync
        def _(eng):
            for f in sch.prog["sync"]:
                f(eng, sems)

        @block.scalar
        def _(eng):
            for f in sch.prog["scalar"]:
                f(eng, sems)

        @block.vector
        def _(eng):
            for f in sch.prog["vector"]:
                f(eng, sems)

        @block.tensor
        def _(eng):
            for f in sch.prog["tensor"]:
                f(eng, sems)

        @block.gpsimd
        def _(eng):
            for f in sch.prog["gpsimd"]:
                f(eng, sems)

    return nc, dbg


def _host_prep(inputs):
    x = np.asarray(inputs["L0_attn_x"], np.float32)
    wq = np.ascontiguousarray(np.asarray(inputs["L0_attn_Wq"], np.float32))
    wk = np.ascontiguousarray(np.asarray(inputs["L0_attn_Wk"], np.float32))
    wv = np.ascontiguousarray(np.asarray(inputs["L0_attn_Wv"], np.float32))
    wo = np.ascontiguousarray(np.asarray(inputs["L0_attn_Wo"], np.float32))
    wt = np.ascontiguousarray(np.asarray(inputs["L0_ffn_weight"], np.float32).T)
    qsc = np.asarray(inputs["L0_attn_q_scale"], np.float32)
    ksc = np.asarray(inputs["L0_attn_k_scale"], np.float32)
    gam = np.asarray(inputs["L0_ffn_norm_weight"], np.float32)

    qs2 = np.ascontiguousarray(qsc.reshape(2, 128).T)
    ks2 = np.ascontiguousarray(ksc.reshape(2, 128).T)
    gam16 = np.ascontiguousarray(gam.reshape(16, 128).T)
    ones128 = np.ones((128, 128), dtype=BF)
    ident = np.eye(128, dtype=np.float32)

    half = 64
    theta = 1.0 / (10000.0 ** (np.arange(0, ROT_, 2, dtype=np.float32) / ROT_))
    in_maps = []
    for b in range(B):
        xb_T = np.ascontiguousarray(x[b].T)
        for hh in range(2):
            lo = hh * 2048 - 1024
            xT = np.zeros((E, NLOC), np.float32)
            s0 = max(0, lo)
            xT[:, s0 - lo:3072] = xb_T[:, s0:lo + 3072]
            pos = np.arange(lo, lo + NLOC, dtype=np.float32)
            ang = pos[None, :] * theta[:, None]
            cosT = np.zeros((128, NLOC), np.float32)
            sinT = np.zeros((128, NLOC), np.float32)
            cosT[0::2] = np.cos(ang)
            cosT[1::2] = np.cos(ang)
            sinT[0::2] = -np.sin(ang)
            sinT[1::2] = np.sin(ang)
            masks = np.zeros((48, 128, 512), dtype=BF)
            for qq in range(2):
                for og in range(2):
                    A_loc = 8 + qq * 8 + og * 4
                    for kji in range(12):
                        kjt = A_loc - 8 + kji
                        K = lo + kjt * 128 + np.arange(128)[:, None]
                        Q = lo + A_loc * 128 + np.arange(512)[None, :]
                        ok = (K >= 0) & (K <= Q) & (Q - K < WIN_)
                        masks[(qq * 2 + og) * 12 + kji] = ok.astype(BF)
            in_maps.append({
                "xT": xT, "wq": wq, "wk": wk, "wv": wv, "wo": wo, "wt": wt,
                "qs2": qs2, "ks2": ks2, "gam": gam16,
                "cosT": cosT, "sinT": sinT, "masks": masks,
                "ones128": ones128, "ident128": ident,
            })
    return in_maps


ROT_ = 128
WIN_ = 1024

_NC_CACHE = {}


def kernel(**inputs):
    key = "nc"
    if key not in _NC_CACHE:
        _NC_CACHE[key] = build_nc()
    nc, dbg = _NC_CACHE[key]
    in_maps = _host_prep(inputs)
    res = run_bass_kernel_spmd(nc, in_maps, core_ids=list(range(8)))
    out = np.zeros((B, S, E), np.float32)
    for b in range(B):
        for hh in range(2):
            core = b * 2 + hh
            out[b, hh * 2048:(hh + 1) * 2048, :] = res.results[core]["out"].T
    if DEBUG:
        kernel._last = res
    return out


# version 09
# speedup vs baseline: 1.0609x; 1.0142x over previous
import os
import numpy as np
import ml_dtypes
import concourse.bass as bass
import concourse.mybir as mybir
from concourse.bass_utils import run_bass_kernel_spmd

F32 = mybir.dt.float32
BF16 = mybir.dt.bfloat16
I32 = mybir.dt.int32
AF = mybir.ActivationFunctionType
ALU = mybir.AluOpType
BF = ml_dtypes.bfloat16
AXX = mybir.AxisListType.X

B, S, E = 4, 4096, 2048
NH, NKV, HD = 8, 4, 256
QSCALE = 256 ** (-0.5)
EPS_QK = 1e-6
LN_EPS = 1e-5
EPS_NORM = 1e-6
EPS_SCALE = 1e-5
NLOC = 3072
KB = 1024

DEBUG = int(os.environ.get("MK_DEBUG", "0"))


class TView:

    def __init__(self, t, off, width):
        self.t = t
        self.off = off
        self.width = width

    def __getitem__(self, key):
        if isinstance(key, tuple):
            r, c = key
            start = (c.start or 0) + self.off
            stop = (c.stop if c.stop is not None else self.width) + self.off
            return self.t[r, start:stop]
        return self.t[key, self.off:self.off + self.width]


class Sched:
    ENGS = ("sync", "scalar", "vector", "tensor", "gpsimd")

    def __init__(self):
        self.prog = {e: [] for e in self.ENGS}
        self.C = {}
        self.hw = {e: {} for e in self.ENGS}
        self.pend = {}

    def count(self, sem):
        return self.C.get(sem, 0)

    DEFSEM = {"vector": "s_v", "scalar": "s_s"}

    def op(self, eng, fn, wait=(), inc=None, inc_by=1):
        if inc is None and eng in self.DEFSEM:
            inc = self.DEFSEM[eng]
        waits = list(wait)
        if self.pend.pop(eng, None):
            dsem = self.DEFSEM[eng]
            waits.append((dsem, self.count(dsem)))
        wl = []
        for sem, val in waits:
            if val > self.hw[eng].get(sem, 0):
                wl.append((sem, val))
                self.hw[eng][sem] = val
        if inc is not None:
            self.C[inc] = self.count(inc) + inc_by

        def emit(e, sems, waits=wl, fn=fn, inc=inc, inc_by=inc_by):
            for sem, val in waits:
                e.wait_ge(sems[sem], val)
            instr = fn(e)
            if inc is not None:
                instr.then_inc(sems[inc], inc_by)

        self.prog[eng].append(emit)
        return self.count(inc) if inc is not None else None

    def drain(self, eng):
        self.pend[eng] = True


def build_nc():
    nc = bass.Bass("TRN2", target_bir_lowering=False)

    xT_d = nc.dram_tensor("xT", [E, NLOC], F32, kind="ExternalInput").ap()
    wq_d = nc.dram_tensor("wq", [E, 2048], F32, kind="ExternalInput").ap()
    wk_d = nc.dram_tensor("wk", [E, 1024], F32, kind="ExternalInput").ap()
    wv_d = nc.dram_tensor("wv", [E, 1024], F32, kind="ExternalInput").ap()
    wo_d = nc.dram_tensor("wo", [E, 2048], F32, kind="ExternalInput").ap()
    wt_d = nc.dram_tensor("wt", [E, 2048], F32, kind="ExternalInput").ap()
    qs2_d = nc.dram_tensor("qs2", [128, 2], F32, kind="ExternalInput").ap()
    ks2_d = nc.dram_tensor("ks2", [128, 2], F32, kind="ExternalInput").ap()
    gam_d = nc.dram_tensor("gam", [128, 16], F32, kind="ExternalInput").ap()
    cos_d = nc.dram_tensor("cosT", [128, NLOC], F32, kind="ExternalInput").ap()
    sin_d = nc.dram_tensor("sinT", [128, NLOC], F32, kind="ExternalInput").ap()
    masks_d = nc.dram_tensor("masks", [48, 128, 512], BF16, kind="ExternalInput").ap()
    ones_d = nc.dram_tensor("ones128", [128, 128], BF16, kind="ExternalInput").ap()
    id_d = nc.dram_tensor("ident128", [128, 128], F32, kind="ExternalInput").ap()
    out_d = nc.dram_tensor("out", [E, 2048], F32, kind="ExternalOutput").ap()
    x1q0_d = nc.dram_tensor("x1q0", [E, 1024], F32, kind="Internal").ap()
    ccin_d = nc.dram_tensor("ccin", [1, 16], F32, kind="Internal").ap()
    bc_d = nc.dram_tensor("bc_d", [1, 16], F32, kind="Internal").ap()
    ccout_d = nc.dram_tensor("ccout", [1, 16], F32, kind="Internal").ap()

    dbg = {}
    if DEBUG:
        def dbgt(name, shape, dt=F32):
            dbg[name] = nc.dram_tensor("dbg_" + name, shape, dt,
                                       kind="ExternalOutput").ap()
        dbgt("r1", [128, NLOC]); dbgt("k00", [128, 2048], BF16)
        dbgt("v8", [128, 1024], BF16)
        dbgt("q0", [128, 1024], BF16); dbgt("q1", [128, 1024], BF16)
        dbgt("pt", [128, 12 * 512], BF16)
        dbgt("attn0", [128, 1024], BF16); dbgt("attn1", [128, 1024], BF16)
        dbgt("x1c0", [128, 1024]); dbgt("m2", [128, 2048]); dbgt("rr", [128, 2048])
        dbgt("sS", [1, 16]); dbgt("xq0", [128, 512], BF16); dbgt("wq0", [128, 2048], BF16)
        dbgt("ss_k", [128, 512]); dbgt("F_k", [128, 512]); dbgt("r1sq_k", [128, 512])
        dbgt("maskb", [128, 12 * 512], BF16); dbgt("pta", [128, 512], BF16)
        dbgt("smalls", [128, 16]); dbgt("wabs", [128, 16])
        dbgt("wcol", [128, 1]); dbgt("wrowT", [1, 128])
        dbgt("sums", [128, 512]); dbgt("ttmp", [128, 512])
        dbgt("kstar", [128, 512])

    SBUF_BASE = 16512

    def at(name, shape, dt, off):
        return nc.alloc_sbuf_tensor_at(name, shape, dt, offset=SBUF_BASE + off)

    o = [0]

    def nxt(sz):
        off = (o[0] + 31) // 32 * 32
        o[0] = off + sz
        return off

    ones = at("ones", [128, 128], BF16, nxt(256))
    onesf = at("onesf", [128, 128], F32, nxt(512))
    ident = at("ident", [128, 128], F32, nxt(512))
    qs16 = at("qs16", [128, 2], F32, nxt(32))
    ks16 = at("ks16", [128, 2], F32, nxt(32))
    gam16 = at("gam16", [128, 16], F32, nxt(64))
    smalls = at("smalls", [128, 16], F32, nxt(64))
    SM = dict(runmax=0, lmax=1, Scol=2, rS127=3, swcol=4, halfsw=5,
              neghalfsw=6, fscale=7, swrow=8, eps5=10, eps6=11, rmx=12)
    ccs = at("ccs", [1, 16], F32, nxt(64))
    _mo_off = (o[0] + 31) // 32 * 32
    mrow_own = at("mrow_own", [1, 1024], BF16, nxt(2048))
    dbgw_s = at("dbgw_s", [1, 128], F32, _mo_off)
    r1col = at("r1col", [128, 16], F32, nxt(64))
    wabs = at("wabs", [128, 16], F32, nxt(64))
    wcol = at("wcol", [128, 1], F32, nxt(32))
    wrow = at("wrow", [1, 2048], BF16, nxt(4096))
    mrow_kv = at("mrow_kv", [1, 512], BF16, nxt(1024))
    r1_rep = at("r1_rep", [128, NLOC], F32, nxt(NLOC * 4))
    rr_rep = at("rr_rep", [128, 2048], F32, nxt(8192))
    m2_rep = at("m2_rep", [128, 2048], F32, nxt(8192))
    PERM_END = (o[0] + 31) // 32 * 32
    R_K = PERM_END
    R_V = R_K + 32 * KB
    R_Q = R_V + 32 * KB
    R_AT = R_Q + 32 * KB
    R_FX = R_AT + 32 * KB
    FX_END = R_FX + 43 * KB
    assert FX_END + 16512 <= 229376, (PERM_END, FX_END)

    kT = [[at(f"kT{j}_{d}", [128, 2048], BF16, R_K + (j * 2 + d) * 4 * KB)
           for d in range(2)] for j in range(NKV)]
    vt = [at(f"v{t}", [128, 1024], BF16, R_V + t * 2 * KB) for t in range(16)]
    qT = [at(f"qT{c}", [128, 1024], BF16, R_Q + c * 2 * KB) for c in range(16)]
    att = [at(f"att{c}", [128, 1024], BF16, R_AT + c * 2 * KB) for c in range(16)]

    WkbA = at("WkbA", [128, 16 * 1024], BF16, R_Q)
    Wkb = [TView(WkbA, c * 1024, 1024) for c in range(16)]
    WvbA = at("WvbA", [128, 16 * 1024], BF16, R_AT)
    Wvb = [TView(WvbA, c * 1024, 1024) for c in range(16)]
    xbA = [at(f"xbA{g}", [128, 16 * 512], BF16, R_FX + g * 16 * KB)
           for g in range(2)]
    xb = [[TView(xbA[g], c * 512, 512) for c in range(16)] for g in range(2)]
    xstg = [at(f"xstg{i}", [128, 1024], F32, R_FX + 16 * KB + i * 4 * KB)
            for i in range(2)]
    kvs = R_FX + 24 * KB
    kf = [at(f"kf{i}", [128, 512], F32, kvs + i * 2 * KB) for i in range(2)]
    krt = at("krt", [128, 512], F32, kvs + 4 * KB)
    krs = at("krs", [128, 512], F32, kvs + 6 * KB)
    Fsc = at("Fsc", [128, 512], F32, kvs + 8 * KB)
    r1sq = at("r1sq", [128, 512], F32, kvs + 10 * KB)
    cosk = at("cosk", [128, 512], F32, kvs + 12 * KB)
    sink = at("sink", [128, 512], F32, kvs + 14 * KB)
    ksqb = at("ksqb", [128, 512], BF16, kvs + 16 * KB)
    mscr = at("mscr", [128, 512], F32, kvs + 4 * KB)
    vscr = at("vscr", [128, 512], F32, kvs + 6 * KB)

    wqstg = at("wqstg8", [128, 8 * 256], F32, R_FX + 32 * KB)
    WqbA = [at(f"WqbA{i}", [128, 16 * 256], BF16, R_AT + i * 8 * KB)
            for i in range(2)]
    qf = [at(f"qf{i}", [128, 512], F32, R_AT + 16 * KB + i * 2 * KB)
          for i in range(2)]
    qFsc = at("qFsc", [128, 512], F32, R_AT + 20 * KB)
    qr1sq = at("qr1sq", [128, 512], F32, R_AT + 22 * KB)
    qsqb = at("qsqb", [128, 512], BF16, R_AT + 24 * KB)
    cosq = at("cosq", [128, 512], F32, R_AT + 25 * KB)
    sinq = at("sinq", [128, 512], F32, R_AT + 27 * KB)
    qrt = at("qrt", [128, 512], F32, R_AT + 29 * KB)

    maskb = at("maskb", [128, 12 * 512], BF16, R_FX)
    pT = [at(f"pT{i}", [128, 12 * 512], BF16, R_FX + 12 * KB + i * 12 * KB)
          for i in range(2)]
    pTa = [at(f"pTa{i}", [128, 512], BF16, R_FX + 36 * KB + i * KB)
           for i in range(2)]
    ttmp = at("ttmp", [128, 512], F32, R_FX + 38 * KB)
    rec = at("rec", [128, 512], F32, R_FX + 40 * KB)

    wostg = at("wostg", [128, 16 * 256], F32, R_FX)
    Wob = [at(f"Wob{i}", [128, 16 * 256], BF16, R_FX + 16 * KB + i * 8 * KB)
           for i in range(2)]
    xck = [at(f"xck{i}", [128, 512], F32, R_FX + 32 * KB + i * 2 * KB)
           for i in range(4)]
    accA = [at(f"accA{n}", [128, 512], F32, R_Q + n * 2 * KB) for n in range(2)]
    accB = [at(f"accB{n}", [128, 512], F32, R_Q + 4 * KB + n * 2 * KB)
            for n in range(2)]
    x1b_s = at("x1b_s", [128, 512], BF16, R_Q + 8 * KB)
    x1sq_s = at("x1sq_s", [128, 512], BF16, R_Q + 9 * KB)
    m2scr = at("m2scr", [128, 512], F32, R_Q + 10 * KB)
    r2p = at("r2p", [128, 512], F32, R_Q + 12 * KB)
    r3s = at("r3s", [128, 512], F32, R_Q + 14 * KB)
    xnt = at("xnt", [128, 512], F32, R_Q + 16 * KB)
    xnu = at("xnu", [128, 512], F32, R_Q + 18 * KB)
    x1stg = [at(f"x1stg{i}", [128, 512], F32, R_Q + 20 * KB + i * 2 * KB)
             for i in range(2)]
    mxstg = [at(f"mxstg{i}", [128, 1024], F32, R_Q + 24 * KB + i * 4 * KB)
             for i in range(2)]
    rmx = None
    x1rA = at("x1rA", [128, 16 * 1024], F32, R_K)
    x1r = [TView(x1rA, c * 1024, 1024) for c in range(16)]

    wqTt = [at(f"wqT{c}", [128, 2048], BF16, R_Q + c * 4 * KB) for c in range(16)]
    wstg = [at(f"wstg{i}", [128, 2048], F32, R_FX + i * 8 * KB) for i in range(2)]
    xqA_t = at("xqA", [128, 16 * 512], BF16, R_FX + 16 * KB)
    xqB_t = at("xqB", [128, 16 * 512], BF16, R_FX)
    xq = [TView(xqA_t, c * 512, 512) for c in range(16)]
    xq2 = [[TView(xqA_t, c * 512, 512) for c in range(16)],
           [TView(xqB_t, c * 512, 512) for c in range(16)]]
    ostg = [at(f"ostg{i}", [128, 512], F32, R_FX + 32 * KB + i * 2 * KB)
            for i in range(2)]
    qnt = [at(f"qnt{i}", [128, 512], F32, R_FX + 36 * KB + i * 2 * KB)
           for i in range(3)]
    qni = at("qni", [128, 512], I32, R_FX + 40 * KB)

    ps_guards = []
    g01 = nc.psum_tensor("ps01", [128, 1024], F32)
    P01t = g01.__enter__()
    ps_guards.append(g01)
    pbt = [TView(P01t, 0, 512), TView(P01t, 512, 512)]
    for i in range(2, 8):
        g = nc.psum_tensor(f"ps{i}", [128, 512], F32)
        t = g.__enter__()
        ps_guards.append(g)
        pbt.append(t)

    def pb(i):
        return pbt[i]

    sems = {}
    sem_guards = {}
    for n in (["s_t", "s_s", "s_v", "s_g", "s_g2"]
              + [f"sx{i}" for i in range(8)] + [f"sy{i}" for i in range(8)]
              + [f"sg{i}" for i in range(8)]):
        g = nc.semaphore(n)
        sems[n] = g.__enter__()
        sem_guards[n] = g

    sch = Sched()
    st = {}
    bankf = {i: ("s_s", 0) for i in range(8)}

    qidx = {"sync": 0, "scalar": 0, "gpsimd": 0}

    def dma(queue, out, in_, wait=()):
        pref = "sx" if queue == "sync" else "sy"
        i = qidx[queue]
        qidx[queue] = i + 1
        lane = f"{pref}{i % 8}"
        prev = sch.count(lane)
        wl = list(wait) + ([(lane, prev)] if prev else [])
        cnt = sch.op(queue, lambda e, out=out, in_=in_:
                     e.dma_start(out=out, in_=in_), wait=wl, inc=lane, inc_by=16)
        return (lane, cnt)

    def dma_barrier(queue):
        pref = {"sync": "sx", "scalar": "sy", "gpsimd": "sg"}[queue]
        return [(f"{pref}{i}", sch.count(f"{pref}{i}"))
                for i in range(8) if sch.count(f"{pref}{i}") > 0]

    def gdma(out, in_, wait=()):
        i = qidx["gpsimd"]
        qidx["gpsimd"] = i + 1
        lane = f"sg{i % 8}"
        prev = sch.count(lane)
        wl = list(wait) + ([(lane, prev)] if prev else [])
        cnt = sch.op("gpsimd", lambda e, out=out, in_=in_:
                     e.dma_start(out=out, in_=in_), wait=wl, inc=lane, inc_by=16)
        return (lane, cnt)

    dma("sync", ones[:], ones_d)
    dma("sync", ident[:], id_d)
    dma("sync", qs16[:], qs2_d)
    dma("sync", ks16[:], ks2_d)
    w0 = dma("sync", gam16[:], gam_d)
    CONSTW = dma_barrier("sync")
    sch.op("vector", lambda e: e.tensor_scalar_mul(qs16[:], qs16[:], QSCALE),
           wait=CONSTW)
    sch.op("vector", lambda e: e.memset(onesf[:], 1.0))
    sch.op("vector", lambda e: e.memset(smalls[:, 0:1], 0.0))
    sch.op("vector", lambda e: e.memset(smalls[:, SM["eps5"]:SM["eps5"] + 1], LN_EPS))
    sch.op("vector", lambda e: e.memset(smalls[:, SM["eps6"]:SM["eps6"] + 1], EPS_QK))
    sch.op("vector", lambda e: e.memset(ccs[:], 0.0), inc="s_v")
    CONST_V = sch.count("s_v")

    for p in range(16):
        wdone = st.get("wmean_used", 0)
        w = dma("scalar", wstg[p % 2][:], wt_d[p * 128:(p + 1) * 128, :],
                wait=([("s_v", wdone)] if p >= 2 else ()))
        st["wmean_used"] = sch.op(
            "vector", lambda e, p=p: e.tensor_reduce(
                wabs[:, p:p + 1], wstg[p % 2][:], AXX, ALU.add,
                apply_absolute_value=True),
            wait=[w], inc="s_v")
    sch.drain("vector")
    sch.op("vector", lambda e: e.tensor_reduce(wcol[:], wabs[:], AXX, ALU.add),
           inc="s_v")
    t1 = sch.op("tensor", lambda e: e.transpose(pb(7)[0:1, 0:128], wcol[:],
                                                ident[:]),
                wait=[("s_v", sch.count("s_v"))] + CONSTW, inc="s_t")
    if DEBUG:
        sdw = sch.op("scalar", lambda e: e.copy(wabs[0:1, 0:16] if False else dbgw_s[:], pb(7)[0:1, 0:128]),
                     wait=[("s_t", t1)], inc="s_s")
        dma("scalar", dbg["wrowT"], dbgw_s[:], wait=[("s_s", sdw)])
        dma("scalar", dbg["wcol"], wcol[:], wait=[("s_v", sch.count("s_v"))])
    sch.op("vector", lambda e: e.tensor_reduce(
        smalls[0:1, SM["swrow"]:SM["swrow"] + 1], pb(7)[0:1, 0:128], AXX,
        ALU.add), wait=[("s_t", t1)])
    sch.drain("vector")
    vv = sch.op("vector", lambda e: e.tensor_scalar(
        smalls[0:1, SM["swrow"]:SM["swrow"] + 1],
        smalls[0:1, SM["swrow"]:SM["swrow"] + 1],
        1.0 / (2048.0 * 2048.0), EPS_SCALE, ALU.mult, ALU.max), inc="s_v")
    wv_ = dma("sync", bc_d[0:1, 0:1],
              smalls[0:1, SM["swrow"]:SM["swrow"] + 1], wait=[("s_v", vv)])
    wv_ = dma("sync", smalls[:, SM["swcol"]:SM["swcol"] + 1],
              bc_d[0:1, 0:1].partition_broadcast(128), wait=[wv_])
    sch.op("vector", lambda e: e.tensor_scalar_mul(
        smalls[:, SM["halfsw"]:SM["halfsw"] + 1],
        smalls[:, SM["swcol"]:SM["swcol"] + 1], 0.5), wait=[wv_])
    sch.drain("vector")
    sch.op("vector", lambda e: e.tensor_scalar_mul(
        smalls[:, SM["neghalfsw"]:SM["neghalfsw"] + 1],
        smalls[:, SM["halfsw"]:SM["halfsw"] + 1], -1.0), inc="s_v")
    bankf[7] = ("s_v", sch.count("s_v"))
    st0_wmean_v = sch.count("s_v")

    def load_panels_cast(w_dram, destA, ncols, extra_wait=(), per_g=None):
        ws = []
        for g in range(4):
            wt = list(extra_wait) if g == 0 else []
            if per_g:
                wt += per_g[g]
            ws.append(gdma(
                destA[:, g * 4 * ncols:(g + 1) * 4 * ncols]
                .rearrange("p (a t) -> p a t", a=4),
                w_dram[g * 512:(g + 1) * 512, 0:ncols]
                .rearrange("(a p) t -> p a t", p=128),
                wait=wt))
        return ws

    def colsum_row(chunks, ncols, row_off, wwait):
        for ng in range(0, ncols, 512):
            nn = min(512, ncols - ng)
            for c in range(16):
                sch.op("tensor", lambda e, c=c, ng=ng, nn=nn, chunks=chunks:
                       e.matmul(pb(4)[0:1, 0:nn], ones[:, 0:1],
                                chunks[c][:, ng:ng + nn],
                                start=(c == 0), stop=(c == 15)),
                       wait=(list(wwait) + [bankf[4]] if c == 0 else ()),
                       inc=("s_t" if c == 15 else None))
            tcur = sch.count("s_t")
            scur = sch.op("scalar", lambda e, ng=ng, nn=nn, row_off=row_off:
                          e.copy(wrow[0:1, row_off + ng:row_off + ng + nn],
                                 pb(4)[0:1, 0:nn]),
                          wait=[("s_t", tcur)], inc="s_s")
            bankf[4] = ("s_s", scur)

    def rsqrt_act(dst, src, bias, wait=(), pre_scale=1.0):
        dap = dst[:] if hasattr(dst, "ap") else dst
        sap = src[:] if hasattr(src, "ap") else src
        bcol = SM["eps5"] if bias == LN_EPS else SM["eps6"]
        nparts = dap.shape[0]
        bap = smalls[0:nparts, bcol:bcol + 1]
        sch.op("scalar", lambda e: e.activation(dap, sap, AF.Ln,
                                                bias=bap, scale=pre_scale),
               wait=wait)
        sch.drain("scalar")
        return sch.op("scalar", lambda e: e.activation(dap, dap, AF.Exp,
                                                       scale=-0.5), inc="s_s")

    def kv_phase(qq):
        mxg = [("s_v", st.get("mx0_v", 0))]
        first = 0 if qq == 0 else 16
        ngrp = 4 if qq == 0 else 2

        def load_xgroup(gi):
            t0 = first + gi * 4
            c0 = t0 * 128
            xgate = (st.get("gate", []) if qq == 0
                     else [("s_t", st.get("rq_free", 0))])
            ws = []
            for g4 in range(4):
                ws.append(gdma(
                    xbA[0][:, g4 * 4 * 512:(g4 + 1) * 4 * 512]
                    .rearrange("p (a t) -> p a t", a=4),
                    xT_d[g4 * 512:(g4 + 1) * 512, c0:c0 + 512]
                    .rearrange("(a p) t -> p a t", p=128),
                    wait=(([("s_t", st.get("xb0_fr", 0)),
                            ("s_v", st0_wmean_v)] + xgate)
                          if g4 == 0 else ())))
            return ws

        xws0 = load_xgroup(0)
        wWk = load_panels_cast(wk_d, WkbA, 1024,
                               extra_wait=st.get("gate", []),
                               per_g=[[], [], mxg, mxg])
        wWv = load_panels_cast(wv_d, WvbA, 1024,
                               extra_wait=st.get("gate", []))

        for gi in range(ngrp):
            t0 = first + gi * 4
            c0 = t0 * 128
            xws = xws0 if gi == 0 else load_xgroup(gi)
            for c in range(16):
                scur = sch.op("scalar", lambda e, c=c:
                              e.activation(ksqb[:], xb[0][c][:], AF.Square),
                              wait=[xws[c // 4],
                                    ("s_t", st.get("ksqb_fr", 0))], inc="s_s")
                sch.op("tensor", lambda e, c=c: e.matmul(
                    pb(5)[:, 0:512], ones[:], xb[0][c][:],
                    start=(c == 0), stop=(c == 15)),
                    wait=([bankf[5]] if c == 0 else []) + [xws[c // 4]],
                    inc=("s_t" if c == 15 else None))
                st["ksqb_fr"] = sch.op("tensor", lambda e, c=c: e.matmul(
                    pb(6)[:, 0:512], ones[:], ksqb[:],
                    start=(c == 0), stop=(c == 15)),
                    wait=([bankf[6], ("s_s", scur)] if c == 0
                          else [("s_s", scur)]),
                    inc="s_t")
            tstats = sch.count("s_t")
            st["ksq_mm"] = tstats
            if gi == 0:
                colsum_row(Wkb, 1024, 0, wWk)
            sch.op("vector", lambda e: e.tensor_scalar_mul(
                mscr[:], pb(5)[:, 0:512], 1.0 / 2048.0), wait=[("s_t", tstats)])
            sch.drain("vector")
            sch.op("vector", lambda e: e.tensor_mul(vscr[:], mscr[:], mscr[:]))
            sch.drain("vector")
            v1 = sch.op("vector", lambda e: e.scalar_tensor_tensor(
                vscr[:], pb(6)[:, 0:512], 1.0 / 2048.0, vscr[:],
                ALU.mult, ALU.subtract), inc="s_v")
            bankf[5] = ("s_v", v1)
            bankf[6] = ("s_v", v1)
            s1 = rsqrt_act(r1_rep[:, c0:c0 + 512], vscr, LN_EPS,
                           wait=[("s_v", v1)])
            v2 = sch.op("vector", lambda e: e.tensor_scalar_mul(
                mrow_kv[0:1, 0:512], mscr[0:1, 0:512], -1.0), inc="s_v")
            if qq == 1 or gi >= 2:
                og = gi if qq == 1 else gi - 2
                sch.op("vector", lambda e, og=og: e.tensor_scalar_mul(
                    mrow_own[0:1, og * 512:(og + 1) * 512],
                    mscr[0:1, 0:512], -1.0), inc="s_v")
            for tt_ in range(4):
                slot = (t0 + tt_) % 16
                tcur = sch.op("tensor", lambda e, tt_=tt_, c0=c0:
                              e.transpose(pb(7)[0:128, 0:1],
                                          r1_rep[0:1, c0 + tt_ * 128:c0 + (tt_ + 1) * 128],
                                          ident[0:1, 0:1]),
                              wait=[("s_s", s1), bankf[7]], inc="s_t")
                scur = sch.op("scalar", lambda e, slot=slot:
                              e.copy(r1col[:, slot:slot + 1], pb(7)[0:128, 0:1]),
                              wait=[("s_t", tcur)], inc="s_s")
                bankf[7] = ("s_s", scur)

            cos_w1 = dma("sync", cosk[:], cos_d[:, c0:c0 + 512],
                         wait=[("s_v", st.get("cosk_fr", 0))] + st.get("gate", []))
            cos_w2 = dma("sync", sink[:], sin_d[:, c0:c0 + 512],
                         wait=st.get("gate", []))
            r1sq_v = sch.op("vector", lambda e, c0=c0: e.tensor_mul(
                r1sq[:], r1_rep[:, c0:c0 + 512], r1_rep[:, c0:c0 + 512]),
                wait=[("s_s", s1), ("s_v", st.get("r1sq_fr", 0))], inc="s_v")
            for j in range(NKV):
                for par in range(2):
                    kc = 2 * j + par
                    bk = kc % 2
                    for ic in range(16):
                        sch.op("tensor", lambda e, ic=ic, kc=kc, bk=bk:
                               e.matmul(pb(bk)[:, 0:512],
                                        Wkb[ic][:, kc * 128:(kc + 1) * 128],
                                        xb[0][ic][:], start=(ic == 0), stop=False),
                               wait=([bankf[bk], ("s_v", v2)] if ic == 0 else ()))
                    tcur = sch.op("tensor", lambda e, kc=kc, bk=bk:
                                  e.matmul(pb(bk)[:, 0:512],
                                           wrow[0:1, kc * 128:(kc + 1) * 128],
                                           mrow_kv[0:1, 0:512],
                                           start=False, stop=True),
                                  inc="s_t")
                    scur = sch.op("scalar", lambda e, par=par, bk=bk:
                                  e.copy(kf[par][:], pb(bk)[:, 0:512]),
                                  wait=[("s_t", tcur),
                                        ("s_v", st.get("kf_fr", 0))], inc="s_s")
                    sch.drain("scalar")
                    scur = sch.op("scalar", lambda e, bk=bk:
                                  e.activation(ksqb[:], pb(bk)[:, 0:512],
                                               AF.Square),
                                  wait=[("s_t", st.get("ksq_mm", 0))], inc="s_s")
                    bankf[bk] = ("s_s", scur)
                    st["ksq_mm"] = sch.op("tensor", lambda e, par=par:
                                          e.matmul(pb(4)[:, 0:512], ones[:], ksqb[:],
                                                   start=(par == 0), stop=(par == 1)),
                                          wait=[("s_s", scur)] + ([bankf[4]] if par == 0 else []),
                                          inc="s_t")
                tsum = sch.count("s_t")
                dbg_this = DEBUG and qq == 0 and gi == 2 and j == 0
                if dbg_this:
                    dma("scalar", dbg["kstar"], kf[0][:], wait=[("s_s", sch.count("s_s"))])
                vF = sch.op("vector", lambda e: e.tensor_mul(
                    Fsc[:], pb(4)[:, 0:512], r1sq[:]),
                    wait=[("s_t", tsum), ("s_v", r1sq_v)], inc="s_v")
                if dbg_this:
                    sdbg = sch.op("scalar", lambda e: e.copy(vscr[:], pb(4)[:, 0:512]),
                                  wait=[("s_t", tsum), ("s_v", vF)], inc="s_s")
                    bankf[4] = ("s_s", sdbg)
                    dma("scalar", dbg["ss_k"], vscr[:], wait=[("s_s", sdbg)])
                bankf[4] = ("s_v", vF)
                sF = rsqrt_act(Fsc, Fsc, EPS_QK, wait=[("s_v", vF)],
                               pre_scale=1.0 / HD)
                vF = sch.op("vector", lambda e, c0=c0: e.tensor_mul(
                    Fsc[:], Fsc[:], r1_rep[:, c0:c0 + 512]),
                    wait=[("s_s", sF)], inc="s_v")
                sch.drain("vector")
                if dbg_this:
                    dma("scalar", dbg["F_k"], Fsc[:], wait=[("s_v", vF)])
                    dma("scalar", dbg["r1sq_k"], r1sq[:], wait=[("s_v", vF)])
                slotc = (t0 % 16) * 128
                vpl = sch.op("vector", lambda e, j=j, slotc=slotc:
                             e.scalar_tensor_tensor(
                                 kT[j][1][:, slotc:slotc + 512], kf[1][:],
                                 ks16[:, 1:2], Fsc[:], ALU.mult, ALU.mult),
                             wait=[("s_v", vF)], inc="s_v")
                sch.op("vector", lambda e: e.scalar_tensor_tensor(
                    krt[:], kf[0][:], ks16[:, 0:1], Fsc[:], ALU.mult, ALU.mult))
                sch.drain("vector")
                shmask = []
                for i2 in range(16):
                    shmask += [2 * i2 + 1, 2 * i2]
                sch.op("vector", lambda e: e.stream_shuffle(krs[:], krt[:], shmask))
                sch.drain("vector")
                sch.op("vector", lambda e: e.tensor_mul(krt[:], krt[:], cosk[:]),
                       wait=[cos_w1, cos_w2])
                sch.drain("vector")
                sch.op("vector", lambda e: e.tensor_mul(krs[:], krs[:], sink[:]))
                sch.drain("vector")
                vrot = sch.op("vector", lambda e, j=j, slotc=slotc:
                              e.tensor_add(kT[j][0][:, slotc:slotc + 512],
                                           krt[:], krs[:]), inc="s_v")
                st["kf_fr"] = vrot
            st["cosk_fr"] = sch.count("s_v")
            st["r1sq_fr"] = sch.count("s_v")
            if gi == 0:
                colsum_row(Wvb, 1024, 1024, wWv)
                st["wsumv"] = bankf[4]

            for tt_ in range(4):
                slot = (t0 + tt_) % 16
                for ng in range(2):
                    bk = ng % 2
                    for ic in range(16):
                        sch.op("tensor", lambda e, ic=ic, tt_=tt_, ng=ng, bk=bk:
                               e.matmul(pb(bk)[:, 0:512],
                                        xb[0][ic][:, tt_ * 128:(tt_ + 1) * 128],
                                        Wvb[ic][:, ng * 512:(ng + 1) * 512],
                                        start=(ic == 0), stop=False),
                               wait=([bankf[bk]] if ic == 0 else ()))
                    tcur = sch.op("tensor", lambda e, tt_=tt_, ng=ng, bk=bk:
                                  e.matmul(pb(bk)[:, 0:512],
                                           mrow_kv[0:1, tt_ * 128:(tt_ + 1) * 128],
                                           wrow[0:1, 1024 + ng * 512:1024 + (ng + 1) * 512],
                                           start=False, stop=True),
                                  wait=[st.get("wsumv", ("s_s", 0))], inc="s_t")
                    vcur = sch.op("vector", lambda e, slot=slot, ng=ng, bk=bk:
                                  e.tensor_scalar_mul(
                                      vt[slot][:, ng * 512:(ng + 1) * 512],
                                      pb(bk)[:, 0:512], r1col[:, slot:slot + 1]),
                                  wait=[("s_t", tcur)], inc="s_v")
                    bankf[bk] = ("s_v", vcur)
            st["xb0_fr"] = sch.count("s_t")
        st["kv_done_t"] = sch.count("s_t")
        st["kv_done_v"] = sch.count("s_v")

    def q_phase(qq):
        xbws = []
        for og in range(2):
            c0 = 1024 + qq * 1024 + og * 512
            for g4 in range(4):
                xbws.append(gdma(
                    xbA[og][:, g4 * 4 * 512:(g4 + 1) * 4 * 512]
                    .rearrange("p (a t) -> p a t", a=4),
                    xT_d[g4 * 512:(g4 + 1) * 512, c0:c0 + 512]
                    .rearrange("(a p) t -> p a t", p=128),
                    wait=([("s_t", st.get("xb_own_fr", 0)),
                           ("s_t", st.get("kv_done_t", 0)),
                           ("s_v", st.get("kv_done_v", 0))] if g4 == 0 and og == 0
                          else ())))
        vxb_w = xbws

        shmask = []
        for i2 in range(16):
            shmask += [2 * i2 + 1, 2 * i2]
        for blk in range(8):
            h = blk
            wqw = gdma(WqbA[blk % 2][:].rearrange("p (a t) -> p a t", a=16),
                       wq_d[:, h * 256:(h + 1) * 256]
                       .rearrange("(a p) t -> p a t", p=128),
                       wait=[("s_t", st.get(f"wqb_fr{blk % 2}", 0))]
                       + (vxb_w if blk == 0 else []))
            vwq = wqw
            for c in range(16):
                sch.op("tensor", lambda e, c=c, blk=blk:
                       e.matmul(pb(4)[0:1, 0:256], ones[:, 0:1],
                                WqbA[blk % 2][:, c * 256:(c + 1) * 256],
                                start=(c == 0), stop=(c == 15)),
                       wait=([vwq, bankf[4]] if c == 0 else ()),
                       inc=("s_t" if c == 15 else None))
            tcur = sch.count("s_t")
            swsum = sch.op("scalar", lambda e: e.copy(wrow[0:1, 0:256],
                                                      pb(4)[0:1, 0:256]),
                           wait=[("s_t", tcur)], inc="s_s")
            bankf[4] = ("s_s", swsum)

            for og in range(2):
                c0_loc = 1024 + qq * 1024 + og * 512
                cw1 = dma("sync", cosq[:], cos_d[:, c0_loc:c0_loc + 512],
                          wait=[("s_v", st.get("cosq_fr", 0)),
                                ("s_t", st.get("kv_done_t", 0))])
                cw2 = dma("sync", sinq[:], sin_d[:, c0_loc:c0_loc + 512],
                          wait=[("s_t", st.get("kv_done_t", 0))])
                vr1 = sch.op("vector", lambda e, c0_loc=c0_loc: e.tensor_mul(
                    qr1sq[:], r1_rep[:, c0_loc:c0_loc + 512],
                    r1_rep[:, c0_loc:c0_loc + 512]),
                    wait=[("s_v", st.get("qr1sq_fr", 0))], inc="s_v")
                for par in range(2):
                    bk = par
                    for ic in range(16):
                        sch.op("tensor", lambda e, ic=ic, par=par, og=og, bk=bk, blk=blk:
                               e.matmul(pb(bk)[:, 0:512],
                                        WqbA[blk % 2][:, ic * 256 + par * 128:
                                                      ic * 256 + (par + 1) * 128],
                                        xb[og][ic][:], start=(ic == 0), stop=False),
                               wait=(([bankf[bk], ("s_s", swsum)] + vxb_w)
                                     if ic == 0 else ()))
                    tcur = sch.op("tensor", lambda e, par=par, og=og, bk=bk:
                                  e.matmul(pb(bk)[:, 0:512],
                                           wrow[0:1, par * 128:(par + 1) * 128],
                                           mrow_own[0:1, og * 512:(og + 1) * 512],
                                           start=False, stop=True), inc="s_t")
                    st[f"wqb_fr{blk % 2}"] = tcur
                    scur = sch.op("scalar", lambda e, par=par, bk=bk:
                                  e.copy(qf[par][:], pb(bk)[:, 0:512]),
                                  wait=[("s_t", tcur),
                                        ("s_v", st.get("qf_fr", 0))], inc="s_s")
                    sch.drain("scalar")
                    scur = sch.op("scalar", lambda e, bk=bk:
                                  e.activation(qsqb[:], pb(bk)[:, 0:512], AF.Square),
                                  wait=[("s_t", st.get("qsq_mm", 0))], inc="s_s")
                    bankf[bk] = ("s_s", scur)
                    st["qsq_mm"] = sch.op(
                        "tensor", lambda e, par=par:
                        e.matmul(pb(4)[:, 0:512], ones[:], qsqb[:],
                                 start=(par == 0), stop=(par == 1)),
                        wait=[("s_s", scur)] + ([bankf[4]] if par == 0 else []),
                        inc="s_t")
                tsum = sch.count("s_t")
                vF = sch.op("vector", lambda e: e.tensor_mul(
                    qFsc[:], pb(4)[:, 0:512], qr1sq[:]),
                    wait=[("s_t", tsum), ("s_v", vr1)], inc="s_v")
                bankf[4] = ("s_v", vF)
                sF = rsqrt_act(qFsc, qFsc, EPS_QK, wait=[("s_v", vF)],
                               pre_scale=1.0 / HD)
                vF = sch.op("vector", lambda e, c0_loc=c0_loc: e.tensor_mul(
                    qFsc[:], qFsc[:], r1_rep[:, c0_loc:c0_loc + 512]),
                    wait=[("s_s", sF)], inc="s_v")
                sch.drain("vector")
                sch.op("vector", lambda e, h=h, og=og: e.scalar_tensor_tensor(
                    qT[2 * h + 1][:, og * 512:(og + 1) * 512], qf[1][:],
                    qs16[:, 1:2], qFsc[:], ALU.mult, ALU.mult),
                    wait=[("s_v", vF)], inc="s_v")
                sch.op("vector", lambda e: e.scalar_tensor_tensor(
                    qrt[:], qf[0][:], qs16[:, 0:1], qFsc[:], ALU.mult, ALU.mult))
                sch.drain("vector")
                sch.op("vector", lambda e: e.stream_shuffle(krs[:] if False else qf[0][:], qrt[:], shmask))
                sch.drain("vector")
                sch.op("vector", lambda e: e.tensor_mul(qrt[:], qrt[:], cosq[:]),
                       wait=[cw1, cw2])
                sch.drain("vector")
                sch.op("vector", lambda e: e.tensor_mul(qf[0][:], qf[0][:], sinq[:]))
                sch.drain("vector")
                st["cosq_fr"] = sch.op(
                    "vector", lambda e, h=h, og=og:
                    e.tensor_add(qT[2 * h][:, og * 512:(og + 1) * 512],
                                 qrt[:], qf[0][:]), inc="s_v")
                st["qr1sq_fr"] = st["cosq_fr"]
                st["qf_fr"] = st["cosq_fr"]
        st["xb_own_fr"] = sch.count("s_t")
        st["q_done_v"] = sch.count("s_v")

    def attn_phase(qq):
        qv = st["q_done_v"]
        sch.op("vector", lambda e: e.memset(pTa[0][:], 0.0),
               wait=[("s_v", st.get("kv_done_v", 0)), ("s_v", qv)])
        sch.op("vector", lambda e: e.memset(pTa[1][:], 0.0), inc="s_v")
        st["pTa_fr0"] = sch.count("s_v")
        st["pTa_fr1"] = sch.count("s_v")
        for og in range(2):
            mset = (qq * 2 + og) * 12
            mw = dma("sync", maskb[:].rearrange("p (k f) -> p k f", k=12),
                     masks_d[mset:mset + 12].rearrange("k p f -> p k f"),
                     wait=[("s_v", st.get("maskb_fr", 0)),
                           ("s_t", st.get("xb_own_fr", 0))])
            A_loc = 8 + qq * 8 + og * 4
            for h in range(NH):
                j = h // 2
                buf = h % 2
                pready = {}
                for kji in range(12):
                    kjt = A_loc - 8 + kji
                    slot = (kjt % 16) * 128
                    a0 = max(0, (kji - 8) * 128)
                    a1 = min(512, (kji - 8) * 128 + 1152)
                    bk = kji % 2
                    skip_mask = (4 <= kji <= 7) and not (qq == 0 and og == 0)
                    for dc in range(2):
                        sch.op("tensor",
                               lambda e, j=j, dc=dc, slot=slot, h=h, og=og,
                               a0=a0, a1=a1, bk=bk:
                               e.matmul(pb(bk)[:, a0:a1],
                                        kT[j][dc][:, slot:slot + 128],
                                        qT[2 * h + dc][:, og * 512 + a0:og * 512 + a1],
                                        start=(dc == 0), stop=(dc == 1)),
                               wait=([bankf[bk], ("s_v", qv)] if dc == 0 else ()),
                               inc=("s_t" if dc == 1 else None))
                    tsc = sch.count("s_t")
                    sch.op("scalar", lambda e, a0=a0, a1=a1, bk=bk:
                           e.activation(ttmp[:, a0:a1], pb(bk)[:, a0:a1],
                                        AF.Tanh, scale=0.02),
                           wait=[("s_t", tsc),
                                 ("s_t", st.get(f"pv_done{buf}", 0))])
                    sch.drain("scalar")
                    if skip_mask:
                        scur = sch.op("scalar", lambda e, buf=buf, kji=kji, a0=a0, a1=a1:
                                      e.activation(pT[buf][:, kji * 512 + a0:kji * 512 + a1],
                                                   ttmp[:, a0:a1], AF.Exp, scale=50.0),
                                      inc="s_s")
                        sch.drain("scalar")
                        bankf[bk] = ("s_s", scur)
                        pready[kji] = ("s_s", scur)
                    else:
                        scur = sch.op("scalar", lambda e, kji=kji, a0=a0, a1=a1:
                                      e.activation(pTa[kji % 2][:, a0:a1],
                                                   ttmp[:, a0:a1], AF.Exp, scale=50.0),
                                      wait=[("s_v", st.get(f"pTa_fr{kji % 2}", 0))],
                                      inc="s_s")
                        sch.drain("scalar")
                        bankf[bk] = ("s_s", scur)
                        st[f"pTa_fr{kji % 2}"] = sch.op(
                            "vector", lambda e, buf=buf, kji=kji:
                            e.tensor_mul(pT[buf][:, kji * 512:(kji + 1) * 512],
                                         pTa[kji % 2][:],
                                         maskb[:, kji * 512:(kji + 1) * 512]),
                            wait=[("s_s", scur), mw,
                                  ("s_t", st.get(f"pv_done{buf}", 0))],
                            inc="s_v")
                        pready[kji] = ("s_v", st[f"pTa_fr{kji % 2}"])
                for kji in range(12):
                    kjt = A_loc - 8 + kji
                    slot = (kjt % 16)
                    for dc in range(2):
                        bo = 2 + dc
                        sch.op("tensor",
                               lambda e, slot=slot, j=j, dc=dc, buf=buf, kji=kji, bo=bo:
                               e.matmul(pb(bo)[:, 0:512],
                                        vt[slot][:, j * 256 + dc * 128:j * 256 + (dc + 1) * 128],
                                        pT[buf][:, kji * 512:(kji + 1) * 512],
                                        start=(kji == 0), stop=(kji == 11)),
                               wait=([pready[kji]] +
                                     ([bankf[2], bankf[3]] if kji == 0 and dc == 0 else [])),
                               inc=("s_t" if (kji == 11 and dc == 1) else None))
                for kji in range(12):
                    sch.op("tensor", lambda e, buf=buf, kji=kji:
                           e.matmul(pb(4)[0:128, 0:512], ones[:],
                                    pT[buf][:, kji * 512:(kji + 1) * 512],
                                    start=(kji == 0), stop=(kji == 11)),
                           wait=([pready[kji]] + ([bankf[4]] if kji == 0 else [])),
                           inc=("s_t" if kji == 11 else None))
                st[f"pv_done{buf}"] = sch.count("s_t")
                tpv = sch.count("s_t")
                vrec = sch.op("vector", lambda e: e.reciprocal(
                    rec[:], pb(4)[:, 0:512]), wait=[("s_t", tpv)], inc="s_v")
                bankf[4] = ("s_v", vrec)
                sch.drain("vector")
                v1 = sch.op("vector", lambda e, h=h, og=og:
                            e.tensor_tensor(att[2 * h][:, og * 512:(og + 1) * 512],
                                            pb(2)[:, 0:512], rec[:], ALU.mult),
                            wait=[("s_t", tpv)], inc="s_v")
                v2 = sch.op("vector", lambda e, h=h, og=og:
                            e.tensor_tensor(att[2 * h + 1][:, og * 512:(og + 1) * 512],
                                            pb(3)[:, 0:512], rec[:], ALU.mult),
                            inc="s_v")
                bankf[2] = ("s_v", v1)
                bankf[3] = ("s_v", v2)
            st["maskb_fr"] = sch.count("s_v")
        st["attn_done_v"] = sch.count("s_v")
        st["rq_free"] = sch.count("s_t")
        st["kv_ring_free"] = sch.count("s_t")

    def wo_phase(qq):
        av = st["attn_done_v"]
        z = None
        for n in range(2):
            sch.op("vector", lambda e, n=n: e.memset(accA[n][:], 0.0),
                   wait=[("s_v", st.get("accs_fr", 0))])
            z = sch.op("vector", lambda e, n=n: e.memset(accB[n][:], 0.0),
                       inc="s_v")
        for blk in range(8):
            wws = [gdma(Wob[blk % 2][:].rearrange("p (a t) -> p a t", a=16),
                        wo_d[:, blk * 256:(blk + 1) * 256]
                        .rearrange("(a p) t -> p a t", p=128),
                        wait=[("s_t", st.get(f"wob_fr{blk % 2}", 0))]
                        + ([("s_v", av)] if blk < 2 else []))]
            vcast = None
            for po in range(2):
                oc = blk * 2 + po
                for ng in range(2):
                    bk = po * 2 + ng
                    xw = dma("sync", xck[bk][:],
                             xT_d[oc * 128:(oc + 1) * 128,
                                  1024 + qq * 1024 + ng * 512:
                                  1024 + qq * 1024 + (ng + 1) * 512],
                             wait=[("s_v", st.get(f"xck_fr{bk}", 0)),
                                   ("s_v", av)])
                    for ic in range(16):
                        sch.op("tensor",
                               lambda e, blk=blk, ic=ic, po=po, ng=ng, bk=bk:
                               e.matmul(pb(bk)[:, 0:512],
                                        Wob[blk % 2][:, ic * 256 + po * 128:ic * 256 + (po + 1) * 128],
                                        att[ic][:, ng * 512:(ng + 1) * 512],
                                        start=(ic == 0), stop=(ic == 15)),
                               wait=(wws + [bankf[bk], ("s_v", av)]
                                     if ic == 0 else ()),
                               inc=("s_t" if ic == 15 else None))
                    st[f"wob_fr{blk % 2}"] = sch.count("s_t")
                    tcur = sch.count("s_t")
                    if qq == 1:
                        vx1 = sch.op("vector",
                                     lambda e, oc=oc, ng=ng, bk=bk:
                                     e.tensor_add(x1r[oc][:, ng * 512:(ng + 1) * 512],
                                                  pb(bk)[:, 0:512], xck[bk][:]),
                                     wait=[("s_t", tcur), xw], inc="s_v")
                        x1ap = lambda oc=oc, ng=ng: x1r[oc][:, ng * 512:(ng + 1) * 512]
                    else:
                        vx1 = sch.op("vector",
                                     lambda e, ng=ng, bk=bk:
                                     e.tensor_add(x1stg[bk % 2][:],
                                                  pb(bk)[:, 0:512], xck[bk][:]),
                                     wait=[("s_t", tcur), xw,
                                           st.get(f"x1stg_fr{bk % 2}", ("s_v", 0))],
                                     inc="s_v")
                        st[f"x1stg_fr{bk % 2}"] = dma(
                            "scalar", x1q0_d[oc * 128:(oc + 1) * 128,
                                             ng * 512:(ng + 1) * 512],
                            x1stg[bk % 2][:], wait=[("s_v", vx1)])
                        x1ap = lambda bk=bk: x1stg[bk % 2][:]
                    st[f"xck_fr{bk}"] = vx1
                    bankf[bk] = ("s_v", vx1)
                    vb = sch.op("vector", lambda e, x1ap=x1ap: e.tensor_copy(
                        x1b_s[:], x1ap()),
                        wait=[("s_v", vx1), ("s_t", st.get("x1bs_fr", 0))],
                        inc="s_v")
                    sq = sch.op("scalar", lambda e: e.activation(
                        x1sq_s[:], x1b_s[:], AF.Square), wait=[("s_v", vb)],
                        inc="s_s")
                    t5 = sch.op("tensor", lambda e: e.matmul(
                        pb(5)[:, 0:512], ones[:], x1b_s[:], start=True, stop=True),
                        wait=[bankf[5], ("s_v", vb)], inc="s_t")
                    t6 = sch.op("tensor", lambda e: e.matmul(
                        pb(6)[:, 0:512], ones[:], x1sq_s[:], start=True, stop=True),
                        wait=[bankf[6], ("s_s", sq)], inc="s_t")
                    st["x1bs_fr"] = t6
                    va = sch.op("vector", lambda e, ng=ng: e.tensor_add(
                        accA[ng][:], accA[ng][:], pb(5)[:, 0:512]),
                        wait=[("s_t", t5)], inc="s_v")
                    bankf[5] = ("s_v", va)
                    sch.drain("vector")
                    vb2 = sch.op("vector", lambda e, ng=ng: e.tensor_add(
                        accB[ng][:], accB[ng][:], pb(6)[:, 0:512]),
                        wait=[("s_t", t6)], inc="s_v")
                    bankf[6] = ("s_v", vb2)
                    sch.drain("vector")
        for ng in range(2):
            coff = qq * 1024 + ng * 512
            sch.op("vector", lambda e, ng=ng, coff=coff: e.tensor_scalar_mul(
                m2_rep[:, coff:coff + 512], accA[ng][:], 1.0 / 2048.0))
            sch.drain("vector")
            sch.op("vector", lambda e, coff=coff: e.tensor_mul(
                m2scr[:], m2_rep[:, coff:coff + 512], m2_rep[:, coff:coff + 512]))
            sch.drain("vector")
            vv2 = sch.op("vector", lambda e, ng=ng: e.scalar_tensor_tensor(
                m2scr[:], accB[ng][:], 1.0 / 2048.0, m2scr[:],
                ALU.mult, ALU.subtract), inc="s_v")
            s2 = rsqrt_act(r2p, m2scr, LN_EPS, wait=[("s_v", vv2)])
            sch.op("vector", lambda e: e.tensor_mul(r3s[:], r2p[:], r2p[:]),
                   wait=[("s_s", s2)])
            sch.drain("vector")
            vv3 = sch.op("vector", lambda e: e.tensor_mul(
                r3s[:], r3s[:], m2scr[:]), inc="s_v")
            s3 = rsqrt_act(r3s, r3s, EPS_NORM, wait=[("s_v", vv3)])
            vv4 = sch.op("vector", lambda e, coff=coff: e.tensor_mul(
                rr_rep[:, coff:coff + 512], r2p[:], r3s[:]),
                wait=[("s_s", s3)], inc="s_v")
        st["accs_fr"] = sch.count("s_v")

    def max_pass(qq):
        if qq == 1:
            for oc in range(16):
                for ng in range(2):
                    coff = qq * 1024 + ng * 512
                    sch.op("vector", lambda e, oc=oc, ng=ng, coff=coff: e.tensor_sub(
                        xnt[:], x1r[oc][:, ng * 512:(ng + 1) * 512],
                        m2_rep[:, coff:coff + 512]))
                    sch.drain("vector")
                    sch.op("vector", lambda e, oc=oc, coff=coff: e.scalar_tensor_tensor(
                        xnu[:], xnt[:], gam16[:, oc:oc + 1],
                        rr_rep[:, coff:coff + 512], ALU.mult, ALU.mult))
                    sch.drain("vector")
                    sch.op("vector", lambda e: e.tensor_reduce(
                        smalls[:, SM["rmx"]:SM["rmx"] + 1], xnu[:], AXX, ALU.max, apply_absolute_value=True))
                    sch.drain("vector")
                    sch.op("vector", lambda e: e.tensor_max(
                        smalls[:, SM["runmax"]:SM["runmax"] + 1],
                        smalls[:, SM["runmax"]:SM["runmax"] + 1], smalls[:, SM["rmx"]:SM["rmx"] + 1]), inc="s_v")
                    sch.drain("vector")
            return
        for sl in range(16):
            oc0 = (sl // 2) * 2
            ng = sl % 2
            coff = ng * 512
            w = dma("scalar", mxstg[sl % 2][:].rearrange("p (a t) -> p a t", a=2),
                    x1q0_d[oc0 * 128:(oc0 + 2) * 128, ng * 512:(ng + 1) * 512]
                    .rearrange("(a p) t -> p a t", p=128),
                    wait=[("s_v", st.get(f"mx_fr{sl % 2}", 0))])
            for half in range(2):
                oc = oc0 + half
                sch.op("vector", lambda e, half=half, coff=coff, sl=sl: e.tensor_sub(
                    xnt[:], mxstg[sl % 2][:, half * 512:(half + 1) * 512],
                    m2_rep[:, coff:coff + 512]), wait=[w])
                sch.drain("vector")
                sch.op("vector", lambda e, oc=oc, coff=coff: e.scalar_tensor_tensor(
                    xnu[:], xnt[:], gam16[:, oc:oc + 1], rr_rep[:, coff:coff + 512],
                    ALU.mult, ALU.mult))
                sch.drain("vector")
                vm = sch.op("vector", lambda e: e.tensor_reduce(
                    smalls[:, SM["rmx"]:SM["rmx"] + 1], xnu[:], AXX, ALU.max, apply_absolute_value=True))
                st[f"mx_fr{sl % 2}"] = vm
                sch.drain("vector")
                sch.op("vector", lambda e: e.tensor_max(
                    smalls[:, SM["runmax"]:SM["runmax"] + 1],
                    smalls[:, SM["runmax"]:SM["runmax"] + 1], smalls[:, SM["rmx"]:SM["rmx"] + 1]), inc="s_v")
                sch.drain("vector")

    def collective_trigger():
        tcur = sch.op("tensor", lambda e: e.transpose(
            pb(7)[0:1, 0:128], smalls[:, SM["runmax"]:SM["runmax"] + 1], ident[:]),
            wait=[("s_v", sch.count("s_v")), bankf[7]], inc="s_t")
        sch.op("vector", lambda e: e.tensor_reduce(
            smalls[0:1, SM["lmax"]:SM["lmax"] + 1], pb(7)[0:1, 0:128], AXX,
            ALU.max), wait=[("s_t", tcur)])
        sch.drain("vector")
        sch.op("vector", lambda e: e.tensor_scalar_max(
            smalls[0:1, SM["lmax"]:SM["lmax"] + 1],
            smalls[0:1, SM["lmax"]:SM["lmax"] + 1], EPS_SCALE))
        sch.drain("vector")
        vv1 = sch.op("vector", lambda e: e.tensor_copy(
            ccs[0:1, 0:1], smalls[0:1, SM["lmax"]:SM["lmax"] + 1]), inc="s_v")
        bankf[7] = ("s_v", vv1)
        w = dma("sync", ccin_d, ccs[:], wait=[("s_v", vv1)])
        g1 = sch.op("gpsimd", lambda e: e.collective_compute(
            "AllReduce", ALU.max, replica_groups=[[0, 1, 2, 3, 4, 5, 6, 7]],
            ins=[ccin_d], outs=[ccout_d]), wait=[w], inc="s_g")
        g2 = sch.op("gpsimd", lambda e: e.dma_start(out=ccs[:], in_=ccout_d),
                    wait=[("s_g", g1)], inc="s_g2", inc_by=16)
        st["g2"] = g2

    def collective_scales():
        g2 = st["g2"]
        w2 = dma("sync", smalls[:, SM["Scol"]:SM["Scol"] + 1],
                 ccout_d[0:1, 0:1].partition_broadcast(128), wait=[("s_g2", g2)])
        sch.op("vector", lambda e: e.reciprocal(
            smalls[:, SM["rS127"]:SM["rS127"] + 1],
            smalls[:, SM["Scol"]:SM["Scol"] + 1]), wait=[w2])
        sch.drain("vector")
        sch.op("vector", lambda e: e.tensor_scalar_mul(
            smalls[:, SM["rS127"]:SM["rS127"] + 1],
            smalls[:, SM["rS127"]:SM["rS127"] + 1], 127.0))
        sch.drain("vector")
        sch.op("vector", lambda e: e.tensor_mul(
            smalls[:, SM["fscale"]:SM["fscale"] + 1],
            smalls[:, SM["Scol"]:SM["Scol"] + 1],
            smalls[:, SM["swcol"]:SM["swcol"] + 1]))
        sch.drain("vector")
        sch.op("vector", lambda e: e.tensor_scalar_mul(
            smalls[:, SM["fscale"]:SM["fscale"] + 1],
            smalls[:, SM["fscale"]:SM["fscale"] + 1], -1.0 / 127.0))
        sch.drain("vector")
        vv2 = sch.op("vector", lambda e: e.tensor_scalar_mul(
            rr_rep[:], rr_rep[:], smalls[:, SM["rS127"]:SM["rS127"] + 1]),
            inc="s_v")
        sch.drain("vector")
        if DEBUG:
            dma("scalar", dbg["sS"], ccs[:], wait=[("s_g2", g2)])
        st["scales_v"] = vv2

    def wtern():
        for p in range(16):
            w = dma("scalar", wstg[p % 2][:], wt_d[p * 128:(p + 1) * 128, :],
                    wait=[("s_v", st.get(f"wstg_fr{p % 2}", 0)),
                          ("s_v", st["pre_coll_v"])])
            va = sch.op("vector", lambda e, p=p: e.tensor_single_scalar(
                wqTt[p][:], wstg[p % 2][:],
                smalls[:, SM["halfsw"]:SM["halfsw"] + 1], ALU.is_gt),
                wait=[w, ("s_v", st["pre_coll_v"]),
                      ("s_t", st.get("rq_free2", 0))], inc="s_v")
            sch.drain("vector")
            for ng in range(4):
                vb = sch.op("vector", lambda e, p=p, ng=ng: e.tensor_single_scalar(
                    xq[ng][:], wstg[p % 2][:, ng * 512:(ng + 1) * 512],
                    smalls[:, SM["neghalfsw"]:SM["neghalfsw"] + 1], ALU.is_lt),
                    wait=[("s_v", va)], inc="s_v")
                sch.drain("vector")
                st[f"wstg_fr{p % 2}"] = sch.op(
                    "vector", lambda e, p=p, ng=ng: e.tensor_sub(
                        wqTt[p][:, ng * 512:(ng + 1) * 512],
                        wqTt[p][:, ng * 512:(ng + 1) * 512], xq[ng][:]),
                    inc="s_v")
                sch.drain("vector")
        st["wq_ready"] = sch.count("s_v")
        if DEBUG:
            dma("scalar", dbg["wq0"], wqTt[0][:], wait=[("s_v", st["wq_ready"])])

    def ffn():
        groups = [(1, 0), (1, 1), (0, 0), (0, 1)]

        def prefetch(gi):
            qqf, ng = groups[gi]
            if qqf != 0:
                return []
            rj = x1rA[:].rearrange("p (a t) -> p a t", a=16)
            w = dma("scalar",
                    rj[:, :, ng * 512:(ng + 1) * 512],
                    x1q0_d[:, ng * 512:(ng + 1) * 512]
                    .rearrange("(a p) t -> p a t", p=128),
                    wait=[("s_v", st.get(f"x1half_fr{ng}", 0))])
            return [w]

        def quant_chunk(gi, ic, x1w):
            qqf, ng = groups[gi]
            par = gi % 2
            xqg = xq2[par]
            coff = qqf * 1024 + ng * 512
            xsl = lambda oc, ng=ng: x1r[oc][:, ng * 512:(ng + 1) * 512]
            sch.op("vector", lambda e, ic=ic, coff=coff, xsl=xsl: e.tensor_sub(
                qnt[0][:], xsl(ic), m2_rep[:, coff:coff + 512]),
                wait=list(x1w) + [("s_t", st.get(f"xq_fr{par}", 0))])
            sch.drain("vector")
            sch.op("vector", lambda e, ic=ic, coff=coff: e.scalar_tensor_tensor(
                qnt[1][:], qnt[0][:], gam16[:, ic:ic + 1],
                rr_rep[:, coff:coff + 512], ALU.mult, ALU.mult))
            sch.drain("vector")
            vcl = sch.op("vector", lambda e: e.tensor_scalar(
                qnt[0][:], qnt[1][:], 127.0, -128.0, ALU.min, ALU.max),
                inc="s_v")
            sch.op("scalar", lambda e: e.copy(qni[:], qnt[0][:]),
                   wait=[("s_v", vcl)], inc="s_s")
            sch.drain("scalar")
            scp2 = sch.op("scalar", lambda e: e.copy(qnt[1][:], qni[:]),
                          inc="s_s")
            sch.op("vector", lambda e: e.tensor_sub(qnt[2][:], qnt[0][:],
                                                    qnt[1][:]),
                   wait=[("s_s", scp2)])
            sch.drain("vector")
            sch.op("vector", lambda e: e.scalar_tensor_tensor(
                qnt[0][:], qnt[2][:], 0.5, qnt[1][:], ALU.is_gt, ALU.add))
            sch.drain("vector")
            v = sch.op("vector", lambda e, ic=ic, xqg=xqg: e.scalar_tensor_tensor(
                xqg[ic][:], qnt[2][:], -0.5, qnt[0][:], ALU.is_lt,
                ALU.subtract), inc="s_v")
            sch.drain("vector")
            return v

        x1w = prefetch(0)
        for ic in range(16):
            quant_chunk(0, ic, x1w)
        vq = {0: sch.count("s_v")}
        x1w_next = prefetch(1)
        for gi, (qqf, ng) in enumerate(groups):
            par = gi % 2
            xqg = xq2[par]
            coff = qqf * 1024 + ng * 512
            xsl = lambda oc, ng=ng: x1r[oc][:, ng * 512:(ng + 1) * 512]
            for oc in range(16):
                bk = oc % 2
                for ic in range(16):
                    sch.op("tensor", lambda e, oc=oc, ic=ic, bk=bk, xqg=xqg:
                           e.matmul(pb(bk)[:, 0:512],
                                    wqTt[ic][:, oc * 128:(oc + 1) * 128],
                                    xqg[ic][:], start=(ic == 0), stop=(ic == 15)),
                           wait=([bankf[bk], ("s_v", vq[gi]),
                                  ("s_v", st.get("wq_ready", 0))]
                                 if ic == 0 else ()),
                           inc=("s_t" if ic == 15 else None))
                tcur = sch.count("s_t")
                vo = sch.op("vector", lambda e, oc=oc, bk=bk, xsl=xsl:
                            e.scalar_tensor_tensor(
                                ostg[bk][:], pb(bk)[:, 0:512],
                                smalls[:, SM["fscale"]:SM["fscale"] + 1],
                                xsl(oc), ALU.mult, ALU.add),
                            wait=[("s_t", tcur),
                                  st.get(f"ostg_fr{bk}", ("s_v", 0))], inc="s_v")
                bankf[bk] = ("s_v", vo)
                st[f"ostg_fr{bk}"] = dma(
                    "scalar", out_d[oc * 128:(oc + 1) * 128, coff:coff + 512],
                    ostg[bk][:], wait=[("s_v", vo)])
                if gi + 1 < 4:
                    quant_chunk(gi + 1, oc, x1w_next)
            st[f"xq_fr{par}"] = sch.count("s_t")
            st[f"x1half_fr{ng}"] = sch.count("s_v")
            if gi + 1 < 4:
                vq[gi + 1] = sch.count("s_v")
            if gi + 2 < 4:
                x1w_next = prefetch(gi + 2)

    def dbg_dump(qq):
        if not DEBUG or qq != 0:
            return
        v = sch.count("s_v")
        s = sch.count("s_s")
        dma("scalar", dbg["r1"], r1_rep[:], wait=[("s_s", s), ("s_v", v)])
        dma("scalar", dbg["k00"], kT[0][0][:], wait=[("s_v", v)])
        dma("scalar", dbg["v8"], vt[8][:], wait=[("s_v", v)])

    def dbg_dump_q(qq):
        if not DEBUG or qq != 0:
            return
        v = sch.count("s_v")
        dma("scalar", dbg["q0"], qT[0][:], wait=[("s_v", v)])
        dma("scalar", dbg["q1"], qT[1][:], wait=[("s_v", v)])

    def dbg_dump_attn(qq):
        if not DEBUG or qq != 0:
            return
        v = sch.count("s_v")
        dma("scalar", dbg["attn0"], att[0][:], wait=[("s_v", v)])
        dma("scalar", dbg["attn1"], att[1][:], wait=[("s_v", v)])

    def dbg_dump_wo(qq):
        if not DEBUG:
            return
        v = sch.count("s_v")
        if qq == 1:
            dma("scalar", dbg["x1c0"], x1r[0][:], wait=[("s_v", v)])
            dma("scalar", dbg["m2"], m2_rep[:], wait=[("s_v", v)])
            dma("scalar", dbg["rr"], rr_rep[:], wait=[("s_v", v)])

    st["gate"] = [("s_v", CONST_V)]
    for qq in (0, 1):
        kv_phase(qq)
        dbg_dump(qq)
        q_phase(qq)
        dbg_dump_q(qq)
        attn_phase(qq)
        dbg_dump_attn(qq)
        st["accs_fr"] = st.get("accs_fr", 0)
        wo_phase(qq)
        if qq == 0:
            st["gate"] = [("s_t", sch.count("s_t")),
                          ("s_v", sch.count("s_v")),
                          ("s_s", sch.count("s_s"))]
        max_pass(qq)
        dbg_dump_wo(qq)
        if qq == 0:
            st["mx0_v"] = sch.count("s_v")
    st["pre_coll_v"] = sch.count("s_v")
    st["rq_free2"] = sch.count("s_t")
    collective_trigger()
    wtern()
    collective_scales()
    ffn()

    if DEBUG:
        dma("scalar", dbg["smalls"], smalls[:], wait=[("s_v", sch.count("s_v"))])
        dma("scalar", dbg["wabs"], wabs[:], wait=[("s_v", sch.count("s_v"))])
    for sem_n, v in dma_barrier("scalar") + dma_barrier("sync"):
        sch.prog["gpsimd"].append(
            lambda e, sems, sem_n=sem_n, v=v: e.wait_ge(sems[sem_n], v))

    with nc.Block() as block:
        @block.sync
        def _(eng):
            for f in sch.prog["sync"]:
                f(eng, sems)

        @block.scalar
        def _(eng):
            for f in sch.prog["scalar"]:
                f(eng, sems)

        @block.vector
        def _(eng):
            for f in sch.prog["vector"]:
                f(eng, sems)

        @block.tensor
        def _(eng):
            for f in sch.prog["tensor"]:
                f(eng, sems)

        @block.gpsimd
        def _(eng):
            for f in sch.prog["gpsimd"]:
                f(eng, sems)

    return nc, dbg


def _host_prep(inputs):
    x = np.asarray(inputs["L0_attn_x"], np.float32)
    wq = np.ascontiguousarray(np.asarray(inputs["L0_attn_Wq"], np.float32))
    wk = np.ascontiguousarray(np.asarray(inputs["L0_attn_Wk"], np.float32))
    wv = np.ascontiguousarray(np.asarray(inputs["L0_attn_Wv"], np.float32))
    wo = np.ascontiguousarray(np.asarray(inputs["L0_attn_Wo"], np.float32))
    wt = np.ascontiguousarray(np.asarray(inputs["L0_ffn_weight"], np.float32).T)
    qsc = np.asarray(inputs["L0_attn_q_scale"], np.float32)
    ksc = np.asarray(inputs["L0_attn_k_scale"], np.float32)
    gam = np.asarray(inputs["L0_ffn_norm_weight"], np.float32)

    qs2 = np.ascontiguousarray(qsc.reshape(2, 128).T)
    ks2 = np.ascontiguousarray(ksc.reshape(2, 128).T)
    gam16 = np.ascontiguousarray(gam.reshape(16, 128).T)
    ones128 = np.ones((128, 128), dtype=BF)
    ident = np.eye(128, dtype=np.float32)

    half = 64
    theta = 1.0 / (10000.0 ** (np.arange(0, ROT_, 2, dtype=np.float32) / ROT_))
    in_maps = []
    for b in range(B):
        xb_T = np.ascontiguousarray(x[b].T)
        for hh in range(2):
            lo = hh * 2048 - 1024
            xT = np.zeros((E, NLOC), np.float32)
            s0 = max(0, lo)
            xT[:, s0 - lo:3072] = xb_T[:, s0:lo + 3072]
            pos = np.arange(lo, lo + NLOC, dtype=np.float32)
            ang = pos[None, :] * theta[:, None]
            cosT = np.zeros((128, NLOC), np.float32)
            sinT = np.zeros((128, NLOC), np.float32)
            cosT[0::2] = np.cos(ang)
            cosT[1::2] = np.cos(ang)
            sinT[0::2] = -np.sin(ang)
            sinT[1::2] = np.sin(ang)
            masks = np.zeros((48, 128, 512), dtype=BF)
            for qq in range(2):
                for og in range(2):
                    A_loc = 8 + qq * 8 + og * 4
                    for kji in range(12):
                        kjt = A_loc - 8 + kji
                        K = lo + kjt * 128 + np.arange(128)[:, None]
                        Q = lo + A_loc * 128 + np.arange(512)[None, :]
                        ok = (K >= 0) & (K <= Q) & (Q - K < WIN_)
                        masks[(qq * 2 + og) * 12 + kji] = ok.astype(BF)
            in_maps.append({
                "xT": xT, "wq": wq, "wk": wk, "wv": wv, "wo": wo, "wt": wt,
                "qs2": qs2, "ks2": ks2, "gam": gam16,
                "cosT": cosT, "sinT": sinT, "masks": masks,
                "ones128": ones128, "ident128": ident,
            })
    return in_maps


ROT_ = 128
WIN_ = 1024

_NC_CACHE = {}


def kernel(**inputs):
    key = "nc"
    if key not in _NC_CACHE:
        _NC_CACHE[key] = build_nc()
    nc, dbg = _NC_CACHE[key]
    in_maps = _host_prep(inputs)
    res = run_bass_kernel_spmd(nc, in_maps, core_ids=list(range(8)))
    out = np.zeros((B, S, E), np.float32)
    for b in range(B):
        for hh in range(2):
            core = b * 2 + hh
            out[b, hh * 2048:(hh + 1) * 2048, :] = res.results[core]["out"].T
    if DEBUG:
        kernel._last = res
    return out


# version 10
# speedup vs baseline: 1.0750x; 1.0133x over previous
import os
import numpy as np
import ml_dtypes
import concourse.bass as bass
import concourse.mybir as mybir
from concourse.bass_utils import run_bass_kernel_spmd

F32 = mybir.dt.float32
BF16 = mybir.dt.bfloat16
I32 = mybir.dt.int32
AF = mybir.ActivationFunctionType
ALU = mybir.AluOpType
BF = ml_dtypes.bfloat16
AXX = mybir.AxisListType.X

B, S, E = 4, 4096, 2048
NH, NKV, HD = 8, 4, 256
QSCALE = 256 ** (-0.5)
EPS_QK = 1e-6
LN_EPS = 1e-5
EPS_NORM = 1e-6
EPS_SCALE = 1e-5
NLOC = 3072
KB = 1024

DEBUG = int(os.environ.get("MK_DEBUG", "0"))


class TView:

    def __init__(self, t, off, width):
        self.t = t
        self.off = off
        self.width = width

    def __getitem__(self, key):
        if isinstance(key, tuple):
            r, c = key
            start = (c.start or 0) + self.off
            stop = (c.stop if c.stop is not None else self.width) + self.off
            return self.t[r, start:stop]
        return self.t[key, self.off:self.off + self.width]


class Sched:
    ENGS = ("sync", "scalar", "vector", "tensor", "gpsimd")

    def __init__(self):
        self.prog = {e: [] for e in self.ENGS}
        self.C = {}
        self.hw = {e: {} for e in self.ENGS}
        self.pend = {}

    def count(self, sem):
        return self.C.get(sem, 0)

    DEFSEM = {"vector": "s_v", "scalar": "s_s"}

    def op(self, eng, fn, wait=(), inc=None, inc_by=1):
        if inc is None and eng in self.DEFSEM:
            inc = self.DEFSEM[eng]
        waits = list(wait)
        if self.pend.pop(eng, None):
            dsem = self.DEFSEM[eng]
            waits.append((dsem, self.count(dsem)))
        wl = []
        for sem, val in waits:
            if val > self.hw[eng].get(sem, 0):
                wl.append((sem, val))
                self.hw[eng][sem] = val
        if inc is not None:
            self.C[inc] = self.count(inc) + inc_by

        def emit(e, sems, waits=wl, fn=fn, inc=inc, inc_by=inc_by):
            for sem, val in waits:
                e.wait_ge(sems[sem], val)
            instr = fn(e)
            if inc is not None:
                instr.then_inc(sems[inc], inc_by)

        self.prog[eng].append(emit)
        return self.count(inc) if inc is not None else None

    def drain(self, eng):
        self.pend[eng] = True


def build_nc():
    nc = bass.Bass("TRN2", target_bir_lowering=False)

    xT_d = nc.dram_tensor("xT", [E, NLOC], F32, kind="ExternalInput").ap()
    wq_d = nc.dram_tensor("wq", [E, 2048], F32, kind="ExternalInput").ap()
    wk_d = nc.dram_tensor("wk", [E, 1024], F32, kind="ExternalInput").ap()
    wv_d = nc.dram_tensor("wv", [E, 1024], F32, kind="ExternalInput").ap()
    wo_d = nc.dram_tensor("wo", [E, 2048], F32, kind="ExternalInput").ap()
    wt_d = nc.dram_tensor("wt", [E, 2048], F32, kind="ExternalInput").ap()
    qs2_d = nc.dram_tensor("qs2", [128, 2], F32, kind="ExternalInput").ap()
    ks2_d = nc.dram_tensor("ks2", [128, 2], F32, kind="ExternalInput").ap()
    gam_d = nc.dram_tensor("gam", [128, 16], F32, kind="ExternalInput").ap()
    cos_d = nc.dram_tensor("cosT", [128, NLOC], F32, kind="ExternalInput").ap()
    sin_d = nc.dram_tensor("sinT", [128, NLOC], F32, kind="ExternalInput").ap()
    masks_d = nc.dram_tensor("masks", [48, 128, 512], BF16, kind="ExternalInput").ap()
    ones_d = nc.dram_tensor("ones128", [128, 128], BF16, kind="ExternalInput").ap()
    id_d = nc.dram_tensor("ident128", [128, 128], F32, kind="ExternalInput").ap()
    out_d = nc.dram_tensor("out", [E, 2048], F32, kind="ExternalOutput").ap()
    x1q0_d = nc.dram_tensor("x1q0", [E, 1024], F32, kind="Internal").ap()
    wqT_d = nc.dram_tensor("wqTd", [E, 2048], BF16, kind="Internal").ap()
    ccin_d = nc.dram_tensor("ccin", [1, 16], F32, kind="Internal").ap()
    bc_d = nc.dram_tensor("bc_d", [1, 16], F32, kind="Internal").ap()
    ccout_d = nc.dram_tensor("ccout", [1, 16], F32, kind="Internal").ap()

    dbg = {}
    if DEBUG:
        def dbgt(name, shape, dt=F32):
            dbg[name] = nc.dram_tensor("dbg_" + name, shape, dt,
                                       kind="ExternalOutput").ap()
        dbgt("r1", [128, NLOC]); dbgt("k00", [128, 2048], BF16)
        dbgt("v8", [128, 1024], BF16)
        dbgt("q0", [128, 1024], BF16); dbgt("q1", [128, 1024], BF16)
        dbgt("pt", [128, 12 * 512], BF16)
        dbgt("attn0", [128, 1024], BF16); dbgt("attn1", [128, 1024], BF16)
        dbgt("x1c0", [128, 1024]); dbgt("m2", [128, 2048]); dbgt("rr", [128, 2048])
        dbgt("sS", [1, 16]); dbgt("xq0", [128, 512], BF16); dbgt("wq0", [128, 2048], BF16)
        dbgt("ss_k", [128, 512]); dbgt("F_k", [128, 512]); dbgt("r1sq_k", [128, 512])
        dbgt("maskb", [128, 12 * 512], BF16); dbgt("pta", [128, 512], BF16)
        dbgt("smalls", [128, 16]); dbgt("wabs", [128, 16])
        dbgt("wcol", [128, 1]); dbgt("wrowT", [1, 128])
        dbgt("sums", [128, 512]); dbgt("ttmp", [128, 512])
        dbgt("kstar", [128, 512])

    SBUF_BASE = 16512

    def at(name, shape, dt, off):
        return nc.alloc_sbuf_tensor_at(name, shape, dt, offset=SBUF_BASE + off)

    o = [0]

    def nxt(sz):
        off = (o[0] + 31) // 32 * 32
        o[0] = off + sz
        return off

    ones = at("ones", [128, 128], BF16, nxt(256))
    onesf = at("onesf", [128, 128], F32, nxt(512))
    ident = at("ident", [128, 128], F32, nxt(512))
    qs16 = at("qs16", [128, 2], F32, nxt(32))
    ks16 = at("ks16", [128, 2], F32, nxt(32))
    gam16 = at("gam16", [128, 16], F32, nxt(64))
    smalls = at("smalls", [128, 16], F32, nxt(64))
    SM = dict(runmax=0, lmax=1, Scol=2, rS127=3, swcol=4, halfsw=5,
              neghalfsw=6, fscale=7, swrow=8, eps5=10, eps6=11, rmx=12)
    ccs = at("ccs", [1, 16], F32, nxt(64))
    _mo_off = (o[0] + 31) // 32 * 32
    mrow_own = at("mrow_own", [1, 1024], BF16, nxt(2048))
    dbgw_s = at("dbgw_s", [1, 128], F32, _mo_off)
    r1col = at("r1col", [128, 16], F32, nxt(64))
    wabs = at("wabs", [128, 16], F32, nxt(64))
    wcol = at("wcol", [128, 1], F32, nxt(32))
    wrow = at("wrow", [1, 2048], BF16, nxt(4096))
    mrow_kv = at("mrow_kv", [1, 512], BF16, nxt(1024))
    r1_rep = at("r1_rep", [128, NLOC], F32, nxt(NLOC * 4))
    _o_rr = nxt(8192)
    rr_rep = at("rr_rep", [128, 2048], F32, _o_rr)
    _o_m2 = nxt(8192)
    m2_rep = at("m2_rep", [128, 2048], F32, _o_m2)
    terW = [at("terW0", [128, 2048], F32, _o_m2), None]
    terA = at("terA", [128, 2048], BF16, _o_rr)
    terB = at("terB", [128, 2048], BF16, _o_rr + 4 * KB)
    PERM_END = (o[0] + 31) // 32 * 32
    R_K = PERM_END
    R_V = R_K + 32 * KB
    R_Q = R_V + 32 * KB
    R_AT = R_Q + 32 * KB
    R_FX = R_AT + 32 * KB
    FX_END = R_FX + 43 * KB
    assert FX_END + 16512 <= 229376, (PERM_END, FX_END)

    kT = [[at(f"kT{j}_{d}", [128, 2048], BF16, R_K + (j * 2 + d) * 4 * KB)
           for d in range(2)] for j in range(NKV)]
    vt = [at(f"v{t}", [128, 1024], BF16, R_V + t * 2 * KB) for t in range(16)]
    qT = [at(f"qT{c}", [128, 1024], BF16, R_Q + c * 2 * KB) for c in range(16)]
    att = [at(f"att{c}", [128, 1024], BF16, R_AT + c * 2 * KB) for c in range(16)]

    WkbA = at("WkbA", [128, 16 * 1024], BF16, R_Q)
    Wkb = [TView(WkbA, c * 1024, 1024) for c in range(16)]
    WvbA = at("WvbA", [128, 16 * 1024], BF16, R_AT)
    Wvb = [TView(WvbA, c * 1024, 1024) for c in range(16)]
    xbA = [at(f"xbA{g}", [128, 16 * 512], BF16, R_FX + g * 16 * KB)
           for g in range(2)]
    xb = [[TView(xbA[g], c * 512, 512) for c in range(16)] for g in range(2)]
    xstg = [at(f"xstg{i}", [128, 1024], F32, R_FX + 16 * KB + i * 4 * KB)
            for i in range(2)]
    terW[1] = at("terW1", [128, 2048], F32, R_FX + 16 * KB)
    kvs = R_FX + 24 * KB
    kf = [at(f"kf{i}", [128, 512], F32, kvs + i * 2 * KB) for i in range(2)]
    krt = at("krt", [128, 512], F32, kvs + 4 * KB)
    krs = at("krs", [128, 512], F32, kvs + 6 * KB)
    Fsc = at("Fsc", [128, 512], F32, kvs + 8 * KB)
    r1sq = at("r1sq", [128, 512], F32, kvs + 10 * KB)
    cosk = at("cosk", [128, 512], F32, kvs + 12 * KB)
    sink = at("sink", [128, 512], F32, kvs + 14 * KB)
    ksqb = at("ksqb", [128, 512], BF16, kvs + 16 * KB)
    mscr = at("mscr", [128, 512], F32, kvs + 4 * KB)
    vscr = at("vscr", [128, 512], F32, kvs + 6 * KB)

    wqstg = at("wqstg8", [128, 8 * 256], F32, R_FX + 32 * KB)
    WqbA = [at(f"WqbA{i}", [128, 16 * 256], BF16, R_AT + i * 8 * KB)
            for i in range(2)]
    qf = [at(f"qf{i}", [128, 512], F32, R_AT + 16 * KB + i * 2 * KB)
          for i in range(2)]
    qFsc = at("qFsc", [128, 512], F32, R_AT + 20 * KB)
    qr1sq = at("qr1sq", [128, 512], F32, R_AT + 22 * KB)
    qsqb = at("qsqb", [128, 512], BF16, R_AT + 24 * KB)
    cosq = at("cosq", [128, 512], F32, R_AT + 25 * KB)
    sinq = at("sinq", [128, 512], F32, R_AT + 27 * KB)
    qrt = at("qrt", [128, 512], F32, R_AT + 29 * KB)

    maskb = at("maskb", [128, 12 * 512], BF16, R_FX)
    pT = [at(f"pT{i}", [128, 12 * 512], BF16, R_FX + 12 * KB + i * 12 * KB)
          for i in range(2)]
    pTa = [at(f"pTa{i}", [128, 512], BF16, R_FX + 36 * KB + i * KB)
           for i in range(2)]
    ttmp = at("ttmp", [128, 512], F32, R_FX + 38 * KB)
    rec = at("rec", [128, 512], F32, R_FX + 40 * KB)

    wostg = at("wostg", [128, 16 * 256], F32, R_FX)
    Wob = [at(f"Wob{i}", [128, 16 * 256], BF16, R_FX + 16 * KB + i * 8 * KB)
           for i in range(2)]
    xck = [at(f"xck{i}", [128, 512], F32, R_FX + 32 * KB + i * 2 * KB)
           for i in range(4)]
    accA = [at(f"accA{n}", [128, 512], F32, R_Q + n * 2 * KB) for n in range(2)]
    accB = [at(f"accB{n}", [128, 512], F32, R_Q + 4 * KB + n * 2 * KB)
            for n in range(2)]
    x1b_s = at("x1b_s", [128, 512], BF16, R_Q + 8 * KB)
    x1sq_s = at("x1sq_s", [128, 512], BF16, R_Q + 9 * KB)
    m2scr = at("m2scr", [128, 512], F32, R_Q + 10 * KB)
    r2p = at("r2p", [128, 512], F32, R_Q + 12 * KB)
    r3s = at("r3s", [128, 512], F32, R_Q + 14 * KB)
    xnt = at("xnt", [128, 512], F32, R_Q + 16 * KB)
    xnu = at("xnu", [128, 512], F32, R_Q + 18 * KB)
    x1stg = [at(f"x1stg{i}", [128, 512], F32, R_Q + 20 * KB + i * 2 * KB)
             for i in range(2)]
    mxstg = [at(f"mxstg{i}", [128, 1024], F32, R_Q + 24 * KB + i * 4 * KB)
             for i in range(2)]
    rmx = None
    x1rA = at("x1rA", [128, 16 * 1024], F32, R_K)
    x1r = [TView(x1rA, c * 1024, 1024) for c in range(16)]

    wqTt = [at(f"wqT{c}", [128, 2048], BF16, R_Q + c * 4 * KB) for c in range(16)]
    wstg = [at(f"wstg{i}", [128, 2048], F32, R_FX + i * 8 * KB) for i in range(2)]
    xqA_t = at("xqA", [128, 16 * 512], BF16, R_FX + 16 * KB)
    xqB_t = at("xqB", [128, 16 * 512], BF16, R_FX)
    xq = [TView(xqA_t, c * 512, 512) for c in range(16)]
    xq2 = [[TView(xqA_t, c * 512, 512) for c in range(16)],
           [TView(xqB_t, c * 512, 512) for c in range(16)]]
    ostg = [at(f"ostg{i}", [128, 512], F32, R_FX + 32 * KB + i * 2 * KB)
            for i in range(2)]
    qnt = [at(f"qnt{i}", [128, 512], F32, R_FX + 36 * KB + i * 2 * KB)
           for i in range(3)]
    qni = at("qni", [128, 512], I32, R_FX + 40 * KB)

    ps_guards = []
    g01 = nc.psum_tensor("ps01", [128, 1024], F32)
    P01t = g01.__enter__()
    ps_guards.append(g01)
    pbt = [TView(P01t, 0, 512), TView(P01t, 512, 512)]
    for i in range(2, 8):
        g = nc.psum_tensor(f"ps{i}", [128, 512], F32)
        t = g.__enter__()
        ps_guards.append(g)
        pbt.append(t)

    def pb(i):
        return pbt[i]

    sems = {}
    sem_guards = {}
    for n in (["s_t", "s_s", "s_v", "s_g", "s_g2"]
              + [f"sx{i}" for i in range(8)] + [f"sy{i}" for i in range(8)]
              + [f"sg{i}" for i in range(8)]):
        g = nc.semaphore(n)
        sems[n] = g.__enter__()
        sem_guards[n] = g

    sch = Sched()
    st = {}
    bankf = {i: ("s_s", 0) for i in range(8)}

    qidx = {"sync": 0, "scalar": 0, "gpsimd": 0}

    def dma(queue, out, in_, wait=()):
        pref = "sx" if queue == "sync" else "sy"
        i = qidx[queue]
        qidx[queue] = i + 1
        lane = f"{pref}{i % 8}"
        prev = sch.count(lane)
        wl = list(wait) + ([(lane, prev)] if prev else [])
        cnt = sch.op(queue, lambda e, out=out, in_=in_:
                     e.dma_start(out=out, in_=in_), wait=wl, inc=lane, inc_by=16)
        return (lane, cnt)

    def dma_barrier(queue):
        pref = {"sync": "sx", "scalar": "sy", "gpsimd": "sg"}[queue]
        return [(f"{pref}{i}", sch.count(f"{pref}{i}"))
                for i in range(8) if sch.count(f"{pref}{i}") > 0]

    def gdma(out, in_, wait=()):
        i = qidx["gpsimd"]
        qidx["gpsimd"] = i + 1
        lane = f"sg{i % 8}"
        prev = sch.count(lane)
        wl = list(wait) + ([(lane, prev)] if prev else [])
        cnt = sch.op("gpsimd", lambda e, out=out, in_=in_:
                     e.dma_start(out=out, in_=in_), wait=wl, inc=lane, inc_by=16)
        return (lane, cnt)

    dma("sync", ones[:], ones_d)
    dma("sync", ident[:], id_d)
    dma("sync", qs16[:], qs2_d)
    dma("sync", ks16[:], ks2_d)
    w0 = dma("sync", gam16[:], gam_d)
    CONSTW = dma_barrier("sync")
    sch.op("vector", lambda e: e.tensor_scalar_mul(qs16[:], qs16[:], QSCALE),
           wait=CONSTW)
    sch.op("vector", lambda e: e.memset(onesf[:], 1.0))
    sch.op("vector", lambda e: e.memset(smalls[:, 0:1], 0.0))
    sch.op("vector", lambda e: e.memset(smalls[:, SM["eps5"]:SM["eps5"] + 1], LN_EPS))
    sch.op("vector", lambda e: e.memset(smalls[:, SM["eps6"]:SM["eps6"] + 1], EPS_QK))
    sch.op("vector", lambda e: e.memset(ccs[:], 0.0), inc="s_v")
    CONST_V = sch.count("s_v")

    for p in range(16):
        wdone = st.get("wmean_used", 0)
        w = dma("scalar", wstg[p % 2][:], wt_d[p * 128:(p + 1) * 128, :],
                wait=([("s_v", wdone)] if p >= 2 else ()))
        st["wmean_used"] = sch.op(
            "vector", lambda e, p=p: e.tensor_reduce(
                wabs[:, p:p + 1], wstg[p % 2][:], AXX, ALU.add,
                apply_absolute_value=True),
            wait=[w], inc="s_v")
    sch.drain("vector")
    sch.op("vector", lambda e: e.tensor_reduce(wcol[:], wabs[:], AXX, ALU.add),
           inc="s_v")
    t1 = sch.op("tensor", lambda e: e.transpose(pb(7)[0:1, 0:128], wcol[:],
                                                ident[:]),
                wait=[("s_v", sch.count("s_v"))] + CONSTW, inc="s_t")
    if DEBUG:
        sdw = sch.op("scalar", lambda e: e.copy(wabs[0:1, 0:16] if False else dbgw_s[:], pb(7)[0:1, 0:128]),
                     wait=[("s_t", t1)], inc="s_s")
        dma("scalar", dbg["wrowT"], dbgw_s[:], wait=[("s_s", sdw)])
        dma("scalar", dbg["wcol"], wcol[:], wait=[("s_v", sch.count("s_v"))])
    sch.op("vector", lambda e: e.tensor_reduce(
        smalls[0:1, SM["swrow"]:SM["swrow"] + 1], pb(7)[0:1, 0:128], AXX,
        ALU.add), wait=[("s_t", t1)])
    sch.drain("vector")
    vv = sch.op("vector", lambda e: e.tensor_scalar(
        smalls[0:1, SM["swrow"]:SM["swrow"] + 1],
        smalls[0:1, SM["swrow"]:SM["swrow"] + 1],
        1.0 / (2048.0 * 2048.0), EPS_SCALE, ALU.mult, ALU.max), inc="s_v")
    wv_ = dma("sync", bc_d[0:1, 0:1],
              smalls[0:1, SM["swrow"]:SM["swrow"] + 1], wait=[("s_v", vv)])
    wv_ = dma("sync", smalls[:, SM["swcol"]:SM["swcol"] + 1],
              bc_d[0:1, 0:1].partition_broadcast(128), wait=[wv_])
    sch.op("vector", lambda e: e.tensor_scalar_mul(
        smalls[:, SM["halfsw"]:SM["halfsw"] + 1],
        smalls[:, SM["swcol"]:SM["swcol"] + 1], 0.5), wait=[wv_])
    sch.drain("vector")
    sch.op("vector", lambda e: e.tensor_scalar_mul(
        smalls[:, SM["neghalfsw"]:SM["neghalfsw"] + 1],
        smalls[:, SM["halfsw"]:SM["halfsw"] + 1], -1.0), inc="s_v")
    bankf[7] = ("s_v", sch.count("s_v"))
    st0_wmean_v = sch.count("s_v")

    def load_panels_cast(w_dram, destA, ncols, extra_wait=(), per_g=None):
        ws = []
        for g in range(4):
            wt = list(extra_wait) if g == 0 else []
            if per_g:
                wt += per_g[g]
            ws.append(gdma(
                destA[:, g * 4 * ncols:(g + 1) * 4 * ncols]
                .rearrange("p (a t) -> p a t", a=4),
                w_dram[g * 512:(g + 1) * 512, 0:ncols]
                .rearrange("(a p) t -> p a t", p=128),
                wait=wt))
        return ws

    def colsum_row(chunks, ncols, row_off, wwait):
        for ng in range(0, ncols, 512):
            nn = min(512, ncols - ng)
            for c in range(16):
                sch.op("tensor", lambda e, c=c, ng=ng, nn=nn, chunks=chunks:
                       e.matmul(pb(4)[0:1, 0:nn], ones[:, 0:1],
                                chunks[c][:, ng:ng + nn],
                                start=(c == 0), stop=(c == 15)),
                       wait=(list(wwait) + [bankf[4]] if c == 0 else ()),
                       inc=("s_t" if c == 15 else None))
            tcur = sch.count("s_t")
            scur = sch.op("scalar", lambda e, ng=ng, nn=nn, row_off=row_off:
                          e.copy(wrow[0:1, row_off + ng:row_off + ng + nn],
                                 pb(4)[0:1, 0:nn]),
                          wait=[("s_t", tcur)], inc="s_s")
            bankf[4] = ("s_s", scur)

    def rsqrt_act(dst, src, bias, wait=(), pre_scale=1.0):
        dap = dst[:] if hasattr(dst, "ap") else dst
        sap = src[:] if hasattr(src, "ap") else src
        bcol = SM["eps5"] if bias == LN_EPS else SM["eps6"]
        nparts = dap.shape[0]
        bap = smalls[0:nparts, bcol:bcol + 1]
        sch.op("scalar", lambda e: e.activation(dap, sap, AF.Ln,
                                                bias=bap, scale=pre_scale),
               wait=wait)
        sch.drain("scalar")
        return sch.op("scalar", lambda e: e.activation(dap, dap, AF.Exp,
                                                       scale=-0.5), inc="s_s")

    tern_ctr = [0]

    def tern_step():
        p = tern_ctr[0]
        if p >= 16:
            return
        tern_ctr[0] = p + 1
        buf = p % 2
        w = dma("scalar", terW[buf][:], wt_d[p * 128:(p + 1) * 128, :],
                wait=[st.get(f"terW_fr{buf}", ("s_v", 0))])
        sch.op("vector", lambda e: e.tensor_single_scalar(
            terA[:], terW[buf][:],
            smalls[:, SM["halfsw"]:SM["halfsw"] + 1], ALU.is_gt),
            wait=[w, ("s_v", st0_wmean_v), st.get("terA_fr", ("s_v", 0))])
        sch.drain("vector")
        sch.op("vector", lambda e: e.tensor_single_scalar(
            terB[:], terW[buf][:],
            smalls[:, SM["neghalfsw"]:SM["neghalfsw"] + 1], ALU.is_lt))
        sch.drain("vector")
        vs = sch.op("vector", lambda e: e.tensor_sub(
            terA[:], terA[:], terB[:]), inc="s_v")
        st[f"terW_fr{buf}"] = ("s_v", vs)
        st["terA_fr"] = dma("sync", wqT_d[p * 128:(p + 1) * 128, :],
                            terA[:], wait=[("s_v", vs)])

    def kv_phase(qq):
        mxg = [("s_v", st.get("mx0_v", 0))]
        first = 0 if qq == 0 else 16
        ngrp = 4 if qq == 0 else 2

        def load_xgroup(gi):
            t0 = first + gi * 4
            c0 = t0 * 128
            xgate = (st.get("gate", []) if qq == 0
                     else [("s_t", st.get("rq_free", 0))])
            ws = []
            for g4 in range(4):
                ws.append(gdma(
                    xbA[0][:, g4 * 4 * 512:(g4 + 1) * 4 * 512]
                    .rearrange("p (a t) -> p a t", a=4),
                    xT_d[g4 * 512:(g4 + 1) * 512, c0:c0 + 512]
                    .rearrange("(a p) t -> p a t", p=128),
                    wait=(([("s_t", st.get("xb0_fr", 0)),
                            ("s_v", st0_wmean_v)] + xgate)
                          if g4 == 0 else ())))
            return ws

        xws0 = load_xgroup(0)
        wWk = load_panels_cast(wk_d, WkbA, 1024,
                               extra_wait=st.get("gate", []),
                               per_g=[[], [], mxg, mxg])
        wWv = load_panels_cast(wv_d, WvbA, 1024,
                               extra_wait=st.get("gate", []))

        for gi in range(ngrp):
            t0 = first + gi * 4
            c0 = t0 * 128
            xws = xws0 if gi == 0 else load_xgroup(gi)
            for c in range(16):
                scur = sch.op("scalar", lambda e, c=c:
                              e.activation(ksqb[:], xb[0][c][:], AF.Square),
                              wait=[xws[c // 4],
                                    ("s_t", st.get("ksqb_fr", 0))], inc="s_s")
                sch.op("tensor", lambda e, c=c: e.matmul(
                    pb(5)[:, 0:512], ones[:], xb[0][c][:],
                    start=(c == 0), stop=(c == 15)),
                    wait=([bankf[5]] if c == 0 else []) + [xws[c // 4]],
                    inc=("s_t" if c == 15 else None))
                st["ksqb_fr"] = sch.op("tensor", lambda e, c=c: e.matmul(
                    pb(6)[:, 0:512], ones[:], ksqb[:],
                    start=(c == 0), stop=(c == 15)),
                    wait=([bankf[6], ("s_s", scur)] if c == 0
                          else [("s_s", scur)]),
                    inc="s_t")
            tstats = sch.count("s_t")
            st["ksq_mm"] = tstats
            if gi == 0:
                colsum_row(Wkb, 1024, 0, wWk)
            sch.op("vector", lambda e: e.tensor_scalar_mul(
                mscr[:], pb(5)[:, 0:512], 1.0 / 2048.0), wait=[("s_t", tstats)])
            sch.drain("vector")
            sch.op("vector", lambda e: e.tensor_mul(vscr[:], mscr[:], mscr[:]))
            sch.drain("vector")
            v1 = sch.op("vector", lambda e: e.scalar_tensor_tensor(
                vscr[:], pb(6)[:, 0:512], 1.0 / 2048.0, vscr[:],
                ALU.mult, ALU.subtract), inc="s_v")
            bankf[5] = ("s_v", v1)
            bankf[6] = ("s_v", v1)
            s1 = rsqrt_act(r1_rep[:, c0:c0 + 512], vscr, LN_EPS,
                           wait=[("s_v", v1)])
            v2 = sch.op("vector", lambda e: e.tensor_scalar_mul(
                mrow_kv[0:1, 0:512], mscr[0:1, 0:512], -1.0), inc="s_v")
            if qq == 1 or gi >= 2:
                og = gi if qq == 1 else gi - 2
                sch.op("vector", lambda e, og=og: e.tensor_scalar_mul(
                    mrow_own[0:1, og * 512:(og + 1) * 512],
                    mscr[0:1, 0:512], -1.0), inc="s_v")
            for tt_ in range(4):
                slot = (t0 + tt_) % 16
                tcur = sch.op("tensor", lambda e, tt_=tt_, c0=c0:
                              e.transpose(pb(7)[0:128, 0:1],
                                          r1_rep[0:1, c0 + tt_ * 128:c0 + (tt_ + 1) * 128],
                                          ident[0:1, 0:1]),
                              wait=[("s_s", s1), bankf[7]], inc="s_t")
                scur = sch.op("scalar", lambda e, slot=slot:
                              e.copy(r1col[:, slot:slot + 1], pb(7)[0:128, 0:1]),
                              wait=[("s_t", tcur)], inc="s_s")
                bankf[7] = ("s_s", scur)

            cos_w1 = dma("sync", cosk[:], cos_d[:, c0:c0 + 512],
                         wait=[("s_v", st.get("cosk_fr", 0))] + st.get("gate", []))
            cos_w2 = dma("sync", sink[:], sin_d[:, c0:c0 + 512],
                         wait=st.get("gate", []))
            r1sq_v = sch.op("vector", lambda e, c0=c0: e.tensor_mul(
                r1sq[:], r1_rep[:, c0:c0 + 512], r1_rep[:, c0:c0 + 512]),
                wait=[("s_s", s1), ("s_v", st.get("r1sq_fr", 0))], inc="s_v")
            for j in range(NKV):
                for par in range(2):
                    kc = 2 * j + par
                    bk = kc % 2
                    for ic in range(16):
                        sch.op("tensor", lambda e, ic=ic, kc=kc, bk=bk:
                               e.matmul(pb(bk)[:, 0:512],
                                        Wkb[ic][:, kc * 128:(kc + 1) * 128],
                                        xb[0][ic][:], start=(ic == 0), stop=False),
                               wait=([bankf[bk], ("s_v", v2)] if ic == 0 else ()))
                    tcur = sch.op("tensor", lambda e, kc=kc, bk=bk:
                                  e.matmul(pb(bk)[:, 0:512],
                                           wrow[0:1, kc * 128:(kc + 1) * 128],
                                           mrow_kv[0:1, 0:512],
                                           start=False, stop=True),
                                  inc="s_t")
                    scur = sch.op("scalar", lambda e, par=par, bk=bk:
                                  e.copy(kf[par][:], pb(bk)[:, 0:512]),
                                  wait=[("s_t", tcur),
                                        ("s_v", st.get("kf_fr", 0))], inc="s_s")
                    sch.drain("scalar")
                    scur = sch.op("scalar", lambda e, bk=bk:
                                  e.activation(ksqb[:], pb(bk)[:, 0:512],
                                               AF.Square),
                                  wait=[("s_t", st.get("ksq_mm", 0))], inc="s_s")
                    bankf[bk] = ("s_s", scur)
                    st["ksq_mm"] = sch.op("tensor", lambda e, par=par:
                                          e.matmul(pb(4)[:, 0:512], ones[:], ksqb[:],
                                                   start=(par == 0), stop=(par == 1)),
                                          wait=[("s_s", scur)] + ([bankf[4]] if par == 0 else []),
                                          inc="s_t")
                tsum = sch.count("s_t")
                dbg_this = DEBUG and qq == 0 and gi == 2 and j == 0
                if dbg_this:
                    dma("scalar", dbg["kstar"], kf[0][:], wait=[("s_s", sch.count("s_s"))])
                vF = sch.op("vector", lambda e: e.tensor_mul(
                    Fsc[:], pb(4)[:, 0:512], r1sq[:]),
                    wait=[("s_t", tsum), ("s_v", r1sq_v)], inc="s_v")
                if dbg_this:
                    sdbg = sch.op("scalar", lambda e: e.copy(vscr[:], pb(4)[:, 0:512]),
                                  wait=[("s_t", tsum), ("s_v", vF)], inc="s_s")
                    bankf[4] = ("s_s", sdbg)
                    dma("scalar", dbg["ss_k"], vscr[:], wait=[("s_s", sdbg)])
                bankf[4] = ("s_v", vF)
                sF = rsqrt_act(Fsc, Fsc, EPS_QK, wait=[("s_v", vF)],
                               pre_scale=1.0 / HD)
                vF = sch.op("vector", lambda e, c0=c0: e.tensor_mul(
                    Fsc[:], Fsc[:], r1_rep[:, c0:c0 + 512]),
                    wait=[("s_s", sF)], inc="s_v")
                sch.drain("vector")
                if dbg_this:
                    dma("scalar", dbg["F_k"], Fsc[:], wait=[("s_v", vF)])
                    dma("scalar", dbg["r1sq_k"], r1sq[:], wait=[("s_v", vF)])
                slotc = (t0 % 16) * 128
                vpl = sch.op("vector", lambda e, j=j, slotc=slotc:
                             e.scalar_tensor_tensor(
                                 kT[j][1][:, slotc:slotc + 512], kf[1][:],
                                 ks16[:, 1:2], Fsc[:], ALU.mult, ALU.mult),
                             wait=[("s_v", vF)], inc="s_v")
                sch.op("vector", lambda e: e.scalar_tensor_tensor(
                    krt[:], kf[0][:], ks16[:, 0:1], Fsc[:], ALU.mult, ALU.mult))
                sch.drain("vector")
                shmask = []
                for i2 in range(16):
                    shmask += [2 * i2 + 1, 2 * i2]
                sch.op("vector", lambda e: e.stream_shuffle(krs[:], krt[:], shmask))
                sch.drain("vector")
                sch.op("vector", lambda e: e.tensor_mul(krt[:], krt[:], cosk[:]),
                       wait=[cos_w1, cos_w2])
                sch.drain("vector")
                sch.op("vector", lambda e: e.tensor_mul(krs[:], krs[:], sink[:]))
                sch.drain("vector")
                vrot = sch.op("vector", lambda e, j=j, slotc=slotc:
                              e.tensor_add(kT[j][0][:, slotc:slotc + 512],
                                           krt[:], krs[:]), inc="s_v")
                st["kf_fr"] = vrot
                if qq == 0:
                    tern_step()
            st["cosk_fr"] = sch.count("s_v")
            st["r1sq_fr"] = sch.count("s_v")
            if gi == 0:
                colsum_row(Wvb, 1024, 1024, wWv)
                st["wsumv"] = bankf[4]

            for tt_ in range(4):
                slot = (t0 + tt_) % 16
                for ng in range(2):
                    bk = ng % 2
                    for ic in range(16):
                        sch.op("tensor", lambda e, ic=ic, tt_=tt_, ng=ng, bk=bk:
                               e.matmul(pb(bk)[:, 0:512],
                                        xb[0][ic][:, tt_ * 128:(tt_ + 1) * 128],
                                        Wvb[ic][:, ng * 512:(ng + 1) * 512],
                                        start=(ic == 0), stop=False),
                               wait=([bankf[bk]] if ic == 0 else ()))
                    tcur = sch.op("tensor", lambda e, tt_=tt_, ng=ng, bk=bk:
                                  e.matmul(pb(bk)[:, 0:512],
                                           mrow_kv[0:1, tt_ * 128:(tt_ + 1) * 128],
                                           wrow[0:1, 1024 + ng * 512:1024 + (ng + 1) * 512],
                                           start=False, stop=True),
                                  wait=[st.get("wsumv", ("s_s", 0))], inc="s_t")
                    vcur = sch.op("vector", lambda e, slot=slot, ng=ng, bk=bk:
                                  e.tensor_scalar_mul(
                                      vt[slot][:, ng * 512:(ng + 1) * 512],
                                      pb(bk)[:, 0:512], r1col[:, slot:slot + 1]),
                                  wait=[("s_t", tcur)], inc="s_v")
                    bankf[bk] = ("s_v", vcur)
            st["xb0_fr"] = sch.count("s_t")
        st["kv_done_t"] = sch.count("s_t")
        st["kv_done_v"] = sch.count("s_v")

    def q_phase(qq):
        xbws = []
        for og in range(2):
            c0 = 1024 + qq * 1024 + og * 512
            for g4 in range(4):
                xbws.append(gdma(
                    xbA[og][:, g4 * 4 * 512:(g4 + 1) * 4 * 512]
                    .rearrange("p (a t) -> p a t", a=4),
                    xT_d[g4 * 512:(g4 + 1) * 512, c0:c0 + 512]
                    .rearrange("(a p) t -> p a t", p=128),
                    wait=([("s_t", st.get("xb_own_fr", 0)),
                           ("s_t", st.get("kv_done_t", 0)),
                           ("s_v", st.get("kv_done_v", 0))] if g4 == 0 and og == 0
                          else ())))
        vxb_w = xbws

        shmask = []
        for i2 in range(16):
            shmask += [2 * i2 + 1, 2 * i2]
        for blk in range(8):
            h = blk
            wqw = gdma(WqbA[blk % 2][:].rearrange("p (a t) -> p a t", a=16),
                       wq_d[:, h * 256:(h + 1) * 256]
                       .rearrange("(a p) t -> p a t", p=128),
                       wait=[("s_t", st.get(f"wqb_fr{blk % 2}", 0))]
                       + (vxb_w if blk == 0 else []))
            vwq = wqw
            for c in range(16):
                sch.op("tensor", lambda e, c=c, blk=blk:
                       e.matmul(pb(4)[0:1, 0:256], ones[:, 0:1],
                                WqbA[blk % 2][:, c * 256:(c + 1) * 256],
                                start=(c == 0), stop=(c == 15)),
                       wait=([vwq, bankf[4]] if c == 0 else ()),
                       inc=("s_t" if c == 15 else None))
            tcur = sch.count("s_t")
            swsum = sch.op("scalar", lambda e: e.copy(wrow[0:1, 0:256],
                                                      pb(4)[0:1, 0:256]),
                           wait=[("s_t", tcur)], inc="s_s")
            bankf[4] = ("s_s", swsum)

            for og in range(2):
                c0_loc = 1024 + qq * 1024 + og * 512
                cw1 = dma("sync", cosq[:], cos_d[:, c0_loc:c0_loc + 512],
                          wait=[("s_v", st.get("cosq_fr", 0)),
                                ("s_t", st.get("kv_done_t", 0))])
                cw2 = dma("sync", sinq[:], sin_d[:, c0_loc:c0_loc + 512],
                          wait=[("s_t", st.get("kv_done_t", 0))])
                vr1 = sch.op("vector", lambda e, c0_loc=c0_loc: e.tensor_mul(
                    qr1sq[:], r1_rep[:, c0_loc:c0_loc + 512],
                    r1_rep[:, c0_loc:c0_loc + 512]),
                    wait=[("s_v", st.get("qr1sq_fr", 0))], inc="s_v")
                for par in range(2):
                    bk = par
                    for ic in range(16):
                        sch.op("tensor", lambda e, ic=ic, par=par, og=og, bk=bk, blk=blk:
                               e.matmul(pb(bk)[:, 0:512],
                                        WqbA[blk % 2][:, ic * 256 + par * 128:
                                                      ic * 256 + (par + 1) * 128],
                                        xb[og][ic][:], start=(ic == 0), stop=False),
                               wait=(([bankf[bk], ("s_s", swsum)] + vxb_w)
                                     if ic == 0 else ()))
                    tcur = sch.op("tensor", lambda e, par=par, og=og, bk=bk:
                                  e.matmul(pb(bk)[:, 0:512],
                                           wrow[0:1, par * 128:(par + 1) * 128],
                                           mrow_own[0:1, og * 512:(og + 1) * 512],
                                           start=False, stop=True), inc="s_t")
                    st[f"wqb_fr{blk % 2}"] = tcur
                    scur = sch.op("scalar", lambda e, par=par, bk=bk:
                                  e.copy(qf[par][:], pb(bk)[:, 0:512]),
                                  wait=[("s_t", tcur),
                                        ("s_v", st.get("qf_fr", 0))], inc="s_s")
                    sch.drain("scalar")
                    scur = sch.op("scalar", lambda e, bk=bk:
                                  e.activation(qsqb[:], pb(bk)[:, 0:512], AF.Square),
                                  wait=[("s_t", st.get("qsq_mm", 0))], inc="s_s")
                    bankf[bk] = ("s_s", scur)
                    st["qsq_mm"] = sch.op(
                        "tensor", lambda e, par=par:
                        e.matmul(pb(4)[:, 0:512], ones[:], qsqb[:],
                                 start=(par == 0), stop=(par == 1)),
                        wait=[("s_s", scur)] + ([bankf[4]] if par == 0 else []),
                        inc="s_t")
                tsum = sch.count("s_t")
                vF = sch.op("vector", lambda e: e.tensor_mul(
                    qFsc[:], pb(4)[:, 0:512], qr1sq[:]),
                    wait=[("s_t", tsum), ("s_v", vr1)], inc="s_v")
                bankf[4] = ("s_v", vF)
                sF = rsqrt_act(qFsc, qFsc, EPS_QK, wait=[("s_v", vF)],
                               pre_scale=1.0 / HD)
                vF = sch.op("vector", lambda e, c0_loc=c0_loc: e.tensor_mul(
                    qFsc[:], qFsc[:], r1_rep[:, c0_loc:c0_loc + 512]),
                    wait=[("s_s", sF)], inc="s_v")
                sch.drain("vector")
                sch.op("vector", lambda e, h=h, og=og: e.scalar_tensor_tensor(
                    qT[2 * h + 1][:, og * 512:(og + 1) * 512], qf[1][:],
                    qs16[:, 1:2], qFsc[:], ALU.mult, ALU.mult),
                    wait=[("s_v", vF)], inc="s_v")
                sch.op("vector", lambda e: e.scalar_tensor_tensor(
                    qrt[:], qf[0][:], qs16[:, 0:1], qFsc[:], ALU.mult, ALU.mult))
                sch.drain("vector")
                sch.op("vector", lambda e: e.stream_shuffle(krs[:] if False else qf[0][:], qrt[:], shmask))
                sch.drain("vector")
                sch.op("vector", lambda e: e.tensor_mul(qrt[:], qrt[:], cosq[:]),
                       wait=[cw1, cw2])
                sch.drain("vector")
                sch.op("vector", lambda e: e.tensor_mul(qf[0][:], qf[0][:], sinq[:]))
                sch.drain("vector")
                st["cosq_fr"] = sch.op(
                    "vector", lambda e, h=h, og=og:
                    e.tensor_add(qT[2 * h][:, og * 512:(og + 1) * 512],
                                 qrt[:], qf[0][:]), inc="s_v")
                st["qr1sq_fr"] = st["cosq_fr"]
                st["qf_fr"] = st["cosq_fr"]
        st["xb_own_fr"] = sch.count("s_t")
        st["q_done_v"] = sch.count("s_v")

    def attn_phase(qq):
        qv = st["q_done_v"]
        sch.op("vector", lambda e: e.memset(pTa[0][:], 0.0),
               wait=[("s_v", st.get("kv_done_v", 0)), ("s_v", qv)])
        sch.op("vector", lambda e: e.memset(pTa[1][:], 0.0), inc="s_v")
        st["pTa_fr0"] = sch.count("s_v")
        st["pTa_fr1"] = sch.count("s_v")
        for og in range(2):
            mset = (qq * 2 + og) * 12
            mw = dma("sync", maskb[:].rearrange("p (k f) -> p k f", k=12),
                     masks_d[mset:mset + 12].rearrange("k p f -> p k f"),
                     wait=[("s_v", st.get("maskb_fr", 0)),
                           ("s_t", st.get("xb_own_fr", 0))])
            A_loc = 8 + qq * 8 + og * 4
            for h in range(NH):
                j = h // 2
                buf = h % 2
                pready = {}
                for kji in range(12):
                    kjt = A_loc - 8 + kji
                    slot = (kjt % 16) * 128
                    a0 = max(0, (kji - 8) * 128)
                    a1 = min(512, (kji - 8) * 128 + 1152)
                    bk = kji % 2
                    skip_mask = (4 <= kji <= 7) and not (qq == 0 and og == 0)
                    for dc in range(2):
                        sch.op("tensor",
                               lambda e, j=j, dc=dc, slot=slot, h=h, og=og,
                               a0=a0, a1=a1, bk=bk:
                               e.matmul(pb(bk)[:, a0:a1],
                                        kT[j][dc][:, slot:slot + 128],
                                        qT[2 * h + dc][:, og * 512 + a0:og * 512 + a1],
                                        start=(dc == 0), stop=(dc == 1)),
                               wait=([bankf[bk], ("s_v", qv)] if dc == 0 else ()),
                               inc=("s_t" if dc == 1 else None))
                    tsc = sch.count("s_t")
                    sch.op("scalar", lambda e, a0=a0, a1=a1, bk=bk:
                           e.activation(ttmp[:, a0:a1], pb(bk)[:, a0:a1],
                                        AF.Tanh, scale=0.02),
                           wait=[("s_t", tsc),
                                 ("s_t", st.get(f"pv_done{buf}", 0))])
                    sch.drain("scalar")
                    if skip_mask:
                        scur = sch.op("scalar", lambda e, buf=buf, kji=kji, a0=a0, a1=a1:
                                      e.activation(pT[buf][:, kji * 512 + a0:kji * 512 + a1],
                                                   ttmp[:, a0:a1], AF.Exp, scale=50.0),
                                      inc="s_s")
                        sch.drain("scalar")
                        bankf[bk] = ("s_s", scur)
                        pready[kji] = ("s_s", scur)
                    else:
                        scur = sch.op("scalar", lambda e, kji=kji, a0=a0, a1=a1:
                                      e.activation(pTa[kji % 2][:, a0:a1],
                                                   ttmp[:, a0:a1], AF.Exp, scale=50.0),
                                      wait=[("s_v", st.get(f"pTa_fr{kji % 2}", 0))],
                                      inc="s_s")
                        sch.drain("scalar")
                        bankf[bk] = ("s_s", scur)
                        st[f"pTa_fr{kji % 2}"] = sch.op(
                            "vector", lambda e, buf=buf, kji=kji:
                            e.tensor_mul(pT[buf][:, kji * 512:(kji + 1) * 512],
                                         pTa[kji % 2][:],
                                         maskb[:, kji * 512:(kji + 1) * 512]),
                            wait=[("s_s", scur), mw,
                                  ("s_t", st.get(f"pv_done{buf}", 0))],
                            inc="s_v")
                        pready[kji] = ("s_v", st[f"pTa_fr{kji % 2}"])
                for kji in range(12):
                    kjt = A_loc - 8 + kji
                    slot = (kjt % 16)
                    for dc in range(2):
                        bo = 2 + dc
                        sch.op("tensor",
                               lambda e, slot=slot, j=j, dc=dc, buf=buf, kji=kji, bo=bo:
                               e.matmul(pb(bo)[:, 0:512],
                                        vt[slot][:, j * 256 + dc * 128:j * 256 + (dc + 1) * 128],
                                        pT[buf][:, kji * 512:(kji + 1) * 512],
                                        start=(kji == 0), stop=(kji == 11)),
                               wait=([pready[kji]] +
                                     ([bankf[2], bankf[3]] if kji == 0 and dc == 0 else [])),
                               inc=("s_t" if (kji == 11 and dc == 1) else None))
                for kji in range(12):
                    sch.op("tensor", lambda e, buf=buf, kji=kji:
                           e.matmul(pb(4)[0:128, 0:512], ones[:],
                                    pT[buf][:, kji * 512:(kji + 1) * 512],
                                    start=(kji == 0), stop=(kji == 11)),
                           wait=([pready[kji]] + ([bankf[4]] if kji == 0 else [])),
                           inc=("s_t" if kji == 11 else None))
                st[f"pv_done{buf}"] = sch.count("s_t")
                tpv = sch.count("s_t")
                vrec = sch.op("vector", lambda e: e.reciprocal(
                    rec[:], pb(4)[:, 0:512]), wait=[("s_t", tpv)], inc="s_v")
                bankf[4] = ("s_v", vrec)
                sch.drain("vector")
                v1 = sch.op("vector", lambda e, h=h, og=og:
                            e.tensor_tensor(att[2 * h][:, og * 512:(og + 1) * 512],
                                            pb(2)[:, 0:512], rec[:], ALU.mult),
                            wait=[("s_t", tpv)], inc="s_v")
                v2 = sch.op("vector", lambda e, h=h, og=og:
                            e.tensor_tensor(att[2 * h + 1][:, og * 512:(og + 1) * 512],
                                            pb(3)[:, 0:512], rec[:], ALU.mult),
                            inc="s_v")
                bankf[2] = ("s_v", v1)
                bankf[3] = ("s_v", v2)
            st["maskb_fr"] = sch.count("s_v")
        st["attn_done_v"] = sch.count("s_v")
        st["rq_free"] = sch.count("s_t")
        st["kv_ring_free"] = sch.count("s_t")

    def wo_phase(qq):
        av = st["attn_done_v"]
        z = None
        for n in range(2):
            sch.op("vector", lambda e, n=n: e.memset(accA[n][:], 0.0),
                   wait=[("s_v", st.get("accs_fr", 0))])
            z = sch.op("vector", lambda e, n=n: e.memset(accB[n][:], 0.0),
                       inc="s_v")
        for blk in range(8):
            wws = [gdma(Wob[blk % 2][:].rearrange("p (a t) -> p a t", a=16),
                        wo_d[:, blk * 256:(blk + 1) * 256]
                        .rearrange("(a p) t -> p a t", p=128),
                        wait=[("s_t", st.get(f"wob_fr{blk % 2}", 0))]
                        + ([("s_v", av)] if blk < 2 else []))]
            vcast = None
            for po in range(2):
                oc = blk * 2 + po
                for ng in range(2):
                    bk = po * 2 + ng
                    xw = dma("sync", xck[bk][:],
                             xT_d[oc * 128:(oc + 1) * 128,
                                  1024 + qq * 1024 + ng * 512:
                                  1024 + qq * 1024 + (ng + 1) * 512],
                             wait=[("s_v", st.get(f"xck_fr{bk}", 0)),
                                   ("s_v", av)])
                    for ic in range(16):
                        sch.op("tensor",
                               lambda e, blk=blk, ic=ic, po=po, ng=ng, bk=bk:
                               e.matmul(pb(bk)[:, 0:512],
                                        Wob[blk % 2][:, ic * 256 + po * 128:ic * 256 + (po + 1) * 128],
                                        att[ic][:, ng * 512:(ng + 1) * 512],
                                        start=(ic == 0), stop=(ic == 15)),
                               wait=(wws + [bankf[bk], ("s_v", av)]
                                     if ic == 0 else ()),
                               inc=("s_t" if ic == 15 else None))
                    st[f"wob_fr{blk % 2}"] = sch.count("s_t")
                    tcur = sch.count("s_t")
                    if qq == 1:
                        vx1 = sch.op("vector",
                                     lambda e, oc=oc, ng=ng, bk=bk:
                                     e.tensor_add(x1r[oc][:, ng * 512:(ng + 1) * 512],
                                                  pb(bk)[:, 0:512], xck[bk][:]),
                                     wait=[("s_t", tcur), xw], inc="s_v")
                        x1ap = lambda oc=oc, ng=ng: x1r[oc][:, ng * 512:(ng + 1) * 512]
                    else:
                        vx1 = sch.op("vector",
                                     lambda e, ng=ng, bk=bk:
                                     e.tensor_add(x1stg[bk % 2][:],
                                                  pb(bk)[:, 0:512], xck[bk][:]),
                                     wait=[("s_t", tcur), xw,
                                           st.get(f"x1stg_fr{bk % 2}", ("s_v", 0))],
                                     inc="s_v")
                        st[f"x1stg_fr{bk % 2}"] = dma(
                            "scalar", x1q0_d[oc * 128:(oc + 1) * 128,
                                             ng * 512:(ng + 1) * 512],
                            x1stg[bk % 2][:], wait=[("s_v", vx1)])
                        x1ap = lambda bk=bk: x1stg[bk % 2][:]
                    st[f"xck_fr{bk}"] = vx1
                    bankf[bk] = ("s_v", vx1)
                    vb = sch.op("vector", lambda e, x1ap=x1ap: e.tensor_copy(
                        x1b_s[:], x1ap()),
                        wait=[("s_v", vx1), ("s_t", st.get("x1bs_fr", 0))],
                        inc="s_v")
                    sq = sch.op("scalar", lambda e: e.activation(
                        x1sq_s[:], x1b_s[:], AF.Square), wait=[("s_v", vb)],
                        inc="s_s")
                    t5 = sch.op("tensor", lambda e: e.matmul(
                        pb(5)[:, 0:512], ones[:], x1b_s[:], start=True, stop=True),
                        wait=[bankf[5], ("s_v", vb)], inc="s_t")
                    t6 = sch.op("tensor", lambda e: e.matmul(
                        pb(6)[:, 0:512], ones[:], x1sq_s[:], start=True, stop=True),
                        wait=[bankf[6], ("s_s", sq)], inc="s_t")
                    st["x1bs_fr"] = t6
                    va = sch.op("vector", lambda e, ng=ng: e.tensor_add(
                        accA[ng][:], accA[ng][:], pb(5)[:, 0:512]),
                        wait=[("s_t", t5)], inc="s_v")
                    bankf[5] = ("s_v", va)
                    sch.drain("vector")
                    vb2 = sch.op("vector", lambda e, ng=ng: e.tensor_add(
                        accB[ng][:], accB[ng][:], pb(6)[:, 0:512]),
                        wait=[("s_t", t6)], inc="s_v")
                    bankf[6] = ("s_v", vb2)
                    sch.drain("vector")
        for ng in range(2):
            coff = qq * 1024 + ng * 512
            sch.op("vector", lambda e, ng=ng, coff=coff: e.tensor_scalar_mul(
                m2_rep[:, coff:coff + 512], accA[ng][:], 1.0 / 2048.0))
            sch.drain("vector")
            sch.op("vector", lambda e, coff=coff: e.tensor_mul(
                m2scr[:], m2_rep[:, coff:coff + 512], m2_rep[:, coff:coff + 512]))
            sch.drain("vector")
            vv2 = sch.op("vector", lambda e, ng=ng: e.scalar_tensor_tensor(
                m2scr[:], accB[ng][:], 1.0 / 2048.0, m2scr[:],
                ALU.mult, ALU.subtract), inc="s_v")
            s2 = rsqrt_act(r2p, m2scr, LN_EPS, wait=[("s_v", vv2)])
            sch.op("vector", lambda e: e.tensor_mul(r3s[:], r2p[:], r2p[:]),
                   wait=[("s_s", s2)])
            sch.drain("vector")
            vv3 = sch.op("vector", lambda e: e.tensor_mul(
                r3s[:], r3s[:], m2scr[:]), inc="s_v")
            s3 = rsqrt_act(r3s, r3s, EPS_NORM, wait=[("s_v", vv3)])
            vv4 = sch.op("vector", lambda e, coff=coff: e.tensor_mul(
                rr_rep[:, coff:coff + 512], r2p[:], r3s[:]),
                wait=[("s_s", s3)], inc="s_v")
        st["accs_fr"] = sch.count("s_v")

    def max_pass(qq):
        if qq == 1:
            for oc in range(16):
                for ng in range(2):
                    coff = qq * 1024 + ng * 512
                    sch.op("vector", lambda e, oc=oc, ng=ng, coff=coff: e.tensor_sub(
                        xnt[:], x1r[oc][:, ng * 512:(ng + 1) * 512],
                        m2_rep[:, coff:coff + 512]))
                    sch.drain("vector")
                    sch.op("vector", lambda e, oc=oc, coff=coff: e.scalar_tensor_tensor(
                        xnu[:], xnt[:], gam16[:, oc:oc + 1],
                        rr_rep[:, coff:coff + 512], ALU.mult, ALU.mult))
                    sch.drain("vector")
                    sch.op("vector", lambda e: e.tensor_reduce(
                        smalls[:, SM["rmx"]:SM["rmx"] + 1], xnu[:], AXX, ALU.max, apply_absolute_value=True))
                    sch.drain("vector")
                    sch.op("vector", lambda e: e.tensor_max(
                        smalls[:, SM["runmax"]:SM["runmax"] + 1],
                        smalls[:, SM["runmax"]:SM["runmax"] + 1], smalls[:, SM["rmx"]:SM["rmx"] + 1]), inc="s_v")
                    sch.drain("vector")
            return
        for sl in range(16):
            oc0 = (sl // 2) * 2
            ng = sl % 2
            coff = ng * 512
            w = dma("scalar", mxstg[sl % 2][:].rearrange("p (a t) -> p a t", a=2),
                    x1q0_d[oc0 * 128:(oc0 + 2) * 128, ng * 512:(ng + 1) * 512]
                    .rearrange("(a p) t -> p a t", p=128),
                    wait=[("s_v", st.get(f"mx_fr{sl % 2}", 0))])
            for half in range(2):
                oc = oc0 + half
                sch.op("vector", lambda e, half=half, coff=coff, sl=sl: e.tensor_sub(
                    xnt[:], mxstg[sl % 2][:, half * 512:(half + 1) * 512],
                    m2_rep[:, coff:coff + 512]), wait=[w])
                sch.drain("vector")
                sch.op("vector", lambda e, oc=oc, coff=coff: e.scalar_tensor_tensor(
                    xnu[:], xnt[:], gam16[:, oc:oc + 1], rr_rep[:, coff:coff + 512],
                    ALU.mult, ALU.mult))
                sch.drain("vector")
                vm = sch.op("vector", lambda e: e.tensor_reduce(
                    smalls[:, SM["rmx"]:SM["rmx"] + 1], xnu[:], AXX, ALU.max, apply_absolute_value=True))
                st[f"mx_fr{sl % 2}"] = vm
                sch.drain("vector")
                sch.op("vector", lambda e: e.tensor_max(
                    smalls[:, SM["runmax"]:SM["runmax"] + 1],
                    smalls[:, SM["runmax"]:SM["runmax"] + 1], smalls[:, SM["rmx"]:SM["rmx"] + 1]), inc="s_v")
                sch.drain("vector")

    def collective_trigger():
        tcur = sch.op("tensor", lambda e: e.transpose(
            pb(7)[0:1, 0:128], smalls[:, SM["runmax"]:SM["runmax"] + 1], ident[:]),
            wait=[("s_v", sch.count("s_v")), bankf[7]], inc="s_t")
        sch.op("vector", lambda e: e.tensor_reduce(
            smalls[0:1, SM["lmax"]:SM["lmax"] + 1], pb(7)[0:1, 0:128], AXX,
            ALU.max), wait=[("s_t", tcur)])
        sch.drain("vector")
        sch.op("vector", lambda e: e.tensor_scalar_max(
            smalls[0:1, SM["lmax"]:SM["lmax"] + 1],
            smalls[0:1, SM["lmax"]:SM["lmax"] + 1], EPS_SCALE))
        sch.drain("vector")
        vv1 = sch.op("vector", lambda e: e.tensor_copy(
            ccs[0:1, 0:1], smalls[0:1, SM["lmax"]:SM["lmax"] + 1]), inc="s_v")
        bankf[7] = ("s_v", vv1)
        w = dma("sync", ccin_d, ccs[:], wait=[("s_v", vv1)])
        g1 = sch.op("gpsimd", lambda e: e.collective_compute(
            "AllReduce", ALU.max, replica_groups=[[0, 1, 2, 3, 4, 5, 6, 7]],
            ins=[ccin_d], outs=[ccout_d]), wait=[w], inc="s_g")
        g2 = sch.op("gpsimd", lambda e: e.dma_start(out=ccs[:], in_=ccout_d),
                    wait=[("s_g", g1)], inc="s_g2", inc_by=16)
        st["g2"] = g2

    def collective_scales():
        g2 = st["g2"]
        w2 = dma("sync", smalls[:, SM["Scol"]:SM["Scol"] + 1],
                 ccout_d[0:1, 0:1].partition_broadcast(128), wait=[("s_g2", g2)])
        sch.op("vector", lambda e: e.reciprocal(
            smalls[:, SM["rS127"]:SM["rS127"] + 1],
            smalls[:, SM["Scol"]:SM["Scol"] + 1]), wait=[w2])
        sch.drain("vector")
        sch.op("vector", lambda e: e.tensor_scalar_mul(
            smalls[:, SM["rS127"]:SM["rS127"] + 1],
            smalls[:, SM["rS127"]:SM["rS127"] + 1], 127.0))
        sch.drain("vector")
        sch.op("vector", lambda e: e.tensor_mul(
            smalls[:, SM["fscale"]:SM["fscale"] + 1],
            smalls[:, SM["Scol"]:SM["Scol"] + 1],
            smalls[:, SM["swcol"]:SM["swcol"] + 1]))
        sch.drain("vector")
        sch.op("vector", lambda e: e.tensor_scalar_mul(
            smalls[:, SM["fscale"]:SM["fscale"] + 1],
            smalls[:, SM["fscale"]:SM["fscale"] + 1], -1.0 / 127.0))
        sch.drain("vector")
        vv2 = sch.op("vector", lambda e: e.tensor_scalar_mul(
            rr_rep[:], rr_rep[:], smalls[:, SM["rS127"]:SM["rS127"] + 1]),
            inc="s_v")
        sch.drain("vector")
        if DEBUG:
            dma("scalar", dbg["sS"], ccs[:], wait=[("s_g2", g2)])
        st["scales_v"] = vv2

    def wq_load_phase():
        base = [("s_v", st["accs_fr"]), ("s_t", st["rq_free2"])]
        late = [("s_v", st["pre_coll_v"]), ("s_t", st["rq_free2"])]
        st["wq_loads"] = []
        for p in range(16):
            q = "scalar" if p % 2 == 0 else "gpsimd"
            wt = late if p in (4, 5) else base
            if q == "scalar":
                st["wq_loads"].append(dma("scalar", wqTt[p][:],
                                          wqT_d[p * 128:(p + 1) * 128, :],
                                          wait=wt))
            else:
                st["wq_loads"].append(gdma(wqTt[p][:],
                                           wqT_d[p * 128:(p + 1) * 128, :],
                                           wait=wt))

    def ffn():
        groups = [(1, 0), (1, 1), (0, 0), (0, 1)]

        def prefetch(gi):
            qqf, ng = groups[gi]
            if qqf != 0:
                return []
            rj = x1rA[:].rearrange("p (a t) -> p a t", a=16)
            w = dma("scalar",
                    rj[:, :, ng * 512:(ng + 1) * 512],
                    x1q0_d[:, ng * 512:(ng + 1) * 512]
                    .rearrange("(a p) t -> p a t", p=128),
                    wait=[("s_v", st.get(f"x1half_fr{ng}", 0))])
            return [w]

        def quant_chunk(gi, ic, x1w):
            qqf, ng = groups[gi]
            par = gi % 2
            xqg = xq2[par]
            coff = qqf * 1024 + ng * 512
            xsl = lambda oc, ng=ng: x1r[oc][:, ng * 512:(ng + 1) * 512]
            sch.op("vector", lambda e, ic=ic, coff=coff, xsl=xsl: e.tensor_sub(
                qnt[0][:], xsl(ic), m2_rep[:, coff:coff + 512]),
                wait=list(x1w) + [("s_t", st.get(f"xq_fr{par}", 0))])
            sch.drain("vector")
            sch.op("vector", lambda e, ic=ic, coff=coff: e.scalar_tensor_tensor(
                qnt[1][:], qnt[0][:], gam16[:, ic:ic + 1],
                rr_rep[:, coff:coff + 512], ALU.mult, ALU.mult))
            sch.drain("vector")
            vcl = sch.op("vector", lambda e: e.tensor_scalar(
                qnt[0][:], qnt[1][:], 127.0, -128.0, ALU.min, ALU.max),
                inc="s_v")
            sch.op("scalar", lambda e: e.copy(qni[:], qnt[0][:]),
                   wait=[("s_v", vcl)], inc="s_s")
            sch.drain("scalar")
            scp2 = sch.op("scalar", lambda e: e.copy(qnt[1][:], qni[:]),
                          inc="s_s")
            sch.op("vector", lambda e: e.tensor_sub(qnt[2][:], qnt[0][:],
                                                    qnt[1][:]),
                   wait=[("s_s", scp2)])
            sch.drain("vector")
            sch.op("vector", lambda e: e.scalar_tensor_tensor(
                qnt[0][:], qnt[2][:], 0.5, qnt[1][:], ALU.is_gt, ALU.add))
            sch.drain("vector")
            v = sch.op("vector", lambda e, ic=ic, xqg=xqg: e.scalar_tensor_tensor(
                xqg[ic][:], qnt[2][:], -0.5, qnt[0][:], ALU.is_lt,
                ALU.subtract), inc="s_v")
            sch.drain("vector")
            return v

        x1w = prefetch(0)
        for ic in range(16):
            quant_chunk(0, ic, x1w)
        vq = {0: sch.count("s_v")}
        x1w_next = prefetch(1)
        for gi, (qqf, ng) in enumerate(groups):
            par = gi % 2
            xqg = xq2[par]
            coff = qqf * 1024 + ng * 512
            xsl = lambda oc, ng=ng: x1r[oc][:, ng * 512:(ng + 1) * 512]
            for oc in range(16):
                bk = oc % 2
                for ic in range(16):
                    sch.op("tensor", lambda e, oc=oc, ic=ic, bk=bk, xqg=xqg:
                           e.matmul(pb(bk)[:, 0:512],
                                    wqTt[ic][:, oc * 128:(oc + 1) * 128],
                                    xqg[ic][:], start=(ic == 0), stop=(ic == 15)),
                           wait=(([bankf[bk], ("s_v", vq[gi])]
                                  + (st["wq_loads"] if gi == 0 and oc == 0
                                     else []))
                                 if ic == 0 else ()),
                           inc=("s_t" if ic == 15 else None))
                tcur = sch.count("s_t")
                vo = sch.op("vector", lambda e, oc=oc, bk=bk, xsl=xsl:
                            e.scalar_tensor_tensor(
                                ostg[bk][:], pb(bk)[:, 0:512],
                                smalls[:, SM["fscale"]:SM["fscale"] + 1],
                                xsl(oc), ALU.mult, ALU.add),
                            wait=[("s_t", tcur),
                                  st.get(f"ostg_fr{bk}", ("s_v", 0))], inc="s_v")
                bankf[bk] = ("s_v", vo)
                st[f"ostg_fr{bk}"] = dma(
                    "scalar", out_d[oc * 128:(oc + 1) * 128, coff:coff + 512],
                    ostg[bk][:], wait=[("s_v", vo)])
                if gi + 1 < 4:
                    quant_chunk(gi + 1, oc, x1w_next)
            st[f"xq_fr{par}"] = sch.count("s_t")
            st[f"x1half_fr{ng}"] = sch.count("s_v")
            if gi + 1 < 4:
                vq[gi + 1] = sch.count("s_v")
            if gi + 2 < 4:
                x1w_next = prefetch(gi + 2)

    def dbg_dump(qq):
        if not DEBUG or qq != 0:
            return
        v = sch.count("s_v")
        s = sch.count("s_s")
        dma("scalar", dbg["r1"], r1_rep[:], wait=[("s_s", s), ("s_v", v)])
        dma("scalar", dbg["k00"], kT[0][0][:], wait=[("s_v", v)])
        dma("scalar", dbg["v8"], vt[8][:], wait=[("s_v", v)])

    def dbg_dump_q(qq):
        if not DEBUG or qq != 0:
            return
        v = sch.count("s_v")
        dma("scalar", dbg["q0"], qT[0][:], wait=[("s_v", v)])
        dma("scalar", dbg["q1"], qT[1][:], wait=[("s_v", v)])

    def dbg_dump_attn(qq):
        if not DEBUG or qq != 0:
            return
        v = sch.count("s_v")
        dma("scalar", dbg["attn0"], att[0][:], wait=[("s_v", v)])
        dma("scalar", dbg["attn1"], att[1][:], wait=[("s_v", v)])

    def dbg_dump_wo(qq):
        if not DEBUG:
            return
        v = sch.count("s_v")
        if qq == 1:
            dma("scalar", dbg["x1c0"], x1r[0][:], wait=[("s_v", v)])
            dma("scalar", dbg["m2"], m2_rep[:], wait=[("s_v", v)])
            dma("scalar", dbg["rr"], rr_rep[:], wait=[("s_v", v)])

    st["gate"] = [("s_v", CONST_V)]
    for qq in (0, 1):
        kv_phase(qq)
        dbg_dump(qq)
        q_phase(qq)
        dbg_dump_q(qq)
        attn_phase(qq)
        dbg_dump_attn(qq)
        st["accs_fr"] = st.get("accs_fr", 0)
        wo_phase(qq)
        if qq == 0:
            st["gate"] = [("s_t", sch.count("s_t")),
                          ("s_v", sch.count("s_v")),
                          ("s_s", sch.count("s_s"))]
        max_pass(qq)
        dbg_dump_wo(qq)
        if qq == 0:
            st["mx0_v"] = sch.count("s_v")
    st["pre_coll_v"] = sch.count("s_v")
    st["rq_free2"] = sch.count("s_t")
    wq_load_phase()
    collective_trigger()
    collective_scales()
    ffn()

    if DEBUG:
        dma("scalar", dbg["smalls"], smalls[:], wait=[("s_v", sch.count("s_v"))])
        dma("scalar", dbg["wabs"], wabs[:], wait=[("s_v", sch.count("s_v"))])
    for sem_n, v in dma_barrier("scalar") + dma_barrier("sync"):
        sch.prog["gpsimd"].append(
            lambda e, sems, sem_n=sem_n, v=v: e.wait_ge(sems[sem_n], v))

    with nc.Block() as block:
        @block.sync
        def _(eng):
            for f in sch.prog["sync"]:
                f(eng, sems)

        @block.scalar
        def _(eng):
            for f in sch.prog["scalar"]:
                f(eng, sems)

        @block.vector
        def _(eng):
            for f in sch.prog["vector"]:
                f(eng, sems)

        @block.tensor
        def _(eng):
            for f in sch.prog["tensor"]:
                f(eng, sems)

        @block.gpsimd
        def _(eng):
            for f in sch.prog["gpsimd"]:
                f(eng, sems)

    return nc, dbg


def _host_prep(inputs):
    x = np.asarray(inputs["L0_attn_x"], np.float32)
    wq = np.ascontiguousarray(np.asarray(inputs["L0_attn_Wq"], np.float32))
    wk = np.ascontiguousarray(np.asarray(inputs["L0_attn_Wk"], np.float32))
    wv = np.ascontiguousarray(np.asarray(inputs["L0_attn_Wv"], np.float32))
    wo = np.ascontiguousarray(np.asarray(inputs["L0_attn_Wo"], np.float32))
    wt = np.ascontiguousarray(np.asarray(inputs["L0_ffn_weight"], np.float32).T)
    qsc = np.asarray(inputs["L0_attn_q_scale"], np.float32)
    ksc = np.asarray(inputs["L0_attn_k_scale"], np.float32)
    gam = np.asarray(inputs["L0_ffn_norm_weight"], np.float32)

    qs2 = np.ascontiguousarray(qsc.reshape(2, 128).T)
    ks2 = np.ascontiguousarray(ksc.reshape(2, 128).T)
    gam16 = np.ascontiguousarray(gam.reshape(16, 128).T)
    ones128 = np.ones((128, 128), dtype=BF)
    ident = np.eye(128, dtype=np.float32)

    half = 64
    theta = 1.0 / (10000.0 ** (np.arange(0, ROT_, 2, dtype=np.float32) / ROT_))
    in_maps = []
    for b in range(B):
        xb_T = np.ascontiguousarray(x[b].T)
        for hh in range(2):
            lo = hh * 2048 - 1024
            xT = np.zeros((E, NLOC), np.float32)
            s0 = max(0, lo)
            xT[:, s0 - lo:3072] = xb_T[:, s0:lo + 3072]
            pos = np.arange(lo, lo + NLOC, dtype=np.float32)
            ang = pos[None, :] * theta[:, None]
            cosT = np.zeros((128, NLOC), np.float32)
            sinT = np.zeros((128, NLOC), np.float32)
            cosT[0::2] = np.cos(ang)
            cosT[1::2] = np.cos(ang)
            sinT[0::2] = -np.sin(ang)
            sinT[1::2] = np.sin(ang)
            masks = np.zeros((48, 128, 512), dtype=BF)
            for qq in range(2):
                for og in range(2):
                    A_loc = 8 + qq * 8 + og * 4
                    for kji in range(12):
                        kjt = A_loc - 8 + kji
                        K = lo + kjt * 128 + np.arange(128)[:, None]
                        Q = lo + A_loc * 128 + np.arange(512)[None, :]
                        ok = (K >= 0) & (K <= Q) & (Q - K < WIN_)
                        masks[(qq * 2 + og) * 12 + kji] = ok.astype(BF)
            in_maps.append({
                "xT": xT, "wq": wq, "wk": wk, "wv": wv, "wo": wo, "wt": wt,
                "qs2": qs2, "ks2": ks2, "gam": gam16,
                "cosT": cosT, "sinT": sinT, "masks": masks,
                "ones128": ones128, "ident128": ident,
            })
    return in_maps


ROT_ = 128
WIN_ = 1024

_NC_CACHE = {}


def kernel(**inputs):
    key = "nc"
    if key not in _NC_CACHE:
        _NC_CACHE[key] = build_nc()
    nc, dbg = _NC_CACHE[key]
    in_maps = _host_prep(inputs)
    res = run_bass_kernel_spmd(nc, in_maps, core_ids=list(range(8)))
    out = np.zeros((B, S, E), np.float32)
    for b in range(B):
        for hh in range(2):
            core = b * 2 + hh
            out[b, hh * 2048:(hh + 1) * 2048, :] = res.results[core]["out"].T
    if DEBUG:
        kernel._last = res
    return out


# version 11
# speedup vs baseline: 1.0926x; 1.0163x over previous
import os
import numpy as np
import ml_dtypes
import concourse.bass as bass
import concourse.mybir as mybir
from concourse.bass_utils import run_bass_kernel_spmd

F32 = mybir.dt.float32
BF16 = mybir.dt.bfloat16
I32 = mybir.dt.int32
AF = mybir.ActivationFunctionType
ALU = mybir.AluOpType
BF = ml_dtypes.bfloat16
AXX = mybir.AxisListType.X

B, S, E = 4, 4096, 2048
NH, NKV, HD = 8, 4, 256
QSCALE = 256 ** (-0.5)
EPS_QK = 1e-6
LN_EPS = 1e-5
EPS_NORM = 1e-6
EPS_SCALE = 1e-5
NLOC = 3072
KB = 1024

DEBUG = int(os.environ.get("MK_DEBUG", "0"))


class TView:

    def __init__(self, t, off, width):
        self.t = t
        self.off = off
        self.width = width

    def __getitem__(self, key):
        if isinstance(key, tuple):
            r, c = key
            start = (c.start or 0) + self.off
            stop = (c.stop if c.stop is not None else self.width) + self.off
            return self.t[r, start:stop]
        return self.t[key, self.off:self.off + self.width]


class Sched:
    ENGS = ("sync", "scalar", "vector", "tensor", "gpsimd")

    def __init__(self):
        self.prog = {e: [] for e in self.ENGS}
        self.C = {}
        self.hw = {e: {} for e in self.ENGS}
        self.pend = {}

    def count(self, sem):
        return self.C.get(sem, 0)

    DEFSEM = {"vector": "s_v", "scalar": "s_s"}

    def op(self, eng, fn, wait=(), inc=None, inc_by=1):
        if inc is None and eng in self.DEFSEM:
            inc = self.DEFSEM[eng]
        waits = list(wait)
        if self.pend.pop(eng, None):
            dsem = self.DEFSEM[eng]
            waits.append((dsem, self.count(dsem)))
        wl = []
        for sem, val in waits:
            if val > self.hw[eng].get(sem, 0):
                wl.append((sem, val))
                self.hw[eng][sem] = val
        if inc is not None:
            self.C[inc] = self.count(inc) + inc_by

        def emit(e, sems, waits=wl, fn=fn, inc=inc, inc_by=inc_by):
            for sem, val in waits:
                e.wait_ge(sems[sem], val)
            instr = fn(e)
            if inc is not None:
                instr.then_inc(sems[inc], inc_by)

        self.prog[eng].append(emit)
        return self.count(inc) if inc is not None else None

    def drain(self, eng):
        self.pend[eng] = True


def build_nc():
    nc = bass.Bass("TRN2", target_bir_lowering=False)

    xT_d = nc.dram_tensor("xT", [E, NLOC], F32, kind="ExternalInput").ap()
    wq_d = nc.dram_tensor("wq", [E, 2048], F32, kind="ExternalInput").ap()
    wk_d = nc.dram_tensor("wk", [E, 1024], F32, kind="ExternalInput").ap()
    wv_d = nc.dram_tensor("wv", [E, 1024], F32, kind="ExternalInput").ap()
    wo_d = nc.dram_tensor("wo", [E, 2048], F32, kind="ExternalInput").ap()
    wt_d = nc.dram_tensor("wt", [E, 2048], F32, kind="ExternalInput").ap()
    qs2_d = nc.dram_tensor("qs2", [128, 2], F32, kind="ExternalInput").ap()
    ks2_d = nc.dram_tensor("ks2", [128, 2], F32, kind="ExternalInput").ap()
    gam_d = nc.dram_tensor("gam", [128, 16], F32, kind="ExternalInput").ap()
    cos_d = nc.dram_tensor("cosT", [128, NLOC], F32, kind="ExternalInput").ap()
    sin_d = nc.dram_tensor("sinT", [128, NLOC], F32, kind="ExternalInput").ap()
    masks_d = nc.dram_tensor("masks", [48, 128, 512], BF16, kind="ExternalInput").ap()
    ones_d = nc.dram_tensor("ones128", [128, 128], BF16, kind="ExternalInput").ap()
    id_d = nc.dram_tensor("ident128", [128, 128], F32, kind="ExternalInput").ap()
    out_d = nc.dram_tensor("out", [E, 2048], F32, kind="ExternalOutput").ap()
    x1q0_d = nc.dram_tensor("x1q0", [E, 1024], F32, kind="Internal").ap()
    wqT_d = nc.dram_tensor("wqTd", [E, 2048], BF16, kind="Internal").ap()
    ccin_d = nc.dram_tensor("ccin", [1, 16], F32, kind="Internal").ap()
    bc_d = nc.dram_tensor("bc_d", [1, 16], F32, kind="Internal").ap()
    ccout_d = nc.dram_tensor("ccout", [1, 16], F32, kind="Internal").ap()

    dbg = {}
    if DEBUG:
        def dbgt(name, shape, dt=F32):
            dbg[name] = nc.dram_tensor("dbg_" + name, shape, dt,
                                       kind="ExternalOutput").ap()
        dbgt("r1", [128, NLOC]); dbgt("k00", [128, 2048], BF16)
        dbgt("v8", [128, 1024], BF16)
        dbgt("q0", [128, 1024], BF16); dbgt("q1", [128, 1024], BF16)
        dbgt("pt", [128, 12 * 512], BF16)
        dbgt("attn0", [128, 1024], BF16); dbgt("attn1", [128, 1024], BF16)
        dbgt("x1c0", [128, 1024]); dbgt("m2", [128, 2048]); dbgt("rr", [128, 2048])
        dbgt("sS", [1, 16]); dbgt("xq0", [128, 512], BF16); dbgt("wq0", [128, 2048], BF16)
        dbgt("ss_k", [128, 512]); dbgt("F_k", [128, 512]); dbgt("r1sq_k", [128, 512])
        dbgt("maskb", [128, 12 * 512], BF16); dbgt("pta", [128, 512], BF16)
        dbgt("smalls", [128, 16]); dbgt("wabs", [128, 16])
        dbgt("wcol", [128, 1]); dbgt("wrowT", [1, 128])
        dbgt("sums", [128, 512]); dbgt("ttmp", [128, 512])
        dbgt("kstar", [128, 512])

    SBUF_BASE = 16512

    def at(name, shape, dt, off):
        return nc.alloc_sbuf_tensor_at(name, shape, dt, offset=SBUF_BASE + off)

    o = [0]

    def nxt(sz):
        off = (o[0] + 31) // 32 * 32
        o[0] = off + sz
        return off

    ones = at("ones", [128, 128], BF16, nxt(256))
    onesf = at("onesf", [128, 128], F32, nxt(512))
    ident = at("ident", [128, 128], F32, nxt(512))
    qs16 = at("qs16", [128, 2], F32, nxt(32))
    ks16 = at("ks16", [128, 2], F32, nxt(32))
    gam16 = at("gam16", [128, 16], F32, nxt(64))
    smalls = at("smalls", [128, 16], F32, nxt(64))
    SM = dict(runmax=0, lmax=1, Scol=2, rS127=3, swcol=4, halfsw=5,
              neghalfsw=6, fscale=7, swrow=8, eps5=10, eps6=11, rmx=12)
    ccs = at("ccs", [1, 16], F32, nxt(64))
    _mo_off = (o[0] + 31) // 32 * 32
    mrow_own = at("mrow_own", [1, 1024], BF16, nxt(2048))
    dbgw_s = at("dbgw_s", [1, 128], F32, _mo_off)
    r1col = at("r1col", [128, 16], F32, nxt(64))
    wabs = at("wabs", [128, 16], F32, nxt(64))
    wcol = at("wcol", [128, 1], F32, nxt(32))
    wrow = at("wrow", [1, 2048], BF16, nxt(4096))
    mrow_kv = at("mrow_kv", [1, 512], BF16, nxt(1024))
    r1_rep = at("r1_rep", [128, NLOC], F32, nxt(NLOC * 4))
    _o_rr = nxt(8192)
    rr_rep = at("rr_rep", [128, 2048], F32, _o_rr)
    _o_m2 = nxt(8192)
    m2_rep = at("m2_rep", [128, 2048], F32, _o_m2)
    terW = [at("terW0", [128, 2048], F32, _o_m2), None]
    terA = at("terA", [128, 2048], BF16, _o_rr)
    terB = at("terB", [128, 2048], BF16, _o_rr + 4 * KB)
    PERM_END = (o[0] + 31) // 32 * 32
    R_K = PERM_END
    R_V = R_K + 32 * KB
    R_Q = R_V + 32 * KB
    R_AT = R_Q + 32 * KB
    R_FX = R_AT + 32 * KB
    FX_END = R_FX + 43 * KB
    assert FX_END + 16512 <= 229376, (PERM_END, FX_END)

    kT = [[at(f"kT{j}_{d}", [128, 2048], BF16, R_K + (j * 2 + d) * 4 * KB)
           for d in range(2)] for j in range(NKV)]
    vt = [at(f"v{t}", [128, 1024], BF16, R_V + t * 2 * KB) for t in range(16)]
    qT = [at(f"qT{c}", [128, 1024], BF16, R_Q + c * 2 * KB) for c in range(16)]
    att = [at(f"att{c}", [128, 1024], BF16, R_AT + c * 2 * KB) for c in range(16)]

    WkbA = at("WkbA", [128, 16 * 1024], BF16, R_Q)
    Wkb = [TView(WkbA, c * 1024, 1024) for c in range(16)]
    WvbA = at("WvbA", [128, 16 * 1024], BF16, R_AT)
    Wvb = [TView(WvbA, c * 1024, 1024) for c in range(16)]
    xbA = [at(f"xbA{g}", [128, 16 * 512], BF16, R_FX + g * 16 * KB)
           for g in range(2)]
    xb = [[TView(xbA[g], c * 512, 512) for c in range(16)] for g in range(2)]
    xstg = [at(f"xstg{i}", [128, 1024], F32, R_FX + 16 * KB + i * 4 * KB)
            for i in range(2)]
    terW[1] = at("terW1", [128, 2048], F32, R_FX + 16 * KB)
    kvs = R_FX + 24 * KB
    kf = [at(f"kf{i}", [128, 512], F32, kvs + i * 2 * KB) for i in range(2)]
    krt = at("krt", [128, 512], F32, kvs + 4 * KB)
    krs = at("krs", [128, 512], F32, kvs + 6 * KB)
    Fsc = at("Fsc", [128, 512], F32, kvs + 8 * KB)
    r1sq = at("r1sq", [128, 512], F32, kvs + 10 * KB)
    cosk = at("cosk", [128, 512], F32, kvs + 12 * KB)
    sink = at("sink", [128, 512], F32, kvs + 14 * KB)
    ksqb = at("ksqb", [128, 512], BF16, kvs + 16 * KB)
    mscr = at("mscr", [128, 512], F32, kvs + 4 * KB)
    vscr = at("vscr", [128, 512], F32, kvs + 6 * KB)

    wqstg = at("wqstg8", [128, 8 * 256], F32, R_FX + 32 * KB)
    WqbA = [at(f"WqbA{i}", [128, 16 * 256], BF16, R_AT + i * 8 * KB)
            for i in range(2)]
    qf = [at(f"qf{i}", [128, 512], F32, R_AT + 16 * KB + i * 2 * KB)
          for i in range(2)]
    qFsc = at("qFsc", [128, 512], F32, R_AT + 20 * KB)
    qr1sq = at("qr1sq", [128, 512], F32, R_AT + 22 * KB)
    qsqb = at("qsqb", [128, 512], BF16, R_AT + 24 * KB)
    cosq = at("cosq", [128, 512], F32, R_AT + 25 * KB)
    sinq = at("sinq", [128, 512], F32, R_AT + 27 * KB)
    qrt = at("qrt", [128, 512], F32, R_AT + 29 * KB)

    maskb = at("maskb", [128, 12 * 512], BF16, R_FX)
    pT = [at(f"pT{i}", [128, 12 * 512], BF16, R_FX + 12 * KB + i * 12 * KB)
          for i in range(2)]
    pTa = [at(f"pTa{i}", [128, 512], BF16, R_FX + 36 * KB + i * KB)
           for i in range(2)]
    ttmp = at("ttmp", [128, 512], F32, R_FX + 38 * KB)
    rec = at("rec", [128, 512], F32, R_FX + 40 * KB)

    wostg = at("wostg", [128, 16 * 256], F32, R_FX)
    Wob = [at(f"Wob{i}", [128, 16 * 256], BF16, R_FX + 16 * KB + i * 8 * KB)
           for i in range(2)]
    xck = [at(f"xck{i}", [128, 512], F32, R_FX + 32 * KB + i * 2 * KB)
           for i in range(4)]
    accA = [at(f"accA{n}", [128, 512], F32, R_Q + n * 2 * KB) for n in range(2)]
    accB = [at(f"accB{n}", [128, 512], F32, R_Q + 4 * KB + n * 2 * KB)
            for n in range(2)]
    x1b_s = at("x1b_s", [128, 512], BF16, R_Q + 8 * KB)
    x1sq_s = at("x1sq_s", [128, 512], BF16, R_Q + 9 * KB)
    m2scr = at("m2scr", [128, 512], F32, R_Q + 10 * KB)
    r2p = at("r2p", [128, 512], F32, R_Q + 12 * KB)
    r3s = at("r3s", [128, 512], F32, R_Q + 14 * KB)
    xnt = at("xnt", [128, 1024], F32, R_Q + 16 * KB)
    xnu = at("xnu", [128, 1024], F32, R_Q + 20 * KB)
    x1stg = [at(f"x1stg{i}", [128, 512], F32, R_Q + 20 * KB + i * 2 * KB)
             for i in range(2)]
    mxstg = [at(f"mxstg{i}", [128, 1024], F32, R_Q + 24 * KB + i * 4 * KB)
             for i in range(2)]
    rmx = None
    x1rA = at("x1rA", [128, 16 * 1024], F32, R_K)
    x1r = [TView(x1rA, c * 1024, 1024) for c in range(16)]

    wqTt = [at(f"wqT{c}", [128, 2048], BF16, R_Q + c * 4 * KB) for c in range(16)]
    wstg = [at(f"wstg{i}", [128, 2048], F32, R_FX + i * 8 * KB) for i in range(2)]
    xqA_t = at("xqA", [128, 16 * 512], BF16, R_FX + 16 * KB)
    xqB_t = at("xqB", [128, 16 * 512], BF16, R_FX)
    xq = [TView(xqA_t, c * 512, 512) for c in range(16)]
    xq2 = [[TView(xqA_t, c * 512, 512) for c in range(16)],
           [TView(xqB_t, c * 512, 512) for c in range(16)]]
    ostg = [at(f"ostg{i}", [128, 512], F32, R_FX + 32 * KB + i * 2 * KB)
            for i in range(2)]
    qnt = [at(f"qnt{i}", [128, 512], F32, R_FX + 36 * KB + i * 2 * KB)
           for i in range(3)]
    qni = at("qni", [128, 512], I32, R_FX + 40 * KB)

    ps_guards = []
    g01 = nc.psum_tensor("ps01", [128, 1024], F32)
    P01t = g01.__enter__()
    ps_guards.append(g01)
    pbt = [TView(P01t, 0, 512), TView(P01t, 512, 512)]
    for i in range(2, 8):
        g = nc.psum_tensor(f"ps{i}", [128, 512], F32)
        t = g.__enter__()
        ps_guards.append(g)
        pbt.append(t)

    def pb(i):
        return pbt[i]

    sems = {}
    sem_guards = {}
    for n in (["s_t", "s_s", "s_v", "s_g", "s_g2"]
              + [f"sx{i}" for i in range(8)] + [f"sy{i}" for i in range(8)]
              + [f"sg{i}" for i in range(8)]):
        g = nc.semaphore(n)
        sems[n] = g.__enter__()
        sem_guards[n] = g

    sch = Sched()
    st = {}
    bankf = {i: ("s_s", 0) for i in range(8)}

    qidx = {"sync": 0, "scalar": 0, "gpsimd": 0}

    def dma(queue, out, in_, wait=()):
        pref = "sx" if queue == "sync" else "sy"
        i = qidx[queue]
        qidx[queue] = i + 1
        lane = f"{pref}{i % 8}"
        prev = sch.count(lane)
        wl = list(wait) + ([(lane, prev)] if prev else [])
        cnt = sch.op(queue, lambda e, out=out, in_=in_:
                     e.dma_start(out=out, in_=in_), wait=wl, inc=lane, inc_by=16)
        return (lane, cnt)

    def dma_barrier(queue):
        pref = {"sync": "sx", "scalar": "sy", "gpsimd": "sg"}[queue]
        return [(f"{pref}{i}", sch.count(f"{pref}{i}"))
                for i in range(8) if sch.count(f"{pref}{i}") > 0]

    def gdma(out, in_, wait=()):
        i = qidx["gpsimd"]
        qidx["gpsimd"] = i + 1
        lane = f"sg{i % 8}"
        prev = sch.count(lane)
        wl = list(wait) + ([(lane, prev)] if prev else [])
        cnt = sch.op("gpsimd", lambda e, out=out, in_=in_:
                     e.dma_start(out=out, in_=in_), wait=wl, inc=lane, inc_by=16)
        return (lane, cnt)

    dma("sync", ones[:], ones_d)
    dma("sync", ident[:], id_d)
    dma("sync", qs16[:], qs2_d)
    dma("sync", ks16[:], ks2_d)
    w0 = dma("sync", gam16[:], gam_d)
    CONSTW = dma_barrier("sync")
    sch.op("vector", lambda e: e.tensor_scalar_mul(qs16[:], qs16[:], QSCALE),
           wait=CONSTW)
    sch.op("vector", lambda e: e.memset(onesf[:], 1.0))
    sch.op("vector", lambda e: e.memset(smalls[:, 0:1], 0.0))
    sch.op("vector", lambda e: e.memset(smalls[:, SM["eps5"]:SM["eps5"] + 1], LN_EPS))
    sch.op("vector", lambda e: e.memset(smalls[:, SM["eps6"]:SM["eps6"] + 1], EPS_QK))
    sch.op("vector", lambda e: e.memset(ccs[:], 0.0), inc="s_v")
    CONST_V = sch.count("s_v")

    for p in range(16):
        wdone = st.get("wmean_used", 0)
        w = dma("scalar", wstg[p % 2][:], wt_d[p * 128:(p + 1) * 128, :],
                wait=([("s_v", wdone)] if p >= 2 else ()))
        st["wmean_used"] = sch.op(
            "vector", lambda e, p=p: e.tensor_reduce(
                wabs[:, p:p + 1], wstg[p % 2][:], AXX, ALU.add,
                apply_absolute_value=True),
            wait=[w], inc="s_v")
    sch.drain("vector")
    sch.op("vector", lambda e: e.tensor_reduce(wcol[:], wabs[:], AXX, ALU.add),
           inc="s_v")
    t1 = sch.op("tensor", lambda e: e.transpose(pb(7)[0:1, 0:128], wcol[:],
                                                ident[:]),
                wait=[("s_v", sch.count("s_v"))] + CONSTW, inc="s_t")
    if DEBUG:
        sdw = sch.op("scalar", lambda e: e.copy(wabs[0:1, 0:16] if False else dbgw_s[:], pb(7)[0:1, 0:128]),
                     wait=[("s_t", t1)], inc="s_s")
        dma("scalar", dbg["wrowT"], dbgw_s[:], wait=[("s_s", sdw)])
        dma("scalar", dbg["wcol"], wcol[:], wait=[("s_v", sch.count("s_v"))])
    sch.op("vector", lambda e: e.tensor_reduce(
        smalls[0:1, SM["swrow"]:SM["swrow"] + 1], pb(7)[0:1, 0:128], AXX,
        ALU.add), wait=[("s_t", t1)])
    sch.drain("vector")
    vv = sch.op("vector", lambda e: e.tensor_scalar(
        smalls[0:1, SM["swrow"]:SM["swrow"] + 1],
        smalls[0:1, SM["swrow"]:SM["swrow"] + 1],
        1.0 / (2048.0 * 2048.0), EPS_SCALE, ALU.mult, ALU.max), inc="s_v")
    wv_ = dma("sync", bc_d[0:1, 0:1],
              smalls[0:1, SM["swrow"]:SM["swrow"] + 1], wait=[("s_v", vv)])
    wv_ = dma("sync", smalls[:, SM["swcol"]:SM["swcol"] + 1],
              bc_d[0:1, 0:1].partition_broadcast(128), wait=[wv_])
    sch.op("vector", lambda e: e.tensor_scalar_mul(
        smalls[:, SM["halfsw"]:SM["halfsw"] + 1],
        smalls[:, SM["swcol"]:SM["swcol"] + 1], 0.5), wait=[wv_])
    sch.drain("vector")
    sch.op("vector", lambda e: e.tensor_scalar_mul(
        smalls[:, SM["neghalfsw"]:SM["neghalfsw"] + 1],
        smalls[:, SM["halfsw"]:SM["halfsw"] + 1], -1.0), inc="s_v")
    bankf[7] = ("s_v", sch.count("s_v"))
    st0_wmean_v = sch.count("s_v")

    def load_panels_cast(w_dram, destA, ncols, extra_wait=(), per_g=None):
        ws = []
        for g in range(4):
            wt = list(extra_wait) if g == 0 else []
            if per_g:
                wt += per_g[g]
            ws.append(gdma(
                destA[:, g * 4 * ncols:(g + 1) * 4 * ncols]
                .rearrange("p (a t) -> p a t", a=4),
                w_dram[g * 512:(g + 1) * 512, 0:ncols]
                .rearrange("(a p) t -> p a t", p=128),
                wait=wt))
        return ws

    def colsum_row(chunks, ncols, row_off, wwait):
        for ng in range(0, ncols, 512):
            nn = min(512, ncols - ng)
            for c in range(16):
                sch.op("tensor", lambda e, c=c, ng=ng, nn=nn, chunks=chunks:
                       e.matmul(pb(4)[0:1, 0:nn], ones[:, 0:1],
                                chunks[c][:, ng:ng + nn],
                                start=(c == 0), stop=(c == 15)),
                       wait=(list(wwait) + [bankf[4]] if c == 0 else ()),
                       inc=("s_t" if c == 15 else None))
            tcur = sch.count("s_t")
            scur = sch.op("scalar", lambda e, ng=ng, nn=nn, row_off=row_off:
                          e.copy(wrow[0:1, row_off + ng:row_off + ng + nn],
                                 pb(4)[0:1, 0:nn]),
                          wait=[("s_t", tcur)], inc="s_s")
            bankf[4] = ("s_s", scur)

    def rsqrt_act(dst, src, bias, wait=(), pre_scale=1.0):
        dap = dst[:] if hasattr(dst, "ap") else dst
        sap = src[:] if hasattr(src, "ap") else src
        bcol = SM["eps5"] if bias == LN_EPS else SM["eps6"]
        nparts = dap.shape[0]
        bap = smalls[0:nparts, bcol:bcol + 1]
        sch.op("scalar", lambda e: e.activation(dap, sap, AF.Ln,
                                                bias=bap, scale=pre_scale),
               wait=wait)
        sch.drain("scalar")
        return sch.op("scalar", lambda e: e.activation(dap, dap, AF.Exp,
                                                       scale=-0.5), inc="s_s")

    tern_ctr = [0]

    def tern_step():
        p = tern_ctr[0]
        if p >= 16:
            return
        tern_ctr[0] = p + 1
        buf = p % 2
        w = dma("scalar", terW[buf][:], wt_d[p * 128:(p + 1) * 128, :],
                wait=[st.get(f"terW_fr{buf}", ("s_v", 0))])
        sch.op("vector", lambda e: e.tensor_single_scalar(
            terA[:], terW[buf][:],
            smalls[:, SM["halfsw"]:SM["halfsw"] + 1], ALU.is_gt),
            wait=[w, ("s_v", st0_wmean_v), st.get("terA_fr", ("s_v", 0))])
        sch.drain("vector")
        sch.op("vector", lambda e: e.tensor_single_scalar(
            terB[:], terW[buf][:],
            smalls[:, SM["neghalfsw"]:SM["neghalfsw"] + 1], ALU.is_lt))
        sch.drain("vector")
        vs = sch.op("vector", lambda e: e.tensor_sub(
            terA[:], terA[:], terB[:]), inc="s_v")
        st[f"terW_fr{buf}"] = ("s_v", vs)
        st["terA_fr"] = dma("sync", wqT_d[p * 128:(p + 1) * 128, :],
                            terA[:], wait=[("s_v", vs)])

    def kv_phase(qq):
        mxg = [("s_v", st.get("mx0_v", 0))]
        first = 0 if qq == 0 else 16
        ngrp = 4 if qq == 0 else 2

        def load_xgroup(gi):
            t0 = first + gi * 4
            c0 = t0 * 128
            xgate = (st.get("gate", []) if qq == 0
                     else [("s_t", st.get("rq_free", 0))])
            ws = []
            for g4 in range(4):
                ws.append(gdma(
                    xbA[0][:, g4 * 4 * 512:(g4 + 1) * 4 * 512]
                    .rearrange("p (a t) -> p a t", a=4),
                    xT_d[g4 * 512:(g4 + 1) * 512, c0:c0 + 512]
                    .rearrange("(a p) t -> p a t", p=128),
                    wait=(([("s_t", st.get("xb0_fr", 0)),
                            ("s_v", st0_wmean_v)] + xgate)
                          if g4 == 0 else ())))
            return ws

        xws0 = load_xgroup(0)
        wWk = load_panels_cast(wk_d, WkbA, 1024,
                               extra_wait=st.get("gate", []),
                               per_g=[[], [], mxg, mxg])
        wWv = load_panels_cast(wv_d, WvbA, 1024,
                               extra_wait=st.get("gate", []))

        for gi in range(ngrp):
            t0 = first + gi * 4
            c0 = t0 * 128
            xws = xws0 if gi == 0 else load_xgroup(gi)
            for c in range(16):
                scur = sch.op("scalar", lambda e, c=c:
                              e.activation(ksqb[:], xb[0][c][:], AF.Square),
                              wait=[xws[c // 4],
                                    ("s_t", st.get("ksqb_fr", 0))], inc="s_s")
                sch.op("tensor", lambda e, c=c: e.matmul(
                    pb(5)[:, 0:512], ones[:], xb[0][c][:],
                    start=(c == 0), stop=(c == 15)),
                    wait=([bankf[5]] if c == 0 else []) + [xws[c // 4]],
                    inc=("s_t" if c == 15 else None))
                st["ksqb_fr"] = sch.op("tensor", lambda e, c=c: e.matmul(
                    pb(6)[:, 0:512], ones[:], ksqb[:],
                    start=(c == 0), stop=(c == 15)),
                    wait=([bankf[6], ("s_s", scur)] if c == 0
                          else [("s_s", scur)]),
                    inc="s_t")
            tstats = sch.count("s_t")
            st["ksq_mm"] = tstats
            if gi == 0:
                colsum_row(Wkb, 1024, 0, wWk)
            sch.op("vector", lambda e: e.tensor_scalar_mul(
                mscr[:], pb(5)[:, 0:512], 1.0 / 2048.0), wait=[("s_t", tstats)])
            sch.drain("vector")
            sch.op("vector", lambda e: e.tensor_mul(vscr[:], mscr[:], mscr[:]))
            sch.drain("vector")
            v1 = sch.op("vector", lambda e: e.scalar_tensor_tensor(
                vscr[:], pb(6)[:, 0:512], 1.0 / 2048.0, vscr[:],
                ALU.mult, ALU.subtract), inc="s_v")
            bankf[5] = ("s_v", v1)
            bankf[6] = ("s_v", v1)
            s1 = rsqrt_act(r1_rep[:, c0:c0 + 512], vscr, LN_EPS,
                           wait=[("s_v", v1)])
            v2 = sch.op("vector", lambda e: e.tensor_scalar_mul(
                mrow_kv[0:1, 0:512], mscr[0:1, 0:512], -1.0), inc="s_v")
            if qq == 1 or gi >= 2:
                og = gi if qq == 1 else gi - 2
                sch.op("vector", lambda e, og=og: e.tensor_scalar_mul(
                    mrow_own[0:1, og * 512:(og + 1) * 512],
                    mscr[0:1, 0:512], -1.0), inc="s_v")
            for tt_ in range(4):
                slot = (t0 + tt_) % 16
                tcur = sch.op("tensor", lambda e, tt_=tt_, c0=c0:
                              e.transpose(pb(7)[0:128, 0:1],
                                          r1_rep[0:1, c0 + tt_ * 128:c0 + (tt_ + 1) * 128],
                                          ident[0:1, 0:1]),
                              wait=[("s_s", s1), bankf[7]], inc="s_t")
                scur = sch.op("scalar", lambda e, slot=slot:
                              e.copy(r1col[:, slot:slot + 1], pb(7)[0:128, 0:1]),
                              wait=[("s_t", tcur)], inc="s_s")
                bankf[7] = ("s_s", scur)

            cos_w1 = dma("sync", cosk[:], cos_d[:, c0:c0 + 512],
                         wait=[("s_v", st.get("cosk_fr", 0))] + st.get("gate", []))
            cos_w2 = dma("sync", sink[:], sin_d[:, c0:c0 + 512],
                         wait=st.get("gate", []))
            r1sq_v = sch.op("vector", lambda e, c0=c0: e.tensor_mul(
                r1sq[:], r1_rep[:, c0:c0 + 512], r1_rep[:, c0:c0 + 512]),
                wait=[("s_s", s1), ("s_v", st.get("r1sq_fr", 0))], inc="s_v")
            for j in range(NKV):
                for par in range(2):
                    kc = 2 * j + par
                    bk = kc % 2
                    for ic in range(16):
                        sch.op("tensor", lambda e, ic=ic, kc=kc, bk=bk:
                               e.matmul(pb(bk)[:, 0:512],
                                        Wkb[ic][:, kc * 128:(kc + 1) * 128],
                                        xb[0][ic][:], start=(ic == 0), stop=False),
                               wait=([bankf[bk], ("s_v", v2)] if ic == 0 else ()))
                    tcur = sch.op("tensor", lambda e, kc=kc, bk=bk:
                                  e.matmul(pb(bk)[:, 0:512],
                                           wrow[0:1, kc * 128:(kc + 1) * 128],
                                           mrow_kv[0:1, 0:512],
                                           start=False, stop=True),
                                  inc="s_t")
                    scur = sch.op("scalar", lambda e, par=par, bk=bk:
                                  e.copy(kf[par][:], pb(bk)[:, 0:512]),
                                  wait=[("s_t", tcur),
                                        ("s_v", st.get("kf_fr", 0))], inc="s_s")
                    sch.drain("scalar")
                    scur = sch.op("scalar", lambda e, bk=bk:
                                  e.activation(ksqb[:], pb(bk)[:, 0:512],
                                               AF.Square),
                                  wait=[("s_t", st.get("ksq_mm", 0))], inc="s_s")
                    bankf[bk] = ("s_s", scur)
                    st["ksq_mm"] = sch.op("tensor", lambda e, par=par:
                                          e.matmul(pb(4)[:, 0:512], ones[:], ksqb[:],
                                                   start=(par == 0), stop=(par == 1)),
                                          wait=[("s_s", scur)] + ([bankf[4]] if par == 0 else []),
                                          inc="s_t")
                tsum = sch.count("s_t")
                dbg_this = DEBUG and qq == 0 and gi == 2 and j == 0
                if dbg_this:
                    dma("scalar", dbg["kstar"], kf[0][:], wait=[("s_s", sch.count("s_s"))])
                vF = sch.op("vector", lambda e: e.tensor_mul(
                    Fsc[:], pb(4)[:, 0:512], r1sq[:]),
                    wait=[("s_t", tsum), ("s_v", r1sq_v)], inc="s_v")
                if dbg_this:
                    sdbg = sch.op("scalar", lambda e: e.copy(vscr[:], pb(4)[:, 0:512]),
                                  wait=[("s_t", tsum), ("s_v", vF)], inc="s_s")
                    bankf[4] = ("s_s", sdbg)
                    dma("scalar", dbg["ss_k"], vscr[:], wait=[("s_s", sdbg)])
                bankf[4] = ("s_v", vF)
                sF = rsqrt_act(Fsc, Fsc, EPS_QK, wait=[("s_v", vF)],
                               pre_scale=1.0 / HD)
                vF = sch.op("vector", lambda e, c0=c0: e.tensor_mul(
                    Fsc[:], Fsc[:], r1_rep[:, c0:c0 + 512]),
                    wait=[("s_s", sF)], inc="s_v")
                sch.drain("vector")
                if dbg_this:
                    dma("scalar", dbg["F_k"], Fsc[:], wait=[("s_v", vF)])
                    dma("scalar", dbg["r1sq_k"], r1sq[:], wait=[("s_v", vF)])
                slotc = (t0 % 16) * 128
                vpl = sch.op("vector", lambda e, j=j, slotc=slotc:
                             e.scalar_tensor_tensor(
                                 kT[j][1][:, slotc:slotc + 512], kf[1][:],
                                 ks16[:, 1:2], Fsc[:], ALU.mult, ALU.mult),
                             wait=[("s_v", vF)], inc="s_v")
                sch.op("vector", lambda e: e.scalar_tensor_tensor(
                    krt[:], kf[0][:], ks16[:, 0:1], Fsc[:], ALU.mult, ALU.mult))
                sch.drain("vector")
                shmask = []
                for i2 in range(16):
                    shmask += [2 * i2 + 1, 2 * i2]
                sch.op("vector", lambda e: e.stream_shuffle(krs[:], krt[:], shmask))
                sch.drain("vector")
                sch.op("vector", lambda e: e.tensor_mul(krt[:], krt[:], cosk[:]),
                       wait=[cos_w1, cos_w2])
                sch.drain("vector")
                sch.op("vector", lambda e: e.tensor_mul(krs[:], krs[:], sink[:]))
                sch.drain("vector")
                vrot = sch.op("vector", lambda e, j=j, slotc=slotc:
                              e.tensor_add(kT[j][0][:, slotc:slotc + 512],
                                           krt[:], krs[:]), inc="s_v")
                st["kf_fr"] = vrot
                if qq == 0:
                    tern_step()
            st["cosk_fr"] = sch.count("s_v")
            st["r1sq_fr"] = sch.count("s_v")
            if gi == 0:
                colsum_row(Wvb, 1024, 1024, wWv)
                st["wsumv"] = bankf[4]

            for tt_ in range(4):
                slot = (t0 + tt_) % 16
                for ng in range(2):
                    bk = ng % 2
                    for ic in range(16):
                        sch.op("tensor", lambda e, ic=ic, tt_=tt_, ng=ng, bk=bk:
                               e.matmul(pb(bk)[:, 0:512],
                                        xb[0][ic][:, tt_ * 128:(tt_ + 1) * 128],
                                        Wvb[ic][:, ng * 512:(ng + 1) * 512],
                                        start=(ic == 0), stop=False),
                               wait=([bankf[bk]] if ic == 0 else ()))
                    tcur = sch.op("tensor", lambda e, tt_=tt_, ng=ng, bk=bk:
                                  e.matmul(pb(bk)[:, 0:512],
                                           mrow_kv[0:1, tt_ * 128:(tt_ + 1) * 128],
                                           wrow[0:1, 1024 + ng * 512:1024 + (ng + 1) * 512],
                                           start=False, stop=True),
                                  wait=[st.get("wsumv", ("s_s", 0))], inc="s_t")
                    vcur = sch.op("vector", lambda e, slot=slot, ng=ng, bk=bk:
                                  e.tensor_scalar_mul(
                                      vt[slot][:, ng * 512:(ng + 1) * 512],
                                      pb(bk)[:, 0:512], r1col[:, slot:slot + 1]),
                                  wait=[("s_t", tcur)], inc="s_v")
                    bankf[bk] = ("s_v", vcur)
            st["xb0_fr"] = sch.count("s_t")
        st["kv_done_t"] = sch.count("s_t")
        st["kv_done_v"] = sch.count("s_v")

    def q_phase(qq):
        wqw0 = gdma(WqbA[0][:].rearrange("p (a t) -> p a t", a=16),
                    wq_d[:, 0:256].rearrange("(a p) t -> p a t", p=128),
                    wait=[("s_t", st.get("wqb_fr0", 0)),
                          ("s_t", st.get("kv_done_t", 0)),
                          ("s_v", st.get("kv_done_v", 0))])
        vxb = [[], []]
        for og in range(2):
            c0 = 1024 + qq * 1024 + og * 512
            for g4 in range(4):
                vxb[og].append(gdma(
                    xbA[og][:, g4 * 4 * 512:(g4 + 1) * 4 * 512]
                    .rearrange("p (a t) -> p a t", a=4),
                    xT_d[g4 * 512:(g4 + 1) * 512, c0:c0 + 512]
                    .rearrange("(a p) t -> p a t", p=128),
                    wait=([("s_t", st.get("xb_own_fr", 0)),
                           ("s_t", st.get("kv_done_t", 0)),
                           ("s_v", st.get("kv_done_v", 0))] if g4 == 0 and og == 0
                          else ())))

        shmask = []
        for i2 in range(16):
            shmask += [2 * i2 + 1, 2 * i2]
        for blk in range(8):
            h = blk
            wqw = wqw0 if blk == 0 else gdma(
                WqbA[blk % 2][:].rearrange("p (a t) -> p a t", a=16),
                wq_d[:, h * 256:(h + 1) * 256]
                .rearrange("(a p) t -> p a t", p=128),
                wait=[("s_t", st.get(f"wqb_fr{blk % 2}", 0))])
            vwq = wqw
            for c in range(16):
                sch.op("tensor", lambda e, c=c, blk=blk:
                       e.matmul(pb(4)[0:1, 0:256], ones[:, 0:1],
                                WqbA[blk % 2][:, c * 256:(c + 1) * 256],
                                start=(c == 0), stop=(c == 15)),
                       wait=([vwq, bankf[4]] if c == 0 else ()),
                       inc=("s_t" if c == 15 else None))
            tcur = sch.count("s_t")
            swsum = sch.op("scalar", lambda e: e.copy(wrow[0:1, 0:256],
                                                      pb(4)[0:1, 0:256]),
                           wait=[("s_t", tcur)], inc="s_s")
            bankf[4] = ("s_s", swsum)

            for og in range(2):
                c0_loc = 1024 + qq * 1024 + og * 512
                cw1 = dma("sync", cosq[:], cos_d[:, c0_loc:c0_loc + 512],
                          wait=[("s_v", st.get("cosq_fr", 0)),
                                ("s_t", st.get("kv_done_t", 0))])
                cw2 = dma("sync", sinq[:], sin_d[:, c0_loc:c0_loc + 512],
                          wait=[("s_t", st.get("kv_done_t", 0))])
                vr1 = sch.op("vector", lambda e, c0_loc=c0_loc: e.tensor_mul(
                    qr1sq[:], r1_rep[:, c0_loc:c0_loc + 512],
                    r1_rep[:, c0_loc:c0_loc + 512]),
                    wait=[("s_v", st.get("qr1sq_fr", 0))], inc="s_v")
                for par in range(2):
                    bk = par
                    for ic in range(16):
                        sch.op("tensor", lambda e, ic=ic, par=par, og=og, bk=bk, blk=blk:
                               e.matmul(pb(bk)[:, 0:512],
                                        WqbA[blk % 2][:, ic * 256 + par * 128:
                                                      ic * 256 + (par + 1) * 128],
                                        xb[og][ic][:], start=(ic == 0), stop=False),
                               wait=(([bankf[bk], ("s_s", swsum)] + vxb[og])
                                     if ic == 0 else ()))
                    tcur = sch.op("tensor", lambda e, par=par, og=og, bk=bk:
                                  e.matmul(pb(bk)[:, 0:512],
                                           wrow[0:1, par * 128:(par + 1) * 128],
                                           mrow_own[0:1, og * 512:(og + 1) * 512],
                                           start=False, stop=True), inc="s_t")
                    st[f"wqb_fr{blk % 2}"] = tcur
                    scur = sch.op("scalar", lambda e, par=par, bk=bk:
                                  e.copy(qf[par][:], pb(bk)[:, 0:512]),
                                  wait=[("s_t", tcur),
                                        ("s_v", st.get("qf_fr", 0))], inc="s_s")
                    sch.drain("scalar")
                    scur = sch.op("scalar", lambda e, bk=bk:
                                  e.activation(qsqb[:], pb(bk)[:, 0:512], AF.Square),
                                  wait=[("s_t", st.get("qsq_mm", 0))], inc="s_s")
                    bankf[bk] = ("s_s", scur)
                    st["qsq_mm"] = sch.op(
                        "tensor", lambda e, par=par:
                        e.matmul(pb(4)[:, 0:512], ones[:], qsqb[:],
                                 start=(par == 0), stop=(par == 1)),
                        wait=[("s_s", scur)] + ([bankf[4]] if par == 0 else []),
                        inc="s_t")
                tsum = sch.count("s_t")
                vF = sch.op("vector", lambda e: e.tensor_mul(
                    qFsc[:], pb(4)[:, 0:512], qr1sq[:]),
                    wait=[("s_t", tsum), ("s_v", vr1)], inc="s_v")
                bankf[4] = ("s_v", vF)
                sF = rsqrt_act(qFsc, qFsc, EPS_QK, wait=[("s_v", vF)],
                               pre_scale=1.0 / HD)
                vF = sch.op("vector", lambda e, c0_loc=c0_loc: e.tensor_mul(
                    qFsc[:], qFsc[:], r1_rep[:, c0_loc:c0_loc + 512]),
                    wait=[("s_s", sF)], inc="s_v")
                sch.drain("vector")
                sch.op("vector", lambda e, h=h, og=og: e.scalar_tensor_tensor(
                    qT[2 * h + 1][:, og * 512:(og + 1) * 512], qf[1][:],
                    qs16[:, 1:2], qFsc[:], ALU.mult, ALU.mult),
                    wait=[("s_v", vF)], inc="s_v")
                sch.op("vector", lambda e: e.scalar_tensor_tensor(
                    qrt[:], qf[0][:], qs16[:, 0:1], qFsc[:], ALU.mult, ALU.mult))
                sch.drain("vector")
                sch.op("vector", lambda e: e.stream_shuffle(krs[:] if False else qf[0][:], qrt[:], shmask))
                sch.drain("vector")
                sch.op("vector", lambda e: e.tensor_mul(qrt[:], qrt[:], cosq[:]),
                       wait=[cw1, cw2])
                sch.drain("vector")
                sch.op("vector", lambda e: e.tensor_mul(qf[0][:], qf[0][:], sinq[:]))
                sch.drain("vector")
                st["cosq_fr"] = sch.op(
                    "vector", lambda e, h=h, og=og:
                    e.tensor_add(qT[2 * h][:, og * 512:(og + 1) * 512],
                                 qrt[:], qf[0][:]), inc="s_v")
                st["qr1sq_fr"] = st["cosq_fr"]
                st["qf_fr"] = st["cosq_fr"]
        st["xb_own_fr"] = sch.count("s_t")
        st["q_done_v"] = sch.count("s_v")

    def attn_phase(qq):
        qv = st["q_done_v"]
        sch.op("vector", lambda e: e.memset(pTa[0][:], 0.0),
               wait=[("s_v", st.get("kv_done_v", 0)), ("s_v", qv)])
        sch.op("vector", lambda e: e.memset(pTa[1][:], 0.0), inc="s_v")
        st["pTa_fr0"] = sch.count("s_v")
        st["pTa_fr1"] = sch.count("s_v")
        for og in range(2):
            mset = (qq * 2 + og) * 12
            mw = dma("sync", maskb[:].rearrange("p (k f) -> p k f", k=12),
                     masks_d[mset:mset + 12].rearrange("k p f -> p k f"),
                     wait=[("s_v", st.get("maskb_fr", 0)),
                           ("s_t", st.get("xb_own_fr", 0))])
            A_loc = 8 + qq * 8 + og * 4
            for h in range(NH):
                j = h // 2
                buf = h % 2
                pready = {}
                for kji in range(12):
                    kjt = A_loc - 8 + kji
                    slot = (kjt % 16) * 128
                    a0 = max(0, (kji - 8) * 128)
                    a1 = min(512, (kji - 8) * 128 + 1152)
                    bk = kji % 2
                    skip_mask = (4 <= kji <= 7) and not (qq == 0 and og == 0)
                    for dc in range(2):
                        sch.op("tensor",
                               lambda e, j=j, dc=dc, slot=slot, h=h, og=og,
                               a0=a0, a1=a1, bk=bk:
                               e.matmul(pb(bk)[:, a0:a1],
                                        kT[j][dc][:, slot:slot + 128],
                                        qT[2 * h + dc][:, og * 512 + a0:og * 512 + a1],
                                        start=(dc == 0), stop=(dc == 1)),
                               wait=([bankf[bk], ("s_v", qv)] if dc == 0 else ()),
                               inc=("s_t" if dc == 1 else None))
                    tsc = sch.count("s_t")
                    sch.op("scalar", lambda e, a0=a0, a1=a1, bk=bk:
                           e.activation(ttmp[:, a0:a1], pb(bk)[:, a0:a1],
                                        AF.Tanh, scale=0.02),
                           wait=[("s_t", tsc),
                                 ("s_t", st.get(f"pv_done{buf}", 0))])
                    sch.drain("scalar")
                    if skip_mask:
                        scur = sch.op("scalar", lambda e, buf=buf, kji=kji, a0=a0, a1=a1:
                                      e.activation(pT[buf][:, kji * 512 + a0:kji * 512 + a1],
                                                   ttmp[:, a0:a1], AF.Exp, scale=50.0),
                                      inc="s_s")
                        sch.drain("scalar")
                        bankf[bk] = ("s_s", scur)
                        pready[kji] = ("s_s", scur)
                    else:
                        scur = sch.op("scalar", lambda e, kji=kji, a0=a0, a1=a1:
                                      e.activation(pTa[kji % 2][:, a0:a1],
                                                   ttmp[:, a0:a1], AF.Exp, scale=50.0),
                                      wait=[("s_v", st.get(f"pTa_fr{kji % 2}", 0))],
                                      inc="s_s")
                        sch.drain("scalar")
                        bankf[bk] = ("s_s", scur)
                        st[f"pTa_fr{kji % 2}"] = sch.op(
                            "vector", lambda e, buf=buf, kji=kji:
                            e.tensor_mul(pT[buf][:, kji * 512:(kji + 1) * 512],
                                         pTa[kji % 2][:],
                                         maskb[:, kji * 512:(kji + 1) * 512]),
                            wait=[("s_s", scur), mw,
                                  ("s_t", st.get(f"pv_done{buf}", 0))],
                            inc="s_v")
                        pready[kji] = ("s_v", st[f"pTa_fr{kji % 2}"])
                for kji in range(12):
                    kjt = A_loc - 8 + kji
                    slot = (kjt % 16)
                    for dc in range(2):
                        bo = 2 + dc
                        sch.op("tensor",
                               lambda e, slot=slot, j=j, dc=dc, buf=buf, kji=kji, bo=bo:
                               e.matmul(pb(bo)[:, 0:512],
                                        vt[slot][:, j * 256 + dc * 128:j * 256 + (dc + 1) * 128],
                                        pT[buf][:, kji * 512:(kji + 1) * 512],
                                        start=(kji == 0), stop=(kji == 11)),
                               wait=([pready[kji]] +
                                     ([bankf[2], bankf[3]] if kji == 0 and dc == 0 else [])),
                               inc=("s_t" if (kji == 11 and dc == 1) else None))
                for kji in range(12):
                    sch.op("tensor", lambda e, buf=buf, kji=kji:
                           e.matmul(pb(4)[0:128, 0:512], ones[:],
                                    pT[buf][:, kji * 512:(kji + 1) * 512],
                                    start=(kji == 0), stop=(kji == 11)),
                           wait=([pready[kji]] + ([bankf[4]] if kji == 0 else [])),
                           inc=("s_t" if kji == 11 else None))
                st[f"pv_done{buf}"] = sch.count("s_t")
                tpv = sch.count("s_t")
                vrec = sch.op("vector", lambda e: e.reciprocal(
                    rec[:], pb(4)[:, 0:512]), wait=[("s_t", tpv)], inc="s_v")
                bankf[4] = ("s_v", vrec)
                sch.drain("vector")
                v1 = sch.op("vector", lambda e, h=h, og=og:
                            e.tensor_tensor(att[2 * h][:, og * 512:(og + 1) * 512],
                                            pb(2)[:, 0:512], rec[:], ALU.mult),
                            wait=[("s_t", tpv)], inc="s_v")
                v2 = sch.op("vector", lambda e, h=h, og=og:
                            e.tensor_tensor(att[2 * h + 1][:, og * 512:(og + 1) * 512],
                                            pb(3)[:, 0:512], rec[:], ALU.mult),
                            inc="s_v")
                bankf[2] = ("s_v", v1)
                bankf[3] = ("s_v", v2)
            st["maskb_fr"] = sch.count("s_v")
        st["attn_done_v"] = sch.count("s_v")
        st["rq_free"] = sch.count("s_t")
        st["kv_ring_free"] = sch.count("s_t")

    def wo_phase(qq):
        av = st["attn_done_v"]
        z = None
        for n in range(2):
            sch.op("vector", lambda e, n=n: e.memset(accA[n][:], 0.0),
                   wait=[("s_v", st.get("accs_fr", 0))])
            z = sch.op("vector", lambda e, n=n: e.memset(accB[n][:], 0.0),
                       inc="s_v")
        for blk in range(8):
            wws = [gdma(Wob[blk % 2][:].rearrange("p (a t) -> p a t", a=16),
                        wo_d[:, blk * 256:(blk + 1) * 256]
                        .rearrange("(a p) t -> p a t", p=128),
                        wait=[("s_t", st.get(f"wob_fr{blk % 2}", 0))]
                        + ([("s_v", av)] if blk < 2 else []))]
            vcast = None
            for po in range(2):
                oc = blk * 2 + po
                for ng in range(2):
                    bk = po * 2 + ng
                    xw = dma("sync", xck[bk][:],
                             xT_d[oc * 128:(oc + 1) * 128,
                                  1024 + qq * 1024 + ng * 512:
                                  1024 + qq * 1024 + (ng + 1) * 512],
                             wait=[("s_v", st.get(f"xck_fr{bk}", 0)),
                                   ("s_v", av)])
                    for ic in range(16):
                        sch.op("tensor",
                               lambda e, blk=blk, ic=ic, po=po, ng=ng, bk=bk:
                               e.matmul(pb(bk)[:, 0:512],
                                        Wob[blk % 2][:, ic * 256 + po * 128:ic * 256 + (po + 1) * 128],
                                        att[ic][:, ng * 512:(ng + 1) * 512],
                                        start=(ic == 0), stop=(ic == 15)),
                               wait=(wws + [bankf[bk], ("s_v", av)]
                                     if ic == 0 else ()),
                               inc=("s_t" if ic == 15 else None))
                    st[f"wob_fr{blk % 2}"] = sch.count("s_t")
                    tcur = sch.count("s_t")
                    if qq == 1:
                        vx1 = sch.op("vector",
                                     lambda e, oc=oc, ng=ng, bk=bk:
                                     e.tensor_add(x1r[oc][:, ng * 512:(ng + 1) * 512],
                                                  pb(bk)[:, 0:512], xck[bk][:]),
                                     wait=[("s_t", tcur), xw], inc="s_v")
                        x1ap = lambda oc=oc, ng=ng: x1r[oc][:, ng * 512:(ng + 1) * 512]
                    else:
                        vx1 = sch.op("vector",
                                     lambda e, ng=ng, bk=bk:
                                     e.tensor_add(x1stg[bk % 2][:],
                                                  pb(bk)[:, 0:512], xck[bk][:]),
                                     wait=[("s_t", tcur), xw,
                                           st.get(f"x1stg_fr{bk % 2}", ("s_v", 0))],
                                     inc="s_v")
                        st[f"x1stg_fr{bk % 2}"] = dma(
                            "scalar", x1q0_d[oc * 128:(oc + 1) * 128,
                                             ng * 512:(ng + 1) * 512],
                            x1stg[bk % 2][:], wait=[("s_v", vx1)])
                        x1ap = lambda bk=bk: x1stg[bk % 2][:]
                    st[f"xck_fr{bk}"] = vx1
                    bankf[bk] = ("s_v", vx1)
                    vb = sch.op("vector", lambda e, x1ap=x1ap: e.tensor_copy(
                        x1b_s[:], x1ap()),
                        wait=[("s_v", vx1), ("s_t", st.get("x1bs_fr", 0))],
                        inc="s_v")
                    sq = sch.op("scalar", lambda e: e.activation(
                        x1sq_s[:], x1b_s[:], AF.Square), wait=[("s_v", vb)],
                        inc="s_s")
                    t5 = sch.op("tensor", lambda e: e.matmul(
                        pb(5)[:, 0:512], ones[:], x1b_s[:], start=True, stop=True),
                        wait=[bankf[5], ("s_v", vb)], inc="s_t")
                    t6 = sch.op("tensor", lambda e: e.matmul(
                        pb(6)[:, 0:512], ones[:], x1sq_s[:], start=True, stop=True),
                        wait=[bankf[6], ("s_s", sq)], inc="s_t")
                    st["x1bs_fr"] = t6
                    va = sch.op("vector", lambda e, ng=ng: e.tensor_add(
                        accA[ng][:], accA[ng][:], pb(5)[:, 0:512]),
                        wait=[("s_t", t5)], inc="s_v")
                    bankf[5] = ("s_v", va)
                    sch.drain("vector")
                    vb2 = sch.op("vector", lambda e, ng=ng: e.tensor_add(
                        accB[ng][:], accB[ng][:], pb(6)[:, 0:512]),
                        wait=[("s_t", t6)], inc="s_v")
                    bankf[6] = ("s_v", vb2)
                    sch.drain("vector")
        for ng in range(2):
            coff = qq * 1024 + ng * 512
            sch.op("vector", lambda e, ng=ng, coff=coff: e.tensor_scalar_mul(
                m2_rep[:, coff:coff + 512], accA[ng][:], 1.0 / 2048.0))
            sch.drain("vector")
            sch.op("vector", lambda e, coff=coff: e.tensor_mul(
                m2scr[:], m2_rep[:, coff:coff + 512], m2_rep[:, coff:coff + 512]))
            sch.drain("vector")
            vv2 = sch.op("vector", lambda e, ng=ng: e.scalar_tensor_tensor(
                m2scr[:], accB[ng][:], 1.0 / 2048.0, m2scr[:],
                ALU.mult, ALU.subtract), inc="s_v")
            s2 = rsqrt_act(r2p, m2scr, LN_EPS, wait=[("s_v", vv2)])
            sch.op("vector", lambda e: e.tensor_mul(r3s[:], r2p[:], r2p[:]),
                   wait=[("s_s", s2)])
            sch.drain("vector")
            vv3 = sch.op("vector", lambda e: e.tensor_mul(
                r3s[:], r3s[:], m2scr[:]), inc="s_v")
            s3 = rsqrt_act(r3s, r3s, EPS_NORM, wait=[("s_v", vv3)])
            vv4 = sch.op("vector", lambda e, coff=coff: e.tensor_mul(
                rr_rep[:, coff:coff + 512], r2p[:], r3s[:]),
                wait=[("s_s", s3)], inc="s_v")
        st["accs_fr"] = sch.count("s_v")

    def max_pass(qq):
        coff = qq * 1024
        for oc in range(16):
            spill_w = ([st[f"x1stg_fr{i}"] for i in range(2)
                        if f"x1stg_fr{i}" in st] if oc == 0 else [])
            if qq == 1:
                xsrc = x1r[oc]
                w = spill_w
            else:
                w = [dma("scalar", mxstg[oc % 2][:],
                         x1q0_d[oc * 128:(oc + 1) * 128, 0:1024],
                         wait=[("s_v", st.get(f"mx_fr{oc % 2}", 0))])] + spill_w
                xsrc = mxstg[oc % 2]
            vm = sch.op("vector", lambda e, xsrc=xsrc, coff=coff: e.tensor_sub(
                xnt[:], xsrc[:, 0:1024], m2_rep[:, coff:coff + 1024]),
                wait=w, inc="s_v")
            st[f"mx_fr{oc % 2}"] = vm
            sch.drain("vector")
            sch.op("vector", lambda e, oc=oc, coff=coff: e.scalar_tensor_tensor(
                xnu[:], xnt[:], gam16[:, oc:oc + 1], rr_rep[:, coff:coff + 1024],
                ALU.mult, ALU.mult))
            sch.drain("vector")
            sch.op("vector", lambda e: e.tensor_reduce(
                smalls[:, SM["rmx"]:SM["rmx"] + 1], xnu[:], AXX, ALU.max,
                apply_absolute_value=True))
            sch.drain("vector")
            sch.op("vector", lambda e: e.tensor_max(
                smalls[:, SM["runmax"]:SM["runmax"] + 1],
                smalls[:, SM["runmax"]:SM["runmax"] + 1],
                smalls[:, SM["rmx"]:SM["rmx"] + 1]), inc="s_v")
            sch.drain("vector")

    def collective_trigger():
        tcur = sch.op("tensor", lambda e: e.transpose(
            pb(7)[0:1, 0:128], smalls[:, SM["runmax"]:SM["runmax"] + 1], ident[:]),
            wait=[("s_v", sch.count("s_v")), bankf[7]], inc="s_t")
        sch.op("vector", lambda e: e.tensor_reduce(
            smalls[0:1, SM["lmax"]:SM["lmax"] + 1], pb(7)[0:1, 0:128], AXX,
            ALU.max), wait=[("s_t", tcur)])
        sch.drain("vector")
        sch.op("vector", lambda e: e.tensor_scalar_max(
            smalls[0:1, SM["lmax"]:SM["lmax"] + 1],
            smalls[0:1, SM["lmax"]:SM["lmax"] + 1], EPS_SCALE))
        sch.drain("vector")
        vv1 = sch.op("vector", lambda e: e.tensor_copy(
            ccs[0:1, 0:1], smalls[0:1, SM["lmax"]:SM["lmax"] + 1]), inc="s_v")
        bankf[7] = ("s_v", vv1)
        w = dma("sync", ccin_d, ccs[:], wait=[("s_v", vv1)])
        g1 = sch.op("gpsimd", lambda e: e.collective_compute(
            "AllReduce", ALU.max, replica_groups=[[0, 1, 2, 3, 4, 5, 6, 7]],
            ins=[ccin_d], outs=[ccout_d]), wait=[w], inc="s_g")
        g2 = sch.op("gpsimd", lambda e: e.dma_start(out=ccs[:], in_=ccout_d),
                    wait=[("s_g", g1)], inc="s_g2", inc_by=16)
        st["g2"] = g2

    def collective_scales():
        g2 = st["g2"]
        w2 = dma("sync", smalls[:, SM["Scol"]:SM["Scol"] + 1],
                 ccout_d[0:1, 0:1].partition_broadcast(128), wait=[("s_g2", g2)])
        sch.op("vector", lambda e: e.reciprocal(
            smalls[:, SM["rS127"]:SM["rS127"] + 1],
            smalls[:, SM["Scol"]:SM["Scol"] + 1]), wait=[w2])
        sch.drain("vector")
        sch.op("vector", lambda e: e.tensor_scalar_mul(
            smalls[:, SM["rS127"]:SM["rS127"] + 1],
            smalls[:, SM["rS127"]:SM["rS127"] + 1], 127.0))
        sch.drain("vector")
        sch.op("vector", lambda e: e.tensor_mul(
            smalls[:, SM["fscale"]:SM["fscale"] + 1],
            smalls[:, SM["Scol"]:SM["Scol"] + 1],
            smalls[:, SM["swcol"]:SM["swcol"] + 1]))
        sch.drain("vector")
        sch.op("vector", lambda e: e.tensor_scalar_mul(
            smalls[:, SM["fscale"]:SM["fscale"] + 1],
            smalls[:, SM["fscale"]:SM["fscale"] + 1], -1.0 / 127.0))
        sch.drain("vector")
        vv2 = sch.op("vector", lambda e: e.tensor_scalar_mul(
            rr_rep[:], rr_rep[:], smalls[:, SM["rS127"]:SM["rS127"] + 1]),
            inc="s_v")
        sch.drain("vector")
        if DEBUG:
            dma("scalar", dbg["sS"], ccs[:], wait=[("s_g2", g2)])
        st["scales_v"] = vv2

    def wq_load_phase():
        base = [("s_v", st["accs_fr"]), ("s_t", st["rq_free2"])]
        late = [("s_v", st["pre_coll_v"]), ("s_t", st["rq_free2"])]
        st["wq_loads"] = []
        for p in range(16):
            q = "scalar" if p % 2 == 0 else "gpsimd"
            wt = late if p in (4, 5) else base
            if q == "scalar":
                st["wq_loads"].append(dma("scalar", wqTt[p][:],
                                          wqT_d[p * 128:(p + 1) * 128, :],
                                          wait=wt))
            else:
                st["wq_loads"].append(gdma(wqTt[p][:],
                                           wqT_d[p * 128:(p + 1) * 128, :],
                                           wait=wt))

    def ffn():
        groups = [(1, 0), (1, 1), (0, 0), (0, 1)]

        def prefetch(gi):
            qqf, ng = groups[gi]
            if qqf != 0:
                return []
            rj = x1rA[:].rearrange("p (a t) -> p a t", a=16)
            w = dma("scalar",
                    rj[:, :, ng * 512:(ng + 1) * 512],
                    x1q0_d[:, ng * 512:(ng + 1) * 512]
                    .rearrange("(a p) t -> p a t", p=128),
                    wait=[("s_v", st.get(f"x1half_fr{ng}", 0))])
            return [w]

        def quant_chunk(gi, ic, x1w):
            qqf, ng = groups[gi]
            par = gi % 2
            xqg = xq2[par]
            coff = qqf * 1024 + ng * 512
            xsl = lambda oc, ng=ng: x1r[oc][:, ng * 512:(ng + 1) * 512]
            sch.op("vector", lambda e, ic=ic, coff=coff, xsl=xsl: e.tensor_sub(
                qnt[0][:], xsl(ic), m2_rep[:, coff:coff + 512]),
                wait=list(x1w) + [("s_t", st.get(f"xq_fr{par}", 0))])
            sch.drain("vector")
            sch.op("vector", lambda e, ic=ic, coff=coff: e.scalar_tensor_tensor(
                qnt[1][:], qnt[0][:], gam16[:, ic:ic + 1],
                rr_rep[:, coff:coff + 512], ALU.mult, ALU.mult))
            sch.drain("vector")
            vcl = sch.op("vector", lambda e: e.tensor_scalar(
                qnt[0][:], qnt[1][:], 127.0, -128.0, ALU.min, ALU.max),
                inc="s_v")
            sch.op("scalar", lambda e: e.copy(qni[:], qnt[0][:]),
                   wait=[("s_v", vcl)], inc="s_s")
            sch.drain("scalar")
            scp2 = sch.op("scalar", lambda e: e.copy(qnt[1][:], qni[:]),
                          inc="s_s")
            sch.op("vector", lambda e: e.tensor_sub(qnt[2][:], qnt[0][:],
                                                    qnt[1][:]),
                   wait=[("s_s", scp2)])
            sch.drain("vector")
            sch.op("vector", lambda e: e.scalar_tensor_tensor(
                qnt[0][:], qnt[2][:], 0.5, qnt[1][:], ALU.is_gt, ALU.add))
            sch.drain("vector")
            v = sch.op("vector", lambda e, ic=ic, xqg=xqg: e.scalar_tensor_tensor(
                xqg[ic][:], qnt[2][:], -0.5, qnt[0][:], ALU.is_lt,
                ALU.subtract), inc="s_v")
            sch.drain("vector")
            return v

        x1w = prefetch(0)
        for ic in range(16):
            quant_chunk(0, ic, x1w)
        vq = {0: sch.count("s_v")}
        x1w_next = prefetch(1)
        for gi, (qqf, ng) in enumerate(groups):
            par = gi % 2
            xqg = xq2[par]
            coff = qqf * 1024 + ng * 512
            xsl = lambda oc, ng=ng: x1r[oc][:, ng * 512:(ng + 1) * 512]
            for oc in range(16):
                bk = oc % 2
                for ic in range(16):
                    sch.op("tensor", lambda e, oc=oc, ic=ic, bk=bk, xqg=xqg:
                           e.matmul(pb(bk)[:, 0:512],
                                    wqTt[ic][:, oc * 128:(oc + 1) * 128],
                                    xqg[ic][:], start=(ic == 0), stop=(ic == 15)),
                           wait=(([bankf[bk], ("s_v", vq[gi])]
                                  + (st["wq_loads"] if gi == 0 and oc == 0
                                     else []))
                                 if ic == 0 else ()),
                           inc=("s_t" if ic == 15 else None))
                tcur = sch.count("s_t")
                vo = sch.op("vector", lambda e, oc=oc, bk=bk, xsl=xsl:
                            e.scalar_tensor_tensor(
                                ostg[bk][:], pb(bk)[:, 0:512],
                                smalls[:, SM["fscale"]:SM["fscale"] + 1],
                                xsl(oc), ALU.mult, ALU.add),
                            wait=[("s_t", tcur),
                                  st.get(f"ostg_fr{bk}", ("s_v", 0))], inc="s_v")
                bankf[bk] = ("s_v", vo)
                st[f"ostg_fr{bk}"] = dma(
                    "scalar", out_d[oc * 128:(oc + 1) * 128, coff:coff + 512],
                    ostg[bk][:], wait=[("s_v", vo)])
                if gi + 1 < 4:
                    quant_chunk(gi + 1, oc, x1w_next)
            st[f"xq_fr{par}"] = sch.count("s_t")
            st[f"x1half_fr{ng}"] = sch.count("s_v")
            if gi + 1 < 4:
                vq[gi + 1] = sch.count("s_v")
            if gi + 2 < 4:
                x1w_next = prefetch(gi + 2)

    def dbg_dump(qq):
        if not DEBUG or qq != 0:
            return
        v = sch.count("s_v")
        s = sch.count("s_s")
        dma("scalar", dbg["r1"], r1_rep[:], wait=[("s_s", s), ("s_v", v)])
        dma("scalar", dbg["k00"], kT[0][0][:], wait=[("s_v", v)])
        dma("scalar", dbg["v8"], vt[8][:], wait=[("s_v", v)])

    def dbg_dump_q(qq):
        if not DEBUG or qq != 0:
            return
        v = sch.count("s_v")
        dma("scalar", dbg["q0"], qT[0][:], wait=[("s_v", v)])
        dma("scalar", dbg["q1"], qT[1][:], wait=[("s_v", v)])

    def dbg_dump_attn(qq):
        if not DEBUG or qq != 0:
            return
        v = sch.count("s_v")
        dma("scalar", dbg["attn0"], att[0][:], wait=[("s_v", v)])
        dma("scalar", dbg["attn1"], att[1][:], wait=[("s_v", v)])

    def dbg_dump_wo(qq):
        if not DEBUG:
            return
        v = sch.count("s_v")
        if qq == 1:
            dma("scalar", dbg["x1c0"], x1r[0][:], wait=[("s_v", v)])
            dma("scalar", dbg["m2"], m2_rep[:], wait=[("s_v", v)])
            dma("scalar", dbg["rr"], rr_rep[:], wait=[("s_v", v)])

    st["gate"] = [("s_v", CONST_V)]
    for qq in (0, 1):
        kv_phase(qq)
        dbg_dump(qq)
        q_phase(qq)
        dbg_dump_q(qq)
        attn_phase(qq)
        dbg_dump_attn(qq)
        st["accs_fr"] = st.get("accs_fr", 0)
        wo_phase(qq)
        if qq == 0:
            st["gate"] = [("s_t", sch.count("s_t")),
                          ("s_v", sch.count("s_v")),
                          ("s_s", sch.count("s_s"))]
        max_pass(qq)
        dbg_dump_wo(qq)
        if qq == 0:
            st["mx0_v"] = sch.count("s_v")
    st["pre_coll_v"] = sch.count("s_v")
    st["rq_free2"] = sch.count("s_t")
    wq_load_phase()
    collective_trigger()
    collective_scales()
    ffn()

    if DEBUG:
        dma("scalar", dbg["smalls"], smalls[:], wait=[("s_v", sch.count("s_v"))])
        dma("scalar", dbg["wabs"], wabs[:], wait=[("s_v", sch.count("s_v"))])
    for sem_n, v in dma_barrier("scalar") + dma_barrier("sync"):
        sch.prog["gpsimd"].append(
            lambda e, sems, sem_n=sem_n, v=v: e.wait_ge(sems[sem_n], v))

    with nc.Block() as block:
        @block.sync
        def _(eng):
            for f in sch.prog["sync"]:
                f(eng, sems)

        @block.scalar
        def _(eng):
            for f in sch.prog["scalar"]:
                f(eng, sems)

        @block.vector
        def _(eng):
            for f in sch.prog["vector"]:
                f(eng, sems)

        @block.tensor
        def _(eng):
            for f in sch.prog["tensor"]:
                f(eng, sems)

        @block.gpsimd
        def _(eng):
            for f in sch.prog["gpsimd"]:
                f(eng, sems)

    return nc, dbg


def _host_prep(inputs):
    x = np.asarray(inputs["L0_attn_x"], np.float32)
    wq = np.ascontiguousarray(np.asarray(inputs["L0_attn_Wq"], np.float32))
    wk = np.ascontiguousarray(np.asarray(inputs["L0_attn_Wk"], np.float32))
    wv = np.ascontiguousarray(np.asarray(inputs["L0_attn_Wv"], np.float32))
    wo = np.ascontiguousarray(np.asarray(inputs["L0_attn_Wo"], np.float32))
    wt = np.ascontiguousarray(np.asarray(inputs["L0_ffn_weight"], np.float32).T)
    qsc = np.asarray(inputs["L0_attn_q_scale"], np.float32)
    ksc = np.asarray(inputs["L0_attn_k_scale"], np.float32)
    gam = np.asarray(inputs["L0_ffn_norm_weight"], np.float32)

    qs2 = np.ascontiguousarray(qsc.reshape(2, 128).T)
    ks2 = np.ascontiguousarray(ksc.reshape(2, 128).T)
    gam16 = np.ascontiguousarray(gam.reshape(16, 128).T)
    ones128 = np.ones((128, 128), dtype=BF)
    ident = np.eye(128, dtype=np.float32)

    half = 64
    theta = 1.0 / (10000.0 ** (np.arange(0, ROT_, 2, dtype=np.float32) / ROT_))
    in_maps = []
    for b in range(B):
        xb_T = np.ascontiguousarray(x[b].T)
        for hh in range(2):
            lo = hh * 2048 - 1024
            xT = np.zeros((E, NLOC), np.float32)
            s0 = max(0, lo)
            xT[:, s0 - lo:3072] = xb_T[:, s0:lo + 3072]
            pos = np.arange(lo, lo + NLOC, dtype=np.float32)
            ang = pos[None, :] * theta[:, None]
            cosT = np.zeros((128, NLOC), np.float32)
            sinT = np.zeros((128, NLOC), np.float32)
            cosT[0::2] = np.cos(ang)
            cosT[1::2] = np.cos(ang)
            sinT[0::2] = -np.sin(ang)
            sinT[1::2] = np.sin(ang)
            masks = np.zeros((48, 128, 512), dtype=BF)
            for qq in range(2):
                for og in range(2):
                    A_loc = 8 + qq * 8 + og * 4
                    for kji in range(12):
                        kjt = A_loc - 8 + kji
                        K = lo + kjt * 128 + np.arange(128)[:, None]
                        Q = lo + A_loc * 128 + np.arange(512)[None, :]
                        ok = (K >= 0) & (K <= Q) & (Q - K < WIN_)
                        masks[(qq * 2 + og) * 12 + kji] = ok.astype(BF)
            in_maps.append({
                "xT": xT, "wq": wq, "wk": wk, "wv": wv, "wo": wo, "wt": wt,
                "qs2": qs2, "ks2": ks2, "gam": gam16,
                "cosT": cosT, "sinT": sinT, "masks": masks,
                "ones128": ones128, "ident128": ident,
            })
    return in_maps


ROT_ = 128
WIN_ = 1024

_NC_CACHE = {}


def kernel(**inputs):
    key = "nc"
    if key not in _NC_CACHE:
        _NC_CACHE[key] = build_nc()
    nc, dbg = _NC_CACHE[key]
    in_maps = _host_prep(inputs)
    res = run_bass_kernel_spmd(nc, in_maps, core_ids=list(range(8)))
    out = np.zeros((B, S, E), np.float32)
    for b in range(B):
        for hh in range(2):
            core = b * 2 + hh
            out[b, hh * 2048:(hh + 1) * 2048, :] = res.results[core]["out"].T
    if DEBUG:
        kernel._last = res
    return out


# version 12
# speedup vs baseline: 1.1344x; 1.0383x over previous
import os
import numpy as np
import ml_dtypes
import concourse.bass as bass
import concourse.mybir as mybir
from concourse.bass_utils import run_bass_kernel_spmd

F32 = mybir.dt.float32
BF16 = mybir.dt.bfloat16
I32 = mybir.dt.int32
AF = mybir.ActivationFunctionType
ALU = mybir.AluOpType
BF = ml_dtypes.bfloat16
AXX = mybir.AxisListType.X

B, S, E = 4, 4096, 2048
NH, NKV, HD = 8, 4, 256
QSCALE = 256 ** (-0.5)
EPS_QK = 1e-6
LN_EPS = 1e-5
EPS_NORM = 1e-6
EPS_SCALE = 1e-5
NLOC = 3072
KB = 1024

DEBUG = int(os.environ.get("MK_DEBUG", "0"))


class TView:

    def __init__(self, t, off, width):
        self.t = t
        self.off = off
        self.width = width

    def __getitem__(self, key):
        if isinstance(key, tuple):
            r, c = key
            start = (c.start or 0) + self.off
            stop = (c.stop if c.stop is not None else self.width) + self.off
            return self.t[r, start:stop]
        return self.t[key, self.off:self.off + self.width]


class Sched:
    ENGS = ("sync", "scalar", "vector", "tensor", "gpsimd")

    def __init__(self):
        self.prog = {e: [] for e in self.ENGS}
        self.C = {}
        self.hw = {e: {} for e in self.ENGS}
        self.pend = {}

    def count(self, sem):
        return self.C.get(sem, 0)

    DEFSEM = {"vector": "s_v", "scalar": "s_s"}

    def op(self, eng, fn, wait=(), inc=None, inc_by=1):
        if inc is None and eng in self.DEFSEM:
            inc = self.DEFSEM[eng]
        waits = list(wait)
        if self.pend.pop(eng, None):
            dsem = self.DEFSEM[eng]
            waits.append((dsem, self.count(dsem)))
        wl = []
        for sem, val in waits:
            if val > self.hw[eng].get(sem, 0):
                wl.append((sem, val))
                self.hw[eng][sem] = val
        if inc is not None:
            self.C[inc] = self.count(inc) + inc_by

        def emit(e, sems, waits=wl, fn=fn, inc=inc, inc_by=inc_by):
            for sem, val in waits:
                e.wait_ge(sems[sem], val)
            instr = fn(e)
            if inc is not None:
                instr.then_inc(sems[inc], inc_by)

        self.prog[eng].append(emit)
        return self.count(inc) if inc is not None else None

    def drain(self, eng):
        self.pend[eng] = True


def build_nc():
    nc = bass.Bass("TRN2", target_bir_lowering=False)

    xT_d = nc.dram_tensor("xT", [E, NLOC], F32, kind="ExternalInput").ap()
    wq_d = nc.dram_tensor("wq", [E, 2048], F32, kind="ExternalInput").ap()
    wk_d = nc.dram_tensor("wk", [E, 1024], F32, kind="ExternalInput").ap()
    wv_d = nc.dram_tensor("wv", [E, 1024], F32, kind="ExternalInput").ap()
    wo_d = nc.dram_tensor("wo", [E, 2048], F32, kind="ExternalInput").ap()
    wt_d = nc.dram_tensor("wt", [E, 2048], F32, kind="ExternalInput").ap()
    qs2_d = nc.dram_tensor("qs2", [128, 2], F32, kind="ExternalInput").ap()
    ks2_d = nc.dram_tensor("ks2", [128, 2], F32, kind="ExternalInput").ap()
    gam_d = nc.dram_tensor("gam", [128, 16], F32, kind="ExternalInput").ap()
    cos_d = nc.dram_tensor("cosT", [128, NLOC], F32, kind="ExternalInput").ap()
    sin_d = nc.dram_tensor("sinT", [128, NLOC], F32, kind="ExternalInput").ap()
    masks_d = nc.dram_tensor("masks", [48, 128, 512], BF16, kind="ExternalInput").ap()
    ones_d = nc.dram_tensor("ones128", [128, 128], BF16, kind="ExternalInput").ap()
    id_d = nc.dram_tensor("ident128", [128, 128], F32, kind="ExternalInput").ap()
    out_d = nc.dram_tensor("out", [E, 2048], F32, kind="ExternalOutput").ap()
    x1q0_d = nc.dram_tensor("x1q0", [E, 1024], F32, kind="Internal").ap()
    wqT_d = nc.dram_tensor("wqTd", [E, 2048], BF16, kind="Internal").ap()
    ccin_d = nc.dram_tensor("ccin", [1, 16], F32, kind="Internal").ap()
    bc_d = nc.dram_tensor("bc_d", [1, 16], F32, kind="Internal").ap()
    ccout_d = nc.dram_tensor("ccout", [1, 16], F32, kind="Internal").ap()

    dbg = {}
    if DEBUG:
        def dbgt(name, shape, dt=F32):
            dbg[name] = nc.dram_tensor("dbg_" + name, shape, dt,
                                       kind="ExternalOutput").ap()
        dbgt("r1", [128, NLOC]); dbgt("k00", [128, 2048], BF16)
        dbgt("v8", [128, 1024], BF16)
        dbgt("q0", [128, 1024], BF16); dbgt("q1", [128, 1024], BF16)
        dbgt("pt", [128, 12 * 512], BF16)
        dbgt("attn0", [128, 1024], BF16); dbgt("attn1", [128, 1024], BF16)
        dbgt("x1c0", [128, 1024]); dbgt("m2", [128, 2048]); dbgt("rr", [128, 2048])
        dbgt("sS", [1, 16]); dbgt("xq0", [128, 512], BF16); dbgt("wq0", [128, 2048], BF16)
        dbgt("ss_k", [128, 512]); dbgt("F_k", [128, 512]); dbgt("r1sq_k", [128, 512])
        dbgt("maskb", [128, 12 * 512], BF16); dbgt("pta", [128, 512], BF16)
        dbgt("smalls", [128, 16]); dbgt("wabs", [128, 16])
        dbgt("wcol", [128, 1]); dbgt("wrowT", [1, 128])
        dbgt("sums", [128, 512]); dbgt("ttmp", [128, 512])
        dbgt("kstar", [128, 512])

    SBUF_BASE = 16512

    def at(name, shape, dt, off):
        return nc.alloc_sbuf_tensor_at(name, shape, dt, offset=SBUF_BASE + off)

    o = [0]

    def nxt(sz):
        off = (o[0] + 31) // 32 * 32
        o[0] = off + sz
        return off

    ones = at("ones", [128, 128], BF16, nxt(256))
    onesf = at("onesf", [128, 128], F32, nxt(512))
    ident = at("ident", [128, 128], F32, nxt(512))
    qs16 = at("qs16", [128, 2], F32, nxt(32))
    ks16 = at("ks16", [128, 2], F32, nxt(32))
    gam16 = at("gam16", [128, 16], F32, nxt(64))
    smalls = at("smalls", [128, 16], F32, nxt(64))
    SM = dict(runmax=0, lmax=1, Scol=2, rS127=3, swcol=4, halfsw=5,
              neghalfsw=6, fscale=7, swrow=8, eps5=10, eps6=11, rmx=12)
    ccs = at("ccs", [1, 16], F32, nxt(64))
    _mo_off = (o[0] + 31) // 32 * 32
    mrow_own = at("mrow_own", [1, 1024], BF16, nxt(2048))
    dbgw_s = at("dbgw_s", [1, 128], F32, _mo_off)
    r1col = at("r1col", [128, 16], F32, nxt(64))
    wabs = at("wabs", [128, 16], F32, nxt(64))
    wcol = at("wcol", [128, 1], F32, nxt(32))
    wrow = at("wrow", [1, 2048], BF16, nxt(4096))
    mrow_kv = at("mrow_kv", [1, 512], BF16, nxt(1024))
    r1_rep = at("r1_rep", [128, NLOC], F32, nxt(NLOC * 4))
    _o_rr = nxt(8192)
    rr_rep = at("rr_rep", [128, 2048], F32, _o_rr)
    _o_m2 = nxt(8192)
    m2_rep = at("m2_rep", [128, 2048], F32, _o_m2)
    terW = [at("terW0", [128, 2048], F32, _o_m2), None]
    terA = at("terA", [128, 2048], BF16, _o_rr)
    terB = at("terB", [128, 2048], BF16, _o_rr + 4 * KB)
    PERM_END = (o[0] + 31) // 32 * 32
    R_K = PERM_END
    R_V = R_K + 32 * KB
    R_Q = R_V + 32 * KB
    R_AT = R_Q + 32 * KB
    R_FX = R_AT + 32 * KB
    FX_END = R_FX + 43 * KB
    assert FX_END + 16512 <= 229376, (PERM_END, FX_END)

    kT = [[at(f"kT{j}_{d}", [128, 2048], BF16, R_K + (j * 2 + d) * 4 * KB)
           for d in range(2)] for j in range(NKV)]
    vt = [at(f"v{t}", [128, 1024], BF16, R_V + t * 2 * KB) for t in range(16)]
    qT = [at(f"qT{c}", [128, 1024], BF16, R_Q + c * 2 * KB) for c in range(16)]
    att = [at(f"att{c}", [128, 1024], BF16, R_AT + c * 2 * KB) for c in range(16)]

    WkbA = at("WkbA", [128, 16 * 1024], BF16, R_Q)
    Wkb = [TView(WkbA, c * 1024, 1024) for c in range(16)]
    WvbA = at("WvbA", [128, 16 * 1024], BF16, R_AT)
    Wvb = [TView(WvbA, c * 1024, 1024) for c in range(16)]
    xbA = [at(f"xbA{g}", [128, 16 * 512], BF16, R_FX + g * 16 * KB)
           for g in range(2)]
    xb = [[TView(xbA[g], c * 512, 512) for c in range(16)] for g in range(2)]
    xstg = [at(f"xstg{i}", [128, 1024], F32, R_FX + 16 * KB + i * 4 * KB)
            for i in range(2)]
    terW[1] = at("terW1", [128, 2048], F32, R_FX + 16 * KB)
    kvs = R_FX + 24 * KB
    kf = [at(f"kf{i}", [128, 512], F32, kvs + i * 2 * KB) for i in range(2)]
    krt = at("krt", [128, 512], F32, kvs + 4 * KB)
    krs = at("krs", [128, 512], F32, kvs + 6 * KB)
    Fsc = at("Fsc", [128, 512], F32, kvs + 8 * KB)
    r1sq = at("r1sq", [128, 512], F32, kvs + 10 * KB)
    cosk = at("cosk", [128, 512], F32, kvs + 12 * KB)
    sink = at("sink", [128, 512], F32, kvs + 14 * KB)
    ksqb = at("ksqb", [128, 512], BF16, kvs + 16 * KB)
    mscr = at("mscr", [128, 512], F32, kvs + 4 * KB)
    vscr = at("vscr", [128, 512], F32, kvs + 6 * KB)

    wqstg = at("wqstg8", [128, 8 * 256], F32, R_FX + 32 * KB)
    WqbA = [at(f"WqbA{i}", [128, 16 * 256], BF16, R_AT + i * 8 * KB)
            for i in range(2)]
    qf = [at(f"qf{i}", [128, 512], F32, R_AT + 16 * KB + i * 2 * KB)
          for i in range(2)]
    qFsc = at("qFsc", [128, 512], F32, R_AT + 20 * KB)
    qr1sq = at("qr1sq", [128, 512], F32, R_AT + 22 * KB)
    qsqb = at("qsqb", [128, 512], BF16, R_AT + 24 * KB)
    cosq = at("cosq", [128, 512], F32, R_AT + 25 * KB)
    sinq = at("sinq", [128, 512], F32, R_AT + 27 * KB)
    qrt = at("qrt", [128, 512], F32, R_AT + 29 * KB)

    maskb = at("maskb", [128, 12 * 512], BF16, R_FX)
    pT = [at(f"pT{i}", [128, 12 * 512], BF16, R_FX + 12 * KB + i * 12 * KB)
          for i in range(2)]
    pTa = [at(f"pTa{i}", [128, 512], BF16, R_FX + 36 * KB + i * KB)
           for i in range(2)]
    ttmp = at("ttmp", [128, 512], F32, R_FX + 38 * KB)
    rec = at("rec", [128, 512], F32, R_FX + 40 * KB)

    wostg = at("wostg", [128, 16 * 256], F32, R_FX)
    Wob = [at(f"Wob{i}", [128, 16 * 256], BF16, R_FX + 16 * KB + i * 8 * KB)
           for i in range(2)]
    xck = [at(f"xck{i}", [128, 512], F32, R_FX + 32 * KB + i * 2 * KB)
           for i in range(4)]
    accA = [at(f"accA{n}", [128, 512], F32, R_Q + n * 2 * KB) for n in range(2)]
    accB = [at(f"accB{n}", [128, 512], F32, R_Q + 4 * KB + n * 2 * KB)
            for n in range(2)]
    x1b_s = at("x1b_s", [128, 512], BF16, R_Q + 8 * KB)
    x1sq_s = at("x1sq_s", [128, 512], BF16, R_Q + 9 * KB)
    m2scr = at("m2scr", [128, 512], F32, R_Q + 10 * KB)
    r2p = at("r2p", [128, 512], F32, R_Q + 12 * KB)
    r3s = at("r3s", [128, 512], F32, R_Q + 14 * KB)
    xnt = at("xnt", [128, 1024], F32, R_Q + 16 * KB)
    xnu = at("xnu", [128, 1024], F32, R_Q + 20 * KB)
    x1stg = [at(f"x1stg{i}", [128, 512], F32, R_Q + 20 * KB + i * 2 * KB)
             for i in range(2)]
    mxstg = [at(f"mxstg{i}", [128, 1024], F32, R_Q + 24 * KB + i * 4 * KB)
             for i in range(2)]
    rmx = None
    x1rA = at("x1rA", [128, 16 * 1024], F32, R_K)
    x1r = [TView(x1rA, c * 1024, 1024) for c in range(16)]

    wqTt = [at(f"wqT{c}", [128, 2048], BF16, R_Q + c * 4 * KB) for c in range(16)]
    wstg = [at(f"wstg{i}", [128, 2048], F32, R_FX + i * 8 * KB) for i in range(2)]
    xqA_t = at("xqA", [128, 16 * 512], BF16, R_FX + 16 * KB)
    xqB_t = at("xqB", [128, 16 * 512], BF16, R_FX)
    xq = [TView(xqA_t, c * 512, 512) for c in range(16)]
    xq2 = [[TView(xqA_t, c * 512, 512) for c in range(16)],
           [TView(xqB_t, c * 512, 512) for c in range(16)]]
    ostg = [at(f"ostg{i}", [128, 512], F32, R_FX + 32 * KB + i * 2 * KB)
            for i in range(2)]
    qnt = [at(f"qnt{i}", [128, 512], F32, R_FX + 36 * KB + i * 2 * KB)
           for i in range(3)]
    qni = at("qni", [128, 512], I32, R_FX + 40 * KB)

    ps_guards = []
    g01 = nc.psum_tensor("ps01", [128, 1024], F32)
    P01t = g01.__enter__()
    ps_guards.append(g01)
    pbt = [TView(P01t, 0, 512), TView(P01t, 512, 512)]
    for i in range(2, 8):
        g = nc.psum_tensor(f"ps{i}", [128, 512], F32)
        t = g.__enter__()
        ps_guards.append(g)
        pbt.append(t)

    def pb(i):
        return pbt[i]

    sems = {}
    sem_guards = {}
    for n in (["s_t", "s_s", "s_v", "s_g", "s_g2"]
              + [f"sx{i}" for i in range(8)] + [f"sy{i}" for i in range(8)]
              + [f"sg{i}" for i in range(8)]):
        g = nc.semaphore(n)
        sems[n] = g.__enter__()
        sem_guards[n] = g

    sch = Sched()
    st = {}
    bankf = {i: ("s_s", 0) for i in range(8)}

    qidx = {"sync": 0, "scalar": 0, "gpsimd": 0}

    def dma(queue, out, in_, wait=()):
        pref = "sx" if queue == "sync" else "sy"
        i = qidx[queue]
        qidx[queue] = i + 1
        lane = f"{pref}{i % 8}"
        prev = sch.count(lane)
        wl = list(wait) + ([(lane, prev)] if prev else [])
        cnt = sch.op(queue, lambda e, out=out, in_=in_:
                     e.dma_start(out=out, in_=in_), wait=wl, inc=lane, inc_by=16)
        return (lane, cnt)

    def dma_barrier(queue):
        pref = {"sync": "sx", "scalar": "sy", "gpsimd": "sg"}[queue]
        return [(f"{pref}{i}", sch.count(f"{pref}{i}"))
                for i in range(8) if sch.count(f"{pref}{i}") > 0]

    def gdma(out, in_, wait=()):
        i = qidx["gpsimd"]
        qidx["gpsimd"] = i + 1
        lane = f"sg{i % 8}"
        prev = sch.count(lane)
        wl = list(wait) + ([(lane, prev)] if prev else [])
        cnt = sch.op("gpsimd", lambda e, out=out, in_=in_:
                     e.dma_start(out=out, in_=in_), wait=wl, inc=lane, inc_by=16)
        return (lane, cnt)

    dma("sync", ones[:], ones_d)
    dma("sync", ident[:], id_d)
    dma("sync", qs16[:], qs2_d)
    dma("sync", ks16[:], ks2_d)
    w0 = dma("sync", gam16[:], gam_d)
    CONSTW = dma_barrier("sync")
    sch.op("vector", lambda e: e.tensor_scalar_mul(qs16[:], qs16[:], QSCALE),
           wait=CONSTW)
    sch.op("vector", lambda e: e.memset(onesf[:], 1.0))
    sch.op("vector", lambda e: e.memset(smalls[:, 0:1], 0.0))
    sch.op("vector", lambda e: e.memset(smalls[:, SM["eps5"]:SM["eps5"] + 1], LN_EPS))
    sch.op("vector", lambda e: e.memset(smalls[:, SM["eps6"]:SM["eps6"] + 1], EPS_QK))
    sch.op("vector", lambda e: e.memset(ccs[:], 0.0), inc="s_v")
    CONST_V = sch.count("s_v")

    for p in range(16):
        wdone = st.get("wmean_used", 0)
        w = dma("scalar", wstg[p % 2][:], wt_d[p * 128:(p + 1) * 128, :],
                wait=([("s_v", wdone)] if p >= 2 else ()))
        st["wmean_used"] = sch.op(
            "vector", lambda e, p=p: e.tensor_reduce(
                wabs[:, p:p + 1], wstg[p % 2][:], AXX, ALU.add,
                apply_absolute_value=True),
            wait=[w], inc="s_v")
    sch.drain("vector")
    sch.op("vector", lambda e: e.tensor_reduce(wcol[:], wabs[:], AXX, ALU.add),
           inc="s_v")
    t1 = sch.op("tensor", lambda e: e.transpose(pb(7)[0:1, 0:128], wcol[:],
                                                ident[:]),
                wait=[("s_v", sch.count("s_v"))] + CONSTW, inc="s_t")
    if DEBUG:
        sdw = sch.op("scalar", lambda e: e.copy(wabs[0:1, 0:16] if False else dbgw_s[:], pb(7)[0:1, 0:128]),
                     wait=[("s_t", t1)], inc="s_s")
        dma("scalar", dbg["wrowT"], dbgw_s[:], wait=[("s_s", sdw)])
        dma("scalar", dbg["wcol"], wcol[:], wait=[("s_v", sch.count("s_v"))])
    sch.op("vector", lambda e: e.tensor_reduce(
        smalls[0:1, SM["swrow"]:SM["swrow"] + 1], pb(7)[0:1, 0:128], AXX,
        ALU.add), wait=[("s_t", t1)])
    sch.drain("vector")
    vv = sch.op("vector", lambda e: e.tensor_scalar(
        smalls[0:1, SM["swrow"]:SM["swrow"] + 1],
        smalls[0:1, SM["swrow"]:SM["swrow"] + 1],
        1.0 / (2048.0 * 2048.0), EPS_SCALE, ALU.mult, ALU.max), inc="s_v")
    wv_ = dma("sync", bc_d[0:1, 0:1],
              smalls[0:1, SM["swrow"]:SM["swrow"] + 1], wait=[("s_v", vv)])
    wv_ = dma("sync", smalls[:, SM["swcol"]:SM["swcol"] + 1],
              bc_d[0:1, 0:1].partition_broadcast(128), wait=[wv_])
    sch.op("vector", lambda e: e.tensor_scalar_mul(
        smalls[:, SM["halfsw"]:SM["halfsw"] + 1],
        smalls[:, SM["swcol"]:SM["swcol"] + 1], 0.5), wait=[wv_])
    sch.drain("vector")
    sch.op("vector", lambda e: e.tensor_scalar_mul(
        smalls[:, SM["neghalfsw"]:SM["neghalfsw"] + 1],
        smalls[:, SM["halfsw"]:SM["halfsw"] + 1], -1.0), inc="s_v")
    bankf[7] = ("s_v", sch.count("s_v"))
    st0_wmean_v = sch.count("s_v")

    def load_panels_cast(w_dram, destA, ncols, extra_wait=(), per_g=None):
        ws = []
        for g in range(4):
            wt = list(extra_wait) if g == 0 else []
            if per_g:
                wt += per_g[g]
            ws.append(gdma(
                destA[:, g * 4 * ncols:(g + 1) * 4 * ncols]
                .rearrange("p (a t) -> p a t", a=4),
                w_dram[g * 512:(g + 1) * 512, 0:ncols]
                .rearrange("(a p) t -> p a t", p=128),
                wait=wt))
        return ws

    def colsum_row(chunks, ncols, row_off, wwait):
        for ng in range(0, ncols, 512):
            nn = min(512, ncols - ng)
            for c in range(16):
                sch.op("tensor", lambda e, c=c, ng=ng, nn=nn, chunks=chunks:
                       e.matmul(pb(4)[0:1, 0:nn], ones[:, 0:1],
                                chunks[c][:, ng:ng + nn],
                                start=(c == 0), stop=(c == 15)),
                       wait=(list(wwait) + [bankf[4]] if c == 0 else ()),
                       inc=("s_t" if c == 15 else None))
            tcur = sch.count("s_t")
            scur = sch.op("scalar", lambda e, ng=ng, nn=nn, row_off=row_off:
                          e.copy(wrow[0:1, row_off + ng:row_off + ng + nn],
                                 pb(4)[0:1, 0:nn]),
                          wait=[("s_t", tcur)], inc="s_s")
            bankf[4] = ("s_s", scur)

    def rsqrt_act(dst, src, bias, wait=(), pre_scale=1.0):
        dap = dst[:] if hasattr(dst, "ap") else dst
        sap = src[:] if hasattr(src, "ap") else src
        bcol = SM["eps5"] if bias == LN_EPS else SM["eps6"]
        nparts = dap.shape[0]
        bap = smalls[0:nparts, bcol:bcol + 1]
        sch.op("scalar", lambda e: e.activation(dap, sap, AF.Ln,
                                                bias=bap, scale=pre_scale),
               wait=wait)
        sch.drain("scalar")
        return sch.op("scalar", lambda e: e.activation(dap, dap, AF.Exp,
                                                       scale=-0.5), inc="s_s")

    tern_ctr = [0]

    def tern_step():
        p = tern_ctr[0]
        if p >= 16:
            return
        tern_ctr[0] = p + 1
        buf = p % 2
        w = dma("scalar", terW[buf][:], wt_d[p * 128:(p + 1) * 128, :],
                wait=[st.get(f"terW_fr{buf}", ("s_v", 0))])
        sch.op("vector", lambda e: e.tensor_single_scalar(
            terA[:], terW[buf][:],
            smalls[:, SM["halfsw"]:SM["halfsw"] + 1], ALU.is_gt),
            wait=[w, ("s_v", st0_wmean_v), st.get("terA_fr", ("s_v", 0))])
        sch.drain("vector")
        sch.op("vector", lambda e: e.tensor_single_scalar(
            terB[:], terW[buf][:],
            smalls[:, SM["neghalfsw"]:SM["neghalfsw"] + 1], ALU.is_lt))
        sch.drain("vector")
        vs = sch.op("vector", lambda e: e.tensor_sub(
            terA[:], terA[:], terB[:]), inc="s_v")
        st[f"terW_fr{buf}"] = ("s_v", vs)
        st["terA_fr"] = dma("sync", wqT_d[p * 128:(p + 1) * 128, :],
                            terA[:], wait=[("s_v", vs)])

    def kv_phase(qq):
        mxg = [("s_v", st.get("mx0_v", 0))]
        first = 0 if qq == 0 else 16
        ngrp = 4 if qq == 0 else 2

        def load_xgroup(gi):
            t0 = first + gi * 4
            c0 = t0 * 128
            xgate = (st.get("gate", []) if qq == 0
                     else [("s_t", st.get("rq_free", 0))])
            ws = []
            for g4 in range(4):
                ws.append(gdma(
                    xbA[0][:, g4 * 4 * 512:(g4 + 1) * 4 * 512]
                    .rearrange("p (a t) -> p a t", a=4),
                    xT_d[g4 * 512:(g4 + 1) * 512, c0:c0 + 512]
                    .rearrange("(a p) t -> p a t", p=128),
                    wait=(([("s_t", st.get("xb0_fr", 0)),
                            ("s_v", st0_wmean_v)] + xgate)
                          if g4 == 0 else ())))
            return ws

        xws0 = load_xgroup(0)
        wWk = load_panels_cast(wk_d, WkbA, 1024,
                               extra_wait=st.get("gate", []),
                               per_g=[[], [], mxg, mxg])
        wWv = load_panels_cast(wv_d, WvbA, 1024,
                               extra_wait=st.get("gate", []))

        for gi in range(ngrp):
            t0 = first + gi * 4
            c0 = t0 * 128
            xws = xws0 if gi == 0 else load_xgroup(gi)
            for c in range(16):
                scur = sch.op("scalar", lambda e, c=c:
                              e.activation(ksqb[:], xb[0][c][:], AF.Square),
                              wait=[xws[c // 4],
                                    ("s_t", st.get("ksqb_fr", 0))], inc="s_s")
                sch.op("tensor", lambda e, c=c: e.matmul(
                    pb(5)[:, 0:512], ones[:], xb[0][c][:],
                    start=(c == 0), stop=(c == 15)),
                    wait=([bankf[5]] if c == 0 else []) + [xws[c // 4]],
                    inc=("s_t" if c == 15 else None))
                st["ksqb_fr"] = sch.op("tensor", lambda e, c=c: e.matmul(
                    pb(6)[:, 0:512], ones[:], ksqb[:],
                    start=(c == 0), stop=(c == 15)),
                    wait=([bankf[6], ("s_s", scur)] if c == 0
                          else [("s_s", scur)]),
                    inc="s_t")
            tstats = sch.count("s_t")
            st["ksq_mm"] = tstats
            if gi == 0:
                colsum_row(Wkb, 1024, 0, wWk)
            sch.op("vector", lambda e: e.tensor_scalar_mul(
                mscr[:], pb(5)[:, 0:512], 1.0 / 2048.0), wait=[("s_t", tstats)])
            sch.drain("vector")
            sch.op("vector", lambda e: e.tensor_mul(vscr[:], mscr[:], mscr[:]))
            sch.drain("vector")
            v1 = sch.op("vector", lambda e: e.scalar_tensor_tensor(
                vscr[:], pb(6)[:, 0:512], 1.0 / 2048.0, vscr[:],
                ALU.mult, ALU.subtract), inc="s_v")
            bankf[5] = ("s_v", v1)
            bankf[6] = ("s_v", v1)
            s1 = rsqrt_act(r1_rep[:, c0:c0 + 512], vscr, LN_EPS,
                           wait=[("s_v", v1)])
            v2 = sch.op("vector", lambda e: e.tensor_scalar_mul(
                mrow_kv[0:1, 0:512], mscr[0:1, 0:512], -1.0), inc="s_v")
            if qq == 1 or gi >= 2:
                og = gi if qq == 1 else gi - 2
                sch.op("vector", lambda e, og=og: e.tensor_scalar_mul(
                    mrow_own[0:1, og * 512:(og + 1) * 512],
                    mscr[0:1, 0:512], -1.0), inc="s_v")
            for tt_ in range(4):
                slot = (t0 + tt_) % 16
                tcur = sch.op("tensor", lambda e, tt_=tt_, c0=c0:
                              e.transpose(pb(7)[0:128, 0:1],
                                          r1_rep[0:1, c0 + tt_ * 128:c0 + (tt_ + 1) * 128],
                                          ident[0:1, 0:1]),
                              wait=[("s_s", s1), bankf[7]], inc="s_t")
                scur = sch.op("scalar", lambda e, slot=slot:
                              e.copy(r1col[:, slot:slot + 1], pb(7)[0:128, 0:1]),
                              wait=[("s_t", tcur)], inc="s_s")
                bankf[7] = ("s_s", scur)

            cos_w1 = dma("sync", cosk[:], cos_d[:, c0:c0 + 512],
                         wait=[("s_v", st.get("cosk_fr", 0))] + st.get("gate", []))
            cos_w2 = dma("sync", sink[:], sin_d[:, c0:c0 + 512],
                         wait=st.get("gate", []))
            r1sq_v = sch.op("vector", lambda e, c0=c0: e.tensor_mul(
                r1sq[:], r1_rep[:, c0:c0 + 512], r1_rep[:, c0:c0 + 512]),
                wait=[("s_s", s1), ("s_v", st.get("r1sq_fr", 0))], inc="s_v")
            for j in range(NKV):
                for par in range(2):
                    kc = 2 * j + par
                    bk = kc % 2
                    for ic in range(16):
                        sch.op("tensor", lambda e, ic=ic, kc=kc, bk=bk:
                               e.matmul(pb(bk)[:, 0:512],
                                        Wkb[ic][:, kc * 128:(kc + 1) * 128],
                                        xb[0][ic][:], start=(ic == 0), stop=False),
                               wait=([bankf[bk], ("s_v", v2)] if ic == 0 else ()))
                    tcur = sch.op("tensor", lambda e, kc=kc, bk=bk:
                                  e.matmul(pb(bk)[:, 0:512],
                                           wrow[0:1, kc * 128:(kc + 1) * 128],
                                           mrow_kv[0:1, 0:512],
                                           start=False, stop=True),
                                  inc="s_t")
                    scur = sch.op("scalar", lambda e, par=par, bk=bk:
                                  e.copy(kf[par][:], pb(bk)[:, 0:512]),
                                  wait=[("s_t", tcur),
                                        ("s_v", st.get("kf_fr", 0))], inc="s_s")
                    sch.drain("scalar")
                    scur = sch.op("scalar", lambda e, bk=bk:
                                  e.activation(ksqb[:], pb(bk)[:, 0:512],
                                               AF.Square),
                                  wait=[("s_t", st.get("ksq_mm", 0))], inc="s_s")
                    bankf[bk] = ("s_s", scur)
                    st["ksq_mm"] = sch.op("tensor", lambda e, par=par:
                                          e.matmul(pb(4)[:, 0:512], ones[:], ksqb[:],
                                                   start=(par == 0), stop=(par == 1)),
                                          wait=[("s_s", scur)] + ([bankf[4]] if par == 0 else []),
                                          inc="s_t")
                tsum = sch.count("s_t")
                dbg_this = DEBUG and qq == 0 and gi == 2 and j == 0
                if dbg_this:
                    dma("scalar", dbg["kstar"], kf[0][:], wait=[("s_s", sch.count("s_s"))])
                vF = sch.op("vector", lambda e: e.tensor_mul(
                    Fsc[:], pb(4)[:, 0:512], r1sq[:]),
                    wait=[("s_t", tsum), ("s_v", r1sq_v)], inc="s_v")
                if dbg_this:
                    sdbg = sch.op("scalar", lambda e: e.copy(vscr[:], pb(4)[:, 0:512]),
                                  wait=[("s_t", tsum), ("s_v", vF)], inc="s_s")
                    bankf[4] = ("s_s", sdbg)
                    dma("scalar", dbg["ss_k"], vscr[:], wait=[("s_s", sdbg)])
                bankf[4] = ("s_v", vF)
                sF = rsqrt_act(Fsc, Fsc, EPS_QK, wait=[("s_v", vF)],
                               pre_scale=1.0 / HD)
                vF = sch.op("vector", lambda e, c0=c0: e.tensor_mul(
                    Fsc[:], Fsc[:], r1_rep[:, c0:c0 + 512]),
                    wait=[("s_s", sF)], inc="s_v")
                sch.drain("vector")
                if dbg_this:
                    dma("scalar", dbg["F_k"], Fsc[:], wait=[("s_v", vF)])
                    dma("scalar", dbg["r1sq_k"], r1sq[:], wait=[("s_v", vF)])
                slotc = (t0 % 16) * 128
                vpl = sch.op("vector", lambda e, j=j, slotc=slotc:
                             e.scalar_tensor_tensor(
                                 kT[j][1][:, slotc:slotc + 512], kf[1][:],
                                 ks16[:, 1:2], Fsc[:], ALU.mult, ALU.mult),
                             wait=[("s_v", vF)], inc="s_v")
                sch.op("vector", lambda e: e.scalar_tensor_tensor(
                    krt[:], kf[0][:], ks16[:, 0:1], Fsc[:], ALU.mult, ALU.mult))
                sch.drain("vector")
                shmask = []
                for i2 in range(16):
                    shmask += [2 * i2 + 1, 2 * i2]
                sch.op("vector", lambda e: e.stream_shuffle(krs[:], krt[:], shmask))
                sch.drain("vector")
                sch.op("vector", lambda e: e.tensor_mul(krt[:], krt[:], cosk[:]),
                       wait=[cos_w1, cos_w2])
                sch.drain("vector")
                sch.op("vector", lambda e: e.tensor_mul(krs[:], krs[:], sink[:]))
                sch.drain("vector")
                vrot = sch.op("vector", lambda e, j=j, slotc=slotc:
                              e.tensor_add(kT[j][0][:, slotc:slotc + 512],
                                           krt[:], krs[:]), inc="s_v")
                st["kf_fr"] = vrot
                if qq == 0:
                    tern_step()
            st["cosk_fr"] = sch.count("s_v")
            st["r1sq_fr"] = sch.count("s_v")
            if gi == 0:
                colsum_row(Wvb, 1024, 1024, wWv)
                st["wsumv"] = bankf[4]

            for tt_ in range(4):
                slot = (t0 + tt_) % 16
                for ng in range(2):
                    bk = ng % 2
                    for ic in range(16):
                        sch.op("tensor", lambda e, ic=ic, tt_=tt_, ng=ng, bk=bk:
                               e.matmul(pb(bk)[:, 0:512],
                                        xb[0][ic][:, tt_ * 128:(tt_ + 1) * 128],
                                        Wvb[ic][:, ng * 512:(ng + 1) * 512],
                                        start=(ic == 0), stop=False),
                               wait=([bankf[bk]] if ic == 0 else ()))
                    tcur = sch.op("tensor", lambda e, tt_=tt_, ng=ng, bk=bk:
                                  e.matmul(pb(bk)[:, 0:512],
                                           mrow_kv[0:1, tt_ * 128:(tt_ + 1) * 128],
                                           wrow[0:1, 1024 + ng * 512:1024 + (ng + 1) * 512],
                                           start=False, stop=True),
                                  wait=[st.get("wsumv", ("s_s", 0))], inc="s_t")
                    vcur = sch.op("vector", lambda e, slot=slot, ng=ng, bk=bk:
                                  e.tensor_scalar_mul(
                                      vt[slot][:, ng * 512:(ng + 1) * 512],
                                      pb(bk)[:, 0:512], r1col[:, slot:slot + 1]),
                                  wait=[("s_t", tcur)], inc="s_v")
                    bankf[bk] = ("s_v", vcur)
            st["xb0_fr"] = sch.count("s_t")
        st["kv_done_t"] = sch.count("s_t")
        st["kv_done_v"] = sch.count("s_v")

    def q_phase(qq):
        wqw0 = gdma(WqbA[0][:].rearrange("p (a t) -> p a t", a=16),
                    wq_d[:, 0:256].rearrange("(a p) t -> p a t", p=128),
                    wait=[("s_t", st.get("wqb_fr0", 0)),
                          ("s_t", st.get("kv_done_t", 0)),
                          ("s_v", st.get("kv_done_v", 0))])
        vxb = [[], []]
        for og in range(2):
            c0 = 1024 + qq * 1024 + og * 512
            for g4 in range(4):
                vxb[og].append(gdma(
                    xbA[og][:, g4 * 4 * 512:(g4 + 1) * 4 * 512]
                    .rearrange("p (a t) -> p a t", a=4),
                    xT_d[g4 * 512:(g4 + 1) * 512, c0:c0 + 512]
                    .rearrange("(a p) t -> p a t", p=128),
                    wait=([("s_t", st.get("xb_own_fr", 0)),
                           ("s_t", st.get("kv_done_t", 0)),
                           ("s_v", st.get("kv_done_v", 0))] if g4 == 0 and og == 0
                          else ())))

        shmask = []
        for i2 in range(16):
            shmask += [2 * i2 + 1, 2 * i2]
        for blk in range(8):
            h = blk
            wqw = wqw0 if blk == 0 else gdma(
                WqbA[blk % 2][:].rearrange("p (a t) -> p a t", a=16),
                wq_d[:, h * 256:(h + 1) * 256]
                .rearrange("(a p) t -> p a t", p=128),
                wait=[("s_t", st.get(f"wqb_fr{blk % 2}", 0))])
            vwq = wqw
            for c in range(16):
                sch.op("tensor", lambda e, c=c, blk=blk:
                       e.matmul(pb(4)[0:1, 0:256], ones[:, 0:1],
                                WqbA[blk % 2][:, c * 256:(c + 1) * 256],
                                start=(c == 0), stop=(c == 15)),
                       wait=([vwq, bankf[4]] if c == 0 else ()),
                       inc=("s_t" if c == 15 else None))
            tcur = sch.count("s_t")
            swsum = sch.op("scalar", lambda e: e.copy(wrow[0:1, 0:256],
                                                      pb(4)[0:1, 0:256]),
                           wait=[("s_t", tcur)], inc="s_s")
            bankf[4] = ("s_s", swsum)

            for og in range(2):
                c0_loc = 1024 + qq * 1024 + og * 512
                cw1 = dma("sync", cosq[:], cos_d[:, c0_loc:c0_loc + 512],
                          wait=[("s_v", st.get("cosq_fr", 0)),
                                ("s_t", st.get("kv_done_t", 0))])
                cw2 = dma("sync", sinq[:], sin_d[:, c0_loc:c0_loc + 512],
                          wait=[("s_t", st.get("kv_done_t", 0))])
                vr1 = sch.op("vector", lambda e, c0_loc=c0_loc: e.tensor_mul(
                    qr1sq[:], r1_rep[:, c0_loc:c0_loc + 512],
                    r1_rep[:, c0_loc:c0_loc + 512]),
                    wait=[("s_v", st.get("qr1sq_fr", 0))], inc="s_v")
                for par in range(2):
                    bk = par
                    for ic in range(16):
                        sch.op("tensor", lambda e, ic=ic, par=par, og=og, bk=bk, blk=blk:
                               e.matmul(pb(bk)[:, 0:512],
                                        WqbA[blk % 2][:, ic * 256 + par * 128:
                                                      ic * 256 + (par + 1) * 128],
                                        xb[og][ic][:], start=(ic == 0), stop=False),
                               wait=(([bankf[bk], ("s_s", swsum)] + vxb[og])
                                     if ic == 0 else ()))
                    tcur = sch.op("tensor", lambda e, par=par, og=og, bk=bk:
                                  e.matmul(pb(bk)[:, 0:512],
                                           wrow[0:1, par * 128:(par + 1) * 128],
                                           mrow_own[0:1, og * 512:(og + 1) * 512],
                                           start=False, stop=True), inc="s_t")
                    st[f"wqb_fr{blk % 2}"] = tcur
                    scur = sch.op("scalar", lambda e, par=par, bk=bk:
                                  e.copy(qf[par][:], pb(bk)[:, 0:512]),
                                  wait=[("s_t", tcur),
                                        ("s_v", st.get("qf_fr", 0))], inc="s_s")
                    sch.drain("scalar")
                    scur = sch.op("scalar", lambda e, bk=bk:
                                  e.activation(qsqb[:], pb(bk)[:, 0:512], AF.Square),
                                  wait=[("s_t", st.get("qsq_mm", 0))], inc="s_s")
                    bankf[bk] = ("s_s", scur)
                    st["qsq_mm"] = sch.op(
                        "tensor", lambda e, par=par:
                        e.matmul(pb(4)[:, 0:512], ones[:], qsqb[:],
                                 start=(par == 0), stop=(par == 1)),
                        wait=[("s_s", scur)] + ([bankf[4]] if par == 0 else []),
                        inc="s_t")
                tsum = sch.count("s_t")
                vF = sch.op("vector", lambda e: e.tensor_mul(
                    qFsc[:], pb(4)[:, 0:512], qr1sq[:]),
                    wait=[("s_t", tsum), ("s_v", vr1)], inc="s_v")
                bankf[4] = ("s_v", vF)
                sF = rsqrt_act(qFsc, qFsc, EPS_QK, wait=[("s_v", vF)],
                               pre_scale=1.0 / HD)
                vF = sch.op("vector", lambda e, c0_loc=c0_loc: e.tensor_mul(
                    qFsc[:], qFsc[:], r1_rep[:, c0_loc:c0_loc + 512]),
                    wait=[("s_s", sF)], inc="s_v")
                sch.drain("vector")
                sch.op("vector", lambda e, h=h, og=og: e.scalar_tensor_tensor(
                    qT[2 * h + 1][:, og * 512:(og + 1) * 512], qf[1][:],
                    qs16[:, 1:2], qFsc[:], ALU.mult, ALU.mult),
                    wait=[("s_v", vF)], inc="s_v")
                sch.op("vector", lambda e: e.scalar_tensor_tensor(
                    qrt[:], qf[0][:], qs16[:, 0:1], qFsc[:], ALU.mult, ALU.mult))
                sch.drain("vector")
                sch.op("vector", lambda e: e.stream_shuffle(krs[:] if False else qf[0][:], qrt[:], shmask))
                sch.drain("vector")
                sch.op("vector", lambda e: e.tensor_mul(qrt[:], qrt[:], cosq[:]),
                       wait=[cw1, cw2])
                sch.drain("vector")
                sch.op("vector", lambda e: e.tensor_mul(qf[0][:], qf[0][:], sinq[:]))
                sch.drain("vector")
                st["cosq_fr"] = sch.op(
                    "vector", lambda e, h=h, og=og:
                    e.tensor_add(qT[2 * h][:, og * 512:(og + 1) * 512],
                                 qrt[:], qf[0][:]), inc="s_v")
                st["qr1sq_fr"] = st["cosq_fr"]
                st["qf_fr"] = st["cosq_fr"]
        st["xb_own_fr"] = sch.count("s_t")
        st["q_done_v"] = sch.count("s_v")

    def attn_phase(qq):
        qv = st["q_done_v"]
        sch.op("vector", lambda e: e.memset(pTa[0][:], 0.0),
               wait=[("s_v", st.get("kv_done_v", 0)), ("s_v", qv)])
        sch.op("vector", lambda e: e.memset(pTa[1][:], 0.0), inc="s_v")
        st["pTa_fr0"] = sch.count("s_v")
        st["pTa_fr1"] = sch.count("s_v")
        for og in range(2):
            mset = (qq * 2 + og) * 12
            mw = dma("sync", maskb[:].rearrange("p (k f) -> p k f", k=12),
                     masks_d[mset:mset + 12].rearrange("k p f -> p k f"),
                     wait=[("s_v", st.get("maskb_fr", 0)),
                           ("s_t", st.get("xb_own_fr", 0))])
            A_loc = 8 + qq * 8 + og * 4
            for h in range(NH):
                j = h // 2
                buf = h % 2
                pready = {}
                for kji in range(12):
                    kjt = A_loc - 8 + kji
                    slot = (kjt % 16) * 128
                    a0 = max(0, (kji - 8) * 128)
                    a1 = min(512, (kji - 8) * 128 + 1152)
                    bk = (0, 1, 5, 6)[kji % 4]
                    skip_mask = (4 <= kji <= 7) and not (qq == 0 and og == 0)
                    for dc in range(2):
                        sch.op("tensor",
                               lambda e, j=j, dc=dc, slot=slot, h=h, og=og,
                               a0=a0, a1=a1, bk=bk:
                               e.matmul(pb(bk)[:, a0:a1],
                                        kT[j][dc][:, slot:slot + 128],
                                        qT[2 * h + dc][:, og * 512 + a0:og * 512 + a1],
                                        start=(dc == 0), stop=(dc == 1)),
                               wait=([bankf[bk], ("s_v", qv)] if dc == 0 else ()),
                               inc=("s_t" if dc == 1 else None))
                    tsc = sch.count("s_t")
                    sch.op("scalar", lambda e, a0=a0, a1=a1, bk=bk:
                           e.activation(ttmp[:, a0:a1], pb(bk)[:, a0:a1],
                                        AF.Tanh, scale=0.02),
                           wait=[("s_t", tsc),
                                 ("s_t", st.get(f"pv_done{buf}", 0))])
                    sch.drain("scalar")
                    if skip_mask:
                        scur = sch.op("scalar", lambda e, buf=buf, kji=kji, a0=a0, a1=a1:
                                      e.activation(pT[buf][:, kji * 512 + a0:kji * 512 + a1],
                                                   ttmp[:, a0:a1], AF.Exp, scale=50.0),
                                      inc="s_s")
                        sch.drain("scalar")
                        bankf[bk] = ("s_s", scur)
                        pready[kji] = ("s_s", scur)
                    else:
                        scur = sch.op("scalar", lambda e, kji=kji, a0=a0, a1=a1:
                                      e.activation(pTa[kji % 2][:, a0:a1],
                                                   ttmp[:, a0:a1], AF.Exp, scale=50.0),
                                      wait=[("s_v", st.get(f"pTa_fr{kji % 2}", 0))],
                                      inc="s_s")
                        sch.drain("scalar")
                        bankf[bk] = ("s_s", scur)
                        st[f"pTa_fr{kji % 2}"] = sch.op(
                            "vector", lambda e, buf=buf, kji=kji:
                            e.tensor_mul(pT[buf][:, kji * 512:(kji + 1) * 512],
                                         pTa[kji % 2][:],
                                         maskb[:, kji * 512:(kji + 1) * 512]),
                            wait=[("s_s", scur), mw,
                                  ("s_t", st.get(f"pv_done{buf}", 0))],
                            inc="s_v")
                        pready[kji] = ("s_v", st[f"pTa_fr{kji % 2}"])
                for kji in range(12):
                    kjt = A_loc - 8 + kji
                    slot = (kjt % 16)
                    for dc in range(2):
                        bo = 2 + dc
                        sch.op("tensor",
                               lambda e, slot=slot, j=j, dc=dc, buf=buf, kji=kji, bo=bo:
                               e.matmul(pb(bo)[:, 0:512],
                                        vt[slot][:, j * 256 + dc * 128:j * 256 + (dc + 1) * 128],
                                        pT[buf][:, kji * 512:(kji + 1) * 512],
                                        start=(kji == 0), stop=(kji == 11)),
                               wait=([pready[kji]] +
                                     ([bankf[2], bankf[3]] if kji == 0 and dc == 0 else [])),
                               inc=("s_t" if (kji == 11 and dc == 1) else None))
                for kji in range(12):
                    sch.op("tensor", lambda e, buf=buf, kji=kji:
                           e.matmul(pb(4)[0:128, 0:512], ones[:],
                                    pT[buf][:, kji * 512:(kji + 1) * 512],
                                    start=(kji == 0), stop=(kji == 11)),
                           wait=([pready[kji]] + ([bankf[4]] if kji == 0 else [])),
                           inc=("s_t" if kji == 11 else None))
                st[f"pv_done{buf}"] = sch.count("s_t")
                tpv = sch.count("s_t")
                vrec = sch.op("vector", lambda e: e.reciprocal(
                    rec[:], pb(4)[:, 0:512]), wait=[("s_t", tpv)], inc="s_v")
                bankf[4] = ("s_v", vrec)
                sch.drain("vector")
                v1 = sch.op("vector", lambda e, h=h, og=og:
                            e.tensor_tensor(att[2 * h][:, og * 512:(og + 1) * 512],
                                            pb(2)[:, 0:512], rec[:], ALU.mult),
                            wait=[("s_t", tpv)], inc="s_v")
                v2 = sch.op("vector", lambda e, h=h, og=og:
                            e.tensor_tensor(att[2 * h + 1][:, og * 512:(og + 1) * 512],
                                            pb(3)[:, 0:512], rec[:], ALU.mult),
                            inc="s_v")
                bankf[2] = ("s_v", v1)
                bankf[3] = ("s_v", v2)
            st["maskb_fr"] = sch.count("s_v")
        st["attn_done_v"] = sch.count("s_v")
        st["rq_free"] = sch.count("s_t")
        st["kv_ring_free"] = sch.count("s_t")

    def wo_phase(qq):
        av = st["attn_done_v"]
        z = None
        for n in range(2):
            sch.op("vector", lambda e, n=n: e.memset(accA[n][:], 0.0),
                   wait=[("s_v", st.get("accs_fr", 0))])
            z = sch.op("vector", lambda e, n=n: e.memset(accB[n][:], 0.0),
                       inc="s_v")
        for blk in range(8):
            wws = [gdma(Wob[blk % 2][:].rearrange("p (a t) -> p a t", a=16),
                        wo_d[:, blk * 256:(blk + 1) * 256]
                        .rearrange("(a p) t -> p a t", p=128),
                        wait=[("s_t", st.get(f"wob_fr{blk % 2}", 0))]
                        + ([("s_v", av)] if blk < 2 else []))]
            vcast = None
            for po in range(2):
                oc = blk * 2 + po
                for ng in range(2):
                    bk = po * 2 + ng
                    xw = dma("sync", xck[bk][:],
                             xT_d[oc * 128:(oc + 1) * 128,
                                  1024 + qq * 1024 + ng * 512:
                                  1024 + qq * 1024 + (ng + 1) * 512],
                             wait=[("s_v", st.get(f"xck_fr{bk}", 0)),
                                   ("s_v", av)])
                    for ic in range(16):
                        sch.op("tensor",
                               lambda e, blk=blk, ic=ic, po=po, ng=ng, bk=bk:
                               e.matmul(pb(bk)[:, 0:512],
                                        Wob[blk % 2][:, ic * 256 + po * 128:ic * 256 + (po + 1) * 128],
                                        att[ic][:, ng * 512:(ng + 1) * 512],
                                        start=(ic == 0), stop=(ic == 15)),
                               wait=(wws + [bankf[bk], ("s_v", av)]
                                     if ic == 0 else ()),
                               inc=("s_t" if ic == 15 else None))
                    st[f"wob_fr{blk % 2}"] = sch.count("s_t")
                    tcur = sch.count("s_t")
                    if qq == 1:
                        vx1 = sch.op("vector",
                                     lambda e, oc=oc, ng=ng, bk=bk:
                                     e.tensor_add(x1r[oc][:, ng * 512:(ng + 1) * 512],
                                                  pb(bk)[:, 0:512], xck[bk][:]),
                                     wait=[("s_t", tcur), xw], inc="s_v")
                        x1ap = lambda oc=oc, ng=ng: x1r[oc][:, ng * 512:(ng + 1) * 512]
                    else:
                        vx1 = sch.op("vector",
                                     lambda e, ng=ng, bk=bk:
                                     e.tensor_add(x1stg[bk % 2][:],
                                                  pb(bk)[:, 0:512], xck[bk][:]),
                                     wait=[("s_t", tcur), xw,
                                           st.get(f"x1stg_fr{bk % 2}", ("s_v", 0))],
                                     inc="s_v")
                        st[f"x1stg_fr{bk % 2}"] = dma(
                            "scalar", x1q0_d[oc * 128:(oc + 1) * 128,
                                             ng * 512:(ng + 1) * 512],
                            x1stg[bk % 2][:], wait=[("s_v", vx1)])
                        x1ap = lambda bk=bk: x1stg[bk % 2][:]
                    st[f"xck_fr{bk}"] = vx1
                    bankf[bk] = ("s_v", vx1)
                    vb = sch.op("vector", lambda e, x1ap=x1ap: e.tensor_copy(
                        x1b_s[:], x1ap()),
                        wait=[("s_v", vx1), ("s_t", st.get("x1bs_fr", 0))],
                        inc="s_v")
                    sq = sch.op("scalar", lambda e: e.activation(
                        x1sq_s[:], x1b_s[:], AF.Square), wait=[("s_v", vb)],
                        inc="s_s")
                    t5 = sch.op("tensor", lambda e: e.matmul(
                        pb(5)[:, 0:512], ones[:], x1b_s[:], start=True, stop=True),
                        wait=[bankf[5], ("s_v", vb)], inc="s_t")
                    t6 = sch.op("tensor", lambda e: e.matmul(
                        pb(6)[:, 0:512], ones[:], x1sq_s[:], start=True, stop=True),
                        wait=[bankf[6], ("s_s", sq)], inc="s_t")
                    st["x1bs_fr"] = t6
                    va = sch.op("vector", lambda e, ng=ng: e.tensor_add(
                        accA[ng][:], accA[ng][:], pb(5)[:, 0:512]),
                        wait=[("s_t", t5)], inc="s_v")
                    bankf[5] = ("s_v", va)
                    sch.drain("vector")
                    vb2 = sch.op("vector", lambda e, ng=ng: e.tensor_add(
                        accB[ng][:], accB[ng][:], pb(6)[:, 0:512]),
                        wait=[("s_t", t6)], inc="s_v")
                    bankf[6] = ("s_v", vb2)
                    sch.drain("vector")
        for ng in range(2):
            coff = qq * 1024 + ng * 512
            sch.op("vector", lambda e, ng=ng, coff=coff: e.tensor_scalar_mul(
                m2_rep[:, coff:coff + 512], accA[ng][:], 1.0 / 2048.0))
            sch.drain("vector")
            sch.op("vector", lambda e, coff=coff: e.tensor_mul(
                m2scr[:], m2_rep[:, coff:coff + 512], m2_rep[:, coff:coff + 512]))
            sch.drain("vector")
            vv2 = sch.op("vector", lambda e, ng=ng: e.scalar_tensor_tensor(
                m2scr[:], accB[ng][:], 1.0 / 2048.0, m2scr[:],
                ALU.mult, ALU.subtract), inc="s_v")
            s2 = rsqrt_act(r2p, m2scr, LN_EPS, wait=[("s_v", vv2)])
            sch.op("vector", lambda e: e.tensor_mul(r3s[:], r2p[:], r2p[:]),
                   wait=[("s_s", s2)])
            sch.drain("vector")
            vv3 = sch.op("vector", lambda e: e.tensor_mul(
                r3s[:], r3s[:], m2scr[:]), inc="s_v")
            s3 = rsqrt_act(r3s, r3s, EPS_NORM, wait=[("s_v", vv3)])
            vv4 = sch.op("vector", lambda e, coff=coff: e.tensor_mul(
                rr_rep[:, coff:coff + 512], r2p[:], r3s[:]),
                wait=[("s_s", s3)], inc="s_v")
        st["accs_fr"] = sch.count("s_v")

    def max_pass(qq):
        coff = qq * 1024
        for oc in range(16):
            spill_w = ([st[f"x1stg_fr{i}"] for i in range(2)
                        if f"x1stg_fr{i}" in st] if oc == 0 else [])
            if qq == 1:
                xsrc = x1r[oc]
                w = spill_w
            else:
                w = [dma("scalar", mxstg[oc % 2][:],
                         x1q0_d[oc * 128:(oc + 1) * 128, 0:1024],
                         wait=[("s_v", st.get(f"mx_fr{oc % 2}", 0))])] + spill_w
                xsrc = mxstg[oc % 2]
            vm = sch.op("vector", lambda e, xsrc=xsrc, coff=coff: e.tensor_sub(
                xnt[:], xsrc[:, 0:1024], m2_rep[:, coff:coff + 1024]),
                wait=w, inc="s_v")
            st[f"mx_fr{oc % 2}"] = vm
            sch.drain("vector")
            sch.op("vector", lambda e, oc=oc, coff=coff: e.scalar_tensor_tensor(
                xnu[:], xnt[:], gam16[:, oc:oc + 1], rr_rep[:, coff:coff + 1024],
                ALU.mult, ALU.mult))
            sch.drain("vector")
            sch.op("vector", lambda e: e.tensor_reduce(
                smalls[:, SM["rmx"]:SM["rmx"] + 1], xnu[:], AXX, ALU.max,
                apply_absolute_value=True))
            sch.drain("vector")
            sch.op("vector", lambda e: e.tensor_max(
                smalls[:, SM["runmax"]:SM["runmax"] + 1],
                smalls[:, SM["runmax"]:SM["runmax"] + 1],
                smalls[:, SM["rmx"]:SM["rmx"] + 1]), inc="s_v")
            sch.drain("vector")

    def collective_trigger():
        tcur = sch.op("tensor", lambda e: e.transpose(
            pb(7)[0:1, 0:128], smalls[:, SM["runmax"]:SM["runmax"] + 1], ident[:]),
            wait=[("s_v", sch.count("s_v")), bankf[7]], inc="s_t")
        sch.op("vector", lambda e: e.tensor_reduce(
            smalls[0:1, SM["lmax"]:SM["lmax"] + 1], pb(7)[0:1, 0:128], AXX,
            ALU.max), wait=[("s_t", tcur)])
        sch.drain("vector")
        sch.op("vector", lambda e: e.tensor_scalar_max(
            smalls[0:1, SM["lmax"]:SM["lmax"] + 1],
            smalls[0:1, SM["lmax"]:SM["lmax"] + 1], EPS_SCALE))
        sch.drain("vector")
        vv1 = sch.op("vector", lambda e: e.tensor_copy(
            ccs[0:1, 0:1], smalls[0:1, SM["lmax"]:SM["lmax"] + 1]), inc="s_v")
        bankf[7] = ("s_v", vv1)
        w = dma("sync", ccin_d, ccs[:], wait=[("s_v", vv1)])
        g1 = sch.op("gpsimd", lambda e: e.collective_compute(
            "AllReduce", ALU.max, replica_groups=[[0, 1, 2, 3, 4, 5, 6, 7]],
            ins=[ccin_d], outs=[ccout_d]), wait=[w], inc="s_g")
        g2 = sch.op("gpsimd", lambda e: e.dma_start(out=ccs[:], in_=ccout_d),
                    wait=[("s_g", g1)], inc="s_g2", inc_by=16)
        st["g2"] = g2

    def collective_scales():
        g2 = st["g2"]
        w2 = dma("sync", smalls[:, SM["Scol"]:SM["Scol"] + 1],
                 ccout_d[0:1, 0:1].partition_broadcast(128), wait=[("s_g2", g2)])
        sch.op("vector", lambda e: e.reciprocal(
            smalls[:, SM["rS127"]:SM["rS127"] + 1],
            smalls[:, SM["Scol"]:SM["Scol"] + 1]), wait=[w2])
        sch.drain("vector")
        sch.op("vector", lambda e: e.tensor_scalar_mul(
            smalls[:, SM["rS127"]:SM["rS127"] + 1],
            smalls[:, SM["rS127"]:SM["rS127"] + 1], 127.0))
        sch.drain("vector")
        sch.op("vector", lambda e: e.tensor_mul(
            smalls[:, SM["fscale"]:SM["fscale"] + 1],
            smalls[:, SM["Scol"]:SM["Scol"] + 1],
            smalls[:, SM["swcol"]:SM["swcol"] + 1]))
        sch.drain("vector")
        sch.op("vector", lambda e: e.tensor_scalar_mul(
            smalls[:, SM["fscale"]:SM["fscale"] + 1],
            smalls[:, SM["fscale"]:SM["fscale"] + 1], -1.0 / 127.0))
        sch.drain("vector")
        vv2 = sch.op("vector", lambda e: e.tensor_scalar_mul(
            rr_rep[:], rr_rep[:], smalls[:, SM["rS127"]:SM["rS127"] + 1]),
            inc="s_v")
        sch.drain("vector")
        if DEBUG:
            dma("scalar", dbg["sS"], ccs[:], wait=[("s_g2", g2)])
        st["scales_v"] = vv2

    def wq_load_phase():
        base = [("s_v", st["accs_fr"]), ("s_t", st["rq_free2"])]
        late = [("s_v", st["pre_coll_v"]), ("s_t", st["rq_free2"])]
        st["wq_loads"] = []
        for p in range(16):
            q = "scalar" if p % 2 == 0 else "gpsimd"
            wt = late if p in (4, 5) else base
            if q == "scalar":
                st["wq_loads"].append(dma("scalar", wqTt[p][:],
                                          wqT_d[p * 128:(p + 1) * 128, :],
                                          wait=wt))
            else:
                st["wq_loads"].append(gdma(wqTt[p][:],
                                           wqT_d[p * 128:(p + 1) * 128, :],
                                           wait=wt))

    def ffn():
        groups = [(1, 0), (1, 1), (0, 0), (0, 1)]

        def prefetch(gi):
            qqf, ng = groups[gi]
            if qqf != 0:
                return []
            rj = x1rA[:].rearrange("p (a t) -> p a t", a=16)
            w = dma("scalar",
                    rj[:, :, ng * 512:(ng + 1) * 512],
                    x1q0_d[:, ng * 512:(ng + 1) * 512]
                    .rearrange("(a p) t -> p a t", p=128),
                    wait=[("s_v", st.get(f"x1half_fr{ng}", 0))])
            return [w]

        def quant_chunk(gi, ic, x1w):
            qqf, ng = groups[gi]
            par = gi % 2
            xqg = xq2[par]
            coff = qqf * 1024 + ng * 512
            xsl = lambda oc, ng=ng: x1r[oc][:, ng * 512:(ng + 1) * 512]
            sch.op("vector", lambda e, ic=ic, coff=coff, xsl=xsl: e.tensor_sub(
                qnt[0][:], xsl(ic), m2_rep[:, coff:coff + 512]),
                wait=list(x1w) + [("s_t", st.get(f"xq_fr{par}", 0))])
            sch.drain("vector")
            sch.op("vector", lambda e, ic=ic, coff=coff: e.scalar_tensor_tensor(
                qnt[1][:], qnt[0][:], gam16[:, ic:ic + 1],
                rr_rep[:, coff:coff + 512], ALU.mult, ALU.mult))
            sch.drain("vector")
            vcl = sch.op("vector", lambda e: e.tensor_scalar(
                qnt[0][:], qnt[1][:], 127.0, -128.0, ALU.min, ALU.max),
                inc="s_v")
            sch.op("scalar", lambda e: e.copy(qni[:], qnt[0][:]),
                   wait=[("s_v", vcl)], inc="s_s")
            sch.drain("scalar")
            scp2 = sch.op("scalar", lambda e: e.copy(qnt[1][:], qni[:]),
                          inc="s_s")
            sch.op("vector", lambda e: e.tensor_sub(qnt[2][:], qnt[0][:],
                                                    qnt[1][:]),
                   wait=[("s_s", scp2)])
            sch.drain("vector")
            sch.op("vector", lambda e: e.scalar_tensor_tensor(
                qnt[0][:], qnt[2][:], 0.5, qnt[1][:], ALU.is_gt, ALU.add))
            sch.drain("vector")
            v = sch.op("vector", lambda e, ic=ic, xqg=xqg: e.scalar_tensor_tensor(
                xqg[ic][:], qnt[2][:], -0.5, qnt[0][:], ALU.is_lt,
                ALU.subtract), inc="s_v")
            sch.drain("vector")
            return v

        x1w = prefetch(0)
        for ic in range(16):
            quant_chunk(0, ic, x1w)
        vq = {0: sch.count("s_v")}
        x1w_next = prefetch(1)
        for gi, (qqf, ng) in enumerate(groups):
            par = gi % 2
            xqg = xq2[par]
            coff = qqf * 1024 + ng * 512
            xsl = lambda oc, ng=ng: x1r[oc][:, ng * 512:(ng + 1) * 512]
            for oc in range(16):
                bk = oc % 2
                for ic in range(16):
                    sch.op("tensor", lambda e, oc=oc, ic=ic, bk=bk, xqg=xqg:
                           e.matmul(pb(bk)[:, 0:512],
                                    wqTt[ic][:, oc * 128:(oc + 1) * 128],
                                    xqg[ic][:], start=(ic == 0), stop=(ic == 15)),
                           wait=(([bankf[bk], ("s_v", vq[gi])]
                                  + (st["wq_loads"] if gi == 0 and oc == 0
                                     else []))
                                 if ic == 0 else ()),
                           inc=("s_t" if ic == 15 else None))
                tcur = sch.count("s_t")
                vo = sch.op("vector", lambda e, oc=oc, bk=bk, xsl=xsl:
                            e.scalar_tensor_tensor(
                                ostg[bk][:], pb(bk)[:, 0:512],
                                smalls[:, SM["fscale"]:SM["fscale"] + 1],
                                xsl(oc), ALU.mult, ALU.add),
                            wait=[("s_t", tcur),
                                  st.get(f"ostg_fr{bk}", ("s_v", 0))], inc="s_v")
                bankf[bk] = ("s_v", vo)
                st[f"ostg_fr{bk}"] = dma(
                    "scalar", out_d[oc * 128:(oc + 1) * 128, coff:coff + 512],
                    ostg[bk][:], wait=[("s_v", vo)])
                if gi + 1 < 4:
                    quant_chunk(gi + 1, oc, x1w_next)
            st[f"xq_fr{par}"] = sch.count("s_t")
            st[f"x1half_fr{ng}"] = sch.count("s_v")
            if gi + 1 < 4:
                vq[gi + 1] = sch.count("s_v")
            if gi + 2 < 4:
                x1w_next = prefetch(gi + 2)

    def dbg_dump(qq):
        if not DEBUG or qq != 0:
            return
        v = sch.count("s_v")
        s = sch.count("s_s")
        dma("scalar", dbg["r1"], r1_rep[:], wait=[("s_s", s), ("s_v", v)])
        dma("scalar", dbg["k00"], kT[0][0][:], wait=[("s_v", v)])
        dma("scalar", dbg["v8"], vt[8][:], wait=[("s_v", v)])

    def dbg_dump_q(qq):
        if not DEBUG or qq != 0:
            return
        v = sch.count("s_v")
        dma("scalar", dbg["q0"], qT[0][:], wait=[("s_v", v)])
        dma("scalar", dbg["q1"], qT[1][:], wait=[("s_v", v)])

    def dbg_dump_attn(qq):
        if not DEBUG or qq != 0:
            return
        v = sch.count("s_v")
        dma("scalar", dbg["attn0"], att[0][:], wait=[("s_v", v)])
        dma("scalar", dbg["attn1"], att[1][:], wait=[("s_v", v)])

    def dbg_dump_wo(qq):
        if not DEBUG:
            return
        v = sch.count("s_v")
        if qq == 1:
            dma("scalar", dbg["x1c0"], x1r[0][:], wait=[("s_v", v)])
            dma("scalar", dbg["m2"], m2_rep[:], wait=[("s_v", v)])
            dma("scalar", dbg["rr"], rr_rep[:], wait=[("s_v", v)])

    st["gate"] = [("s_v", CONST_V)]
    for qq in (0, 1):
        kv_phase(qq)
        dbg_dump(qq)
        q_phase(qq)
        dbg_dump_q(qq)
        attn_phase(qq)
        dbg_dump_attn(qq)
        st["accs_fr"] = st.get("accs_fr", 0)
        wo_phase(qq)
        if qq == 0:
            st["gate"] = [("s_t", sch.count("s_t")),
                          ("s_v", sch.count("s_v")),
                          ("s_s", sch.count("s_s"))]
        max_pass(qq)
        dbg_dump_wo(qq)
        if qq == 0:
            st["mx0_v"] = sch.count("s_v")
    st["pre_coll_v"] = sch.count("s_v")
    st["rq_free2"] = sch.count("s_t")
    wq_load_phase()
    collective_trigger()
    collective_scales()
    ffn()

    if DEBUG:
        dma("scalar", dbg["smalls"], smalls[:], wait=[("s_v", sch.count("s_v"))])
        dma("scalar", dbg["wabs"], wabs[:], wait=[("s_v", sch.count("s_v"))])
    for sem_n, v in dma_barrier("scalar") + dma_barrier("sync"):
        sch.prog["gpsimd"].append(
            lambda e, sems, sem_n=sem_n, v=v: e.wait_ge(sems[sem_n], v))

    with nc.Block() as block:
        @block.sync
        def _(eng):
            for f in sch.prog["sync"]:
                f(eng, sems)

        @block.scalar
        def _(eng):
            for f in sch.prog["scalar"]:
                f(eng, sems)

        @block.vector
        def _(eng):
            for f in sch.prog["vector"]:
                f(eng, sems)

        @block.tensor
        def _(eng):
            for f in sch.prog["tensor"]:
                f(eng, sems)

        @block.gpsimd
        def _(eng):
            for f in sch.prog["gpsimd"]:
                f(eng, sems)

    return nc, dbg


def _host_prep(inputs):
    x = np.asarray(inputs["L0_attn_x"], np.float32)
    wq = np.ascontiguousarray(np.asarray(inputs["L0_attn_Wq"], np.float32))
    wk = np.ascontiguousarray(np.asarray(inputs["L0_attn_Wk"], np.float32))
    wv = np.ascontiguousarray(np.asarray(inputs["L0_attn_Wv"], np.float32))
    wo = np.ascontiguousarray(np.asarray(inputs["L0_attn_Wo"], np.float32))
    wt = np.ascontiguousarray(np.asarray(inputs["L0_ffn_weight"], np.float32).T)
    qsc = np.asarray(inputs["L0_attn_q_scale"], np.float32)
    ksc = np.asarray(inputs["L0_attn_k_scale"], np.float32)
    gam = np.asarray(inputs["L0_ffn_norm_weight"], np.float32)

    qs2 = np.ascontiguousarray(qsc.reshape(2, 128).T)
    ks2 = np.ascontiguousarray(ksc.reshape(2, 128).T)
    gam16 = np.ascontiguousarray(gam.reshape(16, 128).T)
    ones128 = np.ones((128, 128), dtype=BF)
    ident = np.eye(128, dtype=np.float32)

    half = 64
    theta = 1.0 / (10000.0 ** (np.arange(0, ROT_, 2, dtype=np.float32) / ROT_))
    in_maps = []
    for b in range(B):
        xb_T = np.ascontiguousarray(x[b].T)
        for hh in range(2):
            lo = hh * 2048 - 1024
            xT = np.zeros((E, NLOC), np.float32)
            s0 = max(0, lo)
            xT[:, s0 - lo:3072] = xb_T[:, s0:lo + 3072]
            pos = np.arange(lo, lo + NLOC, dtype=np.float32)
            ang = pos[None, :] * theta[:, None]
            cosT = np.zeros((128, NLOC), np.float32)
            sinT = np.zeros((128, NLOC), np.float32)
            cosT[0::2] = np.cos(ang)
            cosT[1::2] = np.cos(ang)
            sinT[0::2] = -np.sin(ang)
            sinT[1::2] = np.sin(ang)
            masks = np.zeros((48, 128, 512), dtype=BF)
            for qq in range(2):
                for og in range(2):
                    A_loc = 8 + qq * 8 + og * 4
                    for kji in range(12):
                        kjt = A_loc - 8 + kji
                        K = lo + kjt * 128 + np.arange(128)[:, None]
                        Q = lo + A_loc * 128 + np.arange(512)[None, :]
                        ok = (K >= 0) & (K <= Q) & (Q - K < WIN_)
                        masks[(qq * 2 + og) * 12 + kji] = ok.astype(BF)
            in_maps.append({
                "xT": xT, "wq": wq, "wk": wk, "wv": wv, "wo": wo, "wt": wt,
                "qs2": qs2, "ks2": ks2, "gam": gam16,
                "cosT": cosT, "sinT": sinT, "masks": masks,
                "ones128": ones128, "ident128": ident,
            })
    return in_maps


ROT_ = 128
WIN_ = 1024

_NC_CACHE = {}


def kernel(**inputs):
    key = "nc"
    if key not in _NC_CACHE:
        _NC_CACHE[key] = build_nc()
    nc, dbg = _NC_CACHE[key]
    in_maps = _host_prep(inputs)
    res = run_bass_kernel_spmd(nc, in_maps, core_ids=list(range(8)))
    out = np.zeros((B, S, E), np.float32)
    for b in range(B):
        for hh in range(2):
            core = b * 2 + hh
            out[b, hh * 2048:(hh + 1) * 2048, :] = res.results[core]["out"].T
    if DEBUG:
        kernel._last = res
    return out
